# version 05
# baseline (speedup 1.0000x reference)
import numpy as np
from contextlib import ExitStack
import concourse.bass as bass
import concourse.mybir as mybir
from concourse.bass_utils import run_bass_kernel_spmd

F32 = mybir.dt.float32
BF16 = mybir.dt.bfloat16
AF = mybir.ActivationFunctionType
ALU = mybir.AluOpType
AX = mybir.AxisListType

B, T, D = 2, 8192, 1024
NCORE = 8
GRP = 4
SC = 1024
NSC = T // SC
TOK = 2 * SC
DCH = D // 128
EPS_LN = 1e-5
RG = [[0, 1, 2, 3], [4, 5, 6, 7]]
QCOL = 512
NQC = TOK // QCOL


SLOC = TOK
QB = 512
NQB = SLOC // QB


class Cx:
    ENGS = ("t", "v", "s", "g", "q")

    def __init__(self, nc, block, stack, core_id_hint=0):
        self.nc = nc
        self.block = block
        self.stack = stack
        self.sems = {}
        self.cnt = {}
        self.bs = self._sem("bs")
        self.bv = 0
        self.last = {e: None for e in self.ENGS}
        self.last_has_inc = {e: False for e in self.ENGS}
        self.sealed = {e: None for e in self.ENGS}
        self.waited = {e: True for e in self.ENGS}
        for nm in ("ds", "gs", "cc", "pa", "pb", "pc", "pd", "pe_", "pf",
                   "pg", "ph", "sa", "ga", "as_", "ax", "am", "av", "ab",
                   "ay"):
            self._sem(nm)
        self.buf = {}

    def _sem(self, name):
        h = self.stack.enter_context(self.nc.semaphore(name))
        self.sems[name] = h
        self.cnt[name] = 0
        return h

    def sem(self, name):
        return self.sems[name]

    def bump(self, name, by=1):
        self.cnt[name] += by
        return self.cnt[name]

    def _eng(self, name):
        return dict(t=self.block.tensor, v=self.block.vector,
                    s=self.block.scalar, g=self.block.gpsimd,
                    q=self.block.sync)[name]

    def on(self, ename, fn):
        need_wait = (not self.waited[ename]) and self.bv > 0
        bv = self.bv

        def body(eng):
            if need_wait:
                eng.wait_ge(self.bs, bv)
            last = fn(eng)
            if last is not None:
                self.last[ename] = last
                self.last_has_inc[ename] = False
        self._eng(ename)(body)
        self.waited[ename] = True

    def seal(self, ename, semname):
        self.sealed[ename] = (semname, self.cnt[semname])
        self.last[ename] = None

    def barrier(self):
        n = 0
        for e in self.ENGS:
            ins = self.last[e]
            if ins is not None:
                assert not self.last_has_inc[e]
                ins.then_inc(self.bs, 1)
                n += 1
            elif self.sealed[e] is not None:
                semname, val = self.sealed[e]
                sem = self.sems[semname]
                bsl = self.bs

                def carrier(eng, e=e, sem=sem, val=val):
                    eng.wait_ge(sem, val)
                    if e == "v":
                        eng.engine_nop().then_inc(bsl, 1)
                    else:
                        eng.nop().then_inc(bsl, 1)
                self._eng(e)(carrier)
                n += 1
            self.last[e] = None
            self.last_has_inc[e] = False
            self.sealed[e] = None
        self.bv += n
        for e in self.ENGS:
            self.waited[e] = False

    def dma(self, ename, dst, src, sem=None, slow=False):
        sem = sem or ("gs" if ename == "g" else "ds")
        tgt = self.bump(sem, 16)
        s = self.sems[sem]
        need_wait = (not self.waited[ename]) and self.bv > 0
        bv = self.bv

        def body(eng):
            if need_wait:
                eng.wait_ge(self.bs, bv)
            eng.dma_start(dst, src,
                          allow_slow_non_contiguous=slow).then_inc(s, 16)
        self._eng(ename)(body)
        self.waited[ename] = True
        return tgt

    def wait(self, ename, sem, val):
        need_wait = (not self.waited[ename]) and self.bv > 0
        bv = self.bv

        def body(eng):
            if need_wait:
                eng.wait_ge(self.bs, bv)
            eng.wait_ge(self.sems[sem], val)
        self._eng(ename)(body)
        self.waited[ename] = True


def alloc_pool(cx):
    nc, st = cx.nc, cx.stack
    b = cx.buf
    b["xs"] = st.enter_context(nc.sbuf_tensor("xs", [128, DCH * TOK], F32))
    b["xnf"] = st.enter_context(nc.sbuf_tensor("xnf", [128, DCH * TOK], BF16))
    b["ar1"] = st.enter_context(nc.sbuf_tensor("ar1", [128, DCH * TOK], BF16))
    b["ar2"] = st.enter_context(nc.sbuf_tensor("ar2", [128, DCH * TOK], BF16))
    b["ar3"] = st.enter_context(nc.sbuf_tensor("ar3", [128, DCH * TOK], BF16))
    b["wsl"] = st.enter_context(nc.sbuf_tensor("wsl", [128, 4096], BF16))
    b["tsc"] = st.enter_context(nc.sbuf_tensor("tsc", [128, QCOL], F32))
    b["onesb"] = st.enter_context(nc.sbuf_tensor("onesb", [128, 2], BF16))
    b["ones1"] = st.enter_context(nc.sbuf_tensor("ones1", [128, 128], F32))
    b["vrow"] = st.enter_context(nc.sbuf_tensor("vrow", [128, 64], F32))
    b["msk"] = st.enter_context(nc.sbuf_tensor("msk", [128, 768], BF16))
    b["ivals"] = st.enter_context(nc.sbuf_tensor("ivals", [1, 16],
                                                  mybir.dt.int32))
    b["zrow"] = st.enter_context(nc.sbuf_tensor("zrow", [128, 16], BF16))
    b["idf"] = st.enter_context(nc.sbuf_tensor("idf", [128, 128], F32))
    b["p"] = [st.enter_context(nc.psum_tensor(f"p{i}", [128, QCOL], F32))
              for i in range(8)]
    return b


def emit_init_consts(cx):
    b = cx.buf

    def vf(v):
        v.memset(b["onesb"][:], 1.0)
        v.memset(b["zrow"][:], 0.0)
        return v.memset(b["ones1"][:], 1.0)
    cx.on("v", vf)
    cx.barrier()


def emit_norm(cx, src_of, dst_of, kind="ln", w_slice=None, eps=EPS_LN):
    b = cx.buf
    p = b["p"]
    xsb, sq = b["ar1"], b["ar2"]
    onesb, ones1, tsc = b["onesb"], b["ones1"], b["tsc"]
    need_mean = (kind == "ln")

    ph0 = cx.cnt["ph"]

    def vf(v):
        for ch in range(DCH):
            v.tensor_copy(xsb[:, ch * TOK:(ch + 1) * TOK],
                          src_of(ch)).then_inc(cx.sem("ph"), 1)
        cx.bump("ph", DCH)
        return None
    cx.on("v", vf)

    def sf(s):
        last = None
        for ch in range(DCH):
            s.wait_ge(cx.sem("ph"), ph0 + ch + 1)
            last = s.activation(sq[:, ch * TOK:(ch + 1) * TOK],
                                xsb[:, ch * TOK:(ch + 1) * TOK], AF.Square)
        return last
    cx.on("s", sf)
    cx.barrier()

    pc0 = cx.cnt["pc"]
    pd0 = cx.cnt["pd"]
    pe0 = cx.cnt["pe_"]
    pf0 = cx.cnt["pf"]
    pg0 = cx.cnt["pg"]

    def tf(t):
        for qc in range(NQC):
            if qc >= 1:
                t.wait_ge(cx.sem("pd"), pd0 + qc)
            last = None
            for ch in range(DCH):
                csl = slice(ch * TOK + qc * QCOL, ch * TOK + (qc + 1) * QCOL)
                last = t.matmul(p[4][0:1, :], onesb[:, 0:1], sq[:, csl],
                                start=(ch == 0), stop=(ch == DCH - 1))
            if need_mean:
                for ch in range(DCH):
                    csl = slice(ch * TOK + qc * QCOL,
                                ch * TOK + (qc + 1) * QCOL)
                    last = t.matmul(p[5][0:1, :], onesb[:, 0:1], xsb[:, csl],
                                    start=(ch == 0), stop=(ch == DCH - 1))
            last.then_inc(cx.sem("pc"), 1)
            t.wait_ge(cx.sem("pe_"), pe0 + qc + 1)
            if qc >= 1:
                t.wait_ge(cx.sem("pg"), pg0 + qc)
            lb = t.matmul(p[6][:, :], ones1[64:65, :], tsc[64:65, 0:QCOL],
                          start=True, stop=True)
            if need_mean:
                lb = t.matmul(p[7][:, :], ones1[32:33, :], tsc[32:33, 0:QCOL],
                              start=True, stop=True)
            lb.then_inc(cx.sem("pf"), 1)
        cx.bump("pc", NQC)
        cx.bump("pf", NQC)
        return None
    cx.on("t", tf)

    def vwork(v):
        for qc in range(NQC):
            v.wait_ge(cx.sem("pc"), pc0 + qc + 1)
            if kind == "rmsl1":
                v.tensor_scalar_max(tsc[64:65, 0:QCOL], p[4][0:1, :], 1e-24)
            elif kind == "rms":
                v.tensor_scalar(tsc[64:65, 0:QCOL], p[4][0:1, :], 1.0 / D,
                                eps, ALU.mult, ALU.add)
            else:
                v.tensor_scalar_mul(tsc[32:33, 0:QCOL], p[5][0:1, :],
                                    1.0 / D).then_inc(cx.sem("pa"), 1)
                cx.bump("pa")
                v.wait_ge(cx.sem("pa"), cx.cnt["pa"])
                v.tensor_mul(tsc[0:1, 0:QCOL], tsc[32:33, 0:QCOL],
                             tsc[32:33, 0:QCOL]).then_inc(cx.sem("pa"), 1)
                cx.bump("pa")
                v.wait_ge(cx.sem("pa"), cx.cnt["pa"])
                v.scalar_tensor_tensor(tsc[64:65, 0:QCOL], p[4][0:1, :],
                                       1.0 / D, tsc[0:1, 0:QCOL], ALU.mult,
                                       ALU.subtract).then_inc(cx.sem("pa"), 1)
                cx.bump("pa")
                v.wait_ge(cx.sem("pa"), cx.cnt["pa"])
                v.tensor_scalar_add(tsc[64:65, 0:QCOL], tsc[64:65, 0:QCOL],
                                    eps)
            v.engine_nop().then_inc(cx.sem("pd"), 1)
            cx.bump("pd")
            v.wait_ge(cx.sem("pf"), pf0 + qc + 1)
            last = None
            for ch in range(DCH):
                sl = slice(qc * QCOL, (qc + 1) * QCOL)
                s_ap = src_of(ch)[:, sl]
                d_ap = dst_of(ch)[:, sl]
                if need_mean:
                    v.tensor_sub(tsc[:, 0:QCOL], s_ap,
                                 p[7][:, :]).then_inc(cx.sem("pa"), 1)
                    cx.bump("pa")
                    v.wait_ge(cx.sem("pa"), cx.cnt["pa"])
                    if w_slice is None:
                        last = v.tensor_tensor(d_ap, tsc[:, 0:QCOL],
                                               p[6][:, :], ALU.mult)
                    else:
                        last = v.scalar_tensor_tensor(
                            d_ap, tsc[:, 0:QCOL], w_slice(ch), p[6][:, :],
                            ALU.mult, ALU.mult)
                    last.then_inc(cx.sem("pa"), 1)
                    cx.bump("pa")
                    v.wait_ge(cx.sem("pa"), cx.cnt["pa"])
                elif w_slice is None:
                    last = v.tensor_tensor(d_ap, s_ap, p[6][:, :], ALU.mult)
                else:
                    last = v.scalar_tensor_tensor(d_ap, s_ap, w_slice(ch),
                                                  p[6][:, :], ALU.mult,
                                                  ALU.mult)
            v.engine_nop().then_inc(cx.sem("pg"), 1)
            cx.bump("pg")
        return None
    cx.on("v", vwork)
    cx.seal("v", "pg")

    def sstat(s):
        for qc in range(NQC):
            s.wait_ge(cx.sem("pd"), pd0 + qc + 1)
            s.activation(tsc[0:1, 0:QCOL], tsc[64:65, 0:QCOL],
                         AF.Ln).then_inc(cx.sem("sa"), 1)
            cx.bump("sa")
            s.wait_ge(cx.sem("sa"), cx.cnt["sa"])
            s.activation(tsc[64:65, 0:QCOL], tsc[0:1, 0:QCOL], AF.Exp,
                         scale=-0.5).then_inc(cx.sem("sa"), 1)
            cx.bump("sa")
            s.wait_ge(cx.sem("sa"), cx.cnt["sa"])
            if kind == "rmsl1":
                s.mul(tsc[64:65, 0:QCOL], tsc[64:65, 0:QCOL],
                      float(np.sqrt(D))).then_inc(cx.sem("pe_"), 1)
            else:
                s.copy(tsc[64:65, 0:QCOL],
                       tsc[64:65, 0:QCOL]).then_inc(cx.sem("pe_"), 1)
            cx.bump("pe_")
        return None
    cx.on("s", sstat)
    cx.barrier()


WSLAB = 2048


def emit_proj(cx, w_dram, xn_of, n_in, n_out, evac_v=None, evac_s=None,
              nqc=NQC, banks=(4, 5), row_off=0, col_off=0, qcw=QCOL):
    b = cx.buf
    p = b["p"]
    wsl = b["wsl"]
    kch = n_in // 128
    slw = min(max(128, (WSLAB // kch) // 128 * 128), n_out, 512)
    nslab = (n_out + slw - 1) // slw
    och_per = slw // 128
    slab_elems = kch * slw
    assert slab_elems <= WSLAB

    gs0 = cx.cnt["gs"]
    pc0 = cx.cnt["pc"]
    pd0 = cx.cnt["pd"]
    pf0 = cx.cnt["pf"]

    def gl(g):
        for si in range(nslab):
            if si >= 1:
                g.wait_ge(cx.sem("gs"), gs0 + 16 * si)
            if si >= 2:
                g.wait_ge(cx.sem("pf"), pf0 + si - 1)
            co = col_off + si * slw
            wr = w_dram[row_off:row_off + n_in, co:co + slw].rearrange(
                "(kc p) o -> p kc o", p=128)
            dst = wsl[:, (si % 2) * WSLAB:(si % 2) * WSLAB +
                      slab_elems].rearrange("p (kc o) -> p kc o", kc=kch)
            g.dma_start(dst, wr).then_inc(cx.sem("gs"), 16)
        return None
    cx.on("g", gl)
    cx.bump("gs", 16 * nslab)

    pairs = []
    for si in range(nslab):
        for ol in range(och_per):
            for qc in range(nqc):
                pairs.append((si, ol, qc))

    def tf(t):
        for i, (si, ol, qc) in enumerate(pairs):
            pp = p[banks[i % 2]]
            if i >= 2:
                t.wait_ge(cx.sem("pd"), pd0 + i - 1)
            if ol == 0 and qc == 0:
                t.wait_ge(cx.sem("gs"), gs0 + 16 * (si + 1))
            sl = slice(qc * qcw, (qc + 1) * qcw)
            base = (si % 2) * WSLAB + ol * 128
            last = None
            for k in range(kch):
                last = t.matmul(pp[:, 0:qcw], wsl[:, base + k * slw:
                                                  base + k * slw + 128],
                                xn_of(k)[:, sl], start=(k == 0),
                                stop=(k == kch - 1))
            last.then_inc(cx.sem("pc"), 1)
        cx.bump("pc", len(pairs))
        return None
    cx.on("t", tf)

    ename = "v" if evac_v is not None else "s"
    evac = evac_v if evac_v is not None else evac_s

    def ef(e):
        for i, (si, ol, qc) in enumerate(pairs):
            e.wait_ge(cx.sem("pc"), pc0 + i + 1)
            o = si * och_per + ol
            last = evac(e, o, qc, p[banks[i % 2]])
            last.then_inc(cx.sem("pd"), 1)
            if ol == och_per - 1 and qc == nqc - 1:
                if ename == "v":
                    e.engine_nop().then_inc(cx.sem("pf"), 1)
                else:
                    e.nop().then_inc(cx.sem("pf"), 1)
        cx.bump("pd", len(pairs))
        cx.bump("pf", nslab)
        return None
    cx.on(ename, ef)
    cx.seal(ename, "pd")
    cx.barrier()


L2F_STOP = 99


def emit_l2_ffn(cx, W):
    b = cx.buf
    xs, xnf = b["xs"], b["xnf"]
    ar1, ar2, ar3 = b["ar1"], b["ar2"], b["ar3"]
    vrow, tsc = b["vrow"], b["tsc"]

    def xs_of(ch):
        return xs[:, ch * TOK:(ch + 1) * TOK]

    def xnf_of(ch):
        return xnf[:, ch * TOK:(ch + 1) * TOK]

    def ar1_of(k):
        return ar1[:, k * TOK:(k + 1) * TOK]

    def ar2_of(k):
        return ar2[:, k * TOK:(k + 1) * TOK]

    cx.dma("q", vrow[:, 0:DCH], W["l2f_mu_r"][:, :])
    cx.dma("q", vrow[:, DCH:2 * DCH], W["l2f_mu_k"][:, :])
    cx.dma("q", b["ivals"][0:1, 0:1], W["prev_base"][0:1, 0:1])

    emit_norm(cx, xs_of, xnf_of, kind="ln")
    if L2F_STOP <= 1:
        return

    ci = W["xg_in"]
    ci3a = ci[0:D, :].rearrange("(c p) t -> p c t", p=128)
    xnf3 = xnf.rearrange("p (c t) -> p c t", c=DCH)
    cx.dma("q", ci3a, xnf3)
    ci3b = ci[D:2 * D, 1:TOK].rearrange("(c p) t -> p c t", p=128)
    xnf3s = xnf.rearrange("p (c t) -> p c t", c=DCH)[:, :, 0:TOK - 1]
    cx.dma("q", ci3b, xnf3s)
    zc = ci[D:2 * D, 0:1].rearrange("(c p) t -> p (c t)", p=128)
    cw = cx.dma("q", zc, b["zrow"][:, 0:DCH], slow=True)
    cx.wait("g", "ds", cw)

    def gco(g):
        g.collective_compute(
            "AllGather", ALU.bypass, replica_groups=RG,
            ins=[W["xg_in"].ap().opt()], outs=[W["xg_out"].ap().opt()],
        ).then_inc(cx.sem("cc"), 1)
        cx.bump("cc")
        return None
    cx.on("g", gco)
    ccv = cx.cnt["cc"]
    if L2F_STOP <= 2:
        return

    iw = cx.cnt["ds"]
    pb0 = cx.cnt["pb"]

    rb = cx.buf.get("pvb_reg")
    if rb is None:
        rb = cx.stack.enter_context(cx.nc.gpsimd.register("pvb"))
        cx.buf["pvb_reg"] = rb

    def gprev(g):
        g.wait_ge(cx.sem("cc"), ccv)
        base = cx.buf.get("pvb_base")
        if base is None:
            g.reg_load(rb, b["ivals"][0:1, 0:1])
            base = g.snap(rb, min_val=0, max_val=GRP * 2 * D - D - 128)
            cx.buf["pvb_base"] = base
        G = W["xg_out"]
        for ch in range(DCH):
            if ch >= 1:
                g.wait_ge(cx.sem("gs"), gs0 + 16 * ch)
            g.dma_start(
                ar3[:, ch * TOK:(ch + 1) * TOK],
                G[bass.ds(base + ch * 128, 128), :],
            ).then_inc(cx.sem("gs"), 16)
        return None
    gs0 = cx.cnt["gs"]
    cx.on("g", gprev)
    cx.bump("gs", 16 * DCH)

    def vmix(v):
        last = None
        for ch in range(DCH):
            v.wait_ge(cx.sem("gs"), gs0 + 16 * (ch + 1))
            xp = ar3[:, ch * TOK:(ch + 1) * TOK]
            cur = xnf_of(ch)
            for qc in range(NQC):
                sl = slice(qc * QCOL, (qc + 1) * QCOL)
                dbuf = tsc[:, 0:QCOL]
                v.tensor_sub(dbuf, cur[:, sl], xp[:, sl]).then_inc(
                    cx.sem("pa"), 1)
                cx.bump("pa")
                v.wait_ge(cx.sem("pa"), cx.cnt["pa"])
                v.scalar_tensor_tensor(ar1[:, ch * TOK + qc * QCOL:
                                           ch * TOK + (qc + 1) * QCOL],
                                       dbuf, vrow[:, ch:ch + 1], xp[:, sl],
                                       ALU.mult, ALU.add)
                last = v.scalar_tensor_tensor(
                    ar2[:, ch * TOK + qc * QCOL: ch * TOK + (qc + 1) * QCOL],
                    dbuf, vrow[:, DCH + ch:DCH + ch + 1], xp[:, sl],
                    ALU.mult, ALU.add)
                last.then_inc(cx.sem("pa"), 1)
                cx.bump("pa")
                v.wait_ge(cx.sem("pa"), cx.cnt["pa"])
        return None
    cx.on("v", vmix)
    cx.seal("v", "pa")
    cx.barrier()
    if L2F_STOP <= 3:
        return

    def ar3_of(k):
        return ar3[:, k * TOK:(k + 1) * TOK]

    def ev_relu(s, o, qc, pp):
        return s.activation(ar3[:, o * TOK + qc * QCOL:
                                o * TOK + (qc + 1) * QCOL], pp[:, :],
                            AF.Relu)
    emit_proj(cx, W["l2f_wk_t"], ar2_of, D, D, evac_s=ev_relu)
    if L2F_STOP <= 4:
        return

    pdv = cx.cnt["pd"]

    def ssq(s):
        s.wait_ge(cx.sem("pd"), pdv)
        last = None
        for ch in range(DCH):
            last = s.activation(ar3[:, ch * TOK:(ch + 1) * TOK],
                                ar3[:, ch * TOK:(ch + 1) * TOK], AF.Square)
        return last
    cx.on("s", ssq)
    cx.barrier()
    if L2F_STOP <= 5:
        return

    def ev_copy(v, o, qc, pp):
        return v.tensor_copy(ar2[:, o * TOK + qc * QCOL:
                                 o * TOK + (qc + 1) * QCOL], pp[:, :])
    emit_proj(cx, W["l2f_wv_t"], ar3_of, D, D, evac_v=ev_copy)
    if L2F_STOP <= 6:
        return

    def ev_sig(s, o, qc, pp):
        return s.activation(ar3[:, o * TOK + qc * QCOL:
                                o * TOK + (qc + 1) * QCOL], pp[:, :],
                            AF.Sigmoid)
    emit_proj(cx, W["l2f_wr_t"], ar1_of, D, D, evac_s=ev_sig)
    if L2F_STOP <= 7:
        return

    def vfin(v):
        last = None
        for ch in range(DCH):
            for qc in range(NQC):
                off = ch * TOK + qc * QCOL
                v.tensor_tensor(tsc[:, 0:QCOL], ar3[:, off:off + QCOL],
                                ar2[:, off:off + QCOL],
                                ALU.mult).then_inc(cx.sem("pa"), 1)
                cx.bump("pa")
                v.wait_ge(cx.sem("pa"), cx.cnt["pa"])
                last = v.scalar_tensor_tensor(xs[:, off:off + QCOL],
                                              tsc[:, 0:QCOL], 0.5,
                                              xs[:, off:off + QCOL],
                                              ALU.mult, ALU.add)
                last.then_inc(cx.sem("pa"), 1)
                cx.bump("pa")
                v.wait_ge(cx.sem("pa"), cx.cnt["pa"])
        return None
    cx.on("v", vfin)
    cx.seal("v", "pa")
    cx.barrier()


def emit_fm_to_tokmajor(cx, src_fm_of, dst_dram, n_feat, ident_bf):
    b = cx.buf
    p = b["p"]
    och = n_feat // 128
    tch = TOK // 128
    stg = b["tsc"].bitcast(BF16)
    pc0 = cx.cnt["pc"]
    pd0 = cx.cnt["pd"]
    pe0 = cx.cnt["pe_"]
    ds0 = cx.cnt["ds"]
    tiles = [(o, tc) for o in range(och) for tc in range(tch)]

    def tf(t):
        for i, (o, tc) in enumerate(tiles):
            pp = p[6 + i % 2].bitcast(BF16)[:, 0:128]
            if i >= 2:
                t.wait_ge(cx.sem("pd"), pd0 + i - 1)
            t.transpose(pp, src_fm_of(o)[:, tc * 128:(tc + 1) * 128],
                        ident_bf).then_inc(cx.sem("pc"), 1)
        cx.bump("pc", len(tiles))
        return None
    cx.on("t", tf)

    def vf(v):
        for i, (o, tc) in enumerate(tiles):
            v.wait_ge(cx.sem("pc"), pc0 + i + 1)
            if i >= 2:
                v.wait_ge(cx.sem("pe_"), pe0 + i - 1)
            v.tensor_copy(stg[:, (i % 2) * 512:(i % 2) * 512 + 128],
                          p[6 + i % 2].bitcast(BF16)[:, 0:128]).then_inc(
                cx.sem("pd"), 1)
        cx.bump("pd", len(tiles))
        return None
    cx.on("v", vf)

    def qf(q):
        for i, (o, tc) in enumerate(tiles):
            q.wait_ge(cx.sem("pd"), pd0 + i + 1)
            q.dma_start(dst_dram[tc * 128:(tc + 1) * 128,
                                 o * 128:(o + 1) * 128],
                        stg[:, (i % 2) * 512:(i % 2) * 512 + 128]).then_inc(
                cx.sem("ds"), 16)
            q.wait_ge(cx.sem("ds"), ds0 + 16 * (i + 1))
            q.nop().then_inc(cx.sem("pe_"), 1)
        return None
    cx.on("q", qf)
    cx.bump("ds", 16 * len(tiles))
    cx.bump("pe_", len(tiles))
    cx.seal("q", "pe_")
    cx.barrier()


def attn_chunks_of(qb):
    out = []
    ncr = (qb + 1) * 4
    for r in range(4):
        for c in range(ncr):
            colstart = max(0, c * 128 - qb * 512)
            diag = c >= qb * 4
            out.append((r, c, colstart, diag))
    return out


def emit_attn_causal(cx, W, wqkv_name, q_off, k_off, v_off, wout_name,
                     scale, residual_scale=1.0, norm_kind="ln",
                     norm_w=None, k_builder=None, q_pre=None):
    b = cx.buf
    p = b["p"]
    xs, xnf = b["xs"], b["xnf"]
    ar1, ar2, ar3 = b["ar1"], b["ar2"], b["ar3"]
    tsc, msk = b["tsc"], b["msk"]
    ones1, onesb = b["ones1"], b["onesb"]
    NH = 8
    wqkv = W[wqkv_name]

    def xs_of(ch):
        return xs[:, ch * TOK:(ch + 1) * TOK]

    def xnf_of(ch):
        return xnf[:, ch * TOK:(ch + 1) * TOK]

    ident = msk[:, 512:640]

    def gid(g):
        g.memset(msk[:, 640:768], 1.0).then_inc(cx.sem("ga"), 1)
        cx.bump("ga")
        g.wait_ge(cx.sem("ga"), cx.cnt["ga"])
        return g.affine_select(ident, msk[:, 640:768], [[-1, 128]],
                               ALU.is_equal, 0.0, channel_multiplier=1)
    cx.on("g", gid)
    cx.dma("g", msk[:, 0:512].rearrange("p (r f) -> p r f", r=4),
           W["trimask"].rearrange("r p f -> p r f"))
    cx.barrier()

    emit_norm(cx, xs_of, xnf_of, kind=norm_kind, w_slice=norm_w)

    if k_builder is None:
        def ev_cp_ar1(v, o, qc, pp):
            return v.tensor_copy(ar1[:, o * TOK + qc * QCOL:
                                     o * TOK + (qc + 1) * QCOL], pp[:, :])
        emit_proj(cx, wqkv, xnf_of, D, D, evac_v=ev_cp_ar1, col_off=k_off)
    else:
        k_builder()

    def ev_cp_ar2(v, o, qc, pp):
        return v.tensor_copy(ar2[:, o * TOK + qc * QCOL:
                                 o * TOK + (qc + 1) * QCOL], pp[:, :])
    emit_proj(cx, wqkv, xnf_of, D, D, evac_v=ev_cp_ar2, col_off=v_off)

    pe0 = cx.cnt["pe_"]
    pc0 = cx.cnt["pc"]
    pd0 = cx.cnt["pd"]

    def sqk(s):
        for hp in range(NH):
            if hp >= 1:
                s.wait_ge(cx.sem("pc"), pc0 + hp * 2 * NQC)
            s.activation(ar3[:, 0:TOK], ar1[:, hp * TOK:(hp + 1) * TOK],
                         AF.Square).then_inc(cx.sem("pe_"), 1)
        cx.bump("pe_", NH)
        return None
    cx.on("s", sqk)

    def tkn(t):
        for hp in range(NH):
            t.wait_ge(cx.sem("pe_"), pe0 + hp + 1)
            for hl in range(2):
                for qc in range(NQC):
                    i = (hp * 2 + hl) * NQC + qc
                    if i >= 2:
                        t.wait_ge(cx.sem("pd"), pd0 + i - 1)
                    t.matmul(p[4 + i % 2][0:1, :],
                             onesb[hl * 64:hl * 64 + 64, 0:1],
                             ar3[hl * 64:hl * 64 + 64,
                                 qc * QCOL:(qc + 1) * QCOL],
                             start=True, stop=True,
                             tile_position=(hl * 64, 0)).then_inc(
                        cx.sem("pc"), 1)
        cx.bump("pc", NH * 2 * NQC)
        return None
    cx.on("t", tkn)

    def vkn(v):
        v.memset(tsc[0:1, 32:48], 0.0).then_inc(cx.sem("pa"), 1)
        cx.bump("pa")
        v.wait_ge(cx.sem("pa"), cx.cnt["pa"])
        for hp in range(NH):
            for hl in range(2):
                for qc in range(NQC):
                    i = (hp * 2 + hl) * NQC + qc
                    v.wait_ge(cx.sem("pc"), pc0 + i + 1)
                    v.reduce_max(tsc[0:1, 48:49], p[4 + i % 2][0:1, :],
                                 axis=AX.X).then_inc(cx.sem("pa"), 1)
                    cx.bump("pa")
                    v.wait_ge(cx.sem("pa"), cx.cnt["pa"])
                    h = hp * 2 + hl
                    v.tensor_tensor(tsc[0:1, 32 + h:33 + h],
                                    tsc[0:1, 32 + h:33 + h],
                                    tsc[0:1, 48:49],
                                    ALU.max).then_inc(cx.sem("pd"), 1)
                    cx.bump("pd")
                    v.wait_ge(cx.sem("pd"), cx.cnt["pd"])
        v.tensor_scalar_mul(tsc[0:1, 32:48], tsc[0:1, 32:48],
                            scale * scale).then_inc(cx.sem("pa"), 1)
        cx.bump("pa")
        return None
    cx.on("v", vkn)
    cx.seal("v", "pa")
    cx.barrier()

    cx.dma("q", W["xgk_in"].rearrange("(c p) t -> p c t", p=128),
           ar1.rearrange("p (c t) -> p c t", c=DCH))
    kmw = cx.dma("q", W["kmx_in"][0:1, 0:16], tsc[0:1, 32:48])
    cx.wait("v", "ds", kmw)
    emit_fm_to_tokmajor(cx, lambda o: ar2[:, o * TOK:(o + 1) * TOK],
                        W["xgv_in"], D, ident)

    if q_pre is not None:
        q_pre()

    def gcoll(g):
        g.wait_ge(cx.sem("ds"), cx.cnt["ds"])
        for kind, op, a, o in (("AllGather", ALU.bypass, "xgk_in", "xgk_out"),
                               ("AllGather", ALU.bypass, "xgv_in", "xgv_out"),
                               ("AllReduce", ALU.max, "kmx_in", "kmx_out")):
            g.collective_compute(kind, op, replica_groups=RG,
                                 ins=[W[a].ap().opt()],
                                 outs=[W[o].ap().opt()]).then_inc(
                cx.sem("cc"), 1)
            cx.bump("cc")
        return None
    cx.on("g", gcoll)
    cx.wait("q", "cc", cx.cnt["cc"])
    kw = cx.dma("q", tsc[0:1, 32:48], W["kmx_out"][0:1, 0:16])
    cx.wait("q", "ds", kw)
    cx.wait("v", "ds", kw)
    cx.barrier()

    GK, GV = W["xgk_out"], W["xgv_out"]
    wout = W[wout_name]
    NIDX = NQB * NH
    chunks = {qb: attn_chunks_of(qb) for qb in range(NQB)}

    gq0 = cx.cnt["gs"]
    pf_qproj = {}
    pf_qnorm = {}
    pf_bias = {}
    ph_sq = {}
    sa_qevac = {}
    ab_negB = {}
    ab_biascopy = {}
    as_scores = {}
    ax_exp = {}
    am_mask = {}
    av_done = {}
    ay_recip = {}
    ay_yevac = {}
    ds_kv = {}
    ds_rbc = {}
    gs_qslab = {}
    gs_wout = {}
    op_mm = {}
    op_evac = {}

    pf = cx.cnt["pf"]
    ph = cx.cnt["ph"]
    sa = cx.cnt["sa"]
    ab = cx.cnt["ab"]
    asx = cx.cnt["as_"]
    ax = cx.cnt["ax"]
    am = cx.cnt["am"]
    av = cx.cnt["av"]
    ay = cx.cnt["ay"]
    ds = cx.cnt["ds"]
    gs = gq0
    pb = cx.cnt["pb"]
    pg = cx.cnt["pg"]
    for qb in range(NQB):
        cl = chunks[qb]
        for hp in range(NH):
            idx = qb * NH + hp
            gs += 16
            gs_qslab[idx] = gs
            pf += 1
            pf_qproj[idx] = pf
            pf += 1
            pf_qnorm[idx] = pf
            pf += 1
            pf_bias[idx] = pf
            sa += 1
            sa_qevac[idx] = sa
            ph += 1
            ph_sq[idx] = ph
            ab += 3
            ab_negB[idx] = ab
            ab += 1
            ab_biascopy[idx] = ab
            ds += 16 * 8
            ds_kv[idx] = ds
            ds += 16 * 4
            ds_rbc[idx] = ds
            for ck in range(len(cl)):
                asx += 1
                as_scores[(idx, ck)] = asx
                ax += 1
                ax_exp[(idx, ck)] = ax
                if cl[ck][3]:
                    am += 1
                    am_mask[(idx, ck)] = am
                av += 1
                av_done[(idx, ck)] = av
            ay += 1
            ay_recip[idx] = ay
            ay += 1
            ay_yevac[idx] = ay
        for si in range(4):
            gs += 16
            gs_wout[(qb, si)] = gs
        for o in range(DCH):
            pb += 1
            op_mm[(qb, o)] = pb
            pg += 1
            op_evac[(qb, o)] = pg
    cx.cnt["pf"] = pf
    cx.cnt["ph"] = ph
    cx.cnt["sa"] = sa
    cx.cnt["ab"] = ab
    cx.cnt["as_"] = asx
    cx.cnt["ax"] = ax
    cx.cnt["am"] = am
    cx.cnt["av"] = av
    cx.cnt["ay"] = ay
    cx.cnt["ds"] = ds
    cx.cnt["gs"] = gs
    cx.cnt["pb"] = pb
    cx.cnt["pg"] = pg

    QT = 0
    PT = 1024
    RB = 3072
    YS = 4096
    QS = 8192
    SQ = 10240

    def gq(g):
        prev_gs = gq0
        for qb in range(NQB):
            for hp in range(NH):
                idx = qb * NH + hp
                g.wait_ge(cx.sem("gs"), prev_gs)
                if idx >= 2:
                    g.wait_ge(cx.sem("pf"), pf_qproj[idx - 2])
                half = idx % 2
                wr = wqkv[0:D, q_off + hp * 128:
                          q_off + (hp + 1) * 128].rearrange(
                    "(kc p) o -> p kc o", p=128)
                g.dma_start(ar3[:, QS + half * 1024:QS + half * 1024 +
                                1024].rearrange("p (kc o) -> p kc o", kc=8),
                            wr).then_inc(cx.sem("gs"), 16)
                prev_gs = gs_qslab[idx]
            for si in range(4):
                g.wait_ge(cx.sem("gs"), prev_gs)
                if si >= 2:
                    g.wait_ge(cx.sem("pb"), op_mm[(qb, (si - 2) * 2 + 1)])
                elif qb >= 1:
                    g.wait_ge(cx.sem("pb"), op_mm[(qb - 1, (si + 2) * 2 + 1)])
                wr = wout[0:D, si * 256:(si + 1) * 256].rearrange(
                    "(kc p) o -> p kc o", p=128)
                g.dma_start(b["wsl"][:, (si % 2) * WSLAB:
                                     (si % 2) * WSLAB + 2048].rearrange(
                    "p (kc o) -> p kc o", kc=8), wr).then_inc(
                    cx.sem("gs"), 16)
                prev_gs = gs_wout[(qb, si)]
        return None
    cx.on("g", gq)

    def qkv(q):
        for idx in range(NIDX):
            qb, hp = divmod(idx, NH)
            ncr = (qb + 1) * 4
            half = idx % 2
            if idx >= 2:
                q.wait_ge(cx.sem("ay"), ay_yevac[idx - 2])
            for r in range(4):
                q.dma_start(
                    ar1[:, half * 8192 + r * ncr * 128:
                        half * 8192 + (r + 1) * ncr * 128],
                    GK[r * D + hp * 128:r * D + (hp + 1) * 128,
                       0:ncr * 128]).then_inc(cx.sem("ds"), 16)
                src3 = GV[r * TOK:r * TOK + ncr * 128,
                          hp * 128:(hp + 1) * 128].rearrange(
                    "(c p) f -> p c f", p=128)
                dst3 = ar2[:, half * 8192 + r * ncr * 128:
                           half * 8192 + (r + 1) * ncr * 128].rearrange(
                    "p (c f) -> p c f", f=128)
                q.dma_start(dst3, src3).then_inc(cx.sem("ds"), 16)
            q.wait_ge(cx.sem("ay"), ay_recip[idx])
            q.dma_start(W["rbc"][0:1, :], tsc[32:33, 0:512]).then_inc(
                cx.sem("ds"), 16)
            q.wait_ge(cx.sem("ds"), ds_kv[idx] + 16)
            q.dma_start(W["rbc"][1:2, :], tsc[64:65, 0:512]).then_inc(
                cx.sem("ds"), 16)
            q.wait_ge(cx.sem("ds"), ds_kv[idx] + 32)
            q.dma_start(ar3[0:64, RB:RB + 1024].bitcast(F32),
                        W["rbc"][0:1, :].broadcast_to([64, 512])).then_inc(
                cx.sem("ds"), 16)
            q.wait_ge(cx.sem("ds"), ds_kv[idx] + 48)
            q.dma_start(ar3[64:128, RB:RB + 1024].bitcast(F32),
                        W["rbc"][1:2, :].broadcast_to([64, 512])).then_inc(
                cx.sem("ds"), 16)
            q.wait_ge(cx.sem("ds"), ds_rbc[idx])
        return None
    cx.on("q", qkv)

    def tmain(t):
        for idx in range(NIDX):
            qb, hp = divmod(idx, NH)
            cl = chunks[qb]
            nkc = len(cl)
            ncr = (qb + 1) * 4
            half = idx % 2
            t.wait_ge(cx.sem("gs"), gs_qslab[idx])
            if idx >= 1:
                t.wait_ge(cx.sem("ab"), ab_biascopy[idx - 1])
                pidx = idx - 1
                t.wait_ge(cx.sem("ax"),
                          ax_exp[(pidx, len(chunks[pidx // NH]) - 1)])
            qsl = ar3[:, QS + half * 1024:QS + half * 1024 + 1024]
            last = None
            for k in range(DCH):
                last = t.matmul(p[7][:, 0:512],
                                qsl[:, k * 128:(k + 1) * 128],
                                xnf_of(k)[:, qb * 512:(qb + 1) * 512],
                                start=(k == 0), stop=(k == DCH - 1))
            last.then_inc(cx.sem("pf"), 1)
            t.wait_ge(cx.sem("ph"), ph_sq[idx])
            if idx >= 1:
                t.wait_ge(cx.sem("ay"), ay_recip[idx - 1])
            sq = ar3[:, SQ:SQ + 512]
            t.matmul(p[5][0:1, 0:512], onesb[0:64, 0:1], sq[0:64, 0:512],
                     start=True, stop=True, tile_position=(0, 0))
            t.matmul(p[6][0:1, 0:512], onesb[64:128, 0:1],
                     sq[64:128, 0:512], start=True, stop=True,
                     tile_position=(64, 0)).then_inc(cx.sem("pf"), 1)
            t.wait_ge(cx.sem("sa"), sa_qevac[idx])
            t.wait_ge(cx.sem("ab"), ab_negB[idx])
            t.matmul(p[7][:, 0:1], ones1[0:1, :], tsc[0:1, 56:57],
                     start=True, stop=True)
            t.matmul(p[7][:, 1:2], ones1[0:1, :], tsc[0:1, 57:58],
                     start=True, stop=True).then_inc(cx.sem("pf"), 1)
            t.wait_ge(cx.sem("ds"), ds_kv[idx])
            if idx >= 1:
                t.wait_ge(cx.sem("ay"), ay_yevac[idx - 1])
            if hp == 0 and qb >= 1:
                t.wait_ge(cx.sem("pg"), op_evac[(qb - 1, DCH - 1)])
            qt = ar3[:, QT + half * 512:QT + half * 512 + 512]
            for ck, (r, c, cs, diag) in enumerate(cl):
                koff = half * 8192 + (r * ncr + c) * 128
                bA = p[0] if ck % 2 == 0 else p[4]
                bB = p[1] if ck % 2 == 0 else p[7]
                if ck >= 2:
                    t.wait_ge(cx.sem("ax"), ax_exp[(idx, ck - 2)])
                if ck == 1:
                    t.wait_ge(cx.sem("ab"), ab_biascopy[idx])
                t.matmul(bA[:, cs:512], ar1[0:64, koff:koff + 128],
                         qt[0:64, cs:512], start=True, stop=True,
                         tile_position=(0, 0))
                t.matmul(bB[:, cs:512], ar1[64:128, koff:koff + 128],
                         qt[64:128, cs:512], start=True, stop=True,
                         tile_position=(64, 0)).then_inc(cx.sem("as_"), 1)
                if diag:
                    t.wait_ge(cx.sem("am"), am_mask[(idx, ck)])
                else:
                    t.wait_ge(cx.sem("ax"), ax_exp[(idx, ck)])
                ptA = ar3[:, PT + (ck % 2) * 512:PT + (ck % 2) * 512 + 512]
                ptB = ar3[:, PT + 1024 + (ck % 2) * 512:
                          PT + 1024 + (ck % 2) * 512 + 512]
                first = (ck == 0)
                lastc = (ck == nkc - 1)
                t.matmul(p[2][0:64, cs:512], ar2[:, koff:koff + 64],
                         ptA[:, cs:512], start=first, stop=lastc,
                         tile_position=(0, 0))
                t.matmul(p[3][64:128, cs:512], ar2[:, koff + 64:koff + 128],
                         ptB[:, cs:512], start=first, stop=lastc,
                         tile_position=(0, 64))
                t.matmul(p[5][0:1, cs:512], onesb[:, 0:1], ptA[:, cs:512],
                         start=first, stop=lastc)
                t.matmul(p[6][0:1, cs:512], onesb[:, 1:2], ptB[:, cs:512],
                         start=first, stop=lastc).then_inc(cx.sem("av"), 1)
            if hp == NH - 1:
                t.wait_ge(cx.sem("ay"), ay_yevac[idx])
                for o in range(DCH):
                    si = o // 2
                    t.wait_ge(cx.sem("gs"), gs_wout[(qb, si)])
                    if o >= 2:
                        t.wait_ge(cx.sem("pg"), op_evac[(qb, o - 2)])
                    pp = p[o % 2]
                    lastm = None
                    for k in range(DCH):
                        base = (si % 2) * WSLAB + k * 256 + (o % 2) * 128
                        wt = b["wsl"][:, base:base + 128]
                        lastm = t.matmul(pp[:, 0:512], wt,
                                         ar3[:, YS + k * 512:
                                             YS + (k + 1) * 512],
                                         start=(k == 0), stop=(k == DCH - 1))
                    lastm.then_inc(cx.sem("pb"), 1)
        return None
    cx.on("t", tmain)

    def smain(s):
        for idx in range(NIDX):
            qb, hp = divmod(idx, NH)
            cl = chunks[qb]
            half = idx % 2
            s.wait_ge(cx.sem("pf"), pf_qproj[idx])
            if idx >= 2:
                pidx = idx - 2
                s.wait_ge(cx.sem("as_"),
                          as_scores[(pidx, len(chunks[pidx // NH]) - 1)])
            qt = ar3[:, QT + half * 512:QT + half * 512 + 512]
            s.activation(qt, p[7][:, 0:512], AF.Copy,
                         scale=scale).then_inc(cx.sem("sa"), 1)
            if idx >= 1:
                s.wait_ge(cx.sem("pf"), pf_qnorm[idx - 1])
            s.wait_ge(cx.sem("sa"), sa_qevac[idx])
            s.activation(ar3[:, SQ:SQ + 512], qt,
                         AF.Square).then_inc(cx.sem("ph"), 1)
            s.wait_ge(cx.sem("ab"), ab_biascopy[idx])
            for ck, (r, c, cs, diag) in enumerate(cl):
                s.wait_ge(cx.sem("as_"), as_scores[(idx, ck)])
                if ck >= 2:
                    s.wait_ge(cx.sem("av"), av_done[(idx, ck - 2)])
                ptA = ar3[:, PT + (ck % 2) * 512:PT + (ck % 2) * 512 + 512]
                ptB = ar3[:, PT + 1024 + (ck % 2) * 512:
                          PT + 1024 + (ck % 2) * 512 + 512]
                s.activation(ptA[:, cs:512],
                             (p[0] if ck % 2 == 0 else p[4])[:, cs:512],
                             AF.Exp, bias=tsc[:, 0:1])
                s.activation(ptB[:, cs:512],
                             (p[1] if ck % 2 == 0 else p[7])[:, cs:512],
                             AF.Exp, bias=tsc[:, 1:2]).then_inc(
                    cx.sem("ax"), 1)
        return None
    cx.on("s", smain)

    def vmain(v):
        for idx in range(NIDX):
            qb, hp = divmod(idx, NH)
            cl = chunks[qb]
            nkc = len(cl)
            half = idx % 2
            v.wait_ge(cx.sem("pf"), pf_qnorm[idx])
            v.reduce_max(tsc[0:1, 52:53], p[5][0:1, 0:512], axis=AX.X)
            v.reduce_max(tsc[0:1, 53:54], p[6][0:1, 0:512],
                         axis=AX.X).then_inc(cx.sem("ab"), 1)
            v.wait_ge(cx.sem("ab"), ab_negB[idx] - 2)
            v.tensor_add(tsc[0:1, 54:56], tsc[0:1, 52:54],
                         tsc[0:1, 32 + 2 * hp:34 + 2 * hp]).then_inc(
                cx.sem("ab"), 1)
            v.wait_ge(cx.sem("ab"), ab_negB[idx] - 1)
            v.tensor_scalar_mul(tsc[0:1, 56:58], tsc[0:1, 54:56],
                                -0.5).then_inc(cx.sem("ab"), 1)
            v.wait_ge(cx.sem("pf"), pf_bias[idx])
            v.tensor_copy(tsc[:, 0:2], p[7][:, 0:2]).then_inc(
                cx.sem("ab"), 1)
            for ck, (r, c, cs, diag) in enumerate(cl):
                if not diag:
                    continue
                v.wait_ge(cx.sem("ax"), ax_exp[(idx, ck)])
                ptA = ar3[:, PT + (ck % 2) * 512:PT + (ck % 2) * 512 + 512]
                ptB = ar3[:, PT + 1024 + (ck % 2) * 512:
                          PT + 1024 + (ck % 2) * 512 + 512]
                we = min(cs + 128, 512)
                v.tensor_mul(ptA[:, cs:we], ptA[:, cs:we],
                             msk[:, r * 128:r * 128 + (we - cs)])
                v.tensor_mul(ptB[:, cs:we], ptB[:, cs:we],
                             msk[:, r * 128:r * 128 + (we - cs)]).then_inc(
                    cx.sem("am"), 1)
            v.wait_ge(cx.sem("av"), av_done[(idx, nkc - 1)])
            v.reciprocal(tsc[32:33, 0:512], p[5][0:1, 0:512])
            v.reciprocal(tsc[64:65, 0:512], p[6][0:1, 0:512]).then_inc(
                cx.sem("ay"), 1)
            v.wait_ge(cx.sem("ds"), ds_rbc[idx])
            if hp == 0 and qb >= 1:
                v.wait_ge(cx.sem("pb"), op_mm[(qb - 1, DCH - 1)])
            ys = ar3[:, YS + hp * 512:YS + (hp + 1) * 512]
            v.tensor_tensor(ys[0:64, :], p[2][0:64, 0:512],
                            ar3[0:64, RB:RB + 1024].bitcast(F32), ALU.mult)
            v.tensor_tensor(ys[64:128, :], p[3][64:128, 0:512],
                            ar3[64:128, RB:RB + 1024].bitcast(F32),
                            ALU.mult).then_inc(cx.sem("ay"), 1)
            if hp == NH - 1:
                for o in range(DCH):
                    v.wait_ge(cx.sem("pb"), op_mm[(qb, o)])
                    xso = xs[:, o * TOK + qb * 512:o * TOK + (qb + 1) * 512]
                    v.scalar_tensor_tensor(xso, p[o % 2][:, 0:512],
                                           residual_scale, xso, ALU.mult,
                                           ALU.add).then_inc(cx.sem("pg"), 1)
        return None
    cx.on("v", vmain)
    cx.seal("v", "pg")
    cx.barrier()


def emit_tokproj_rope_fm(cx, W, w_name, col_off, dst_fm_of, ropetab_name,
                         scale=1.0, tab4=False):
    b = cx.buf
    p = b["p"]
    xnf = b["xnf"]
    wsl = b["wsl"]
    ar3 = b["ar3"]
    ident = b["msk"][:, 512:640]
    tch = TOK // 128
    TBW = 2048 if tab4 else 1024
    RT, ST, SC = 0, 2 * TBW, 2 * TBW + 1024

    def xnf_of(ch):
        return xnf[:, ch * TOK:(ch + 1) * TOK]

    wk_slab = b["ar2"]
    gs0 = cx.cnt["gs"]
    wr = W[w_name][0:D, col_off:col_off + D].rearrange(
        "(kc p) o -> p kc o", p=128)
    cx.dma("g", wk_slab[:, 0:8192].rearrange("p (kc o) -> p kc o", kc=DCH),
           wr)

    gs_tab = {}
    pc_proj = {}
    pd_evac = {}
    pa_rope = {}
    ph_tp = {}
    pe_fmev = {}
    gs = cx.cnt["gs"]
    pc = cx.cnt["pc"]
    pd = cx.cnt["pd"]
    pa = cx.cnt["pa"]
    ph = cx.cnt["ph"]
    pe = cx.cnt["pe_"]
    for tc in range(tch):
        gs += 16
        gs_tab[tc] = gs
        for nh in range(2):
            pc += 1
            pc_proj[(tc, nh)] = pc
            pd += 1
            pd_evac[(tc, nh)] = pd
        pa += 7
        pa_rope[tc] = pa
        for tl in range(8):
            ph += 1
            ph_tp[(tc, tl)] = ph
            pe += 1
            pe_fmev[(tc, tl)] = pe
    cx.cnt["gs"] = gs
    cx.cnt["pc"] = pc
    cx.cnt["pd"] = pd
    cx.cnt["pa"] = pa
    cx.cnt["ph"] = ph
    cx.cnt["pe_"] = pe

    def gtab(g):
        prev = gs0 + 16
        for tc in range(tch):
            g.wait_ge(cx.sem("gs"), prev)
            if tc >= 2:
                g.wait_ge(cx.sem("pa"), pa_rope[tc - 2])
            g.dma_start(ar3[:, RT + (tc % 2) * TBW:
                            RT + (tc % 2) * TBW + TBW],
                        W[ropetab_name][tc * 128:(tc + 1) * 128,
                                        :]).then_inc(cx.sem("gs"), 16)
            prev = gs_tab[tc]
        return None
    cx.on("g", gtab)

    def tf(t):
        t.wait_ge(cx.sem("gs"), gs0 + 16)
        for tc in range(tch):
            for nh in range(2):
                if tc >= 1:
                    t.wait_ge(cx.sem("pd"), pd_evac[(tc - 1, nh)])
                pp = p[4 + nh]
                last = None
                for k in range(DCH):
                    last = t.matmul(
                        pp[:, 0:512],
                        xnf_of(k)[:, tc * 128:(tc + 1) * 128],
                        wk_slab[:, k * 1024 + nh * 512:
                                k * 1024 + nh * 512 + 512],
                        start=(k == 0), stop=(k == DCH - 1))
                last.then_inc(cx.sem("pc"), 1)
            t.wait_ge(cx.sem("pa"), pa_rope[tc])
            for tl in range(8):
                i = tc * 8 + tl
                if i >= 2:
                    t.wait_ge(cx.sem("pe_"), pe_fmev[((i - 2) // 8,
                                                      (i - 2) % 8)])
                pt = p[6 + i % 2].bitcast(BF16)[:, 0:128]
                t.transpose(pt, ar3[:, ST + tl * 128:ST + tl * 128 + 128],
                            ident).then_inc(cx.sem("ph"), 1)
        return None
    cx.on("t", tf)

    def sf(s):
        for tc in range(tch):
            for nh in range(2):
                s.wait_ge(cx.sem("pc"), pc_proj[(tc, nh)])
                if tc >= 1:
                    s.wait_ge(cx.sem("ph"), ph_tp[(tc - 1, 7)])
                s.activation(ar3[:, ST + nh * 512:ST + nh * 512 + 512],
                             p[4 + nh][:, 0:512], AF.Copy,
                             scale=scale).then_inc(cx.sem("pd"), 1)
        return None
    cx.on("s", sf)

    def vf(v):
        for tc in range(tch):
            v.wait_ge(cx.sem("pd"), pd_evac[(tc, 1)])
            v.wait_ge(cx.sem("gs"), gs_tab[tc])
            st = ar3[:, ST:ST + 1024]
            e = st.rearrange("p (h i two) -> p h i two", two=2, i=32)
            tb0 = RT + (tc % 2) * TBW
            c3 = ar3[:, tb0:tb0 + 512].rearrange("p (h i) -> p h i", i=32)
            s3 = ar3[:, tb0 + 512:tb0 + 1024].rearrange(
                "p (h i) -> p h i", i=32)
            if tab4:
                s1t = ar3[:, tb0 + 1024:tb0 + 1536].rearrange(
                    "p (h i) -> p h i", i=32)
                c1t = ar3[:, tb0 + 1536:tb0 + 2048].rearrange(
                    "p (h i) -> p h i", i=32)
            else:
                s1t, c1t = s3, c3
            sc1 = ar3[:, SC:SC + 512].rearrange("p (h i) -> p h i", i=32)
            sc2 = ar3[:, SC + 512:SC + 1024].rearrange("p (h i) -> p h i",
                                                       i=32)
            sc3 = ar3[:, SC + 1024:SC + 1536].rearrange("p (h i) -> p h i",
                                                        i=32)
            pa_base = pa_rope[tc] - 7

            def step(ins, k):
                ins.then_inc(cx.sem("pa"), 1)
                v.wait_ge(cx.sem("pa"), pa_base + k)
            step(v.tensor_tensor(sc1, e[..., 1], s3, ALU.mult), 1)
            step(v.tensor_tensor(sc2, e[..., 1], c1t, ALU.mult), 2)
            step(v.tensor_tensor(sc3, e[..., 0], s1t, ALU.mult), 3)
            step(v.tensor_tensor(e[..., 0], e[..., 0], c3, ALU.mult), 4)
            step(v.tensor_tensor(e[..., 0], e[..., 0], sc1, ALU.subtract), 5)
            step(v.tensor_tensor(e[..., 1], sc3, sc2, ALU.add), 6)
            v.engine_nop().then_inc(cx.sem("pa"), 1)
            for tl in range(8):
                v.wait_ge(cx.sem("ph"), ph_tp[(tc, tl)])
                o = tl
                v.tensor_copy(dst_fm_of(o)[:, tc * 128:(tc + 1) * 128],
                              p[6 + (tc * 8 + tl) % 2].bitcast(BF16)[:,
                                                                    0:128]
                              ).then_inc(cx.sem("pe_"), 1)
        return None
    cx.on("v", vf)
    cx.seal("v", "pe_")
    cx.barrier()
def emit_l2_attn(cx, W):
    b = cx.buf
    p = b["p"]
    xs, xnf = b["xs"], b["xnf"]
    ar1, ar2, ar3 = b["ar1"], b["ar2"], b["ar3"]
    tsc, msk = b["tsc"], b["msk"]
    ones1, onesb = b["ones1"], b["onesb"]
    NH = 8
    scale = 0.125

    def xs_of(ch):
        return xs[:, ch * TOK:(ch + 1) * TOK]

    def xnf_of(ch):
        return xnf[:, ch * TOK:(ch + 1) * TOK]

    ident = msk[:, 512:640]

    def gid(g):
        g.memset(msk[:, 640:768], 1.0).then_inc(cx.sem("ga"), 1)
        cx.bump("ga")
        g.wait_ge(cx.sem("ga"), cx.cnt["ga"])
        return g.affine_select(ident, msk[:, 640:768], [[-1, 128]],
                               ALU.is_equal, 0.0, channel_multiplier=1)
    cx.on("g", gid)
    cx.barrier()

    emit_norm(cx, xs_of, xnf_of, kind="ln")

    emit_tokproj_rope_fm(cx, W, "l2a_wk", 0,
                         lambda o: ar1[:, o * TOK:(o + 1) * TOK],
                         "ropek", scale=1.0)

    def ev_cp_ar2(v, o, qc, pp):
        return v.tensor_copy(ar2[:, o * TOK + qc * QCOL:
                                 o * TOK + (qc + 1) * QCOL], pp[:, :])
    emit_proj(cx, W["l2a_wv"], xnf_of, D, D, evac_v=ev_cp_ar2)

    pe0 = cx.cnt["pe_"]
    pc0 = cx.cnt["pc"]
    pd0 = cx.cnt["pd"]

    def sqk(s):
        for hp in range(NH):
            if hp >= 1:
                s.wait_ge(cx.sem("pc"), pc0 + hp * 2 * NQC)
            s.activation(ar3[:, 0:TOK], ar1[:, hp * TOK:(hp + 1) * TOK],
                         AF.Square).then_inc(cx.sem("pe_"), 1)
        cx.bump("pe_", NH)
        return None
    cx.on("s", sqk)

    def tkn(t):
        for hp in range(NH):
            t.wait_ge(cx.sem("pe_"), pe0 + hp + 1)
            for hl in range(2):
                for qc in range(NQC):
                    i = (hp * 2 + hl) * NQC + qc
                    if i >= 2:
                        t.wait_ge(cx.sem("pd"), pd0 + i - 1)
                    t.matmul(p[4 + i % 2][0:1, :],
                             onesb[hl * 64:hl * 64 + 64, 0:1],
                             ar3[hl * 64:hl * 64 + 64,
                                 qc * QCOL:(qc + 1) * QCOL],
                             start=True, stop=True,
                             tile_position=(hl * 64, 0)).then_inc(
                        cx.sem("pc"), 1)
        cx.bump("pc", NH * 2 * NQC)
        return None
    cx.on("t", tkn)

    def vkn(v):
        v.memset(tsc[0:1, 32:48], 0.0).then_inc(cx.sem("pa"), 1)
        cx.bump("pa")
        v.wait_ge(cx.sem("pa"), cx.cnt["pa"])
        for hp in range(NH):
            for hl in range(2):
                for qc in range(NQC):
                    i = (hp * 2 + hl) * NQC + qc
                    v.wait_ge(cx.sem("pc"), pc0 + i + 1)
                    v.reduce_max(tsc[0:1, 48:49], p[4 + i % 2][0:1, :],
                                 axis=AX.X).then_inc(cx.sem("pa"), 1)
                    cx.bump("pa")
                    v.wait_ge(cx.sem("pa"), cx.cnt["pa"])
                    h = hp * 2 + hl
                    v.tensor_tensor(tsc[0:1, 32 + h:33 + h],
                                    tsc[0:1, 32 + h:33 + h],
                                    tsc[0:1, 48:49],
                                    ALU.max).then_inc(cx.sem("pd"), 1)
                    cx.bump("pd")
                    v.wait_ge(cx.sem("pd"), cx.cnt["pd"])
        v.tensor_scalar_mul(tsc[0:1, 32:48], tsc[0:1, 32:48],
                            scale * scale).then_inc(cx.sem("pa"), 1)
        cx.bump("pa")
        return None
    cx.on("v", vkn)
    cx.seal("v", "pa")
    cx.barrier()

    cx.dma("q", W["xgk_in"].rearrange("(c p) t -> p c t", p=128),
           ar1.rearrange("p (c t) -> p c t", c=DCH))
    kmw = cx.dma("q", W["kmx_in"][0:1, 0:16], tsc[0:1, 32:48])
    cx.wait("v", "ds", kmw)
    emit_fm_to_tokmajor(cx, lambda o: ar2[:, o * TOK:(o + 1) * TOK],
                        W["xgv_in"], D, ident)

    def gcoll(g):
        g.wait_ge(cx.sem("ds"), cx.cnt["ds"])
        for kind, op, a, o in (("AllGather", ALU.bypass, "xgk_in",
                                "xgk_out"),
                               ("AllGather", ALU.bypass, "xgv_in",
                                "xgv_out"),
                               ("AllReduce", ALU.max, "kmx_in", "kmx_out")):
            g.collective_compute(kind, op, replica_groups=RG,
                                 ins=[W[a].ap().opt()],
                                 outs=[W[o].ap().opt()]).then_inc(
                cx.sem("cc"), 1)
            cx.bump("cc")
        return None
    cx.on("g", gcoll)
    cx.wait("q", "cc", cx.cnt["cc"])
    kw = cx.dma("q", tsc[0:1, 32:48], W["kmx_out"][0:1, 0:16])
    cx.wait("q", "ds", kw)
    cx.wait("v", "ds", kw)
    cx.barrier()

    GK, GV = W["xgk_out"], W["xgv_out"]
    wout = W["l2a_wo"]
    wqkv = W["l2a_wq"]
    NIDX = NQB * NH
    cl_all = [(r, c, 0, False) for r in range(4) for c in range(16)]
    nkc = len(cl_all)
    ncr = 16

    gq0 = cx.cnt["gs"]
    pf_qproj = {}
    pf_qnorm = {}
    pf_bias = {}
    ph_sq = {}
    sa_qevac = {}
    ab_negB = {}
    ab_biascopy = {}
    as_scores = {}
    ax_exp = {}
    av_done = {}
    ay_recip = {}
    ay_yevac = {}
    ds_kv = {}
    ds_rbc = {}
    gs_qslab = {}
    gs_wout = {}
    gs_qrope = {}
    op_mm = {}
    op_evac = {}
    pa_ropeq = {}
    pe_tpq = {}
    ga_qev = {}

    pf = cx.cnt["pf"]
    ph = cx.cnt["ph"]
    sa = cx.cnt["sa"]
    ab = cx.cnt["ab"]
    asx = cx.cnt["as_"]
    ax = cx.cnt["ax"]
    av = cx.cnt["av"]
    ay = cx.cnt["ay"]
    ds = cx.cnt["ds"]
    gs = gq0
    pb = cx.cnt["pb"]
    pg = cx.cnt["pg"]
    pa = cx.cnt["pa"]
    pe = cx.cnt["pe_"]
    ga = cx.cnt["ga"]
    for qb in range(NQB):
        gs += 16
        gs_qrope[qb] = gs
        for hp in range(NH):
            idx = qb * NH + hp
            gs += 16
            gs_qslab[idx] = gs
            pf += 1
            pf_qproj[idx] = pf
            sa += 1
            sa_qevac[idx] = sa
            pa += 7
            pa_ropeq[idx] = pa
            for tl in range(4):
                pe += 1
                pe_tpq[(idx, tl)] = pe
                ga += 1
                ga_qev[(idx, tl)] = ga
            ph += 1
            ph_sq[idx] = ph
            pf += 1
            pf_qnorm[idx] = pf
            ab += 3
            ab_negB[idx] = ab
            pf += 1
            pf_bias[idx] = pf
            ab += 1
            ab_biascopy[idx] = ab
            ds += 16 * 8
            ds_kv[idx] = ds
            ds += 16 * 4
            ds_rbc[idx] = ds
            for ck in range(nkc):
                asx += 1
                as_scores[(idx, ck)] = asx
                ax += 1
                ax_exp[(idx, ck)] = ax
                av += 1
                av_done[(idx, ck)] = av
            ay += 1
            ay_recip[idx] = ay
            ay += 1
            ay_yevac[idx] = ay
        for si in range(4):
            gs += 16
            gs_wout[(qb, si)] = gs
        for o in range(DCH):
            pb += 1
            op_mm[(qb, o)] = pb
            pg += 1
            op_evac[(qb, o)] = pg
    cx.cnt["pf"] = pf
    cx.cnt["ph"] = ph
    cx.cnt["sa"] = sa
    cx.cnt["ab"] = ab
    cx.cnt["as_"] = asx
    cx.cnt["ax"] = ax
    cx.cnt["av"] = av
    cx.cnt["ay"] = ay
    cx.cnt["ds"] = ds
    cx.cnt["gs"] = gs
    cx.cnt["pb"] = pb
    cx.cnt["pg"] = pg
    cx.cnt["pa"] = pa
    cx.cnt["pe_"] = pe
    cx.cnt["ga"] = ga

    QT = 0
    PT = 1024
    RB = 3072
    YS = 4096
    QS = 8192
    SQ = 10240
    QR = 10752
    QG = 11776
    SCQ = 12288

    def gq(g):
        prev_gs = gq0
        for qb in range(NQB):
            g.wait_ge(cx.sem("gs"), prev_gs)
            if qb >= 1:
                g.wait_ge(cx.sem("pa"), pa_ropeq[(qb - 1) * NH + NH - 1])
            g.dma_start(ar3[:, QR:QR + 512],
                        W["ropeq"][qb * 128:(qb + 1) * 128, :]).then_inc(
                cx.sem("gs"), 16)
            prev_gs = gs_qrope[qb]
            for hp in range(NH):
                idx = qb * NH + hp
                g.wait_ge(cx.sem("gs"), prev_gs)
                if idx >= 2:
                    g.wait_ge(cx.sem("pf"), pf_qproj[idx - 2])
                half = idx % 2
                wr = wqkv[0:D, hp * 128:(hp + 1) * 128].rearrange(
                    "(kc p) o -> p kc o", p=128)
                g.dma_start(ar3[:, QS + half * 1024:QS + half * 1024 +
                                1024].rearrange("p (kc o) -> p kc o", kc=8),
                            wr).then_inc(cx.sem("gs"), 16)
                prev_gs = gs_qslab[idx]
            for si in range(4):
                g.wait_ge(cx.sem("gs"), prev_gs)
                if si >= 2:
                    g.wait_ge(cx.sem("pb"), op_mm[(qb, (si - 2) * 2 + 1)])
                elif qb >= 1:
                    g.wait_ge(cx.sem("pb"),
                              op_mm[(qb - 1, (si + 2) * 2 + 1)])
                wr = wout[0:D, si * 256:(si + 1) * 256].rearrange(
                    "(kc p) o -> p kc o", p=128)
                g.dma_start(b["wsl"][:, (si % 2) * WSLAB:
                                     (si % 2) * WSLAB + 2048].rearrange(
                    "p (kc o) -> p kc o", kc=8), wr).then_inc(
                    cx.sem("gs"), 16)
                prev_gs = gs_wout[(qb, si)]
        return None
    cx.on("g", gq)

    def qkv(q):
        for idx in range(NIDX):
            qb, hp = divmod(idx, NH)
            half = idx % 2
            if idx >= 2:
                q.wait_ge(cx.sem("ay"), ay_yevac[idx - 2])
            for r in range(4):
                q.dma_start(
                    ar1[:, half * 8192 + r * ncr * 128:
                        half * 8192 + (r + 1) * ncr * 128],
                    GK[r * D + hp * 128:r * D + (hp + 1) * 128,
                       0:ncr * 128]).then_inc(cx.sem("ds"), 16)
                src3 = GV[r * TOK:r * TOK + ncr * 128,
                          hp * 128:(hp + 1) * 128].rearrange(
                    "(c p) f -> p c f", p=128)
                dst3 = ar2[:, half * 8192 + r * ncr * 128:
                           half * 8192 + (r + 1) * ncr * 128].rearrange(
                    "p (c f) -> p c f", f=128)
                q.dma_start(dst3, src3).then_inc(cx.sem("ds"), 16)
            q.wait_ge(cx.sem("ay"), ay_recip[idx])
            q.dma_start(W["rbc"][0:1, :], tsc[32:33, 0:512]).then_inc(
                cx.sem("ds"), 16)
            q.wait_ge(cx.sem("ds"), ds_kv[idx] + 16)
            q.dma_start(W["rbc"][1:2, :], tsc[64:65, 0:512]).then_inc(
                cx.sem("ds"), 16)
            q.wait_ge(cx.sem("ds"), ds_kv[idx] + 32)
            q.dma_start(ar3[0:64, RB:RB + 1024].bitcast(F32),
                        W["rbc"][0:1, :].broadcast_to([64, 512])).then_inc(
                cx.sem("ds"), 16)
            q.wait_ge(cx.sem("ds"), ds_kv[idx] + 48)
            q.dma_start(ar3[64:128, RB:RB + 1024].bitcast(F32),
                        W["rbc"][1:2, :].broadcast_to([64, 512])).then_inc(
                cx.sem("ds"), 16)
            q.wait_ge(cx.sem("ds"), ds_rbc[idx])
        return None
    cx.on("q", qkv)

    def tmain(t):
        for idx in range(NIDX):
            qb, hp = divmod(idx, NH)
            half = idx % 2
            t.wait_ge(cx.sem("gs"), gs_qslab[idx])
            if idx >= 1:
                t.wait_ge(cx.sem("ab"), ab_biascopy[idx - 1])
                t.wait_ge(cx.sem("ax"), ax_exp[(idx - 1, nkc - 1)])
            qsl = ar3[:, QS + half * 1024:QS + half * 1024 + 1024]
            last = None
            for tc in range(4):
                for k in range(DCH):
                    last = t.matmul(
                        p[7][:, tc * 128:(tc + 1) * 128],
                        xnf_of(k)[:, qb * 512 + tc * 128:
                                  qb * 512 + (tc + 1) * 128],
                        qsl[:, k * 128:(k + 1) * 128],
                        start=(k == 0), stop=(k == DCH - 1))
            last.then_inc(cx.sem("pf"), 1)
            t.wait_ge(cx.sem("pa"), pa_ropeq[idx])
            if idx >= 1:
                t.wait_ge(cx.sem("ay"), ay_recip[idx - 1])
            qt = ar3[:, QT + half * 512:QT + half * 512 + 512]
            for tl in range(4):
                pt = p[5 + tl % 2].bitcast(BF16)[:, 0:128]
                if tl >= 2:
                    t.wait_ge(cx.sem("ga"), ga_qev[(idx, tl - 2)])
                elif idx >= 1:
                    t.wait_ge(cx.sem("ga"), ga_qev[(idx - 1, tl + 2)])
                t.transpose(pt, ar3[:, QG + tl * 128:QG + tl * 128 + 128],
                            ident).then_inc(cx.sem("pe_"), 1)
            t.wait_ge(cx.sem("ph"), ph_sq[idx])
            sq = ar3[:, SQ:SQ + 512]
            t.matmul(p[5][0:1, 0:512], onesb[0:64, 0:1], sq[0:64, 0:512],
                     start=True, stop=True, tile_position=(0, 0))
            t.matmul(p[6][0:1, 0:512], onesb[64:128, 0:1],
                     sq[64:128, 0:512], start=True, stop=True,
                     tile_position=(64, 0)).then_inc(cx.sem("pf"), 1)
            t.wait_ge(cx.sem("sa"), sa_qevac[idx])
            t.wait_ge(cx.sem("ab"), ab_negB[idx])
            t.matmul(p[7][:, 0:1], ones1[0:1, :], tsc[0:1, 56:57],
                     start=True, stop=True)
            t.matmul(p[7][:, 1:2], ones1[0:1, :], tsc[0:1, 57:58],
                     start=True, stop=True).then_inc(cx.sem("pf"), 1)
            t.wait_ge(cx.sem("ds"), ds_kv[idx])
            if idx >= 1:
                t.wait_ge(cx.sem("ay"), ay_yevac[idx - 1])
            if hp == 0 and qb >= 1:
                t.wait_ge(cx.sem("pg"), op_evac[(qb - 1, DCH - 1)])
            for ck, (r, c, cs, diag) in enumerate(cl_all):
                koff = half * 8192 + (r * ncr + c) * 128
                bA = p[0] if ck % 2 == 0 else p[4]
                bB = p[1] if ck % 2 == 0 else p[7]
                if ck >= 2:
                    t.wait_ge(cx.sem("ax"), ax_exp[(idx, ck - 2)])
                if ck == 1:
                    t.wait_ge(cx.sem("ab"), ab_biascopy[idx])
                t.matmul(bA[:, 0:512], ar1[0:64, koff:koff + 128],
                         qt[0:64, 0:512], start=True, stop=True,
                         tile_position=(0, 0))
                t.matmul(bB[:, 0:512], ar1[64:128, koff:koff + 128],
                         qt[64:128, 0:512], start=True, stop=True,
                         tile_position=(64, 0)).then_inc(cx.sem("as_"), 1)
                t.wait_ge(cx.sem("ax"), ax_exp[(idx, ck)])
                ptA = ar3[:, PT + (ck % 2) * 512:PT + (ck % 2) * 512 + 512]
                ptB = ar3[:, PT + 1024 + (ck % 2) * 512:
                          PT + 1024 + (ck % 2) * 512 + 512]
                first = (ck == 0)
                lastc = (ck == nkc - 1)
                t.matmul(p[2][0:64, 0:512], ar2[:, koff:koff + 64],
                         ptA[:, 0:512], start=first, stop=lastc,
                         tile_position=(0, 0))
                t.matmul(p[3][64:128, 0:512], ar2[:, koff + 64:koff + 128],
                         ptB[:, 0:512], start=first, stop=lastc,
                         tile_position=(0, 64))
                t.matmul(p[5][0:1, 0:512], onesb[:, 0:1], ptA[:, 0:512],
                         start=first, stop=lastc)
                t.matmul(p[6][0:1, 0:512], onesb[:, 1:2], ptB[:, 0:512],
                         start=first, stop=lastc).then_inc(cx.sem("av"), 1)
            if hp == NH - 1:
                t.wait_ge(cx.sem("ay"), ay_yevac[idx])
                for o in range(DCH):
                    si = o // 2
                    t.wait_ge(cx.sem("gs"), gs_wout[(qb, si)])
                    if o >= 2:
                        t.wait_ge(cx.sem("pg"), op_evac[(qb, o - 2)])
                    pp = p[o % 2]
                    lastm = None
                    for k in range(DCH):
                        base = (si % 2) * WSLAB + k * 256 + (o % 2) * 128
                        wt = b["wsl"][:, base:base + 128]
                        lastm = t.matmul(pp[:, 0:512], wt,
                                         ar3[:, YS + k * 512:
                                             YS + (k + 1) * 512],
                                         start=(k == 0),
                                         stop=(k == DCH - 1))
                    lastm.then_inc(cx.sem("pb"), 1)
        return None
    cx.on("t", tmain)

    def smain(s):
        for idx in range(NIDX):
            qb, hp = divmod(idx, NH)
            half = idx % 2
            s.wait_ge(cx.sem("pf"), pf_qproj[idx])
            if idx >= 2:
                s.wait_ge(cx.sem("as_"), as_scores[(idx - 2, nkc - 1)])
            if idx >= 1:
                s.wait_ge(cx.sem("pe_"), pe_tpq[(idx - 1, 3)])
            s.activation(ar3[:, QG:QG + 512], p[7][:, 0:512], AF.Copy,
                         scale=scale).then_inc(cx.sem("sa"), 1)
            qt = ar3[:, QT + half * 512:QT + half * 512 + 512]
            if idx >= 1:
                s.wait_ge(cx.sem("pf"), pf_qnorm[idx - 1])
            s.wait_ge(cx.sem("ga"), ga_qev[(idx, 3)])
            s.activation(ar3[:, SQ:SQ + 512], qt,
                         AF.Square).then_inc(cx.sem("ph"), 1)
            s.wait_ge(cx.sem("ab"), ab_biascopy[idx])
            for ck in range(nkc):
                s.wait_ge(cx.sem("as_"), as_scores[(idx, ck)])
                if ck >= 2:
                    s.wait_ge(cx.sem("av"), av_done[(idx, ck - 2)])
                ptA = ar3[:, PT + (ck % 2) * 512:PT + (ck % 2) * 512 + 512]
                ptB = ar3[:, PT + 1024 + (ck % 2) * 512:
                          PT + 1024 + (ck % 2) * 512 + 512]
                s.activation(ptA[:, 0:512],
                             (p[0] if ck % 2 == 0 else p[4])[:, 0:512],
                             AF.Exp, bias=tsc[:, 0:1])
                s.activation(ptB[:, 0:512],
                             (p[1] if ck % 2 == 0 else p[7])[:, 0:512],
                             AF.Exp, bias=tsc[:, 1:2]).then_inc(
                    cx.sem("ax"), 1)
        return None
    cx.on("s", smain)

    def vmain(v):
        for idx in range(NIDX):
            qb, hp = divmod(idx, NH)
            half = idx % 2
            v.wait_ge(cx.sem("sa"), sa_qevac[idx])
            v.wait_ge(cx.sem("gs"), gs_qrope[qb])
            st = ar3[:, QG:QG + 512]
            e = st.rearrange("p (tc h i two) -> p tc h i two", two=2, i=32,
                             h=2)
            ctab = ar3[:, QR:QR + 256].rearrange(
                "p (tc h i) -> p tc h i", i=32, h=2)
            stab = ar3[:, QR + 256:QR + 512].rearrange(
                "p (tc h i) -> p tc h i", i=32, h=2)
            sc1 = ar3[:, SCQ:SCQ + 256].rearrange(
                "p (tc h i) -> p tc h i", i=32, h=2)
            sc2 = ar3[:, SCQ + 256:SCQ + 512].rearrange(
                "p (tc h i) -> p tc h i", i=32, h=2)
            sc3 = ar3[:, SCQ + 512:SCQ + 768].rearrange(
                "p (tc h i) -> p tc h i", i=32, h=2)
            pa_base = pa_ropeq[idx] - 7

            def step(ins, k):
                ins.then_inc(cx.sem("pa"), 1)
                v.wait_ge(cx.sem("pa"), pa_base + k)
            step(v.tensor_tensor(sc1, e[..., 1], stab, ALU.mult), 1)
            step(v.tensor_tensor(sc2, e[..., 1], ctab, ALU.mult), 2)
            step(v.tensor_tensor(sc3, e[..., 0], stab, ALU.mult), 3)
            step(v.tensor_tensor(e[..., 0], e[..., 0], ctab, ALU.mult), 4)
            step(v.tensor_tensor(e[..., 0], e[..., 0], sc1,
                                 ALU.subtract), 5)
            step(v.tensor_tensor(e[..., 1], sc3, sc2, ALU.add), 6)
            v.engine_nop().then_inc(cx.sem("pa"), 1)
            qt = ar3[:, QT + half * 512:QT + half * 512 + 512]
            for tl in range(4):
                v.wait_ge(cx.sem("pe_"), pe_tpq[(idx, tl)])
                v.tensor_copy(qt[:, tl * 128:(tl + 1) * 128],
                              p[5 + tl % 2].bitcast(BF16)[:, 0:128]
                              ).then_inc(cx.sem("ga"), 1)
            v.wait_ge(cx.sem("pf"), pf_qnorm[idx])
            v.reduce_max(tsc[0:1, 52:53], p[5][0:1, 0:512], axis=AX.X)
            v.reduce_max(tsc[0:1, 53:54], p[6][0:1, 0:512],
                         axis=AX.X).then_inc(cx.sem("ab"), 1)
            v.wait_ge(cx.sem("ab"), ab_negB[idx] - 2)
            v.tensor_add(tsc[0:1, 54:56], tsc[0:1, 52:54],
                         tsc[0:1, 32 + 2 * hp:34 + 2 * hp]).then_inc(
                cx.sem("ab"), 1)
            v.wait_ge(cx.sem("ab"), ab_negB[idx] - 1)
            v.tensor_scalar_mul(tsc[0:1, 56:58], tsc[0:1, 54:56],
                                -0.5).then_inc(cx.sem("ab"), 1)
            v.wait_ge(cx.sem("pf"), pf_bias[idx])
            v.tensor_copy(tsc[:, 0:2], p[7][:, 0:2]).then_inc(
                cx.sem("ab"), 1)
            v.wait_ge(cx.sem("av"), av_done[(idx, nkc - 1)])
            v.reciprocal(tsc[32:33, 0:512], p[5][0:1, 0:512])
            v.reciprocal(tsc[64:65, 0:512], p[6][0:1, 0:512]).then_inc(
                cx.sem("ay"), 1)
            v.wait_ge(cx.sem("ds"), ds_rbc[idx])
            if hp == 0 and qb >= 1:
                v.wait_ge(cx.sem("pb"), op_mm[(qb - 1, DCH - 1)])
            ys = ar3[:, YS + hp * 512:YS + (hp + 1) * 512]
            v.tensor_tensor(ys[0:64, :], p[2][0:64, 0:512],
                            ar3[0:64, RB:RB + 1024].bitcast(F32), ALU.mult)
            v.tensor_tensor(ys[64:128, :], p[3][64:128, 0:512],
                            ar3[64:128, RB:RB + 1024].bitcast(F32),
                            ALU.mult).then_inc(cx.sem("ay"), 1)
            if hp == NH - 1:
                for o in range(DCH):
                    v.wait_ge(cx.sem("pb"), op_mm[(qb, o)])
                    xso = xs[:, o * TOK + qb * 512:o * TOK + (qb + 1) * 512]
                    v.scalar_tensor_tensor(xso, p[o % 2][:, 0:512], 1.0,
                                           xso, ALU.mult,
                                           ALU.add).then_inc(cx.sem("pg"), 1)
        return None
    cx.on("v", vmain)
    cx.seal("v", "pg")
    cx.barrier()


def emit_l1_attn(cx, W):
    b = cx.buf
    p = b["p"]
    xs, xnf = b["xs"], b["xnf"]
    ar1, ar2, ar3 = b["ar1"], b["ar2"], b["ar3"]
    tsc, msk = b["tsc"], b["msk"]
    ones1, onesb = b["ones1"], b["onesb"]
    vrow = b["vrow"]
    NH = 8

    def xs_of(ch):
        return xs[:, ch * TOK:(ch + 1) * TOK]

    def xnf_of(ch):
        return xnf[:, ch * TOK:(ch + 1) * TOK]

    ident = msk[:, 512:640]

    def gid(g):
        g.memset(msk[:, 640:768], 1.0).then_inc(cx.sem("ga"), 1)
        cx.bump("ga")
        g.wait_ge(cx.sem("ga"), cx.cnt["ga"])
        return g.affine_select(ident, msk[:, 640:768], [[-1, 128]],
                               ALU.is_equal, 0.0, channel_multiplier=1)
    cx.on("g", gid)
    cx.dma("g", msk[:, 0:512].rearrange("p (r f) -> p r f", r=4),
           W["trimask"].rearrange("r p f -> p r f"))
    cx.dma("q", vrow[:, 32:40], W["l1a_bg2"][:, :])
    cx.dma("q", vrow[:, 41:49], W["l1a_gamma"][:, :])
    cx.dma("q", vrow[:, 49:57], W["l1a_beta"][:, :])
    cx.dma("q", vrow[0:64, 40:41], W["l1a_bg1"][:, :])
    cx.barrier()

    emit_norm(cx, xs_of, xnf_of, kind="ln")

    def ev_silu_ar1(s, o, qc, pp):
        return s.activation(ar1[:, o * TOK + qc * QCOL:
                                o * TOK + (qc + 1) * QCOL], pp[:, :],
                            AF.Silu)
    emit_proj(cx, W["l1a_wqkv_t"], xnf_of, D, D, evac_s=ev_silu_ar1,
              col_off=D)

    def ev_cp_ar2(v, o, qc, pp):
        return v.tensor_copy(ar2[:, o * TOK + qc * QCOL:
                                 o * TOK + (qc + 1) * QCOL], pp[:, :])
    emit_proj(cx, W["l1a_wqkv_t"], xnf_of, D, D, evac_v=ev_cp_ar2,
              col_off=2 * D)

    cx.dma("q", W["xgk_in"].rearrange("(c p) t -> p c t", p=128),
           ar1.rearrange("p (c t) -> p c t", c=DCH))
    emit_fm_to_tokmajor(cx, lambda o: ar2[:, o * TOK:(o + 1) * TOK],
                        W["xgv_in"], D, ident)

    def gcoll(g):
        g.wait_ge(cx.sem("ds"), cx.cnt["ds"])
        for a, o in (("xgk_in", "xgk_out"), ("xgv_in", "xgv_out")):
            g.collective_compute("AllGather", ALU.bypass, replica_groups=RG,
                                 ins=[W[a].ap().opt()],
                                 outs=[W[o].ap().opt()]).then_inc(
                cx.sem("cc"), 1)
            cx.bump("cc")
        return None
    cx.on("g", gcoll)
    cx.wait("q", "cc", cx.cnt["cc"])
    GW = 13824
    cx.dma("g", ar3[:, GW:GW + 512].rearrange("p (kc o) -> p kc o", kc=8),
           W["l1a_wg1_t"].rearrange("(kc p) o -> p kc o", p=128))
    gw2 = cx.dma("g", ar3[0:64, GW + 512:GW + 1536].rearrange(
        "p (oc o) -> p oc o", oc=8),
        W["l1a_wg2_t"].rearrange("p (oc o) -> p oc o", oc=8))
    cx.wait("t", "gs", gw2)
    cx.barrier()

    GK, GV = W["xgk_out"], W["xgv_out"]
    wqkv = W["l1a_wqkv_t"]
    wout = W["l1a_wout_t"]
    NIDX = NQB * NH
    chunks = {qb: attn_chunks_of(qb) for qb in range(NQB)}

    gq0 = cx.cnt["gs"]
    pf_qproj = {}
    sa_qevac = {}
    as_scores = {}
    ax_ev = {}
    am_mask = {}
    av_done = {}
    ay_yevac = {}
    ds_kv = {}
    gs_qslab = {}
    gs_wout = {}
    op_mm = {}
    op_evac = {}
    pc_g1 = {}
    pd_g1ev = {}
    pc_g2 = {}
    pd_g2 = {}
    pd_sq = {}
    pc_sums = {}
    pe_stats = {}
    sa_rstd = {}
    pf_bc = {}
    ab_apply = {}

    pf = cx.cnt["pf"]
    sa = cx.cnt["sa"]
    ab = cx.cnt["ab"]
    asx = cx.cnt["as_"]
    ax = cx.cnt["ax"]
    am = cx.cnt["am"]
    av = cx.cnt["av"]
    ay = cx.cnt["ay"]
    ds = cx.cnt["ds"]
    gs = gq0
    pb = cx.cnt["pb"]
    pg = cx.cnt["pg"]
    pc = cx.cnt["pc"]
    pd = cx.cnt["pd"]
    pe = cx.cnt["pe_"]
    for qb in range(NQB):
        cl = chunks[qb]
        pc += 1
        pc_g1[qb] = pc
        pd += 1
        pd_g1ev[qb] = pd
        for o in range(DCH):
            pc += 1
            pc_g2[(qb, o)] = pc
            pd += 1
            pd_g2[(qb, o)] = pd
        for hp in range(NH):
            idx = qb * NH + hp
            gs += 16
            gs_qslab[idx] = gs
            pf += 1
            pf_qproj[idx] = pf
            sa += 1
            sa_qevac[idx] = sa
            ds += 16 * 8
            ds_kv[idx] = ds
            for ck in range(len(cl)):
                asx += 1
                as_scores[(idx, ck)] = asx
                ax += 1
                ax_ev[(idx, ck)] = ax
                if cl[ck][3]:
                    am += 1
                    am_mask[(idx, ck)] = am
                av += 1
                av_done[(idx, ck)] = av
            ay += 1
            ay_yevac[idx] = ay
        for ch in range(DCH):
            pd += 1
            pd_sq[(qb, ch)] = pd
            pc += 1
            pc_sums[(qb, ch)] = pc
        pe += 1
        pe_stats[qb] = pe
        sa += 2
        sa_rstd[qb] = sa
        pf += 1
        pf_bc[qb] = pf
        ab += 8
        ab_apply[qb] = ab
        for si in range(4):
            gs += 16
            gs_wout[(qb, si)] = gs
        for o in range(DCH):
            pb += 1
            op_mm[(qb, o)] = pb
            pg += 1
            op_evac[(qb, o)] = pg
    cx.cnt["pf"] = pf
    cx.cnt["sa"] = sa
    cx.cnt["ab"] = ab
    cx.cnt["as_"] = asx
    cx.cnt["ax"] = ax
    cx.cnt["am"] = am
    cx.cnt["av"] = av
    cx.cnt["ay"] = ay
    cx.cnt["ds"] = ds
    cx.cnt["gs"] = gs
    cx.cnt["pb"] = pb
    cx.cnt["pg"] = pg
    cx.cnt["pc"] = pc
    cx.cnt["pd"] = pd
    cx.cnt["pe_"] = pe

    QT = 0
    PT = 1024
    GT = 3072
    YS = 7168
    QS = 11264
    SQ = 13312

    def gq(g):
        prev_gs = gq0
        for qb in range(NQB):
            for hp in range(NH):
                idx = qb * NH + hp
                g.wait_ge(cx.sem("gs"), prev_gs)
                if idx >= 2:
                    g.wait_ge(cx.sem("pf"), pf_qproj[idx - 2])
                half = idx % 2
                wr = wqkv[0:D, hp * 128:(hp + 1) * 128].rearrange(
                    "(kc p) o -> p kc o", p=128)
                g.dma_start(ar3[:, QS + half * 1024:QS + half * 1024 +
                                1024].rearrange("p (kc o) -> p kc o", kc=8),
                            wr).then_inc(cx.sem("gs"), 16)
                prev_gs = gs_qslab[idx]
            for si in range(4):
                g.wait_ge(cx.sem("gs"), prev_gs)
                if si >= 2:
                    g.wait_ge(cx.sem("pb"), op_mm[(qb, (si - 2) * 2 + 1)])
                elif qb >= 1:
                    g.wait_ge(cx.sem("pb"),
                              op_mm[(qb - 1, (si + 2) * 2 + 1)])
                wr = wout[0:D, si * 256:(si + 1) * 256].rearrange(
                    "(kc p) o -> p kc o", p=128)
                g.dma_start(b["wsl"][:, (si % 2) * WSLAB:
                                     (si % 2) * WSLAB + 2048].rearrange(
                    "p (kc o) -> p kc o", kc=8), wr).then_inc(
                    cx.sem("gs"), 16)
                prev_gs = gs_wout[(qb, si)]
        return None
    cx.on("g", gq)

    def qkv(q):
        for idx in range(NIDX):
            qb, hp = divmod(idx, NH)
            ncr = (qb + 1) * 4
            half = idx % 2
            if idx >= 2:
                q.wait_ge(cx.sem("ay"), ay_yevac[idx - 2])
            for r in range(4):
                q.dma_start(
                    ar1[:, half * 8192 + r * ncr * 128:
                        half * 8192 + (r + 1) * ncr * 128],
                    GK[r * D + hp * 128:r * D + (hp + 1) * 128,
                       0:ncr * 128]).then_inc(cx.sem("ds"), 16)
                src3 = GV[r * TOK:r * TOK + ncr * 128,
                          hp * 128:(hp + 1) * 128].rearrange(
                    "(c p) f -> p c f", p=128)
                dst3 = ar2[:, half * 8192 + r * ncr * 128:
                           half * 8192 + (r + 1) * ncr * 128].rearrange(
                    "p (c f) -> p c f", f=128)
                q.dma_start(dst3, src3).then_inc(cx.sem("ds"), 16)
            q.wait_ge(cx.sem("ds"), ds_kv[idx])
        return None
    cx.on("q", qkv)

    def tmain(t):
        for idx in range(NIDX):
            qb, hp = divmod(idx, NH)
            cl = chunks[qb]
            nkc = len(cl)
            ncr = (qb + 1) * 4
            half = idx % 2
            if hp == 0:
                if qb >= 1:
                    t.wait_ge(cx.sem("pg"), op_evac[(qb - 1, DCH - 1)])
                    t.wait_ge(cx.sem("ab"), ab_apply[qb - 1])
                lastm = None
                for k in range(DCH):
                    lastm = t.matmul(p[5][0:64, 0:512],
                                     ar3[:, GW + k * 64:GW + (k + 1) * 64],
                                     xnf_of(k)[:, qb * 512:(qb + 1) * 512],
                                     start=(k == 0), stop=(k == DCH - 1))
                lastm.then_inc(cx.sem("pc"), 1)
                t.wait_ge(cx.sem("pd"), pd_g1ev[qb])
                for o in range(DCH):
                    if o >= 1:
                        t.wait_ge(cx.sem("pd"), pd_g2[(qb, o - 1)])
                    t.matmul(p[6][:, 0:512],
                             ar3[0:64, GW + 512 + o * 128:
                                 GW + 512 + (o + 1) * 128],
                             ar3[0:64, SQ:SQ + 512], start=True,
                             stop=True).then_inc(cx.sem("pc"), 1)
            t.wait_ge(cx.sem("gs"), gs_qslab[idx])
            if idx >= 1:
                t.wait_ge(cx.sem("sa"), sa_qevac[idx - 1])
            qsl = ar3[:, QS + half * 1024:QS + half * 1024 + 1024]
            last = None
            for k in range(DCH):
                last = t.matmul(p[7][:, 0:512],
                                qsl[:, k * 128:(k + 1) * 128],
                                xnf_of(k)[:, qb * 512:(qb + 1) * 512],
                                start=(k == 0), stop=(k == DCH - 1))
            last.then_inc(cx.sem("pf"), 1)
            t.wait_ge(cx.sem("ds"), ds_kv[idx])
            t.wait_ge(cx.sem("sa"), sa_qevac[idx])
            if idx >= 1:
                t.wait_ge(cx.sem("ay"), ay_yevac[idx - 1])
            qt = ar3[:, QT + half * 512:QT + half * 512 + 512]
            for ck, (r, c, cs, diag) in enumerate(cl):
                koff = half * 8192 + (r * ncr + c) * 128
                bA = p[0] if ck % 2 == 0 else p[4]
                bB = p[1] if ck % 2 == 0 else p[6]
                if ck >= 2:
                    t.wait_ge(cx.sem("ax"), ax_ev[(idx, ck - 2)])
                if ck == 1 and hp == 0:
                    t.wait_ge(cx.sem("pd"), pd_g2[(qb, DCH - 1)])
                t.matmul(bA[:, cs:512], ar1[0:64, koff:koff + 128],
                         qt[0:64, cs:512], start=True, stop=True,
                         tile_position=(0, 0))
                t.matmul(bB[:, cs:512], ar1[64:128, koff:koff + 128],
                         qt[64:128, cs:512], start=True, stop=True,
                         tile_position=(64, 0)).then_inc(cx.sem("as_"), 1)
                if diag:
                    t.wait_ge(cx.sem("am"), am_mask[(idx, ck)])
                else:
                    t.wait_ge(cx.sem("ax"), ax_ev[(idx, ck)])
                ptA = ar3[:, PT + (ck % 2) * 512:PT + (ck % 2) * 512 + 512]
                ptB = ar3[:, PT + 1024 + (ck % 2) * 512:
                          PT + 1024 + (ck % 2) * 512 + 512]
                first = (ck == 0)
                lastc = (ck == nkc - 1)
                t.matmul(p[2][0:64, cs:512], ar2[:, koff:koff + 64],
                         ptA[:, cs:512], start=first, stop=lastc,
                         tile_position=(0, 0))
                t.matmul(p[3][64:128, cs:512], ar2[:, koff + 64:koff + 128],
                         ptB[:, cs:512], start=first, stop=lastc,
                         tile_position=(0, 64)).then_inc(cx.sem("av"), 1)
            if hp == NH - 1:
                t.wait_ge(cx.sem("ay"), ay_yevac[idx])
                for ch in range(DCH):
                    t.wait_ge(cx.sem("pd"), pd_sq[(qb, ch)])
                    t.matmul(p[5][0:1, 0:512], onesb[:, 0:1],
                             ar3[:, YS + ch * 512:YS + (ch + 1) * 512],
                             start=(ch == 0), stop=(ch == DCH - 1))
                    t.matmul(p[6][0:1, 0:512], onesb[:, 1:2],
                             ar3[:, SQ:SQ + 512], start=(ch == 0),
                             stop=(ch == DCH - 1)).then_inc(cx.sem("pc"), 1)
                t.wait_ge(cx.sem("sa"), sa_rstd[qb])
                t.matmul(p[5][:, 0:512], ones1[0:1, :], tsc[0:1, 0:512],
                         start=True, stop=True)
                t.matmul(p[6][:, 0:512], ones1[64:65, :],
                         tsc[64:65, 0:512], start=True,
                         stop=True).then_inc(cx.sem("pf"), 1)
                t.wait_ge(cx.sem("ab"), ab_apply[qb])
                for o in range(DCH):
                    si = o // 2
                    t.wait_ge(cx.sem("gs"), gs_wout[(qb, si)])
                    if o >= 2:
                        t.wait_ge(cx.sem("pg"), op_evac[(qb, o - 2)])
                    pp = p[o % 2]
                    lastm = None
                    for k in range(DCH):
                        base = (si % 2) * WSLAB + k * 256 + (o % 2) * 128
                        wt = b["wsl"][:, base:base + 128]
                        lastm = t.matmul(pp[:, 0:512], wt,
                                         ar3[:, YS + k * 512:
                                             YS + (k + 1) * 512],
                                         start=(k == 0),
                                         stop=(k == DCH - 1))
                    lastm.then_inc(cx.sem("pb"), 1)
        return None
    cx.on("t", tmain)

    def smain(s):
        for qb in range(NQB):
            cl = chunks[qb]
            nkc = len(cl)
            for o in range(DCH):
                s.wait_ge(cx.sem("pc"), pc_g2[(qb, o)])
                s.activation(ar3[:, GT + o * 512:GT + (o + 1) * 512],
                             p[6][:, 0:512], AF.Sigmoid,
                             bias=vrow[:, 32 + o:33 + o]).then_inc(
                    cx.sem("pd"), 1)
            for hp in range(NH):
                idx = qb * NH + hp
                half = idx % 2
                s.wait_ge(cx.sem("pf"), pf_qproj[idx])
                if idx >= 2:
                    s.wait_ge(cx.sem("as_"),
                              as_scores[(idx - 2,
                                         len(chunks[(idx - 2) // NH]) - 1)])
                qt = ar3[:, QT + half * 512:QT + half * 512 + 512]
                s.activation(qt, p[7][:, 0:512], AF.Silu).then_inc(
                    cx.sem("sa"), 1)
                for ck, (r, c, cs, diag) in enumerate(cl):
                    s.wait_ge(cx.sem("as_"), as_scores[(idx, ck)])
                    if ck >= 2:
                        s.wait_ge(cx.sem("av"), av_done[(idx, ck - 2)])
                    ptA = ar3[:, PT + (ck % 2) * 512:
                              PT + (ck % 2) * 512 + 512]
                    ptB = ar3[:, PT + 1024 + (ck % 2) * 512:
                              PT + 1024 + (ck % 2) * 512 + 512]
                    s.activation(ptA[:, cs:512],
                                 (p[0] if ck % 2 == 0 else p[4])[:, cs:512],
                                 AF.Copy)
                    s.activation(ptB[:, cs:512],
                                 (p[1] if ck % 2 == 0 else p[6])[:, cs:512],
                                 AF.Copy).then_inc(cx.sem("ax"), 1)
            for ch in range(DCH):
                if ch == 0:
                    s.wait_ge(cx.sem("ay"), ay_yevac[qb * NH + NH - 1])
                else:
                    s.wait_ge(cx.sem("pc"), pc_sums[(qb, ch - 1)])
                s.activation(ar3[:, SQ:SQ + 512],
                             ar3[:, YS + ch * 512:YS + (ch + 1) * 512],
                             AF.Square).then_inc(cx.sem("pd"), 1)
            s.wait_ge(cx.sem("pe_"), pe_stats[qb])
            s.activation(tsc[96:97, 0:512] if False else tsc[0:1, 32:48]
                         if False else tsc[32 + 32:65, 0:512], tsc[32:33,
                                                                   0:512],
                         AF.Ln).then_inc(cx.sem("sa"), 1)
            s.wait_ge(cx.sem("sa"), sa_rstd[qb] - 1)
            s.activation(tsc[64:65, 0:512], tsc[64:65, 0:512], AF.Exp,
                         scale=-0.5).then_inc(cx.sem("sa"), 1)
        return None
    cx.on("s", smain)

    def vmain(v):
        for qb in range(NQB):
            cl = chunks[qb]
            nkc = len(cl)
            v.wait_ge(cx.sem("pc"), pc_g1[qb])
            v.tensor_scalar_add(ar3[0:64, SQ:SQ + 512], p[5][0:64, 0:512],
                                vrow[0:64, 40:41]).then_inc(cx.sem("pd"), 1)
            for hp in range(NH):
                idx = qb * NH + hp
                for ck, (r, c, cs, diag) in enumerate(cl):
                    if not diag:
                        continue
                    v.wait_ge(cx.sem("ax"), ax_ev[(idx, ck)])
                    ptA = ar3[:, PT + (ck % 2) * 512:
                              PT + (ck % 2) * 512 + 512]
                    ptB = ar3[:, PT + 1024 + (ck % 2) * 512:
                              PT + 1024 + (ck % 2) * 512 + 512]
                    we = min(cs + 128, 512)
                    v.tensor_mul(ptA[:, cs:we], ptA[:, cs:we],
                                 msk[:, r * 128:r * 128 + (we - cs)])
                    v.tensor_mul(ptB[:, cs:we], ptB[:, cs:we],
                                 msk[:, r * 128:r * 128 +
                                     (we - cs)]).then_inc(cx.sem("am"), 1)
                v.wait_ge(cx.sem("av"), av_done[(idx, nkc - 1)])
                v.wait_ge(cx.sem("pd"), pd_g2[(qb, hp)])
                ys = ar3[:, YS + hp * 512:YS + (hp + 1) * 512]
                gt = ar3[:, GT + hp * 512:GT + (hp + 1) * 512]
                v.tensor_tensor(ys[0:64, :], p[2][0:64, 0:512],
                                gt[0:64, :], ALU.mult)
                v.tensor_tensor(ys[64:128, :], p[3][64:128, 0:512],
                                gt[64:128, :], ALU.mult).then_inc(
                    cx.sem("ay"), 1)
            v.wait_ge(cx.sem("pc"), pc_sums[(qb, DCH - 1)])
            v.tensor_scalar_mul(tsc[0:1, 0:512], p[5][0:1, 0:512],
                                1.0 / D).then_inc(cx.sem("pa"), 1)
            cx.bump("pa")
            v.wait_ge(cx.sem("pa"), cx.cnt["pa"])
            v.tensor_mul(tsc[96:97, 0:512] if False else
                         tsc[32:33, 0:512], tsc[0:1, 0:512],
                         tsc[0:1, 0:512]).then_inc(cx.sem("pa"), 1)
            cx.bump("pa")
            v.wait_ge(cx.sem("pa"), cx.cnt["pa"])
            v.scalar_tensor_tensor(tsc[32:33, 0:512], p[6][0:1, 0:512],
                                   1.0 / D, tsc[32:33, 0:512], ALU.mult,
                                   ALU.subtract).then_inc(cx.sem("pa"), 1)
            cx.bump("pa")
            v.wait_ge(cx.sem("pa"), cx.cnt["pa"])
            v.tensor_scalar_add(tsc[32:33, 0:512], tsc[32:33, 0:512],
                                EPS_LN).then_inc(cx.sem("pe_"), 1)
            v.wait_ge(cx.sem("pf"), pf_bc[qb])
            for ch in range(DCH):
                if ch >= 1:
                    v.wait_ge(cx.sem("ab"), ab_apply[qb] - 8 + ch)
                ysc = ar3[:, YS + ch * 512:YS + (ch + 1) * 512]
                v.tensor_sub(tsc[:, 0:512],
                             ysc, p[5][:, 0:512]).then_inc(cx.sem("pa"), 1)
                cx.bump("pa")
                v.wait_ge(cx.sem("pa"), cx.cnt["pa"])
                v.scalar_tensor_tensor(tsc[:, 0:512], tsc[:, 0:512],
                                       vrow[:, 41 + ch:42 + ch],
                                       p[6][:, 0:512], ALU.mult,
                                       ALU.mult).then_inc(cx.sem("pa"), 1)
                cx.bump("pa")
                v.wait_ge(cx.sem("pa"), cx.cnt["pa"])
                v.tensor_scalar_add(ysc, tsc[:, 0:512],
                                    vrow[:, 49 + ch:50 + ch]).then_inc(
                    cx.sem("ab"), 1)
            for o in range(DCH):
                v.wait_ge(cx.sem("pb"), op_mm[(qb, o)])
                xso = xs[:, o * TOK + qb * 512:o * TOK + (qb + 1) * 512]
                v.scalar_tensor_tensor(xso, p[o % 2][:, 0:512], 1.0, xso,
                                       ALU.mult, ALU.add).then_inc(
                    cx.sem("pg"), 1)
        return None
    cx.on("v", vmain)
    cx.seal("v", "pg")
    cx.barrier()


def emit_l3_ffn(cx, W):
    b = cx.buf
    p = b["p"]
    xs, xnf = b["xs"], b["xnf"]
    ar1, ar2, ar3 = b["ar1"], b["ar2"], b["ar3"]
    tsc, wsl = b["tsc"], b["wsl"]
    ones1, onesb = b["ones1"], b["onesb"]
    HK = 16

    def xs_of(ch):
        return xs[:, ch * TOK:(ch + 1) * TOK]

    def xnf_of(ch):
        return xnf[:, ch * TOK:(ch + 1) * TOK]

    def h_of(k):
        return (ar1 if k < 8 else ar2)[:, (k % 8) * TOK:(k % 8) * TOK + TOK]

    AST = 0
    SST = 8192
    SCL = 12288
    WB = 13312
    WSE = 13824
    WT = 14336

    def gid(g):
        g.memset(b["msk"][:, 640:768], 1.0).then_inc(cx.sem("ga"), 1)
        cx.bump("ga")
        g.wait_ge(cx.sem("ga"), cx.cnt["ga"])
        return g.affine_select(b["msk"][:, 512:640], b["msk"][:, 640:768],
                               [[-1, 128]], ALU.is_equal, 0.0,
                               channel_multiplier=1)
    cx.on("g", gid)
    cx.barrier()

    emit_norm(cx, xs_of, xnf_of, kind="ln")

    bw = cx.dma("q", tsc[:, 0:16],
                W["l3f_geb"][0:1, :].broadcast_to([128, 16]))
    cx.wait("v", "ds", bw)

    gw2 = None
    cx.dma("g", wsl[:, 0:16].rearrange("p (kc o) -> p kc o", kc=8),
           W["l3f_wg_t"].rearrange("(kc p) o -> p kc o", p=128))
    gw2 = cx.dma("g", wsl[:, 16:80].rearrange("p (kc o) -> p kc o", kc=8),
                 W["l3f_we_t"].rearrange("(kc p) o -> p kc o", p=128))
    cx.wait("t", "gs", gw2)

    NTC = 16
    pc_gl = {}
    ab_rel = {}
    ph_pre = {}
    sa_exp = {}
    am_wse = {}
    pe_tp = {}
    av_wt = {}
    pc = cx.cnt["pc"]
    ab = cx.cnt["ab"]
    ph = cx.cnt["ph"]
    sa = cx.cnt["sa"]
    am = cx.cnt["am"]
    pe = cx.cnt["pe_"]
    av = cx.cnt["av"]
    for tc in range(NTC):
        pc += 1
        pc_gl[tc] = pc
        ab += 1
        ab_rel[tc] = ab
        ph += 1
        ph_pre[tc] = ph
        sa += 1
        sa_exp[tc] = sa
        am += 1
        am_wse[tc] = am
        pe += 1
        pe_tp[tc] = pe
        av += 1
        av_wt[tc] = av
    cx.cnt["pc"] = pc
    cx.cnt["ab"] = ab
    cx.cnt["ph"] = ph
    cx.cnt["sa"] = sa
    cx.cnt["am"] = am
    cx.cnt["pe_"] = pe
    cx.cnt["av"] = av

    def tg(t):
        for tc in range(NTC):
            if tc >= 2:
                t.wait_ge(cx.sem("ab"), ab_rel[tc - 2])
            last = None
            for k in range(DCH):
                last = t.matmul(p[4 + tc % 2][:, 0:2],
                                xnf_of(k)[:, tc * 128:(tc + 1) * 128],
                                wsl[:, k * 2:k * 2 + 2], start=(k == 0),
                                stop=(k == DCH - 1))
            for k in range(DCH):
                last = t.matmul(p[4 + tc % 2][:, 8:16],
                                xnf_of(k)[:, tc * 128:(tc + 1) * 128],
                                wsl[:, 16 + k * 8:16 + k * 8 + 8],
                                start=(k == 0), stop=(k == DCH - 1))
            last.then_inc(cx.sem("pc"), 1)
            t.wait_ge(cx.sem("am"), am_wse[tc])
            if tc >= 2:
                t.wait_ge(cx.sem("av"), av_wt[tc - 2])
            t.transpose(p[6 + tc % 2].bitcast(BF16)[0:8, 0:128],
                        ar3[:, WSE + (tc % 2) * 8:WSE + (tc % 2) * 8 + 8],
                        b["msk"][:, 512:640]).then_inc(cx.sem("pe_"), 1)
        return None
    cx.on("t", tg)

    def sg(s):
        for tc in range(NTC):
            s.wait_ge(cx.sem("ph"), ph_pre[tc])
            s.activation(tsc[:, 62:64], tsc[:, 62:64], AF.Exp)
            s.activation(tsc[:, 52:60], tsc[:, 52:60],
                         AF.Exp).then_inc(cx.sem("sa"), 1)
        return None
    cx.on("s", sg)

    def vg(v):
        pa = cx.cnt["pa"]

        def st(ins):
            nonlocal pa
            ins.then_inc(cx.sem("pa"), 1)
            pa += 1
            v.wait_ge(cx.sem("pa"), pa)
        for tc in range(NTC):
            v.wait_ge(cx.sem("pc"), pc_gl[tc])
            pp = p[4 + tc % 2]
            st(v.tensor_tensor(tsc[:, 32:34], pp[:, 0:2], tsc[:, 0:2],
                               ALU.add))
            v.tensor_tensor(tsc[:, 34:42], pp[:, 8:16], tsc[:, 8:16],
                            ALU.add).then_inc(cx.sem("ab"), 1)
            v.wait_ge(cx.sem("ab"), ab_rel[tc])
            st(v.tensor_tensor(tsc[:, 42:43], tsc[:, 32:33], tsc[:, 33:34],
                               ALU.is_ge))
            st(v.tensor_scalar(tsc[:, 43:44], tsc[:, 42:43], -1.0, 1.0,
                               ALU.mult, ALU.add))
            st(v.tensor_scalar(tsc[:, 44:48], ones1[:, 0:4], tsc[:, 43:44],
                               None, ALU.mult))
            st(v.tensor_scalar(tsc[:, 48:52], ones1[:, 0:4], tsc[:, 42:43],
                               None, ALU.mult))
            st(v.scalar_tensor_tensor(tsc[:, 52:60], tsc[:, 44:52], -1e9,
                                      tsc[:, 34:42], ALU.mult, ALU.add))
            st(v.reduce_max(tsc[:, 60:61], tsc[:, 52:60], axis=AX.X))
            st(v.reduce_max(tsc[:, 61:62], tsc[:, 32:34], axis=AX.X))
            st(v.tensor_scalar(tsc[:, 62:64], tsc[:, 32:34], tsc[:, 61:62],
                               None, ALU.subtract))
            v.tensor_scalar(tsc[:, 52:60], tsc[:, 52:60], tsc[:, 60:61],
                            None, ALU.subtract).then_inc(cx.sem("ph"), 1)
            pa += 0
            v.wait_ge(cx.sem("sa"), sa_exp[tc])
            st(v.reduce_sum(tsc[:, 64:65], tsc[:, 62:64], axis=AX.X))
            st(v.reciprocal(tsc[:, 65:66], tsc[:, 64:65]))
            st(v.tensor_scalar(tsc[:, 62:64], tsc[:, 62:64], tsc[:, 65:66],
                               None, ALU.mult))
            st(v.tensor_tensor(tsc[:, 66:67], tsc[:, 62:63], tsc[:, 63:64],
                               ALU.subtract))
            st(v.scalar_tensor_tensor(tsc[:, 67:68], tsc[:, 66:67],
                                      tsc[:, 42:43], tsc[:, 63:64],
                                      ALU.mult, ALU.add))
            st(v.reduce_sum(tsc[:, 68:69], tsc[:, 52:60], axis=AX.X))
            st(v.reciprocal(tsc[:, 69:70], tsc[:, 68:69]))
            st(v.tensor_scalar(tsc[:, 52:60], tsc[:, 52:60], tsc[:, 69:70],
                               None, ALU.mult))
            st(v.tensor_scalar(tsc[:, 52:60], tsc[:, 52:60], tsc[:, 67:68],
                               None, ALU.mult))
            st(v.reduce_max(tsc[:, 70:71], tsc[:, 52:60], axis=AX.X))
            st(v.tensor_scalar(tsc[:, 71:79], tsc[:, 52:60], tsc[:, 70:71],
                               None, ALU.is_ge))
            st(v.scalar_tensor_tensor(tsc[:, 79:87], tsc[:, 71:79], -1e9,
                                      tsc[:, 52:60], ALU.mult, ALU.add))
            st(v.reduce_max(tsc[:, 87:88], tsc[:, 79:87], axis=AX.X))
            st(v.tensor_scalar(tsc[:, 88:96], tsc[:, 52:60], tsc[:, 87:88],
                               None, ALU.is_ge))
            st(v.tensor_tensor(tsc[:, 96:104], tsc[:, 52:60],
                               tsc[:, 88:96], ALU.mult))
            v.tensor_copy(ar3[:, WSE + (tc % 2) * 8:WSE + (tc % 2) * 8 + 8],
                          tsc[:, 96:104]).then_inc(cx.sem("am"), 1)
            v.wait_ge(cx.sem("pe_"), pe_tp[tc])
            v.tensor_copy(b["ar3"][0:8, WT + tc * 128:WT + (tc + 1) * 128],
                          p[6 + tc % 2].bitcast(BF16)[0:8, 0:128]).then_inc(
                cx.sem("av"), 1)
        return None
    cx.on("v", vg)
    cx.bump("pa", 24 * NTC)
    cx.seal("v", "av")
    cx.barrier()

    pc0 = cx.cnt["pc"]
    pd0 = cx.cnt["pd"]
    pe0 = cx.cnt["pe_"]
    gs0 = cx.cnt["gs"]

    def gh(g):
        prev = gs0
        for o2 in range(HK):
            for half_ab in range(2):
                sl = o2 * 2 + half_ab
                if prev > gs0:
                    g.wait_ge(cx.sem("gs"), prev)
                if sl >= 4:
                    t_sl = sl - 4
                    g.wait_ge(cx.sem("pc"), pc0 + (t_sl // 2) * 2 * NQC +
                              (t_sl % 2) * NQC + NQC)
                col = o2 * 128 + half_ab * 2048
                wr = W["l3f_w1_t"][0:D, col:col + 128].rearrange(
                    "(kc p) o -> p kc o", p=128)
                g.dma_start(wsl[:, (sl % 4) * 1024:(sl % 4) * 1024 +
                                1024].rearrange("p (kc o) -> p kc o", kc=8),
                            wr).then_inc(cx.sem("gs"), 16)
                prev = gs0 + 16 * (sl + 1)
        return None
    cx.on("g", gh)
    cx.bump("gs", 16 * 2 * HK)

    def th(t):
        for o2 in range(HK):
            for half_ab in range(2):
                sl = o2 * 2 + half_ab
                t.wait_ge(cx.sem("gs"), gs0 + 16 * (sl + 1))
                for qc in range(NQC):
                    j = o2 * NQC + qc
                    bank = p[(half_ab * 2) + (qc % 2)]
                    if j >= 2:
                        if half_ab == 0:
                            t.wait_ge(cx.sem("pd"), pd0 + j - 1)
                        else:
                            t.wait_ge(cx.sem("pe_"), pe0 + j - 1)
                    last = None
                    for k in range(DCH):
                        last = t.matmul(
                            bank[:, 0:512],
                            wsl[:, (sl % 4) * 1024 + k * 128:
                                (sl % 4) * 1024 + (k + 1) * 128],
                            xnf_of(k)[:, qc * QCOL:(qc + 1) * QCOL],
                            start=(k == 0), stop=(k == DCH - 1))
                    last.then_inc(cx.sem("pc"), 1)
        return None
    cx.on("t", th)
    cx.bump("pc", 2 * HK * NQC)

    def sh(s):
        for o2 in range(HK):
            for qc in range(NQC):
                i = (o2 * 2 + 0) * NQC + qc
                s.wait_ge(cx.sem("pc"), pc0 + i + 1)
                if o2 >= 1:
                    s.wait_ge(cx.sem("pe_"), pe0 + (o2 - 1) * NQC + qc + 1)
                s.activation(ar3[:, AST + (o2 % 2) * 2048 + qc * 512:
                                 AST + (o2 % 2) * 2048 + (qc + 1) * 512],
                             p[qc % 2][:, 0:512],
                             AF.Silu).then_inc(cx.sem("pd"), 1)
        return None
    cx.on("s", sh)
    cx.bump("pd", HK * NQC)

    def vh(v):
        for o2 in range(HK):
            for qc in range(NQC):
                ia = (o2 * 2) * NQC + qc
                ib = (o2 * 2 + 1) * NQC + qc
                v.wait_ge(cx.sem("pd"), pd0 + o2 * NQC + qc + 1)
                v.wait_ge(cx.sem("pc"), pc0 + ib + 1)
                v.tensor_tensor(
                    h_of(o2)[:, qc * QCOL:(qc + 1) * QCOL],
                    ar3[:, AST + (o2 % 2) * 2048 + qc * 512:
                        AST + (o2 % 2) * 2048 + (qc + 1) * 512],
                    p[2 + qc % 2][:, 0:512], ALU.mult).then_inc(
                    cx.sem("pe_"), 1)
        return None
    cx.on("v", vh)
    cx.bump("pe_", HK * NQC)
    cx.seal("v", "pe_")
    cx.barrier()

    for tb in range(4):
        tb_sl = slice(tb * 512, (tb + 1) * 512)
        pc0 = cx.cnt["pc"]
        pd0 = cx.cnt["pd"]
        gs0 = cx.cnt["gs"]

        def gsh(g, tb=tb, gs0=gs0, pc0=pc0):
            prev = gs0
            for ph_ in range(2):
                for o2 in range(HK):
                    sl = ph_ * HK + o2
                    g.wait_ge(cx.sem("gs"), prev)
                    if sl >= 4:
                        g.wait_ge(cx.sem("pc"), pc0 + sl - 4 + 1)
                    col = o2 * 128 + ph_ * 2048
                    wr = W["l3f_win_t"][0:D, col:col + 128].rearrange(
                        "(kc p) o -> p kc o", p=128)
                    g.dma_start(wsl[:, (sl % 4) * 1024:(sl % 4) * 1024 +
                                    1024].rearrange("p (kc o) -> p kc o",
                                                    kc=8),
                                wr).then_inc(cx.sem("gs"), 16)
                    prev = gs0 + 16 * (sl + 1)
            for o in range(DCH):
                sl = 2 * HK + o
                g.wait_ge(cx.sem("gs"), prev)
                if o == 0:
                    g.wait_ge(cx.sem("pc"), pc0 + 2 * HK - 2)
                elif o == 1:
                    g.wait_ge(cx.sem("pc"), pc0 + 2 * HK)
                else:
                    g.wait_ge(cx.sem("pc"), pc0 + 2 * HK + o - 2 + 1)
                wr = W["l3f_wout_t"][0:2048, o * 128:(o + 1) * 128
                                     ].rearrange("(kc p) o -> p kc o",
                                                 p=128)
                g.dma_start(wsl[:, (o % 2) * WSLAB:(o % 2) * WSLAB +
                                2048].rearrange("p (kc o) -> p kc o",
                                                kc=16),
                            wr).then_inc(cx.sem("gs"), 16)
                prev = gs0 + 16 * (sl + 1)
        cx.on("g", gsh)
        cx.bump("gs", 16 * (2 * HK + DCH))

        pe0 = cx.cnt["pe_"]
        ab0 = cx.cnt["ab"]

        def tsh(t, tb=tb, gs0=gs0, pc0=pc0, pd0=pd0, pe0=pe0, ab0=ab0):
            for ph_ in range(2):
                for o2 in range(HK):
                    sl = ph_ * HK + o2
                    t.wait_ge(cx.sem("gs"), gs0 + 16 * (sl + 1))
                    if ph_ == 0 and o2 >= 2:
                        t.wait_ge(cx.sem("pd"), pd0 + o2 - 2 + 1)
                    if ph_ == 1 and o2 >= 2:
                        t.wait_ge(cx.sem("pe_"), pe0 + o2 - 2 + 1)
                    bank = p[ph_ * 2 + o2 % 2]
                    last = None
                    for k in range(DCH):
                        last = t.matmul(
                            bank[:, 0:512],
                            wsl[:, (sl % 4) * 1024 + k * 128:
                                (sl % 4) * 1024 + (k + 1) * 128],
                            xnf_of(k)[:, tb_sl], start=(k == 0),
                            stop=(k == DCH - 1))
                    last.then_inc(cx.sem("pc"), 1)
            for o in range(DCH):
                sl = 2 * HK + o
                t.wait_ge(cx.sem("gs"), gs0 + 16 * (sl + 1))
                if o >= 2:
                    t.wait_ge(cx.sem("ab"), ab0 + o - 2 + 1)
                bank = p[4 + o % 2]
                last = None
                for k2 in range(HK):
                    t.wait_ge(cx.sem("pe_"), pe0 + k2 + 1)
                    last = t.matmul(bank[:, 0:512],
                                    wsl[:, (o % 2) * WSLAB + k2 * 128:
                                        (o % 2) * WSLAB + (k2 + 1) * 128],
                                    ar3[:, AST + k2 * 512:
                                        AST + (k2 + 1) * 512],
                                    start=(k2 == 0), stop=(k2 == HK - 1))
                last.then_inc(cx.sem("pc"), 1)
        cx.on("t", tsh)

        def ssh(s, tb=tb, pc0=pc0, pd0=pd0):
            for o2 in range(HK):
                s.wait_ge(cx.sem("pc"), pc0 + o2 + 1)
                s.activation(ar3[:, AST + o2 * 512:AST + (o2 + 1) * 512],
                             p[o2 % 2][:, 0:512],
                             AF.Silu).then_inc(cx.sem("pd"), 1)
        cx.on("s", ssh)

        def vsh(v, tb=tb, pc0=pc0, pd0=pd0, pe0=pe0, ab0=ab0):
            for o2 in range(HK):
                v.wait_ge(cx.sem("pd"), pd0 + o2 + 1)
                v.wait_ge(cx.sem("pc"), pc0 + HK + o2 + 1)
                v.tensor_tensor(ar3[:, AST + o2 * 512:AST + (o2 + 1) * 512],
                                ar3[:, AST + o2 * 512:AST + (o2 + 1) * 512],
                                p[2 + o2 % 2][:, 0:512],
                                ALU.mult).then_inc(cx.sem("pe_"), 1)
            for o in range(DCH):
                v.wait_ge(cx.sem("pc"), pc0 + 2 * HK + o + 1)
                v.tensor_copy(ar3[:, SST + o * 512:SST + (o + 1) * 512],
                              p[4 + o % 2][:, 0:512]).then_inc(
                    cx.sem("ab"), 1)
        cx.on("v", vsh)
        cx.bump("pd", HK)
        cx.bump("pc", 2 * HK + DCH)
        cx.bump("pe_", HK)
        cx.bump("ab", DCH)
        cx.seal("v", "ab")
        cx.barrier()

        pc0 = cx.cnt["pc"]
        pd0 = cx.cnt["pd"]
        gs0 = cx.cnt["gs"]
        ds0 = cx.cnt["ds"]
        pa0 = cx.cnt["pa"]

        def gr(g, tb=tb, gs0=gs0, pc0=pc0):
            prev = gs0
            for e in range(8):
                for k2 in range(HK):
                    sl = e * HK + k2
                    g.wait_ge(cx.sem("gs"), prev)
                    if sl >= 4:
                        g.wait_ge(cx.sem("pc"), pc0 + sl - 4 + 1)
                    wr = W["l3f_w2_t"][e, k2 * 128:(k2 + 1) * 128,
                                       0:1024].rearrange("p o -> p o")
                    g.dma_start(wsl[:, (sl % 4) * 1024:(sl % 4) * 1024 +
                                    1024], wr).then_inc(cx.sem("gs"), 16)
                    prev = gs0 + 16 * (sl + 1)
        cx.on("g", gr)

        def qr(q, tb=tb, ds0=ds0, pa0=pa0):
            for e in range(8):
                if e >= 1:
                    q.wait_ge(cx.sem("pa"), pa0 + e * HK)
                q.dma_start(W["rbc2"][0:1, :],
                            ar3[e:e + 1, WT + tb * 512:WT + (tb + 1) * 512]
                            ).then_inc(cx.sem("ds"), 16)
                q.wait_ge(cx.sem("ds"), ds0 + 32 * e + 16)
                q.dma_start(ar3[:, WB + (e % 2) * 256:WB + (e % 2) * 256 +
                                512] if False else
                            ar3[:, WB:WB + 512],
                            W["rbc2"][0:1, :].broadcast_to([128, 512])
                            ).then_inc(cx.sem("ds"), 16)
                q.wait_ge(cx.sem("ds"), ds0 + 32 * e + 32)
        cx.on("q", qr)

        def vr(v, tb=tb, ds0=ds0, pa0=pa0, pc0=pc0):
            pa = pa0
            for e in range(8):
                v.wait_ge(cx.sem("ds"), ds0 + 32 * e + 32)
                for k2 in range(HK):
                    if e * HK + k2 >= 2:
                        v.wait_ge(cx.sem("pc"), pc0 + e * HK + k2 - 2 + 1)
                    v.tensor_tensor(
                        ar3[:, SCL + ((e * HK + k2) % 2) * 512:
                            SCL + ((e * HK + k2) % 2) * 512 + 512],
                        h_of(k2)[:, tb_sl], ar3[:, WB:WB + 512],
                        ALU.mult).then_inc(cx.sem("pa"), 1)
                    pa += 1
        cx.on("v", vr)
        cx.bump("pa", 8 * HK)
        cx.bump("ds", 32 * 8)
        cx.bump("gs", 16 * 8 * HK)
        cx.bump("pc", 8 * HK)

        def tr(t, tb=tb, gs0=gs0, pa0=pa0, pc0=pc0):
            for e in range(8):
                for k2 in range(HK):
                    sl = e * HK + k2
                    t.wait_ge(cx.sem("gs"), gs0 + 16 * (sl + 1))
                    t.wait_ge(cx.sem("pa"), pa0 + sl + 1)
                    scl = ar3[:, SCL + (sl % 2) * 512:
                              SCL + (sl % 2) * 512 + 512]
                    for o in range(DCH):
                        ins = t.matmul(p[o][:, 0:512],
                                       wsl[:, (sl % 4) * 1024 + o * 128:
                                           (sl % 4) * 1024 + (o + 1) * 128],
                                       scl, start=(sl == 0),
                                       stop=(sl == 8 * HK - 1))
                        if o == DCH - 1:
                            ins.then_inc(cx.sem("pc"), 1)
        cx.on("t", tr)

        def vfin(v, tb=tb, pc0=pc0):
            v.wait_ge(cx.sem("pc"), pc0 + 8 * HK)
            pa = cx.cnt["pa"]
            for o in range(DCH):
                v.tensor_tensor(tsc[:, 0:512], p[o][:, 0:512],
                                ar3[:, SST + o * 512:SST + (o + 1) * 512],
                                ALU.add).then_inc(cx.sem("pa"), 1)
                pa += 1
                v.wait_ge(cx.sem("pa"), pa)
                xso = xs[:, o * TOK + tb * 512:o * TOK + (tb + 1) * 512]
                v.scalar_tensor_tensor(xso, tsc[:, 0:512], 0.5, xso,
                                       ALU.mult, ALU.add).then_inc(
                    cx.sem("pa"), 1)
                pa += 1
                v.wait_ge(cx.sem("pa"), pa)
            cx.cnt["pa"] = pa
            return None
        cx.on("v", vfin)
        cx.seal("v", "pa")
        cx.barrier()


def emit_l3_attn(cx, W):
    emit_attn_causal(cx, W, "l3a_wqkv", 0, D, 2 * D, "l3a_wout",
                     0.125, 1.0)


def emit_l1_ffn(cx, W):
    b = cx.buf
    p = b["p"]
    xs, xnf = b["xs"], b["xnf"]
    ar1, ar2, ar3 = b["ar1"], b["ar2"], b["ar3"]
    tsc, wsl, vrow = b["tsc"], b["wsl"], b["vrow"]
    NCHK = 4
    NTT = 16
    ar2f = ar2.bitcast(F32)
    xnff = xnf.bitcast(F32)
    wslf = wsl.bitcast(F32)

    def xs_of(ch):
        return xs[:, ch * TOK:(ch + 1) * TOK]

    def xnf_of(ch):
        return xnf[:, ch * TOK:(ch + 1) * TOK]

    gw = cx.dma("q", vrow[:, 32:40], W["l1f_gx"][:, :])
    cx.wait("v", "ds", gw)
    emit_norm(cx, xs_of, xnf_of, kind="ln",
              w_slice=lambda ch: vrow[:, 32 + ch:33 + ch], eps=0.0)

    def gid(g):
        g.memset(b["msk"][:, 640:768], 1.0).then_inc(cx.sem("ga"), 1)
        cx.bump("ga")
        g.wait_ge(cx.sem("ga"), cx.cnt["ga"])
        g.affine_select(b["msk"][:, 512:640], b["msk"][:, 640:768],
                        [[-1, 128]], ALU.is_equal, 0.0,
                        channel_multiplier=1).then_inc(cx.sem("ga"), 1)
        cx.bump("ga")
        g.wait_ge(cx.sem("ga"), cx.cnt["ga"])
        return g.affine_select(b["idf"][:, :], b["ones1"][:, :],
                               [[-1, 128]], ALU.is_equal, 0.0,
                               channel_multiplier=1)
    cx.on("g", gid)
    cx.barrier()

    if not cx.buf.get("l1_se_done"):
        cx.buf["l1_se_done"] = True
        gb = cx.dma("g", wsl[:, 0:1024],
                    W["l1f_gsrow"][0:1, :].broadcast_to([128, 1024]))
        cx.wait("v", "gs", gb)
        gs0 = cx.cnt["gs"]
        pa0 = cx.cnt["pa"]
        pd0 = cx.cnt["pd"]
        sa0 = cx.cnt["sa"]
        pe0 = cx.cnt["pe_"]
        pf0 = cx.cnt["pf"]
        ab0 = cx.cnt["ab"]
        ds0 = cx.cnt["ds"]

        def seg(g):
            prev = gs0
            for st in range(32):
                g.wait_ge(cx.sem("gs"), prev)
                if st >= 2:
                    g.wait_ge(cx.sem("pa"), pa0 + (st - 2) * 6 + 6)
                g.dma_start(ar1[:, (st % 2) * 1024:(st % 2) * 1024 + 1024],
                            W["l1f_semb"][st * 128:(st + 1) * 128, :]
                            ).then_inc(cx.sem("gs"), 16)
                prev = gs0 + 16 * (st + 1)
        cx.on("g", seg)
        cx.bump("gs", 16 * 32)

        def sev(v):
            pa = pa0
            ab = ab0
            sqs = ar3.bitcast(F32)[:, 0:512]
            for st in range(32):
                tile = ar1[:, (st % 2) * 1024:(st % 2) * 1024 + 1024]
                v.wait_ge(cx.sem("gs"), gs0 + 16 * (st + 1))
                v.tensor_tensor(sqs, tile[:, 0:512],
                                tile[:, 0:512], ALU.mult).then_inc(
                    cx.sem("pa"), 1)
                pa += 1
                v.wait_ge(cx.sem("pa"), pa)
                v.reduce_sum(tsc[:, 496:497], sqs,
                             axis=AX.X).then_inc(cx.sem("pa"), 1)
                pa += 1
                v.wait_ge(cx.sem("pa"), pa)
                v.tensor_tensor(sqs, tile[:, 512:1024],
                                tile[:, 512:1024], ALU.mult).then_inc(
                    cx.sem("pa"), 1)
                pa += 1
                v.wait_ge(cx.sem("pa"), pa)
                v.reduce_sum(tsc[:, 497:498], sqs,
                             axis=AX.X).then_inc(cx.sem("pa"), 1)
                pa += 1
                v.wait_ge(cx.sem("pa"), pa)
                v.tensor_tensor(tsc[:, 498:499], tsc[:, 496:497],
                                tsc[:, 497:498], ALU.add).then_inc(
                    cx.sem("pa"), 1)
                pa += 1
                v.wait_ge(cx.sem("pa"), pa)
                v.tensor_scalar_max(tsc[:, 499:500], tsc[:, 498:499],
                                    1e-24).then_inc(cx.sem("pd"), 1)
                v.wait_ge(cx.sem("sa"), sa0 + st * 3 + 3)
                if st >= 2:
                    v.wait_ge(cx.sem("pf"), pf0 + (st - 2) * 8 + 8)
                v.scalar_tensor_tensor(
                    ar1[:, 2048 + (st % 2) * 1024:
                        2048 + (st % 2) * 1024 + 1024],
                    tile, tsc[:, 502:503], wsl[:, 0:1024],
                    ALU.mult, ALU.mult).then_inc(cx.sem("pa"), 1)
                pa += 1
                v.wait_ge(cx.sem("pa"), pa)
                v.wait_ge(cx.sem("pf"), pf0 + (st + 1) * 8)
                if st >= 16:
                    v.wait_ge(cx.sem("ds"), ds0 + 16 * 8 * (st // 16))
                last = None
                for k in range(DCH):
                    last = v.tensor_copy(
                        ar2[:, k * 2048 + (st % 16) * 128:
                            k * 2048 + (st % 16) * 128 + 128],
                        p[6 + st % 2].bitcast(BF16)[:, k * 128:(k + 1) * 128])
                last.then_inc(cx.sem("ab"), 1)
        cx.on("v", sev)
        cx.bump("pa", 32 * 6)
        cx.bump("pd", 32)
        cx.bump("ab", 32)

        def ses(s):
            for st in range(32):
                s.wait_ge(cx.sem("pd"), pd0 + st + 1)
                s.activation(tsc[:, 500:501], tsc[:, 499:500],
                             AF.Ln).then_inc(cx.sem("sa"), 1)
                s.wait_ge(cx.sem("sa"), sa0 + st * 3 + 1)
                s.activation(tsc[:, 501:502], tsc[:, 500:501], AF.Exp,
                             scale=-0.5).then_inc(cx.sem("sa"), 1)
                s.wait_ge(cx.sem("sa"), sa0 + st * 3 + 2)
                s.mul(tsc[:, 502:503], tsc[:, 501:502],
                      float(np.sqrt(D))).then_inc(cx.sem("sa"), 1)
        cx.on("s", ses)
        cx.bump("sa", 32 * 3)

        def set_(t):
            for st in range(32):
                t.wait_ge(cx.sem("pa"), pa0 + st * 6 + 6)
                if st >= 2:
                    t.wait_ge(cx.sem("ab"), ab0 + (st - 2) + 1)
                for k in range(DCH):
                    t.transpose(
                        p[6 + st % 2].bitcast(BF16)[:, k * 128:(k + 1) * 128],
                        ar1[:, 2048 + (st % 2) * 1024 + k * 128:
                            2048 + (st % 2) * 1024 + (k + 1) * 128],
                        b["msk"][:, 512:640]).then_inc(cx.sem("pf"), 1)
        cx.on("t", set_)
        cx.bump("pf", 32 * 8)

        def seq(q):
            for half in range(2):
                q.wait_ge(cx.sem("ab"), ab0 + (half + 1) * 16)
                for k in range(DCH):
                    if half > 0 or k > 0:
                        q.wait_ge(cx.sem("ds"), cx.cnt["ds"] - ds0 + ds0)
                    q.dma_start(
                        W["l1_se"][:, k * 4096 + half * 2048:
                                   k * 4096 + half * 2048 + 2048],
                        ar2[:, k * 2048:(k + 1) * 2048]).then_inc(
                        cx.sem("ds"), 16)
                    cx.bump("ds", 16)
        cx.on("q", seq)
        cx.wait("q", "ds", cx.cnt["ds"])
        cx.seal("q", "ds")
        cx.seal("v", "ab")
        cx.barrier()
    else:
        cx.barrier()

    pf0 = cx.cnt["pf"]
    pa0 = cx.cnt["pa"]

    def xtt(t):
        for i in range(NTT * DCH):
            tt, k = i // DCH, i % DCH
            if i >= 2:
                t.wait_ge(cx.sem("pa"), pa0 + (i - 2) + 1)
            t.transpose(p[6 + i % 2].bitcast(BF16)[:, 0:128],
                        xnf_of(k)[:, tt * 128:(tt + 1) * 128],
                        b["msk"][:, 512:640]).then_inc(cx.sem("pf"), 1)
    cx.on("t", xtt)
    cx.bump("pf", NTT * DCH)

    def xtv(v):
        for i in range(NTT * DCH):
            tt, k = i // DCH, i % DCH
            v.wait_ge(cx.sem("pf"), pf0 + i + 1)
            v.tensor_copy(ar3[:, tt * 1024 + k * 128:tt * 1024 +
                              (k + 1) * 128],
                          p[6 + i % 2].bitcast(BF16)[:, 0:128]).then_inc(
                cx.sem("pa"), 1)
    cx.on("v", xtv)
    cx.bump("pa", NTT * DCH)
    cx.seal("v", "pa")
    cx.barrier()

    gs0 = cx.cnt["gs"]
    pc0 = cx.cnt["pc"]
    av0 = cx.cnt["av"]
    pa0 = cx.cnt["pa"]
    ds0 = cx.cnt["ds"]

    def pci(c, s, t, half):
        return pc0 + c * 64 + s * 32 + t * 2 + half + 1

    def avi(c, s, t):
        return av0 + c * 32 + s * 16 + t + 1

    def lg1(g):
        prev = gs0
        for c in range(NCHK):
            for hs in range(4):
                s_, half = hs // 2, hs % 2
                gi = c * 4 + hs
                g.wait_ge(cx.sem("gs"), prev)
                if gi >= 2:
                    pc_, ps, ph_ = divmod(gi - 2, 4)[0], ((gi - 2) % 4) // 2, \
                        (gi - 2) % 2
                    g.wait_ge(cx.sem("pc"), pci((gi - 2) // 4, ps, 15, ph_))
                for k4 in range(4):
                    k = half * 4 + k4
                    g.dma_start(
                        wsl[:, (half * 4 + k4 % 4) * 512 * 0 +
                            (half * 2048) + k4 * 512:
                            (half * 2048) + (k4 + 1) * 512],
                        W["l1_se"][:, k * 4096 + c * 1024 + s_ * 512:
                                   k * 4096 + c * 1024 + s_ * 512 + 512]
                    ).then_inc(cx.sem("gs"), 16)
                    prev = gs0 + 16 * (gi * 4 + k4 + 1)
    cx.on("g", lg1)
    cx.bump("gs", 16 * 64)

    def lt1(t):
        for c in range(NCHK):
            for s_ in range(2):
                for half in range(2):
                    t.wait_ge(cx.sem("gs"),
                              gs0 + 16 * ((c * 4 + s_ * 2 + half) * 4 + 4))
                for t_ in range(NTT):
                    seq = c * 32 + s_ * 16 + t_
                    if seq >= 8:
                        pr = seq - 8
                        t.wait_ge(cx.sem("av"),
                                  avi(pr // 32, (pr % 32) // 16, pr % 16))
                    bank = p[t_ % 8]
                    last = None
                    for k in range(DCH):
                        half = k // 4
                        last = t.matmul(
                            bank[:, 0:512],
                            xnf_of(k)[:, t_ * 128:(t_ + 1) * 128],
                            wsl[:, half * 2048 + (k % 4) * 512:
                                half * 2048 + (k % 4) * 512 + 512],
                            start=(k == 0), stop=(k == DCH - 1))
                        if k == 3:
                            last.then_inc(cx.sem("pc"), 1)
                    last.then_inc(cx.sem("pc"), 1)
    cx.on("t", lt1)
    cx.bump("pc", NCHK * 64)

    def lv1(v):
        pa = pa0
        for c in range(NCHK):
            for s_ in range(2):
                for t_ in range(NTT):
                    v.wait_ge(cx.sem("pc"), pci(c, s_, t_, 1))
                    if c >= 1 and s_ == 0 and t_ == 0:
                        v.wait_ge(cx.sem("ds"), ds0 + 16 * c)
                    bank = p[t_ % 8]
                    v.tensor_copy(ar1[:, t_ * 1024 + s_ * 512:
                                      t_ * 1024 + s_ * 512 + 512],
                                  bank[:, 0:512]).then_inc(cx.sem("pa"), 1)
                    pa += 1
                    v.wait_ge(cx.sem("pa"), pa)
                    mreg = ar2f[:, c * 1024 + s_ * 512:
                                c * 1024 + s_ * 512 + 512]
                    if t_ == 0:
                        v.tensor_copy(mreg, bank[:, 0:512]).then_inc(
                            cx.sem("pa"), 1)
                    else:
                        v.tensor_tensor(mreg, mreg, bank[:, 0:512],
                                        ALU.max).then_inc(cx.sem("pa"), 1)
                    pa += 1
                    v.wait_ge(cx.sem("pa"), pa)
                    v.reduce_max(tsc[:, 0:1], bank[:, 0:512],
                                 axis=AX.X).then_inc(cx.sem("pa"), 1)
                    pa += 1
                    v.wait_ge(cx.sem("pa"), pa)
                    if c == 0 and s_ == 0:
                        v.tensor_copy(tsc[:, 16 + t_:17 + t_],
                                      tsc[:, 0:1]).then_inc(cx.sem("av"), 1)
                    else:
                        v.tensor_tensor(tsc[:, 16 + t_:17 + t_],
                                        tsc[:, 16 + t_:17 + t_],
                                        tsc[:, 0:1], ALU.max).then_inc(
                            cx.sem("av"), 1)
    cx.on("v", lv1)
    cx.bump("pa", NCHK * 32 * 3)
    cx.bump("av", NCHK * 32)

    def lq1(q):
        for c in range(NCHK):
            q.wait_ge(cx.sem("av"), avi(c, 1, 15))
            if c >= 1:
                q.wait_ge(cx.sem("ds"), ds0 + 16 * c)
            q.dma_start(W["l1_l"][:, c * 16384:(c + 1) * 16384],
                        ar1[:, :]).then_inc(cx.sem("ds"), 16)
    cx.on("q", lq1)
    cx.bump("ds", 16 * NCHK)
    cx.wait("q", "ds", cx.cnt["ds"])
    cx.seal("q", "ds")
    cx.seal("v", "av")
    cx.barrier()

    pf0 = cx.cnt["pf"]
    pa0 = cx.cnt["pa"]

    def sta_t(t):
        for j in range(32):
            if j >= 2:
                t.wait_ge(cx.sem("pa"), pa0 + (j - 2) + 1)
            t.transpose(p[6 + j % 2][:, 0:128],
                        ar2f[:, j * 128:(j + 1) * 128],
                        b["idf"][:, :]).then_inc(cx.sem("pf"), 1)
        t.wait_ge(cx.sem("pa"), pa0 + 32 + 1)
        t.transpose(p[5][0:32, 0:128], tsc[:, 64:96],
                    b["idf"][:, :]).then_inc(cx.sem("pf"), 1)
    cx.on("t", sta_t)
    cx.bump("pf", 33)

    def sta_v(v):
        pa = pa0
        for j in range(32):
            v.wait_ge(cx.sem("pf"), pf0 + j + 1)
            v.reduce_max(tsc[:, 64 + j:65 + j], p[6 + j % 2][:, 0:128],
                         axis=AX.X).then_inc(cx.sem("pa"), 1)
            pa += 1
            v.wait_ge(cx.sem("pa"), pa)
        v.tensor_scalar_mul(tsc[:, 32:48], tsc[:, 16:32],
                            -1.0).then_inc(cx.sem("pa"), 1)
        pa += 1
        v.wait_ge(cx.sem("pa"), pa)
        v.wait_ge(cx.sem("pf"), pf0 + 33)
        v.tensor_copy(tsc[0:32, 96:224], p[5][0:32, 0:128]).then_inc(
            cx.sem("pa"), 1)
    cx.on("v", sta_v)
    cx.bump("pa", 34)
    cx.wait("q", "pa", cx.cnt["pa"])
    smw = cx.dma("q", W["l1_sm_in"][:, :], tsc[0:32, 96:224])
    cx.wait("g", "ds", smw)

    def sta_g(g):
        g.collective_compute(
            "AllReduce", ALU.max, replica_groups=RG,
            ins=[W["l1_sm_in"].ap().opt()],
            outs=[W["l1_sm_out"].ap().opt()]).then_inc(cx.sem("cc"), 1)
        cx.bump("cc")
    cx.on("g", sta_g)
    cx.seal("v", "pa")
    cx.barrier()

    cc_sm = cx.cnt["cc"]
    ds0 = cx.cnt["ds"]
    gs0 = cx.cnt["gs"]
    pa0 = cx.cnt["pa"]
    pd0 = cx.cnt["pd"]
    sa0 = cx.cnt["sa"]
    pe0 = cx.cnt["pe_"]
    pc0 = cx.cnt["pc"]
    av0 = cx.cnt["av"]

    def q_ld(c):
        return ds0 + c * (32 + 16 * 16) + 16

    def q_bc(c):
        return ds0 + c * (32 + 16 * 16) + 32

    def q_sp(c, idx):
        return ds0 + c * (32 + 16 * 16) + 32 + 16 * (idx + 1)

    def c1_q(q):
        prev = ds0
        for c in range(NCHK):
            q.wait_ge(cx.sem("cc"), cc_sm)
            if prev > ds0:
                q.wait_ge(cx.sem("ds"), prev)
            if c >= 1:
                q.wait_ge(cx.sem("pc"), pc0 + c * 16)
            q.dma_start(ar1[:, :],
                        W["l1_l"][:, c * 16384:(c + 1) * 16384]).then_inc(
                cx.sem("ds"), 16)
            q.wait_ge(cx.sem("ds"), q_ld(c))
            if c >= 1:
                q.wait_ge(cx.sem("pd"), pd0 + c * NTT)
            q.dma_start(ar2f[:, 0:1024],
                        W["l1_sm_out"][c * 8:(c + 1) * 8, :].rearrange(
                            "j q2 -> () (j q2)").broadcast_to([128, 1024])
                        ).then_inc(cx.sem("ds"), 16)
            prev = q_bc(c)
            for idx in range(16):
                q.wait_ge(cx.sem("av"), av0 + c * 16 + idx + 1)
                q.wait_ge(cx.sem("ds"), prev)
                dch, s_ = idx // 2, idx % 2
                q.dma_start(
                    W["l1_sp_in"][:, dch * 4096 + c * 1024 + s_ * 512:
                                  dch * 4096 + c * 1024 + s_ * 512 + 512],
                    xnff[:, idx * 512:(idx + 1) * 512]).then_inc(
                    cx.sem("ds"), 16)
                prev = q_sp(c, idx)
    cx.on("q", c1_q)
    cx.bump("ds", NCHK * (32 + 16 * 16))

    def c1_v(v):
        pa = pa0
        for c in range(NCHK):
            v.wait_ge(cx.sem("ds"), q_bc(c))
            for t_ in range(NTT):
                v.tensor_tensor(ar1[:, t_ * 1024:(t_ + 1) * 1024],
                                ar1[:, t_ * 1024:(t_ + 1) * 1024],
                                ar2f[:, 0:1024], ALU.subtract).then_inc(
                    cx.sem("pd"), 1)
                v.wait_ge(cx.sem("sa"), sa0 + c * NTT + t_ + 1)
                sreg = ar2f[:, 4096 + c * 1024:4096 + (c + 1) * 1024]
                if t_ == 0:
                    v.tensor_copy(sreg,
                                  ar1[:, t_ * 1024:(t_ + 1) * 1024]
                                  ).then_inc(cx.sem("pe_"), 1)
                else:
                    v.tensor_tensor(sreg, sreg,
                                    ar1[:, t_ * 1024:(t_ + 1) * 1024],
                                    ALU.add).then_inc(cx.sem("pe_"), 1)
            for idx in range(16):
                dch, s_ = idx // 2, idx % 2
                v.wait_ge(cx.sem("pc"), pc0 + c * 16 + s_ * 8 + dch + 1)
                if c >= 1:
                    v.wait_ge(cx.sem("ds"), q_sp(c - 1, idx))
                v.tensor_copy(xnff[:, idx * 512:(idx + 1) * 512],
                              p[dch][:, 0:512]).then_inc(cx.sem("av"), 1)
    cx.on("v", c1_v)
    cx.bump("pd", NCHK * NTT)
    cx.bump("pe_", NCHK * NTT)
    cx.bump("av", NCHK * 16)

    def c1_s(s):
        for c in range(NCHK):
            for t_ in range(NTT):
                s.wait_ge(cx.sem("pd"), pd0 + c * NTT + t_ + 1)
                s.activation(ar1[:, t_ * 1024:(t_ + 1) * 1024],
                             ar1[:, t_ * 1024:(t_ + 1) * 1024],
                             AF.Exp).then_inc(cx.sem("sa"), 1)
    cx.on("s", c1_s)
    cx.bump("sa", NCHK * NTT)

    def c1_t(t):
        for c in range(NCHK):
            for s_ in range(2):
                for dch in range(DCH):
                    idx = dch * 2 + s_
                    seq = c * 16 + (s_ * 8 + dch)
                    if seq >= 8:
                        pr = seq - 8
                        prc, pri = pr // 16, pr % 16
                        pidx = (pri % 8) * 2 + pri // 8
                        t.wait_ge(cx.sem("av"), av0 + prc * 16 + pidx + 1)
                    last = None
                    for t_ in range(NTT):
                        t.wait_ge(cx.sem("pe_"), pe0 + c * NTT + t_ + 1)
                        last = t.matmul(
                            p[dch][:, 0:512],
                            ar3[:, t_ * 1024 + dch * 128:
                                t_ * 1024 + (dch + 1) * 128],
                            ar1[:, t_ * 1024 + s_ * 512:
                                t_ * 1024 + s_ * 512 + 512],
                            start=(t_ == 0), stop=(t_ == NTT - 1))
                    last.then_inc(cx.sem("pc"), 1)
    cx.on("t", c1_t)
    cx.bump("pc", NCHK * 16)
    cx.wait("q", "ds", cx.cnt["ds"])
    cx.seal("q", "ds")
    cx.seal("v", "av")
    cx.barrier()

    pf0 = cx.cnt["pf"]
    pa0 = cx.cnt["pa"]
    ds0 = cx.cnt["ds"]

    def stb_g(g):
        g.wait_ge(cx.sem("ds"), ds0)
        g.collective_compute(
            "AllReduce", ALU.add, replica_groups=RG,
            ins=[W["l1_sp_in"].ap().opt()],
            outs=[W["l1_sp_out"].ap().opt()]).then_inc(cx.sem("cc"), 1)
        cx.bump("cc")
        g.wait_ge(cx.sem("ds"), ds0 + 16)
        g.collective_compute(
            "AllReduce", ALU.add, replica_groups=RG,
            ins=[W["l1_sd_in"].ap().opt()],
            outs=[W["l1_sd_out"].ap().opt()]).then_inc(cx.sem("cc"), 1)
        cx.bump("cc")
    cc_sp = cx.cnt["cc"] + 1
    cc_sd = cx.cnt["cc"] + 2

    def stb_t(t):
        for j in range(32):
            if j >= 2:
                t.wait_ge(cx.sem("pa"), pa0 + (j - 2) + 1)
            t.transpose(p[6 + j % 2][:, 0:128],
                        ar2f[:, 4096 + j * 128:4096 + (j + 1) * 128],
                        b["idf"][:, :]).then_inc(cx.sem("pf"), 1)
        t.wait_ge(cx.sem("pa"), pa0 + 33)
        t.transpose(p[5][0:32, 0:128], tsc[:, 64:96],
                    b["idf"][:, :]).then_inc(cx.sem("pf"), 1)
    cx.on("t", stb_t)
    cx.bump("pf", 33)

    def stb_v(v):
        pa = pa0
        for j in range(32):
            v.wait_ge(cx.sem("pf"), pf0 + j + 1)
            v.reduce_sum(tsc[:, 64 + j:65 + j], p[6 + j % 2][:, 0:128],
                         axis=AX.X).then_inc(cx.sem("pa"), 1)
            pa += 1
            v.wait_ge(cx.sem("pa"), pa)
        v.engine_nop().then_inc(cx.sem("pa"), 1)
        pa += 1
        v.wait_ge(cx.sem("pf"), pf0 + 33)
        v.tensor_copy(tsc[0:32, 96:224], p[5][0:32, 0:128]).then_inc(
            cx.sem("pa"), 1)
        pa += 1
        v.wait_ge(cx.sem("pa"), pa)
        v.wait_ge(cx.sem("ds"), ds0 + 32)
        v.reciprocal(tsc[0:32, 352:480],
                     tsc[0:32, 224:352]).then_inc(cx.sem("pa"), 1)
    cx.on("v", stb_v)
    cx.bump("pa", 35)
    cx.on("g", stb_g)
    cx.wait("q", "pa", pa0 + 34)
    sdw = cx.dma("q", W["l1_sd_in"][:, :], tsc[0:32, 96:224])
    cx.wait("q", "ds", sdw)
    cx.wait("q", "cc", cc_sd)
    cx.dma("q", tsc[0:32, 224:352], W["l1_sd_out"][:, :])
    cx.wait("q", "pa", cx.cnt["pa"])
    drw = cx.dma("q", W["l1_dr"][:, :], tsc[0:32, 352:480])
    cx.wait("q", "ds", cx.cnt["ds"])
    cx.seal("q", "ds")
    cx.seal("v", "pa")
    cx.barrier()

    cc_pp0 = cx.cnt["cc"]
    gs0 = cx.cnt["gs"]
    pa0 = cx.cnt["pa"]
    pd0 = cx.cnt["pd"]
    pc0 = cx.cnt["pc"]
    av0 = cx.cnt["av"]

    bw = cx.dma("q", ar2f[:, 0:4096],
                W["l1_dr"].rearrange("j q2 -> () (j q2)").broadcast_to(
                    [128, 4096]))
    cx.wait("v", "ds", bw)
    ds0 = cx.cnt["ds"]

    def h_g(g):
        g.wait_ge(cx.sem("cc"), cc_sp)
        prev = gs0
        for dch in range(DCH):
            g.wait_ge(cx.sem("gs"), prev)
            dst = (xnf if dch < 4 else ar1)[:, (dch % 4) * 4096:
                                            (dch % 4) * 4096 + 4096]
            g.dma_start(dst, W["l1_sp_out"][:, dch * 4096:
                                            (dch + 1) * 4096]).then_inc(
                cx.sem("gs"), 16)
            prev = gs0 + 16 * (dch + 1)
        for hch in range(DCH):
            g.wait_ge(cx.sem("gs"), prev)
            if hch >= 2:
                g.wait_ge(cx.sem("pc"), pc0 + (hch - 1) * 8)
            for k in range(DCH):
                if k > 0:
                    g.wait_ge(cx.sem("gs"), prev)
                g.dma_start(
                    wsl[:, (hch % 2) * 1024 + k * 128:
                        (hch % 2) * 1024 + (k + 1) * 128],
                    W["l1f_w1_loc"][k * 128:(k + 1) * 128,
                                    hch * 128:(hch + 1) * 128]).then_inc(
                    cx.sem("gs"), 16)
                prev = gs0 + 16 * (8 + hch * 8 + k + 1)
        for dh in range(2):
            g.wait_ge(cx.sem("gs"), prev)
            if dh == 0:
                g.wait_ge(cx.sem("pc"), pc0 + 64)
            else:
                g.wait_ge(cx.sem("pc"), pc0 + 64 + 32)
            for k2 in range(DCH):
                if k2 > 0:
                    g.wait_ge(cx.sem("gs"), prev)
                g.dma_start(
                    wsl[:, k2 * 512:(k2 + 1) * 512],
                    W["l1f_w2_loc"][k2 * 128:(k2 + 1) * 128,
                                    dh * 512:(dh + 1) * 512]).then_inc(
                    cx.sem("gs"), 16)
                prev = gs0 + 16 * (8 + 64 + dh * 8 + k2 + 1)
    cx.on("g", h_g)
    cx.bump("gs", 16 * (8 + 64 + 16))

    def h_v(v):
        pa = pa0
        for dch in range(DCH):
            v.wait_ge(cx.sem("gs"), gs0 + 16 * (dch + 1))
            dst = (xnf if dch < 4 else ar1)[:, (dch % 4) * 4096:
                                            (dch % 4) * 4096 + 4096]
            v.tensor_tensor(dst, dst, ar2f[:, 0:4096],
                            ALU.mult).then_inc(cx.sem("pa"), 1)
            pa += 1
            v.wait_ge(cx.sem("pa"), pa)
        v.engine_nop().then_inc(cx.sem("ph"), 1)
        for i in range(64):
            st, dh = i % 32, i // 32
            v.wait_ge(cx.sem("pc"), pc0 + 64 + dh * 32 + st + 1)
            if i >= 16:
                v.wait_ge(cx.sem("ds"), ds0 + 16 * (i - 16 + 1))
            v.tensor_copy(xnff[:, (i % 16) * 512:(i % 16) * 512 + 512],
                          p[(st * 2 + dh) % 8][:, 0:512]).then_inc(
                cx.sem("av"), 1)
    cx.on("v", h_v)
    cx.bump("pa", DCH)
    cx.bump("ph", 1)
    cx.bump("av", 64)
    ph_scale = cx.cnt["ph"]

    def h_t(t):
        for hch in range(DCH):
            t.wait_ge(cx.sem("gs"), gs0 + 16 * (8 + hch * 8 + 8))
            t.wait_ge(cx.sem("pa"), pa0 + DCH)
            for sgrp in range(8):
                if hch >= 1:
                    t.wait_ge(cx.sem("pd"), pd0 + (hch - 1) * 8 + sgrp + 1)
                last = None
                for k in range(DCH):
                    rhs = (xnf if k < 4 else ar1)[
                        :, (k % 4) * 4096 + sgrp * 512:
                        (k % 4) * 4096 + sgrp * 512 + 512]
                    last = t.matmul(
                        p[sgrp][:, 0:512],
                        wsl[:, (hch % 2) * 1024 + k * 128:
                            (hch % 2) * 1024 + (k + 1) * 128],
                        rhs, start=(k == 0), stop=(k == DCH - 1))
                last.then_inc(cx.sem("pc"), 1)
        t.wait_ge(cx.sem("pd"), pd0 + 64)
        for dh in range(2):
            t.wait_ge(cx.sem("gs"), gs0 + 16 * (8 + 64 + dh * 8 + 8))
            for st in range(32):
                bank = (st * 2 + dh) % 8
                if st >= 4:
                    t.wait_ge(cx.sem("av"), av0 + dh * 32 + (st - 4) + 1)
                else:
                    t.wait_ge(cx.sem("pd"), pd0 + 56 + bank + 1)
                last = None
                for k2 in range(DCH):
                    lsrc = (ar3 if k2 < 4 else ar2)[
                        :, (k2 % 4) * 4096 + st * 128:
                        (k2 % 4) * 4096 + st * 128 + 128]
                    last = t.matmul(
                        p[(st * 2 + dh) % 8][:, 0:512],
                        lsrc, wsl[:, k2 * 512:(k2 + 1) * 512],
                        start=(k2 == 0), stop=(k2 == DCH - 1))
                last.then_inc(cx.sem("pc"), 1)
    cx.on("t", h_t)
    cx.bump("pc", 64 + 64)

    def h_s(s):
        for hch in range(DCH):
            for sgrp in range(8):
                s.wait_ge(cx.sem("pc"), pc0 + hch * 8 + sgrp + 1)
                if hch >= 4:
                    s.wait_ge(cx.sem("ph"), ph_scale)
                dst = (ar3 if hch < 4 else ar2)[
                    :, (hch % 4) * 4096 + sgrp * 512:
                    (hch % 4) * 4096 + sgrp * 512 + 512]
                s.activation(dst, p[sgrp][:, 0:512],
                             AF.Gelu).then_inc(cx.sem("pd"), 1)
    cx.on("s", h_s)
    cx.bump("pd", 64)

    def h_q(q):
        prev = ds0
        for i in range(64):
            st, dh = i % 32, i // 32
            q.wait_ge(cx.sem("av"), av0 + i + 1)
            q.wait_ge(cx.sem("ds"), prev)
            q.dma_start(
                W["l1_pp_in"][:, st * 1024 + dh * 512:
                              st * 1024 + dh * 512 + 512],
                xnff[:, (i % 16) * 512:(i % 16) * 512 + 512]).then_inc(
                cx.sem("ds"), 16)
            prev = ds0 + 16 * (i + 1)
    cx.on("q", h_q)
    cx.bump("ds", 16 * 64)

    def pp_g(g):
        g.wait_ge(cx.sem("ds"), ds0 + 16 * 64)
        g.collective_compute(
            "AllReduce", ALU.add, replica_groups=RG,
            ins=[W["l1_pp_in"].ap().opt()],
            outs=[W["l1_pp_out"].ap().opt()]).then_inc(cx.sem("cc"), 1)
        cx.bump("cc")
    cx.on("g", pp_g)
    cc_pp = cx.cnt["cc"]
    cx.seal("v", "av")
    cx.barrier()

    ds0 = cx.cnt["ds"]
    pa0 = cx.cnt["pa"]
    pd0 = cx.cnt["pd"]
    sa0 = cx.cnt["sa"]
    pf0 = cx.cnt["pf"]

    def bi_q(q):
        prev = ds0
        for c in range(NCHK):
            q.wait_ge(cx.sem("ds"), prev)
            if c >= 1:
                q.wait_ge(cx.sem("pa"), pa0 + c * NTT * 2)
            q.dma_start(ar1[:, :],
                        W["l1_l"][:, c * 16384:(c + 1) * 16384]).then_inc(
                cx.sem("ds"), 16)
            prev = ds0 + 16 * (2 * c + 1)
            q.wait_ge(cx.sem("pa"), pa0 + (c + 1) * NTT * 2)
            q.wait_ge(cx.sem("ds"), prev)
            q.dma_start(W["l1_l"][:, c * 16384:(c + 1) * 16384],
                        ar1[:, :]).then_inc(cx.sem("ds"), 16)
            prev = ds0 + 16 * (2 * c + 2)
        q.wait_ge(cx.sem("pa"), pa0 + NCHK * NTT * 2 + 3)
        q.dma_start(W["l1_rr"][:, :], tsc[0:16, 64:192]).then_inc(
            cx.sem("ds"), 16)
        q.wait_ge(cx.sem("ds"), ds0 + 16 * (2 * NCHK + 1))
        q.dma_start(wslf[:, 0:2048],
                    W["l1_rr"].rearrange("t q2 -> () (t q2)").broadcast_to(
                        [128, 2048])).then_inc(cx.sem("ds"), 16)
        q.wait_ge(cx.sem("ds"), ds0 + 16 * (2 * NCHK + 2))
    cx.on("q", bi_q)
    cx.bump("ds", 16 * (2 * NCHK + 2))

    def bi_s(s):
        for c in range(NCHK):
            s.wait_ge(cx.sem("ds"), ds0 + 16 * (2 * c + 1))
            for t_ in range(NTT):
                if t_ >= 1:
                    s.wait_ge(cx.sem("sa"), sa0 + c * NTT + t_)
                s.activation(ar1[:, t_ * 1024:(t_ + 1) * 1024],
                             ar1[:, t_ * 1024:(t_ + 1) * 1024],
                             AF.Exp, bias=tsc[:, 32 + t_:33 + t_]).then_inc(
                    cx.sem("sa"), 1)
    cx.on("s", bi_s)
    cx.bump("sa", NCHK * NTT)

    def bi_v(v):
        pa = pa0
        for c in range(NCHK):
            for t_ in range(NTT):
                v.wait_ge(cx.sem("sa"), sa0 + c * NTT + t_ + 1)
                v.reduce_sum(tsc[:, 0:1], ar1[:, t_ * 1024:(t_ + 1) * 1024],
                             axis=AX.X).then_inc(cx.sem("pa"), 1)
                pa += 1
                v.wait_ge(cx.sem("pa"), pa)
                if c == 0:
                    v.tensor_copy(tsc[:, 48 + t_:49 + t_],
                                  tsc[:, 0:1]).then_inc(cx.sem("pa"), 1)
                else:
                    v.tensor_tensor(tsc[:, 48 + t_:49 + t_],
                                    tsc[:, 48 + t_:49 + t_], tsc[:, 0:1],
                                    ALU.add).then_inc(cx.sem("pa"), 1)
                pa += 1
                v.wait_ge(cx.sem("pa"), pa)
        v.reciprocal(tsc[:, 0:16], tsc[:, 48:64]).then_inc(cx.sem("pa"), 1)
        pa += 1
        v.wait_ge(cx.sem("pa"), pa)
        v.tensor_scalar_mul(tsc[:, 0:16], tsc[:, 0:16], 0.5).then_inc(
            cx.sem("pd"), 1)
        v.wait_ge(cx.sem("pf"), pf0 + 1)
        v.tensor_copy(tsc[0:16, 64:192], p[5][0:16, 0:128]).then_inc(
            cx.sem("pa"), 1)
        pa += 1
        v.wait_ge(cx.sem("pa"), pa)
        v.engine_nop().then_inc(cx.sem("pa"), 1)
    cx.on("v", bi_v)
    cx.bump("pa", NCHK * NTT * 2 + 3)
    cx.bump("pd", 1)

    def bi_t(t):
        t.wait_ge(cx.sem("pd"), pd0 + 1)
        t.transpose(p[5][0:16, 0:128], tsc[:, 0:16],
                    b["idf"][:, :]).then_inc(cx.sem("pf"), 1)
    cx.on("t", bi_t)
    cx.bump("pf", 1)
    cx.wait("q", "ds", cx.cnt["ds"])
    cx.seal("q", "ds")
    cx.seal("v", "pa")
    cx.barrier()

    ds0 = cx.cnt["ds"]
    gs0 = cx.cnt["gs"]
    pa0 = cx.cnt["pa"]
    pf0 = cx.cnt["pf"]
    pc0 = cx.cnt["pc"]
    ab0 = cx.cnt["ab"]

    def bii_q(q):
        prev = ds0
        for c in range(NCHK):
            q.wait_ge(cx.sem("ds"), prev)
            if c >= 1:
                q.wait_ge(cx.sem("pf"), pf0 + c * 128)
            q.dma_start(ar1[:, :],
                        W["l1_l"][:, c * 16384:(c + 1) * 16384]).then_inc(
                cx.sem("ds"), 16)
            prev = ds0 + 16 * (c + 1)
    cx.on("q", bii_q)
    cx.bump("ds", 16 * NCHK)

    def bii_g(g):
        g.wait_ge(cx.sem("cc"), cc_pp)
        prev = gs0
        for c in range(NCHK):
            for st in range(DCH):
                g.wait_ge(cx.sem("gs"), prev)
                if c >= 1 and st == 0:
                    g.wait_ge(cx.sem("pc"), pc0 + c * 32)
                g.dma_start(
                    xnf[:, st * 1024:(st + 1) * 1024],
                    W["l1_pp_out"][:, (c * 8 + st) * 1024:
                                   (c * 8 + st + 1) * 1024]).then_inc(
                    cx.sem("gs"), 16)
                prev = gs0 + 16 * (c * 8 + st + 1)
    cx.on("g", bii_g)
    cx.bump("gs", 16 * NCHK * DCH)

    def bii_t(t):
        for c in range(NCHK):
            t.wait_ge(cx.sem("ds"), ds0 + 16 * (c + 1))
            for st in range(DCH):
                for t_ in range(NTT):
                    i = st * NTT + t_
                    seq = c * 128 + i
                    if seq >= 2:
                        t.wait_ge(cx.sem("pa"), pa0 + seq - 2 + 1)
                    if c >= 1 and i < 2:
                        t.wait_ge(cx.sem("ab"), ab0 + c * 4 * 8)
                    t.transpose(p[6 + i % 2].bitcast(BF16)[:, 0:128],
                                ar1[:, t_ * 1024 + st * 128:
                                    t_ * 1024 + (st + 1) * 128],
                                b["msk"][:, 512:640]).then_inc(
                        cx.sem("pf"), 1)
            for tg in range(4):
                for dch in range(DCH):
                    gi = c * 32 + tg * 8 + dch
                    if gi >= 8:
                        t.wait_ge(cx.sem("ab"), ab0 + gi - 8 + 1)
                    last = None
                    for st in range(DCH):
                        t.wait_ge(cx.sem("pa"),
                                  pa0 + c * 128 + st * NTT + NTT)
                        t.wait_ge(cx.sem("gs"), gs0 + 16 * (c * 8 + st + 1))
                        last = t.matmul(
                            p[dch][:, 0:512],
                            xnf[:, st * 1024 + dch * 128:
                                st * 1024 + (dch + 1) * 128],
                            ar2[:, st * 2048 + tg * 512:
                                st * 2048 + tg * 512 + 512],
                            start=(st == 0), stop=(st == DCH - 1))
                    last.then_inc(cx.sem("pc"), 1)
    cx.on("t", bii_t)
    cx.bump("pf", NCHK * 128)
    cx.bump("pc", NCHK * 32)

    def bii_v(v):
        pa = pa0
        for c in range(NCHK):
            for st in range(DCH):
                for t_ in range(NTT):
                    i = st * NTT + t_
                    v.wait_ge(cx.sem("pf"), pf0 + c * 128 + i + 1)
                    if c >= 1:
                        v.wait_ge(cx.sem("pc"),
                                  pc0 + (c - 1) * 32 + (t_ // 4) * 8 + 8)
                    v.tensor_tensor(
                        ar2[:, st * 2048 + t_ * 128:
                            st * 2048 + (t_ + 1) * 128],
                        p[6 + i % 2].bitcast(BF16)[:, 0:128],
                        wslf[:, t_ * 128:(t_ + 1) * 128],
                        ALU.mult).then_inc(cx.sem("pa"), 1)
                    pa += 1
            for tg in range(4):
                for dch in range(DCH):
                    gi = c * 32 + tg * 8 + dch
                    v.wait_ge(cx.sem("pc"), pc0 + gi + 1)
                    xsl = xs[:, dch * TOK + tg * 512:dch * TOK +
                             (tg + 1) * 512]
                    v.tensor_tensor(xsl, p[dch][:, 0:512], xsl,
                                    ALU.add).then_inc(cx.sem("ab"), 1)
    cx.on("v", bii_v)
    cx.bump("pa", NCHK * 128)
    cx.bump("ab", NCHK * 32)
    cx.seal("v", "ab")
    cx.barrier()


def _ser(cx):
    return cx


def emit_l0_compose_a(cx, W):
    if cx.buf.get("l0a_comp_done"):
        return
    cx.buf["l0a_comp_done"] = True
    b = cx.buf
    p = b["p"]
    tsc = b["tsc"]
    ar3f = b["ar3"].bitcast(F32)
    comp = W["l0a_comp"]

    for c8 in range(DCH):
        lw = cx.dma("q", ar3f[:, 0:1024], W["ident1k"][c8 * 128:
                                                       (c8 + 1) * 128, :])
        cx.wait("q", "ds", lw)
        sw = cx.dma("q", comp[c8 * 128:(c8 + 1) * 128, 0:1024],
                    ar3f[:, 0:1024])
        cx.wait("q", "ds", sw)
    cx.seal("q", "ds")
    cx.barrier()

    specs = []
    for h in range(16):
        specs.append(("l0a_wqd_t", "l0a_upc", 32, h, 3072 + h * 64))
        specs.append(("l0a_wqd_t", "l0a_upe", 32, h, 3072 + h * 64 + 32))
        specs.append(("l0a_wkd_t", "l0a_kupc", 32, h, 1024 + h * 64))
        specs.append(("l0a_wkd_t", "l0a_kupe", 32, h, 1024 + h * 64 + 32))
        specs.append(("l0a_wvd_t", "l0a_vup", 64, h, 2048 + h * 64))
    ds0 = cx.cnt["ds"]
    pc0 = cx.cnt["pc"]
    pa0 = cx.cnt["pa"]
    it = []
    for (wt, up, w, h, dc) in specs:
        for c8 in range(DCH):
            it.append((wt, up, w, h, dc, c8))

    def cq(q):
        for i, (wt, up, w, h, dc, c8) in enumerate(it):
            if i > 0:
                q.wait_ge(cx.sem("pa"), pa0 + i)
                q.wait_ge(cx.sem("ds"), ds0 + 48 * i)
            q.dma_start(tsc[0:64, 0:128],
                        W[wt][h * 64:(h + 1) * 64,
                              c8 * 128:(c8 + 1) * 128]).then_inc(
                cx.sem("ds"), 16)
            q.wait_ge(cx.sem("ds"), ds0 + 48 * i + 16)
            q.dma_start(tsc[0:64, 128:128 + w],
                        W[up][:, :]).then_inc(cx.sem("ds"), 16)
            q.wait_ge(cx.sem("ds"), ds0 + 48 * i + 32)
            q.wait_ge(cx.sem("pa"), pa0 + i + 1)
            q.dma_start(comp[c8 * 128:(c8 + 1) * 128, dc:dc + w],
                        tsc[:, 256:256 + w]).then_inc(cx.sem("ds"), 16)
    cx.on("q", cq)
    cx.bump("ds", 48 * len(it))

    def ct(t):
        for i, (wt, up, w, h, dc, c8) in enumerate(it):
            t.wait_ge(cx.sem("ds"), ds0 + 48 * i + 32)
            t.matmul(p[0][:, 0:w], tsc[0:64, 0:128], tsc[0:64, 128:128 + w],
                     start=True, stop=True).then_inc(cx.sem("pc"), 1)
    cx.on("t", ct)
    cx.bump("pc", len(it))

    def cv(v):
        for i, (wt, up, w, h, dc, c8) in enumerate(it):
            v.wait_ge(cx.sem("pc"), pc0 + i + 1)
            v.tensor_copy(tsc[:, 256:256 + w], p[0][:, 0:w]).then_inc(
                cx.sem("pa"), 1)
    cx.on("v", cv)
    cx.bump("pa", len(it))
    cx.wait("q", "ds", cx.cnt["ds"])
    cx.seal("q", "ds")
    cx.seal("v", "pa")
    cx.barrier()


def emit_l0_compose_f(cx, W):
    if cx.buf.get("l0f_comp_done"):
        return
    cx.buf["l0f_comp_done"] = True
    b = cx.buf
    p = b["p"]
    tsc = b["tsc"]
    ar3f = b["ar3"].bitcast(F32)
    comp = W["l0f_comp"]

    for c8 in range(DCH):
        lw = cx.dma("q", ar3f[:, 0:1024], W["ident1k"][c8 * 128:
                                                       (c8 + 1) * 128, :])
        cx.wait("q", "ds", lw)
        sw = cx.dma("q", comp[c8 * 128:(c8 + 1) * 128, 0:1024],
                    ar3f[:, 0:1024])
        cx.wait("q", "ds", sw)

    def zv(v):
        return v.memset(ar3f[:, 1024:1280], 0.0)
    cx.on("v", zv)
    cx.barrier()
    z3 = ar3f[:, 1024:1280].rearrange("p (c o) -> p c o", o=32)
    for h in range(16):
        sw = cx.dma("q", comp[:, 2048 + h * 64 + 32:2048 + h * 64 +
                              64].rearrange("(c p) o -> p c o", p=128), z3)
        cx.wait("q", "ds", sw)
    lw = cx.dma("q", ar3f[:, 1280:1536].rearrange("p (c o) -> p c o", o=32),
                W["l0f_wkva"][:, 128:160].rearrange("(c p) o -> p c o",
                                                    p=128))
    cx.wait("q", "ds", lw)
    r3 = ar3f[:, 1280:1536].rearrange("p (c o) -> p c o", o=32)
    for h in range(16):
        sw = cx.dma("q", comp[:, 1024 + h * 64 + 32:1024 + h * 64 +
                              64].rearrange("(c p) o -> p c o", p=128), r3)
        cx.wait("q", "ds", sw)
    lw = cx.dma("q", ar3f[:, 2048:2560], W["l0f_wkvb"][:, :])
    cx.wait("q", "ds", lw)
    cx.seal("q", "ds")
    cx.barrier()

    ds0 = cx.cnt["ds"]
    pc0 = cx.cnt["pc"]
    pa0 = cx.cnt["pa"]

    PERC = 16 * (1 + 32)

    def kq(q):
        for c8 in range(DCH):
            if c8 > 0:
                q.wait_ge(cx.sem("ds"), ds0 + PERC * c8)
            q.dma_start(tsc[:, 0:128],
                        W["l0f_wkva_t"][0:128, c8 * 128:
                                        (c8 + 1) * 128]).then_inc(
                cx.sem("ds"), 16)
            q.wait_ge(cx.sem("ds"), ds0 + PERC * c8 + 16)
            q.wait_ge(cx.sem("pa"), pa0 + c8 + 1)
            n = 1
            for h in range(16):
                for off in (1024, 2048):
                    q.dma_start(comp[c8 * 128:(c8 + 1) * 128,
                                     off + h * 64:off + h * 64 + 32],
                                ar3f[:, 2560 + h * 32:
                                     2560 + (h + 1) * 32]).then_inc(
                        cx.sem("ds"), 16)
                    n += 1
                    q.wait_ge(cx.sem("ds"), ds0 + PERC * c8 + 16 * n)
    cx.on("q", kq)
    cx.bump("ds", PERC * DCH)

    def kt(t):
        for c8 in range(DCH):
            t.wait_ge(cx.sem("ds"), ds0 + PERC * c8 + 16)
            if c8 > 0:
                t.wait_ge(cx.sem("pa"), pa0 + c8)
            t.matmul(p[0][:, 0:512], tsc[:, 0:128], ar3f[:, 2048:2560],
                     start=True, stop=True).then_inc(cx.sem("pc"), 1)
    cx.on("t", kt)
    cx.bump("pc", DCH)

    def kv_(v):
        for c8 in range(DCH):
            v.wait_ge(cx.sem("pc"), pc0 + c8 + 1)
            if c8 > 0:
                v.wait_ge(cx.sem("ds"), ds0 + PERC * c8)
            v.tensor_copy(ar3f[:, 2560:3072], p[0][:, 0:512]).then_inc(
                cx.sem("pa"), 1)
    cx.on("v", kv_)
    cx.bump("pa", DCH)
    cx.wait("q", "ds", cx.cnt["ds"])
    cx.seal("q", "ds")
    cx.seal("v", "pa")
    cx.barrier()


def emit_l0_attn(cx, W):
    b = cx.buf
    emit_l0_compose_a(cx, W)

    def kb():
        emit_tokproj_rope_fm(cx, W, "l0a_comp", 1024,
                             lambda o: b["ar1"][:, o * TOK:(o + 1) * TOK],
                             "ropel0a", scale=1.0, tab4=True)

    def qp():
        cx.wait("g", "ds", cx.cnt["ds"])
        emit_tokproj_rope_fm(cx, W, "l0a_comp", 3072,
                             lambda o: b["xnf"][:, o * TOK:(o + 1) * TOK],
                             "ropel0a", scale=1.0, tab4=True)
    emit_attn_causal(cx, W, "l0a_comp", 0, 1024, 2048, "l0a_wc",
                     0.125, 1.0, k_builder=kb, q_pre=qp)


def emit_l0_moe(cx, W):
    b = cx.buf
    p = b["p"]
    xs, xnf = b["xs"], b["xnf"]
    ar3 = b["ar3"]
    tsc, wsl = b["tsc"], b["wsl"]
    ar3f = ar3.bitcast(F32)

    def xnf_of(ch):
        return xnf[:, ch * TOK:(ch + 1) * TOK]

    def gid0(g):
        g.memset(b["msk"][:, 640:768], 1.0).then_inc(cx.sem("ga"), 1)
        cx.bump("ga")
        g.wait_ge(cx.sem("ga"), cx.cnt["ga"])
        return g.affine_select(b["idf"][:, :], b["ones1"][:, :],
                               [[-1, 128]], ALU.is_equal, 0.0,
                               channel_multiplier=1)
    cx.on("g", gid0)
    cx.barrier()

    gw = cx.dma("g", wsl[:, 0:64].rearrange("p (kc o) -> p kc o", kc=8),
                W["l0f_wgate"].rearrange("(kc p) o -> p kc o", p=128))
    bw = cx.dma("q", tsc[:, 8:16],
                W["l0f_ebias"][0:1, :].broadcast_to([128, 8]))
    cx.wait("t", "gs", gw)
    cx.wait("v", "ds", bw)
    NTC = 16
    pc0 = cx.cnt["pc"]
    pa0 = cx.cnt["pa"]
    pf0 = cx.cnt["pf"]
    ds0 = cx.cnt["ds"]

    ab_pre = cx.cnt["ab"]
    sa0 = cx.cnt["sa"]
    ph0 = cx.cnt["ph"]

    def gt(t):
        for tc in range(NTC):
            if tc >= 2:
                t.wait_ge(cx.sem("pa"), pa0 + (tc - 2) * 12 + 1)
            last = None
            for k in range(DCH):
                last = t.matmul(p[4 + tc % 2][:, 0:8],
                                xnf_of(k)[:, tc * 128:(tc + 1) * 128],
                                wsl[:, k * 8:(k + 1) * 8],
                                start=(k == 0), stop=(k == DCH - 1))
            last.then_inc(cx.sem("pc"), 1)
            t.wait_ge(cx.sem("pa"), pa0 + tc * 12 + 12)
            if tc >= 1:
                t.wait_ge(cx.sem("ab"), ab_pre + tc)
            t.transpose(p[6][0:8, 0:128], tsc[:, 96:104],
                        b["idf"][:, :]).then_inc(cx.sem("pf"), 1)
    cx.on("t", gt)
    cx.bump("pc", NTC)
    cx.bump("pf", NTC)

    def gsx(s):
        for tc in range(NTC):
            s.wait_ge(cx.sem("ph"), ph0 + tc + 1)
            s.activation(tsc[:, 48:56], tsc[:, 48:56],
                         AF.Exp).then_inc(cx.sem("sa"), 1)
    cx.on("s", gsx)
    cx.bump("sa", NTC)

    def gv(v):
        pa = pa0
        ab = cx.cnt["ab"]
        for tc in range(NTC):
            v.wait_ge(cx.sem("pc"), pc0 + tc + 1)

            def st(ins):
                nonlocal pa
                ins.then_inc(cx.sem("pa"), 1)
                pa += 1
                v.wait_ge(cx.sem("pa"), pa)
            st(v.tensor_tensor(tsc[:, 32:40], p[4 + tc % 2][:, 0:8],
                               tsc[:, 8:16], ALU.add))
            st(v.reduce_max(tsc[:, 40:41], tsc[:, 32:40], axis=AX.X))
            v.tensor_scalar(tsc[:, 48:56], tsc[:, 32:40], tsc[:, 40:41],
                            None, ALU.subtract).then_inc(cx.sem("ph"), 1)
            v.wait_ge(cx.sem("sa"), sa0 + tc + 1)
            st(v.reduce_max(tsc[:, 56:57], tsc[:, 48:56], axis=AX.X))
            st(v.tensor_scalar(tsc[:, 57:65], tsc[:, 48:56], tsc[:, 56:57],
                               None, ALU.is_ge))
            st(v.scalar_tensor_tensor(tsc[:, 65:73], tsc[:, 57:65], -1e9,
                                      tsc[:, 48:56], ALU.mult, ALU.add))
            st(v.reduce_max(tsc[:, 73:74], tsc[:, 65:73], axis=AX.X))
            st(v.tensor_scalar(tsc[:, 74:82], tsc[:, 48:56], tsc[:, 73:74],
                               None, ALU.is_ge))
            st(v.tensor_tensor(tsc[:, 82:90], tsc[:, 48:56], tsc[:, 74:82],
                               ALU.mult))
            st(v.reduce_sum(tsc[:, 90:91], tsc[:, 82:90], axis=AX.X))
            st(v.reciprocal(tsc[:, 91:92], tsc[:, 90:91]))
            v.tensor_scalar(tsc[:, 96:104], tsc[:, 82:90], tsc[:, 91:92],
                            None, ALU.mult).then_inc(cx.sem("pa"), 1)
            pa += 1
            v.wait_ge(cx.sem("pa"), pa)
            v.engine_nop().then_inc(cx.sem("pa"), 1)
            pa += 1
            v.wait_ge(cx.sem("pa"), pa)
            v.wait_ge(cx.sem("pf"), pf0 + tc + 1)
            if tc >= 1:
                v.wait_ge(cx.sem("ds"), ds0 + 16 * tc)
            v.tensor_copy(tsc[0:8, 104:232],
                          p[6][0:8, 0:128]).then_inc(cx.sem("ab"), 1)
            ab += 1
        return None
    cx.on("v", gv)
    cx.bump("pa", NTC * 12)
    cx.bump("ph", NTC)
    cx.bump("ab", NTC)
    ab0 = cx.cnt["ab"] - NTC

    def gq(q):
        for tc in range(NTC):
            q.wait_ge(cx.sem("ab"), ab0 + tc + 1)
            if tc > 0:
                q.wait_ge(cx.sem("ds"), ds0 + 16 * tc)
            q.dma_start(W["l0_comb"][:, tc * 128:(tc + 1) * 128],
                        tsc[0:8, 104:232]).then_inc(cx.sem("ds"), 16)
    cx.on("q", gq)
    cx.bump("ds", 16 * NTC)
    cx.wait("q", "ds", cx.cnt["ds"])
    cx.seal("q", "ds")
    cx.seal("v", "ab")
    cx.barrier()

    ar1f = b["ar1"].bitcast(F32)
    for ch in range(DCH):
        lw = cx.dma("q", ar1f[:, 0:2048],
                    W["l0_xsave"][ch * 128:(ch + 1) * 128, :])
        cx.wait("v", "ds", lw)

        def rc(v, ch=ch):
            return v.scalar_tensor_tensor(
                xs[:, ch * TOK:(ch + 1) * TOK], xs[:, ch * TOK:(ch + 1) * TOK],
                0.5, ar1f[:, 0:2048], ALU.mult, ALU.add)
        cx.on("v", rc)
        cx.seal("v", "pa")
        cx.barrier()

    SIL = 0
    HST = 8192

    def h_of(ch):
        return ar3[:, HST + ch * TOK:HST + (ch + 1) * TOK]

    for e in range(9):
        shared = (e == 8)
        wg = W["l0f_swg" if shared else f"l0f_wg{e}"]
        wu = W["l0f_swu" if shared else f"l0f_wu{e}"]
        wd = W["l0f_swd" if shared else f"l0f_wd{e}"]

        def ev_silu(s, o, qc, pp):
            return s.activation(ar3[:, SIL + o * TOK + qc * QCOL:
                                    SIL + o * TOK + (qc + 1) * QCOL],
                                pp[:, :], AF.Silu)
        emit_proj(cx, wg, xnf_of, D, 512, evac_s=ev_silu)

        def ev_mul(v, o, qc, pp):
            return v.tensor_tensor(
                h_of(o)[:, qc * QCOL:(qc + 1) * QCOL],
                ar3[:, SIL + o * TOK + qc * QCOL:
                    SIL + o * TOK + (qc + 1) * QCOL], pp[:, :], ALU.mult)
        emit_proj(cx, wu, xnf_of, D, 512, evac_v=ev_mul)

        if not shared:
            for qc in range(NQC):
                bw2 = cx.dma("q", tsc[:, 0:512],
                             W["l0_comb"][e:e + 1, qc * 512:(qc + 1) * 512]
                             .broadcast_to([128, 512]))
                cx.wait("v", "ds", bw2)

                def sc(v, qc=qc):
                    last = None
                    for ch in range(4):
                        last = v.tensor_tensor(
                            h_of(ch)[:, qc * QCOL:(qc + 1) * QCOL],
                            h_of(ch)[:, qc * QCOL:(qc + 1) * QCOL],
                            tsc[:, 0:512], ALU.mult)
                    return last
                cx.on("v", sc)
                cx.seal("v", "pa")
                cx.barrier()

        def ev_acc(v, o, qc, pp):
            return v.scalar_tensor_tensor(
                xs[:, o * TOK + qc * QCOL:o * TOK + (qc + 1) * QCOL],
                pp[:, :], 0.5,
                xs[:, o * TOK + qc * QCOL:o * TOK + (qc + 1) * QCOL],
                ALU.mult, ALU.add)
        emit_proj(cx, wd, h_of, 512, D, evac_v=ev_acc, banks=(2, 3))


def emit_l0_ffn(cx, W):
    b = cx.buf
    xs = b["xs"]

    def xs_of(ch):
        return xs[:, ch * TOK:(ch + 1) * TOK]

    def xnf_of(ch):
        return b["xnf"][:, ch * TOK:(ch + 1) * TOK]

    g1 = cx.dma("q", b["vrow"][:, 48:56], W["l0f_ln1"][:, :])
    g2 = cx.dma("q", b["vrow"][:, 56:64], W["l0f_ln2"][:, :])
    cx.wait("v", "ds", g2)
    xsv = xs.rearrange("p (c t) -> p c t", c=DCH)
    sw = cx.dma("q", W["l0_xsave"].rearrange("(c p) t -> p c t", p=128), xsv)
    cx.wait("q", "ds", sw)
    cx.seal("q", "ds")
    cx.barrier()
    emit_norm(cx, xs_of, xs_of, kind="ln")
    emit_l0_compose_f(cx, W)

    def kb():
        emit_tokproj_rope_fm(cx, W, "l0f_comp", 1024,
                             lambda o: b["ar1"][:, o * TOK:(o + 1) * TOK],
                             "ropel0f", scale=1.0, tab4=True)

    def qp():
        cx.wait("g", "ds", cx.cnt["ds"])
        emit_tokproj_rope_fm(cx, W, "l0f_wq", 0,
                             lambda o: b["xnf"][:, o * TOK:(o + 1) * TOK],
                             "ropel0f", scale=1.0, tab4=True)
    emit_attn_causal(cx, W, "l0f_comp", 0, 1024, 2048, "l0f_wo",
                     0.125, 1.0, norm_kind="rms",
                     norm_w=lambda ch: b["vrow"][:, 48 + ch:49 + ch],
                     k_builder=kb, q_pre=qp)
    emit_norm(cx, xs_of, xnf_of, kind="rms",
              w_slice=lambda ch: b["vrow"][:, 56 + ch:57 + ch])
    emit_l0_moe(cx, W)


SUBLAYER_EMITTERS = {}
SUBLAYER_EMITTERS["l2_ffn"] = emit_l2_ffn
SUBLAYER_EMITTERS["l3_attn"] = emit_l3_attn
SUBLAYER_EMITTERS["l2_attn"] = emit_l2_attn
SUBLAYER_EMITTERS["l1_attn"] = emit_l1_attn
SUBLAYER_EMITTERS["l3_ffn"] = emit_l3_ffn
SUBLAYER_EMITTERS["l1_ffn"] = emit_l1_ffn
SUBLAYER_EMITTERS["l0_attn"] = emit_l0_attn
SUBLAYER_EMITTERS["l0_ffn"] = emit_l0_ffn


def prep_weights(inputs):
    w = {}
    w["l2f_wr_t"] = np.ascontiguousarray(inputs["L2_ffn_W_r"].T)
    w["l2f_wk_t"] = np.ascontiguousarray(inputs["L2_ffn_W_k"].T)
    w["l2f_wv_t"] = np.ascontiguousarray(inputs["L2_ffn_W_v"].T)
    w["l2f_mu_r"] = np.ascontiguousarray(
        np.asarray(inputs["L2_ffn_mu_r"], np.float32).reshape(DCH, 128).T)
    w["l2f_mu_k"] = np.ascontiguousarray(
        np.asarray(inputs["L2_ffn_mu_k"], np.float32).reshape(DCH, 128).T)
    w["l3a_wqkv"] = np.asarray(inputs["L3_attn_w_qkv"], np.float32)
    w["l3a_wout"] = np.asarray(inputs["L3_attn_w_out"], np.float32)
    w["l2a_wq"] = np.asarray(inputs["L2_attn_Wq"], np.float32)
    w["l2a_wk"] = np.asarray(inputs["L2_attn_Wk"], np.float32)
    w["l2a_wv"] = np.asarray(inputs["L2_attn_Wv"], np.float32)
    w["l2a_wo"] = np.asarray(inputs["L2_attn_Wo"], np.float32)
    w["l1a_wqkv_t"] = np.ascontiguousarray(
        np.asarray(inputs["L1_attn_w_qkv"], np.float32).T)
    w["l1a_wout_t"] = np.ascontiguousarray(
        np.asarray(inputs["L1_attn_w_out"], np.float32).T)
    w["l1a_wg1_t"] = np.ascontiguousarray(
        np.asarray(inputs["L1_attn_w_g1"], np.float32).T)
    w["l1a_wg2_t"] = np.ascontiguousarray(
        np.asarray(inputs["L1_attn_w_g2"], np.float32).T)
    w["l1a_bg1"] = np.asarray(inputs["L1_attn_b_g1"],
                              np.float32).reshape(64, 1)
    w["l1a_bg2"] = np.ascontiguousarray(
        np.asarray(inputs["L1_attn_b_g2"], np.float32).reshape(DCH, 128).T)
    w["l1a_gamma"] = np.ascontiguousarray(
        np.asarray(inputs["L1_attn_gamma"], np.float32).reshape(DCH, 128).T)
    w["l1a_beta"] = np.ascontiguousarray(
        np.asarray(inputs["L1_attn_beta"], np.float32).reshape(DCH, 128).T)
    w["l3f_win_t"] = np.ascontiguousarray(
        np.asarray(inputs["L3_ffn_w_in"], np.float32).T)
    w["l3f_wout_t"] = np.ascontiguousarray(
        np.asarray(inputs["L3_ffn_w_out"], np.float32).T)
    w["l3f_w1_t"] = np.ascontiguousarray(
        np.asarray(inputs["L3_ffn_w1"], np.float32).T)
    w["l3f_w2_t"] = np.ascontiguousarray(
        np.asarray(inputs["L3_ffn_w2"], np.float32).transpose(0, 2, 1))
    w["l3f_wg_t"] = np.ascontiguousarray(
        np.asarray(inputs["L3_ffn_wg"], np.float32).T)
    w["l3f_we_t"] = np.ascontiguousarray(
        np.asarray(inputs["L3_ffn_we"], np.float32).T)
    w["ident1k"] = np.eye(1024, dtype=np.float32)
    w["l0a_wqd_t"] = np.ascontiguousarray(
        np.asarray(inputs["L0_attn_Wq_down"], np.float32).T)
    w["l0a_wkd_t"] = np.ascontiguousarray(
        np.asarray(inputs["L0_attn_Wk_down"], np.float32).T)
    w["l0a_wvd_t"] = np.ascontiguousarray(
        np.asarray(inputs["L0_attn_Wv_down"], np.float32).T)
    w["l0a_upc"] = np.asarray(inputs["L0_attn_Wq_up_c"], np.float32)
    w["l0a_upe"] = np.asarray(inputs["L0_attn_Wq_up_e"], np.float32)
    w["l0a_kupc"] = np.asarray(inputs["L0_attn_Wk_up_c"], np.float32)
    w["l0a_kupe"] = np.asarray(inputs["L0_attn_Wk_up_e"], np.float32)
    w["l0a_vup"] = np.asarray(inputs["L0_attn_Wv_up"], np.float32)
    w["l0a_wc"] = np.asarray(inputs["L0_attn_Wc"], np.float32)
    w["l0f_wq"] = np.asarray(inputs["L0_ffn_Wq"], np.float32)
    w["l0f_wkva"] = np.asarray(inputs["L0_ffn_Wkva"], np.float32)
    w["l0f_wkva_t"] = np.ascontiguousarray(
        np.asarray(inputs["L0_ffn_Wkva"], np.float32).T)
    w["l0f_wkvb"] = np.asarray(inputs["L0_ffn_Wkvb"], np.float32)
    w["l0f_wo"] = np.asarray(inputs["L0_ffn_Wo"], np.float32)
    w["l0f_ln1"] = np.ascontiguousarray(
        np.asarray(inputs["L0_ffn_w_ln1"], np.float32).reshape(DCH, 128).T)
    w["l0f_ln2"] = np.ascontiguousarray(
        np.asarray(inputs["L0_ffn_w_ln2"], np.float32).reshape(DCH, 128).T)
    w["l0f_wgate"] = np.asarray(inputs["L0_ffn_Wgate"], np.float32)
    w["l0f_ebias"] = np.asarray(inputs["L0_ffn_expert_bias"],
                                np.float32).reshape(1, 8)
    for e in range(8):
        w[f"l0f_wg{e}"] = np.ascontiguousarray(
            np.asarray(inputs["L0_ffn_Wg"], np.float32)[e])
        w[f"l0f_wu{e}"] = np.ascontiguousarray(
            np.asarray(inputs["L0_ffn_Wu"], np.float32)[e])
        w[f"l0f_wd{e}"] = np.ascontiguousarray(
            np.asarray(inputs["L0_ffn_Wd"], np.float32)[e])
    w["l0f_swg"] = np.ascontiguousarray(
        np.asarray(inputs["L0_ffn_sWg"], np.float32)[0])
    w["l0f_swu"] = np.ascontiguousarray(
        np.asarray(inputs["L0_ffn_sWu"], np.float32)[0])
    w["l0f_swd"] = np.ascontiguousarray(
        np.asarray(inputs["L0_ffn_sWd"], np.float32)[0])
    w["l1f_gx"] = np.ascontiguousarray(
        np.asarray(inputs["L1_ffn_gamma_x"], np.float32).reshape(DCH, 128).T)
    w["l1f_gsrow"] = np.asarray(inputs["L1_ffn_gamma_slot"],
                                np.float32).reshape(1, D)
    w["l1f_semb"] = np.ascontiguousarray(
        np.asarray(inputs["L1_ffn_slot_embeds"], np.float32).reshape(4096, D))
    geb = np.zeros((1, 16), np.float32)
    geb[0, 0:2] = np.asarray(inputs["L3_ffn_gb"], np.float32)
    geb[0, 8:16] = np.asarray(inputs["L3_ffn_eb"], np.float32)
    w["l3f_geb"] = geb
    return w


def prep_percore(c, inputs=None):
    j = c % GRP
    d = {}
    if inputs is not None:
        w1 = np.asarray(inputs["L1_ffn_w1"], np.float32)
        w2 = np.asarray(inputs["L1_ffn_w2"], np.float32)
        d["l1f_w1_loc"] = np.ascontiguousarray(
            w1[:, j * 1024:(j + 1) * 1024])
        d["l1f_w2_loc"] = np.ascontiguousarray(
            w2[j * 1024:(j + 1) * 1024, :])
    pv = ((j - 1) % GRP) * 2 * D + (D if j == 0 else 0)
    pb = np.zeros((1, 16), np.int32)
    pb[0, 0] = pv
    d["prev_base"] = pb
    tm = np.zeros((4, 128, 128), np.float32)
    pp, ff = np.meshgrid(np.arange(128), np.arange(128), indexing="ij")
    for r in range(4):
        delta = 0 if j >= r else -1
        tm[r] = (pp <= ff + delta).astype(np.float32)
    d["trimask"] = tm
    pos = (4 * np.arange(SLOC) + j).astype(np.float64)
    i32 = np.arange(32, dtype=np.float64)
    coss, sins = [], []
    for sc_ in (1.0, 4.0, 16.0):
        freqs = 1.0 / (10000.0 * sc_) ** (2.0 * i32 / 64.0)
        ang = pos[:, None] * freqs[None, :]
        coss.append(np.cos(ang))
        sins.append(np.sin(ang))
    cosv = np.mean(coss, 0).astype(np.float32)
    sinv = np.mean(sins, 0).astype(np.float32)
    ck_ = np.tile(cosv, (1, 16))
    sk_ = np.tile(sinv, (1, 16))
    d["ropek"] = np.concatenate([ck_, sk_], 1).astype(np.float32)
    rq = np.zeros((512, 512), np.float32)
    for qb in range(4):
        for tc in range(4):
            rows = slice(qb * 128, (qb + 1) * 128)
            src_rows = slice(qb * 512 + tc * 128, qb * 512 + (tc + 1) * 128)
            for h in range(2):
                cslc = slice(tc * 64 + h * 32, tc * 64 + (h + 1) * 32)
                rq[rows, cslc.start:cslc.stop] = cosv[src_rows]
                rq[rows, 256 + cslc.start:256 + cslc.stop] = sinv[src_rows]
    d["ropeq"] = rq
    posf = (4 * np.arange(SLOC) + j).astype(np.float64)

    def mk_l0tab(theta, quirk):
        C0 = np.ones((SLOC, 32)); S0 = np.zeros((SLOC, 32))
        S1 = np.zeros((SLOC, 32)); C1 = np.ones((SLOC, 32))
        if quirk:
            invf = 1.0 / theta ** (2.0 * np.arange(16) / 32.0)
            cosv = np.cos(posf[:, None] * invf[None, :])
            sinv = np.sin(posf[:, None] * invf[None, :])
            for m in range(16):
                C0[:, 16 + m] = cosv[:, (2 * m) % 16]
                S0[:, 16 + m] = sinv[:, (2 * m) % 16]
                C1[:, 16 + m] = cosv[:, (2 * m + 1) % 16]
                S1[:, 16 + m] = sinv[:, (2 * m + 1) % 16]
        else:
            invf = 1.0 / theta ** (2.0 * np.arange(16) / 32.0)
            ang = posf[:, None] * invf[None, :]
            for m in range(16):
                C0[:, 16 + m] = np.cos(ang[:, m])
                S0[:, 16 + m] = np.sin(ang[:, m])
                C1[:, 16 + m] = np.cos(ang[:, m])
                S1[:, 16 + m] = np.sin(ang[:, m])
        blocks = [np.tile(B, (1, 16)) for B in (C0, S0, S1, C1)]
        return np.concatenate(blocks, 1).astype(np.float32)
    d["ropel0a"] = mk_l0tab(10000.0, True)
    d["ropel0f"] = mk_l0tab(100000.0, False)
    return d


def _shard_x(x):
    shards = []
    for c in range(NCORE):
        bidx = c // GRP
        j = c % GRP
        tokens = x[bidx, j::GRP, :]
        shards.append(np.ascontiguousarray(tokens.T))
    return shards


def _unshard_x(shards):
    out = np.zeros((B, T, D), np.float32)
    for c in range(NCORE):
        bidx = c // GRP
        j = c % GRP
        out[bidx, j::GRP, :] = np.asarray(shards[c], dtype=np.float32).T
    return out


INTERNAL_DRAM = {
    "xg_in": ([2 * D, TOK], BF16),
    "xg_out": ([GRP * 2 * D, TOK], BF16),
    "xgk_in": ([D, TOK], BF16),
    "xgk_out": ([GRP * D, TOK], BF16),
    "xgv_in": ([TOK, D], BF16),
    "xgv_out": ([GRP * TOK, D], BF16),
    "kmx_in": ([1, 16], F32),
    "kmx_out": ([1, 16], F32),
    "rbc": ([2, 512], F32),
    "rbc2": ([1, 512], BF16),
    "l1_l": ([128, 65536], BF16),
    "l1_se": ([128, 32768], BF16),
    "l1_sp_in": ([128, 32768], F32),
    "l1_sp_out": ([128, 32768], F32),
    "l1_pp_in": ([128, 32768], F32),
    "l1_pp_out": ([128, 32768], F32),
    "l1_sm_in": ([32, 128], F32),
    "l1_sm_out": ([32, 128], F32),
    "l1_sd_in": ([32, 128], F32),
    "l1_sd_out": ([32, 128], F32),
    "l1_dr": ([32, 128], F32),
    "l1_rr": ([16, 128], F32),
    "l0a_comp": ([1024, 4096], F32),
    "l0f_comp": ([1024, 4096], F32),
    "l0_xsave": ([1024, 2048], F32),
    "l0_comb": ([8, 2048], F32),
}

PERCORE_SPECS = {
    "l1f_w1_loc": ([1024, 1024], F32),
    "l1f_w2_loc": ([1024, 1024], F32),
    "prev_base": ([1, 16], mybir.dt.int32),
    "trimask": ([4, 128, 128], F32),
    "ropek": ([SLOC, 1024], F32),
    "ropeq": ([512, 512], F32),
    "ropel0a": ([SLOC, 2048], F32),
    "ropel0f": ([SLOC, 2048], F32),
}


def build_program(sublayers, weights_np):
    nc = bass.Bass("TRN2", target_bir_lowering=False)
    x_in = nc.dram_tensor("x_in", [D, TOK], F32, kind="ExternalInput")
    x_out = nc.dram_tensor("x_out", [D, TOK], F32, kind="ExternalOutput")
    wd = {}
    for name, arr in weights_np.items():
        wd[name] = nc.dram_tensor(name, list(arr.shape), F32,
                                  kind="ExternalInput")
    for name, (shape, dt) in PERCORE_SPECS.items():
        wd[name] = nc.dram_tensor(name, shape, dt, kind="ExternalInput")
    import os as _os
    _ik = "ExternalOutput" if _os.environ.get("DBG_INTERNAL") else "Internal"
    for name, (shape, dt) in INTERNAL_DRAM.items():
        wd[name] = nc.dram_tensor(name, shape, dt, kind=_ik)

    with ExitStack() as stack:
        block = stack.enter_context(nc.Block())
        cx = Cx(nc, block, stack)
        alloc_pool(cx)
        b = cx.buf
        xr = x_in.rearrange("(c p) t -> p c t", p=128)
        xsv = b["xs"].rearrange("p (c t) -> p c t", c=DCH)
        lw = cx.dma("q", xsv, xr)
        cx.wait("v", "ds", lw)
        emit_init_consts(cx)

        for sl in sublayers:
            SUBLAYER_EMITTERS[sl](cx, wd)

        xo = x_out.rearrange("(c p) t -> p c t", p=128)
        sw = cx.dma("q", xo, xsv)
        cx.wait("q", "ds", sw)
    return nc


def run_chain(x_full, inputs, sublayers):
    w = prep_weights(inputs)
    shards = _shard_x(np.asarray(x_full, np.float32))
    nc = build_program(sublayers, w)
    in_maps = []
    for c in range(NCORE):
        m = dict(w)
        m.update(prep_percore(c, inputs))
        m["x_in"] = shards[c]
        in_maps.append(m)
    res = run_bass_kernel_spmd(nc, in_maps, core_ids=list(range(NCORE)))
    return _unshard_x([res.results[c]["x_out"] for c in range(NCORE)])


def run_chain_sim(x_full, inputs, sublayers):
    from concourse.bass_interp import MultiCoreSim
    w = prep_weights(inputs)
    shards = _shard_x(np.asarray(x_full, np.float32))
    nc = build_program(sublayers, w)
    import os
    nw = int(os.environ.get('SIM_WORKERS', '8'))
    sim = MultiCoreSim(nc, NCORE, num_workers=nw)
    for c in range(NCORE):
        cs = sim.cores[c]
        for k, v in w.items():
            cs.tensor(k)[:] = v
        for k, v in prep_percore(c, inputs).items():
            cs.tensor(k)[:] = v
        cs.tensor("x_in")[:] = shards[c]
    sim.simulate(check_with_hw=False)
    return _unshard_x([sim.cores[c].tensor("x_out") for c in range(NCORE)])


FULL_CHAIN = ["l0_ffn", "l0_attn", "l0_ffn",
              "l1_ffn", "l1_attn", "l1_ffn",
              "l2_ffn", "l2_attn", "l2_ffn",
              "l3_ffn", "l3_attn", "l3_ffn"]


def kernel(**inputs):
    x = np.asarray(inputs["L0_attn_x"], np.float32)
    x = run_chain(x, inputs, FULL_CHAIN[:8])
    out = run_chain(x, inputs, FULL_CHAIN[8:])
    return np.asarray(out, np.float32)


# version 06 (final)
# speedup vs baseline: 1.0010x; 1.0010x over previous
import numpy as np
from contextlib import ExitStack
import concourse.bass as bass
import concourse.mybir as mybir
from concourse.bass_utils import run_bass_kernel_spmd

F32 = mybir.dt.float32
BF16 = mybir.dt.bfloat16
AF = mybir.ActivationFunctionType
ALU = mybir.AluOpType
AX = mybir.AxisListType

B, T, D = 2, 8192, 1024
NCORE = 8
GRP = 4
SC = 1024
NSC = T // SC
TOK = 2 * SC
DCH = D // 128
EPS_LN = 1e-5
RG = [[0, 1, 2, 3], [4, 5, 6, 7]]
QCOL = 512
NQC = TOK // QCOL


SLOC = TOK
QB = 512
NQB = SLOC // QB


class Cx:
    ENGS = ("t", "v", "s", "g", "q")

    def __init__(self, nc, block, stack, core_id_hint=0):
        self.nc = nc
        self.block = block
        self.stack = stack
        self.sems = {}
        self.cnt = {}
        self.bs = self._sem("bs")
        self.bv = 0
        self.last = {e: None for e in self.ENGS}
        self.last_has_inc = {e: False for e in self.ENGS}
        self.sealed = {e: None for e in self.ENGS}
        self.waited = {e: True for e in self.ENGS}
        for nm in ("ds", "gs", "cc", "pa", "pb", "pc", "pd", "pe_", "pf",
                   "pg", "ph", "sa", "ga", "as_", "ax", "am", "av", "ab",
                   "ay"):
            self._sem(nm)
        self.buf = {}

    def _sem(self, name):
        h = self.stack.enter_context(self.nc.semaphore(name))
        self.sems[name] = h
        self.cnt[name] = 0
        return h

    def sem(self, name):
        return self.sems[name]

    def bump(self, name, by=1):
        self.cnt[name] += by
        return self.cnt[name]

    def _eng(self, name):
        return dict(t=self.block.tensor, v=self.block.vector,
                    s=self.block.scalar, g=self.block.gpsimd,
                    q=self.block.sync)[name]

    def on(self, ename, fn):
        need_wait = (not self.waited[ename]) and self.bv > 0
        bv = self.bv

        def body(eng):
            if need_wait:
                eng.wait_ge(self.bs, bv)
            last = fn(eng)
            if last is not None:
                self.last[ename] = last
                self.last_has_inc[ename] = False
        self._eng(ename)(body)
        self.waited[ename] = True

    def seal(self, ename, semname):
        self.sealed[ename] = (semname, self.cnt[semname])
        self.last[ename] = None

    def barrier(self):
        n = 0
        for e in self.ENGS:
            ins = self.last[e]
            if ins is not None:
                assert not self.last_has_inc[e]
                ins.then_inc(self.bs, 1)
                n += 1
            elif self.sealed[e] is not None:
                semname, val = self.sealed[e]
                sem = self.sems[semname]
                bsl = self.bs

                def carrier(eng, e=e, sem=sem, val=val):
                    eng.wait_ge(sem, val)
                    if e == "v":
                        eng.engine_nop().then_inc(bsl, 1)
                    else:
                        eng.nop().then_inc(bsl, 1)
                self._eng(e)(carrier)
                n += 1
            self.last[e] = None
            self.last_has_inc[e] = False
            self.sealed[e] = None
        self.bv += n
        for e in self.ENGS:
            self.waited[e] = False

    def dma(self, ename, dst, src, sem=None, slow=False):
        sem = sem or ("gs" if ename == "g" else "ds")
        tgt = self.bump(sem, 16)
        s = self.sems[sem]
        need_wait = (not self.waited[ename]) and self.bv > 0
        bv = self.bv

        def body(eng):
            if need_wait:
                eng.wait_ge(self.bs, bv)
            eng.dma_start(dst, src,
                          allow_slow_non_contiguous=slow).then_inc(s, 16)
        self._eng(ename)(body)
        self.waited[ename] = True
        return tgt

    def wait(self, ename, sem, val):
        need_wait = (not self.waited[ename]) and self.bv > 0
        bv = self.bv

        def body(eng):
            if need_wait:
                eng.wait_ge(self.bs, bv)
            eng.wait_ge(self.sems[sem], val)
        self._eng(ename)(body)
        self.waited[ename] = True


def alloc_pool(cx):
    nc, st = cx.nc, cx.stack
    b = cx.buf
    b["xs"] = st.enter_context(nc.sbuf_tensor("xs", [128, DCH * TOK], F32))
    b["xnf"] = st.enter_context(nc.sbuf_tensor("xnf", [128, DCH * TOK], BF16))
    b["ar1"] = st.enter_context(nc.sbuf_tensor("ar1", [128, DCH * TOK], BF16))
    b["ar2"] = st.enter_context(nc.sbuf_tensor("ar2", [128, DCH * TOK], BF16))
    b["ar3"] = st.enter_context(nc.sbuf_tensor("ar3", [128, DCH * TOK], BF16))
    b["wsl"] = st.enter_context(nc.sbuf_tensor("wsl", [128, 4096], BF16))
    b["tsc"] = st.enter_context(nc.sbuf_tensor("tsc", [128, QCOL], F32))
    b["onesb"] = st.enter_context(nc.sbuf_tensor("onesb", [128, 2], BF16))
    b["ones1"] = st.enter_context(nc.sbuf_tensor("ones1", [128, 128], F32))
    b["vrow"] = st.enter_context(nc.sbuf_tensor("vrow", [128, 64], F32))
    b["msk"] = st.enter_context(nc.sbuf_tensor("msk", [128, 768], BF16))
    b["ivals"] = st.enter_context(nc.sbuf_tensor("ivals", [1, 16],
                                                  mybir.dt.int32))
    b["zrow"] = st.enter_context(nc.sbuf_tensor("zrow", [128, 16], BF16))
    b["idf"] = st.enter_context(nc.sbuf_tensor("idf", [128, 128], F32))
    b["p"] = [st.enter_context(nc.psum_tensor(f"p{i}", [128, QCOL], F32))
              for i in range(8)]
    return b


def emit_init_consts(cx):
    b = cx.buf

    def vf(v):
        v.memset(b["onesb"][:], 1.0)
        v.memset(b["zrow"][:], 0.0)
        return v.memset(b["ones1"][:], 1.0)
    cx.on("v", vf)
    cx.barrier()


def emit_norm(cx, src_of, dst_of, kind="ln", w_slice=None, eps=EPS_LN):
    b = cx.buf
    p = b["p"]
    xsb, sq = b["ar1"], b["ar2"]
    onesb, ones1, tsc = b["onesb"], b["ones1"], b["tsc"]
    need_mean = (kind == "ln")

    ph0 = cx.cnt["ph"]

    def vf(v):
        for ch in range(DCH):
            v.tensor_copy(xsb[:, ch * TOK:(ch + 1) * TOK],
                          src_of(ch)).then_inc(cx.sem("ph"), 1)
        cx.bump("ph", DCH)
        return None
    cx.on("v", vf)

    def sf(s):
        last = None
        for ch in range(DCH):
            s.wait_ge(cx.sem("ph"), ph0 + ch + 1)
            last = s.activation(sq[:, ch * TOK:(ch + 1) * TOK],
                                xsb[:, ch * TOK:(ch + 1) * TOK], AF.Square)
        return last
    cx.on("s", sf)
    cx.barrier()

    pc0 = cx.cnt["pc"]
    pd0 = cx.cnt["pd"]
    pe0 = cx.cnt["pe_"]
    pf0 = cx.cnt["pf"]
    pg0 = cx.cnt["pg"]

    def tf(t):
        for qc in range(NQC):
            if qc >= 1:
                t.wait_ge(cx.sem("pd"), pd0 + qc)
            last = None
            for ch in range(DCH):
                csl = slice(ch * TOK + qc * QCOL, ch * TOK + (qc + 1) * QCOL)
                last = t.matmul(p[4][0:1, :], onesb[:, 0:1], sq[:, csl],
                                start=(ch == 0), stop=(ch == DCH - 1))
            if need_mean:
                for ch in range(DCH):
                    csl = slice(ch * TOK + qc * QCOL,
                                ch * TOK + (qc + 1) * QCOL)
                    last = t.matmul(p[5][0:1, :], onesb[:, 0:1], xsb[:, csl],
                                    start=(ch == 0), stop=(ch == DCH - 1))
            last.then_inc(cx.sem("pc"), 1)
            t.wait_ge(cx.sem("pe_"), pe0 + qc + 1)
            if qc >= 1:
                t.wait_ge(cx.sem("pg"), pg0 + qc)
            lb = t.matmul(p[6][:, :], ones1[64:65, :], tsc[64:65, 0:QCOL],
                          start=True, stop=True)
            if need_mean:
                lb = t.matmul(p[7][:, :], ones1[32:33, :], tsc[32:33, 0:QCOL],
                              start=True, stop=True)
            lb.then_inc(cx.sem("pf"), 1)
        cx.bump("pc", NQC)
        cx.bump("pf", NQC)
        return None
    cx.on("t", tf)

    def vwork(v):
        for qc in range(NQC):
            v.wait_ge(cx.sem("pc"), pc0 + qc + 1)
            if kind == "rmsl1":
                v.tensor_scalar_max(tsc[64:65, 0:QCOL], p[4][0:1, :], 1e-24)
            elif kind == "rms":
                v.tensor_scalar(tsc[64:65, 0:QCOL], p[4][0:1, :], 1.0 / D,
                                eps, ALU.mult, ALU.add)
            else:
                v.tensor_scalar_mul(tsc[32:33, 0:QCOL], p[5][0:1, :],
                                    1.0 / D).then_inc(cx.sem("pa"), 1)
                cx.bump("pa")
                v.wait_ge(cx.sem("pa"), cx.cnt["pa"])
                v.tensor_mul(tsc[0:1, 0:QCOL], tsc[32:33, 0:QCOL],
                             tsc[32:33, 0:QCOL]).then_inc(cx.sem("pa"), 1)
                cx.bump("pa")
                v.wait_ge(cx.sem("pa"), cx.cnt["pa"])
                v.scalar_tensor_tensor(tsc[64:65, 0:QCOL], p[4][0:1, :],
                                       1.0 / D, tsc[0:1, 0:QCOL], ALU.mult,
                                       ALU.subtract).then_inc(cx.sem("pa"), 1)
                cx.bump("pa")
                v.wait_ge(cx.sem("pa"), cx.cnt["pa"])
                v.tensor_scalar_add(tsc[64:65, 0:QCOL], tsc[64:65, 0:QCOL],
                                    eps)
            v.engine_nop().then_inc(cx.sem("pd"), 1)
            cx.bump("pd")
            v.wait_ge(cx.sem("pf"), pf0 + qc + 1)
            last = None
            for ch in range(DCH):
                sl = slice(qc * QCOL, (qc + 1) * QCOL)
                s_ap = src_of(ch)[:, sl]
                d_ap = dst_of(ch)[:, sl]
                if need_mean:
                    v.tensor_sub(tsc[:, 0:QCOL], s_ap,
                                 p[7][:, :]).then_inc(cx.sem("pa"), 1)
                    cx.bump("pa")
                    v.wait_ge(cx.sem("pa"), cx.cnt["pa"])
                    if w_slice is None:
                        last = v.tensor_tensor(d_ap, tsc[:, 0:QCOL],
                                               p[6][:, :], ALU.mult)
                    else:
                        last = v.scalar_tensor_tensor(
                            d_ap, tsc[:, 0:QCOL], w_slice(ch), p[6][:, :],
                            ALU.mult, ALU.mult)
                    last.then_inc(cx.sem("pa"), 1)
                    cx.bump("pa")
                    v.wait_ge(cx.sem("pa"), cx.cnt["pa"])
                elif w_slice is None:
                    last = v.tensor_tensor(d_ap, s_ap, p[6][:, :], ALU.mult)
                else:
                    last = v.scalar_tensor_tensor(d_ap, s_ap, w_slice(ch),
                                                  p[6][:, :], ALU.mult,
                                                  ALU.mult)
            v.engine_nop().then_inc(cx.sem("pg"), 1)
            cx.bump("pg")
        return None
    cx.on("v", vwork)
    cx.seal("v", "pg")

    def sstat(s):
        for qc in range(NQC):
            s.wait_ge(cx.sem("pd"), pd0 + qc + 1)
            s.activation(tsc[0:1, 0:QCOL], tsc[64:65, 0:QCOL],
                         AF.Ln).then_inc(cx.sem("sa"), 1)
            cx.bump("sa")
            s.wait_ge(cx.sem("sa"), cx.cnt["sa"])
            s.activation(tsc[64:65, 0:QCOL], tsc[0:1, 0:QCOL], AF.Exp,
                         scale=-0.5).then_inc(cx.sem("sa"), 1)
            cx.bump("sa")
            s.wait_ge(cx.sem("sa"), cx.cnt["sa"])
            if kind == "rmsl1":
                s.mul(tsc[64:65, 0:QCOL], tsc[64:65, 0:QCOL],
                      float(np.sqrt(D))).then_inc(cx.sem("pe_"), 1)
            else:
                s.copy(tsc[64:65, 0:QCOL],
                       tsc[64:65, 0:QCOL]).then_inc(cx.sem("pe_"), 1)
            cx.bump("pe_")
        return None
    cx.on("s", sstat)
    cx.barrier()


WSLAB = 2048


def emit_proj(cx, w_dram, xn_of, n_in, n_out, evac_v=None, evac_s=None,
              nqc=NQC, banks=(4, 5), row_off=0, col_off=0, qcw=QCOL):
    b = cx.buf
    p = b["p"]
    wsl = b["wsl"]
    kch = n_in // 128
    slw = min(max(128, (WSLAB // kch) // 128 * 128), n_out, 512)
    nslab = (n_out + slw - 1) // slw
    och_per = slw // 128
    slab_elems = kch * slw
    assert slab_elems <= WSLAB

    gs0 = cx.cnt["gs"]
    pc0 = cx.cnt["pc"]
    pd0 = cx.cnt["pd"]
    pf0 = cx.cnt["pf"]

    def gl(g):
        for si in range(nslab):
            if si >= 1:
                g.wait_ge(cx.sem("gs"), gs0 + 16 * si)
            if si >= 2:
                g.wait_ge(cx.sem("pf"), pf0 + si - 1)
            co = col_off + si * slw
            wr = w_dram[row_off:row_off + n_in, co:co + slw].rearrange(
                "(kc p) o -> p kc o", p=128)
            dst = wsl[:, (si % 2) * WSLAB:(si % 2) * WSLAB +
                      slab_elems].rearrange("p (kc o) -> p kc o", kc=kch)
            g.dma_start(dst, wr).then_inc(cx.sem("gs"), 16)
        return None
    cx.on("g", gl)
    cx.bump("gs", 16 * nslab)

    pairs = []
    for si in range(nslab):
        for ol in range(och_per):
            for qc in range(nqc):
                pairs.append((si, ol, qc))

    def tf(t):
        for i, (si, ol, qc) in enumerate(pairs):
            pp = p[banks[i % 2]]
            if i >= 2:
                t.wait_ge(cx.sem("pd"), pd0 + i - 1)
            if ol == 0 and qc == 0:
                t.wait_ge(cx.sem("gs"), gs0 + 16 * (si + 1))
            sl = slice(qc * qcw, (qc + 1) * qcw)
            base = (si % 2) * WSLAB + ol * 128
            last = None
            for k in range(kch):
                last = t.matmul(pp[:, 0:qcw], wsl[:, base + k * slw:
                                                  base + k * slw + 128],
                                xn_of(k)[:, sl], start=(k == 0),
                                stop=(k == kch - 1))
            last.then_inc(cx.sem("pc"), 1)
        cx.bump("pc", len(pairs))
        return None
    cx.on("t", tf)

    ename = "v" if evac_v is not None else "s"
    evac = evac_v if evac_v is not None else evac_s

    def ef(e):
        for i, (si, ol, qc) in enumerate(pairs):
            e.wait_ge(cx.sem("pc"), pc0 + i + 1)
            o = si * och_per + ol
            last = evac(e, o, qc, p[banks[i % 2]])
            last.then_inc(cx.sem("pd"), 1)
            if ol == och_per - 1 and qc == nqc - 1:
                if ename == "v":
                    e.engine_nop().then_inc(cx.sem("pf"), 1)
                else:
                    e.nop().then_inc(cx.sem("pf"), 1)
        cx.bump("pd", len(pairs))
        cx.bump("pf", nslab)
        return None
    cx.on(ename, ef)
    cx.seal(ename, "pd")
    cx.barrier()


L2F_STOP = 99


def emit_l2_ffn(cx, W):
    b = cx.buf
    xs, xnf = b["xs"], b["xnf"]
    ar1, ar2, ar3 = b["ar1"], b["ar2"], b["ar3"]
    vrow, tsc = b["vrow"], b["tsc"]

    def xs_of(ch):
        return xs[:, ch * TOK:(ch + 1) * TOK]

    def xnf_of(ch):
        return xnf[:, ch * TOK:(ch + 1) * TOK]

    def ar1_of(k):
        return ar1[:, k * TOK:(k + 1) * TOK]

    def ar2_of(k):
        return ar2[:, k * TOK:(k + 1) * TOK]

    cx.dma("q", vrow[:, 0:DCH], W["l2f_mu_r"][:, :])
    cx.dma("q", vrow[:, DCH:2 * DCH], W["l2f_mu_k"][:, :])
    pw = cx.dma("q", vrow[:, 60:64], W["pvsel"][:, :])
    cx.wait("v", "ds", pw)

    emit_norm(cx, xs_of, xnf_of, kind="ln")
    if L2F_STOP <= 1:
        return

    ci = W["xg_in"]
    ci3a = ci[0:D, :].rearrange("(c p) t -> p c t", p=128)
    xnf3 = xnf.rearrange("p (c t) -> p c t", c=DCH)
    cx.dma("q", ci3a, xnf3)
    ci3b = ci[D:2 * D, 1:TOK].rearrange("(c p) t -> p c t", p=128)
    xnf3s = xnf.rearrange("p (c t) -> p c t", c=DCH)[:, :, 0:TOK - 1]
    cx.dma("q", ci3b, xnf3s)
    zc = ci[D:2 * D, 0:1].rearrange("(c p) t -> p (c t)", p=128)
    cw = cx.dma("q", zc, b["zrow"][:, 0:DCH], slow=True)
    cx.wait("g", "ds", cw)

    def gco(g):
        g.collective_compute(
            "AllGather", ALU.bypass, replica_groups=RG,
            ins=[W["xg_in"].ap().opt()], outs=[W["xg_out"].ap().opt()],
        ).then_inc(cx.sem("cc"), 1)
        cx.bump("cc")
        return None
    cx.on("g", gco)
    ccv = cx.cnt["cc"]
    if L2F_STOP <= 2:
        return

    iw = cx.cnt["ds"]
    pb0 = cx.cnt["pb"]
    pa_sel0 = cx.cnt["pa"]

    def gprev(g):
        g.wait_ge(cx.sem("cc"), ccv)
        G = W["xg_out"]
        for ch in range(DCH):
            for i in range(4):
                n = ch * 4 + i
                if n >= 1:
                    g.wait_ge(cx.sem("gs"), gs0 + 16 * n)
                if ch >= 1 and i == 0:
                    g.wait_ge(cx.sem("pa"), pa_sel0 + ch * 12)
                ro = i * 2 * D + (D if i == 3 else 0) + ch * 128
                g.dma_start(
                    ar3[:, 8192 + i * 2048:8192 + (i + 1) * 2048],
                    G[ro:ro + 128, :],
                ).then_inc(cx.sem("gs"), 16)
        return None
    gs0 = cx.cnt["gs"]
    cx.on("g", gprev)
    cx.bump("gs", 16 * DCH * 4)

    def vmix(v):
        last = None
        for ch in range(DCH):
            v.wait_ge(cx.sem("gs"), gs0 + 16 * (ch * 4 + 4))
            xp = ar3[:, (ch % 2) * 2048:(ch % 2) * 2048 + 2048]
            v.tensor_scalar(xp, ar3[:, 8192:10240], vrow[:, 60:61],
                            None, ALU.mult).then_inc(cx.sem("pa"), 1)
            cx.bump("pa")
            v.wait_ge(cx.sem("pa"), cx.cnt["pa"])
            for i in range(1, 4):
                v.scalar_tensor_tensor(
                    xp, ar3[:, 8192 + i * 2048:8192 + (i + 1) * 2048],
                    vrow[:, 60 + i:61 + i], xp, ALU.mult,
                    ALU.add).then_inc(cx.sem("pa"), 1)
                cx.bump("pa")
                v.wait_ge(cx.sem("pa"), cx.cnt["pa"])
            cur = xnf_of(ch)
            for qc in range(NQC):
                sl = slice(qc * QCOL, (qc + 1) * QCOL)
                dbuf = tsc[:, 0:QCOL]
                v.tensor_sub(dbuf, cur[:, sl], xp[:, sl]).then_inc(
                    cx.sem("pa"), 1)
                cx.bump("pa")
                v.wait_ge(cx.sem("pa"), cx.cnt["pa"])
                v.scalar_tensor_tensor(ar1[:, ch * TOK + qc * QCOL:
                                           ch * TOK + (qc + 1) * QCOL],
                                       dbuf, vrow[:, ch:ch + 1], xp[:, sl],
                                       ALU.mult, ALU.add)
                last = v.scalar_tensor_tensor(
                    ar2[:, ch * TOK + qc * QCOL: ch * TOK + (qc + 1) * QCOL],
                    dbuf, vrow[:, DCH + ch:DCH + ch + 1], xp[:, sl],
                    ALU.mult, ALU.add)
                last.then_inc(cx.sem("pa"), 1)
                cx.bump("pa")
                v.wait_ge(cx.sem("pa"), cx.cnt["pa"])
        return None
    cx.on("v", vmix)
    cx.seal("v", "pa")
    cx.barrier()
    if L2F_STOP <= 3:
        return

    def ar3_of(k):
        return ar3[:, k * TOK:(k + 1) * TOK]

    def ev_relu(s, o, qc, pp):
        return s.activation(ar3[:, o * TOK + qc * QCOL:
                                o * TOK + (qc + 1) * QCOL], pp[:, :],
                            AF.Relu)
    emit_proj(cx, W["l2f_wk_t"], ar2_of, D, D, evac_s=ev_relu)
    if L2F_STOP <= 4:
        return

    pdv = cx.cnt["pd"]

    def ssq(s):
        s.wait_ge(cx.sem("pd"), pdv)
        last = None
        for ch in range(DCH):
            last = s.activation(ar3[:, ch * TOK:(ch + 1) * TOK],
                                ar3[:, ch * TOK:(ch + 1) * TOK], AF.Square)
        return last
    cx.on("s", ssq)
    cx.barrier()
    if L2F_STOP <= 5:
        return

    def ev_copy(v, o, qc, pp):
        return v.tensor_copy(ar2[:, o * TOK + qc * QCOL:
                                 o * TOK + (qc + 1) * QCOL], pp[:, :])
    emit_proj(cx, W["l2f_wv_t"], ar3_of, D, D, evac_v=ev_copy)
    if L2F_STOP <= 6:
        return

    def ev_sig(s, o, qc, pp):
        return s.activation(ar3[:, o * TOK + qc * QCOL:
                                o * TOK + (qc + 1) * QCOL], pp[:, :],
                            AF.Sigmoid)
    emit_proj(cx, W["l2f_wr_t"], ar1_of, D, D, evac_s=ev_sig)
    if L2F_STOP <= 7:
        return

    def vfin(v):
        last = None
        for ch in range(DCH):
            for qc in range(NQC):
                off = ch * TOK + qc * QCOL
                v.tensor_tensor(tsc[:, 0:QCOL], ar3[:, off:off + QCOL],
                                ar2[:, off:off + QCOL],
                                ALU.mult).then_inc(cx.sem("pa"), 1)
                cx.bump("pa")
                v.wait_ge(cx.sem("pa"), cx.cnt["pa"])
                last = v.scalar_tensor_tensor(xs[:, off:off + QCOL],
                                              tsc[:, 0:QCOL], 0.5,
                                              xs[:, off:off + QCOL],
                                              ALU.mult, ALU.add)
                last.then_inc(cx.sem("pa"), 1)
                cx.bump("pa")
                v.wait_ge(cx.sem("pa"), cx.cnt["pa"])
        return None
    cx.on("v", vfin)
    cx.seal("v", "pa")
    cx.barrier()


def emit_fm_to_tokmajor(cx, src_fm_of, dst_dram, n_feat, ident_bf):
    b = cx.buf
    p = b["p"]
    och = n_feat // 128
    tch = TOK // 128
    stg = b["tsc"].bitcast(BF16)
    pc0 = cx.cnt["pc"]
    pd0 = cx.cnt["pd"]
    pe0 = cx.cnt["pe_"]
    ds0 = cx.cnt["ds"]
    tiles = [(o, tc) for o in range(och) for tc in range(tch)]

    def tf(t):
        for i, (o, tc) in enumerate(tiles):
            pp = p[6 + i % 2].bitcast(BF16)[:, 0:128]
            if i >= 2:
                t.wait_ge(cx.sem("pd"), pd0 + i - 1)
            t.transpose(pp, src_fm_of(o)[:, tc * 128:(tc + 1) * 128],
                        ident_bf).then_inc(cx.sem("pc"), 1)
        cx.bump("pc", len(tiles))
        return None
    cx.on("t", tf)

    def vf(v):
        for i, (o, tc) in enumerate(tiles):
            v.wait_ge(cx.sem("pc"), pc0 + i + 1)
            if i >= 2:
                v.wait_ge(cx.sem("pe_"), pe0 + i - 1)
            v.tensor_copy(stg[:, (i % 2) * 512:(i % 2) * 512 + 128],
                          p[6 + i % 2].bitcast(BF16)[:, 0:128]).then_inc(
                cx.sem("pd"), 1)
        cx.bump("pd", len(tiles))
        return None
    cx.on("v", vf)

    def qf(q):
        for i, (o, tc) in enumerate(tiles):
            q.wait_ge(cx.sem("pd"), pd0 + i + 1)
            q.dma_start(dst_dram[tc * 128:(tc + 1) * 128,
                                 o * 128:(o + 1) * 128],
                        stg[:, (i % 2) * 512:(i % 2) * 512 + 128]).then_inc(
                cx.sem("ds"), 16)
            q.wait_ge(cx.sem("ds"), ds0 + 16 * (i + 1))
            q.nop().then_inc(cx.sem("pe_"), 1)
        return None
    cx.on("q", qf)
    cx.bump("ds", 16 * len(tiles))
    cx.bump("pe_", len(tiles))
    cx.seal("q", "pe_")
    cx.barrier()


def attn_chunks_of(qb):
    out = []
    ncr = (qb + 1) * 4
    for r in range(4):
        for c in range(ncr):
            colstart = max(0, c * 128 - qb * 512)
            diag = c >= qb * 4
            out.append((r, c, colstart, diag))
    return out


def emit_attn_causal(cx, W, wqkv_name, q_off, k_off, v_off, wout_name,
                     scale, residual_scale=1.0, norm_kind="ln",
                     norm_w=None, k_builder=None, q_pre=None):
    b = cx.buf
    p = b["p"]
    xs, xnf = b["xs"], b["xnf"]
    ar1, ar2, ar3 = b["ar1"], b["ar2"], b["ar3"]
    tsc, msk = b["tsc"], b["msk"]
    ones1, onesb = b["ones1"], b["onesb"]
    NH = 8
    wqkv = W[wqkv_name]

    def xs_of(ch):
        return xs[:, ch * TOK:(ch + 1) * TOK]

    def xnf_of(ch):
        return xnf[:, ch * TOK:(ch + 1) * TOK]

    ident = msk[:, 512:640]

    def gid(g):
        g.memset(msk[:, 640:768], 1.0).then_inc(cx.sem("ga"), 1)
        cx.bump("ga")
        g.wait_ge(cx.sem("ga"), cx.cnt["ga"])
        return g.affine_select(ident, msk[:, 640:768], [[-1, 128]],
                               ALU.is_equal, 0.0, channel_multiplier=1)
    cx.on("g", gid)
    cx.dma("g", msk[:, 0:512].rearrange("p (r f) -> p r f", r=4),
           W["trimask"].rearrange("r p f -> p r f"))
    cx.barrier()

    emit_norm(cx, xs_of, xnf_of, kind=norm_kind, w_slice=norm_w)

    if k_builder is None:
        def ev_cp_ar1(v, o, qc, pp):
            return v.tensor_copy(ar1[:, o * TOK + qc * QCOL:
                                     o * TOK + (qc + 1) * QCOL], pp[:, :])
        emit_proj(cx, wqkv, xnf_of, D, D, evac_v=ev_cp_ar1, col_off=k_off)
    else:
        k_builder()

    def ev_cp_ar2(v, o, qc, pp):
        return v.tensor_copy(ar2[:, o * TOK + qc * QCOL:
                                 o * TOK + (qc + 1) * QCOL], pp[:, :])
    emit_proj(cx, wqkv, xnf_of, D, D, evac_v=ev_cp_ar2, col_off=v_off)

    pe0 = cx.cnt["pe_"]
    pc0 = cx.cnt["pc"]
    pd0 = cx.cnt["pd"]

    def sqk(s):
        for hp in range(NH):
            if hp >= 1:
                s.wait_ge(cx.sem("pc"), pc0 + hp * 2 * NQC)
            s.activation(ar3[:, 0:TOK], ar1[:, hp * TOK:(hp + 1) * TOK],
                         AF.Square).then_inc(cx.sem("pe_"), 1)
        cx.bump("pe_", NH)
        return None
    cx.on("s", sqk)

    def tkn(t):
        for hp in range(NH):
            t.wait_ge(cx.sem("pe_"), pe0 + hp + 1)
            for hl in range(2):
                for qc in range(NQC):
                    i = (hp * 2 + hl) * NQC + qc
                    if i >= 2:
                        t.wait_ge(cx.sem("pd"), pd0 + i - 1)
                    t.matmul(p[4 + i % 2][0:1, :],
                             onesb[hl * 64:hl * 64 + 64, 0:1],
                             ar3[hl * 64:hl * 64 + 64,
                                 qc * QCOL:(qc + 1) * QCOL],
                             start=True, stop=True,
                             tile_position=(hl * 64, 0)).then_inc(
                        cx.sem("pc"), 1)
        cx.bump("pc", NH * 2 * NQC)
        return None
    cx.on("t", tkn)

    def vkn(v):
        v.memset(tsc[0:1, 32:48], 0.0).then_inc(cx.sem("pa"), 1)
        cx.bump("pa")
        v.wait_ge(cx.sem("pa"), cx.cnt["pa"])
        for hp in range(NH):
            for hl in range(2):
                for qc in range(NQC):
                    i = (hp * 2 + hl) * NQC + qc
                    v.wait_ge(cx.sem("pc"), pc0 + i + 1)
                    v.reduce_max(tsc[0:1, 48:49], p[4 + i % 2][0:1, :],
                                 axis=AX.X).then_inc(cx.sem("pa"), 1)
                    cx.bump("pa")
                    v.wait_ge(cx.sem("pa"), cx.cnt["pa"])
                    h = hp * 2 + hl
                    v.tensor_tensor(tsc[0:1, 32 + h:33 + h],
                                    tsc[0:1, 32 + h:33 + h],
                                    tsc[0:1, 48:49],
                                    ALU.max).then_inc(cx.sem("pd"), 1)
                    cx.bump("pd")
                    v.wait_ge(cx.sem("pd"), cx.cnt["pd"])
        v.tensor_scalar_mul(tsc[0:1, 32:48], tsc[0:1, 32:48],
                            scale * scale).then_inc(cx.sem("pa"), 1)
        cx.bump("pa")
        return None
    cx.on("v", vkn)
    cx.seal("v", "pa")
    cx.barrier()

    cx.dma("q", W["xgk_in"].rearrange("(c p) t -> p c t", p=128),
           ar1.rearrange("p (c t) -> p c t", c=DCH))
    kmw = cx.dma("q", W["kmx_in"][0:1, 0:16], tsc[0:1, 32:48])
    cx.wait("v", "ds", kmw)
    emit_fm_to_tokmajor(cx, lambda o: ar2[:, o * TOK:(o + 1) * TOK],
                        W["xgv_in"], D, ident)

    if q_pre is not None:
        q_pre()

    def gcoll(g):
        g.wait_ge(cx.sem("ds"), cx.cnt["ds"])
        for kind, op, a, o in (("AllGather", ALU.bypass, "xgk_in", "xgk_out"),
                               ("AllGather", ALU.bypass, "xgv_in", "xgv_out"),
                               ("AllReduce", ALU.max, "kmx_in", "kmx_out")):
            g.collective_compute(kind, op, replica_groups=RG,
                                 ins=[W[a].ap().opt()],
                                 outs=[W[o].ap().opt()]).then_inc(
                cx.sem("cc"), 1)
            cx.bump("cc")
        return None
    cx.on("g", gcoll)
    cx.wait("q", "cc", cx.cnt["cc"])
    kw = cx.dma("q", tsc[0:1, 32:48], W["kmx_out"][0:1, 0:16])
    cx.wait("q", "ds", kw)
    cx.wait("v", "ds", kw)
    cx.barrier()

    GK, GV = W["xgk_out"], W["xgv_out"]
    wout = W[wout_name]
    NIDX = NQB * NH
    chunks = {qb: attn_chunks_of(qb) for qb in range(NQB)}

    gq0 = cx.cnt["gs"]
    pf_qproj = {}
    pf_qnorm = {}
    pf_bias = {}
    ph_sq = {}
    sa_qevac = {}
    ab_negB = {}
    ab_biascopy = {}
    as_scores = {}
    ax_exp = {}
    am_mask = {}
    av_done = {}
    ay_recip = {}
    ay_yevac = {}
    ds_kv = {}
    ds_rbc = {}
    gs_qslab = {}
    gs_wout = {}
    op_mm = {}
    op_evac = {}

    pf = cx.cnt["pf"]
    ph = cx.cnt["ph"]
    sa = cx.cnt["sa"]
    ab = cx.cnt["ab"]
    asx = cx.cnt["as_"]
    ax = cx.cnt["ax"]
    am = cx.cnt["am"]
    av = cx.cnt["av"]
    ay = cx.cnt["ay"]
    ds = cx.cnt["ds"]
    gs = gq0
    pb = cx.cnt["pb"]
    pg = cx.cnt["pg"]
    for qb in range(NQB):
        cl = chunks[qb]
        for hp in range(NH):
            idx = qb * NH + hp
            gs += 16
            gs_qslab[idx] = gs
            pf += 1
            pf_qproj[idx] = pf
            pf += 1
            pf_qnorm[idx] = pf
            pf += 1
            pf_bias[idx] = pf
            sa += 1
            sa_qevac[idx] = sa
            ph += 1
            ph_sq[idx] = ph
            ab += 3
            ab_negB[idx] = ab
            ab += 1
            ab_biascopy[idx] = ab
            ds += 16 * 8
            ds_kv[idx] = ds
            ds += 16 * 4
            ds_rbc[idx] = ds
            for ck in range(len(cl)):
                asx += 1
                as_scores[(idx, ck)] = asx
                ax += 1
                ax_exp[(idx, ck)] = ax
                if cl[ck][3]:
                    am += 1
                    am_mask[(idx, ck)] = am
                av += 1
                av_done[(idx, ck)] = av
            ay += 1
            ay_recip[idx] = ay
            ay += 1
            ay_yevac[idx] = ay
        for si in range(4):
            gs += 16
            gs_wout[(qb, si)] = gs
        for o in range(DCH):
            pb += 1
            op_mm[(qb, o)] = pb
            pg += 1
            op_evac[(qb, o)] = pg
    cx.cnt["pf"] = pf
    cx.cnt["ph"] = ph
    cx.cnt["sa"] = sa
    cx.cnt["ab"] = ab
    cx.cnt["as_"] = asx
    cx.cnt["ax"] = ax
    cx.cnt["am"] = am
    cx.cnt["av"] = av
    cx.cnt["ay"] = ay
    cx.cnt["ds"] = ds
    cx.cnt["gs"] = gs
    cx.cnt["pb"] = pb
    cx.cnt["pg"] = pg

    QT = 0
    PT = 1024
    RB = 3072
    YS = 4096
    QS = 8192
    SQ = 10240

    def gq(g):
        prev_gs = gq0
        for qb in range(NQB):
            for hp in range(NH):
                idx = qb * NH + hp
                g.wait_ge(cx.sem("gs"), prev_gs)
                if idx >= 2:
                    g.wait_ge(cx.sem("pf"), pf_qproj[idx - 2])
                half = idx % 2
                wr = wqkv[0:D, q_off + hp * 128:
                          q_off + (hp + 1) * 128].rearrange(
                    "(kc p) o -> p kc o", p=128)
                g.dma_start(ar3[:, QS + half * 1024:QS + half * 1024 +
                                1024].rearrange("p (kc o) -> p kc o", kc=8),
                            wr).then_inc(cx.sem("gs"), 16)
                prev_gs = gs_qslab[idx]
            for si in range(4):
                g.wait_ge(cx.sem("gs"), prev_gs)
                if si >= 2:
                    g.wait_ge(cx.sem("pb"), op_mm[(qb, (si - 2) * 2 + 1)])
                elif qb >= 1:
                    g.wait_ge(cx.sem("pb"), op_mm[(qb - 1, (si + 2) * 2 + 1)])
                wr = wout[0:D, si * 256:(si + 1) * 256].rearrange(
                    "(kc p) o -> p kc o", p=128)
                g.dma_start(b["wsl"][:, (si % 2) * WSLAB:
                                     (si % 2) * WSLAB + 2048].rearrange(
                    "p (kc o) -> p kc o", kc=8), wr).then_inc(
                    cx.sem("gs"), 16)
                prev_gs = gs_wout[(qb, si)]
        return None
    cx.on("g", gq)

    def qkv(q):
        for idx in range(NIDX):
            qb, hp = divmod(idx, NH)
            ncr = (qb + 1) * 4
            half = idx % 2
            if idx >= 2:
                q.wait_ge(cx.sem("ay"), ay_yevac[idx - 2])
            for r in range(4):
                q.dma_start(
                    ar1[:, half * 8192 + r * ncr * 128:
                        half * 8192 + (r + 1) * ncr * 128],
                    GK[r * D + hp * 128:r * D + (hp + 1) * 128,
                       0:ncr * 128]).then_inc(cx.sem("ds"), 16)
                src3 = GV[r * TOK:r * TOK + ncr * 128,
                          hp * 128:(hp + 1) * 128].rearrange(
                    "(c p) f -> p c f", p=128)
                dst3 = ar2[:, half * 8192 + r * ncr * 128:
                           half * 8192 + (r + 1) * ncr * 128].rearrange(
                    "p (c f) -> p c f", f=128)
                q.dma_start(dst3, src3).then_inc(cx.sem("ds"), 16)
            q.wait_ge(cx.sem("ay"), ay_recip[idx])
            q.dma_start(W["rbc"][0:1, :], tsc[32:33, 0:512]).then_inc(
                cx.sem("ds"), 16)
            q.wait_ge(cx.sem("ds"), ds_kv[idx] + 16)
            q.dma_start(W["rbc"][1:2, :], tsc[64:65, 0:512]).then_inc(
                cx.sem("ds"), 16)
            q.wait_ge(cx.sem("ds"), ds_kv[idx] + 32)
            q.dma_start(ar3[0:64, RB:RB + 1024].bitcast(F32),
                        W["rbc"][0:1, :].broadcast_to([64, 512])).then_inc(
                cx.sem("ds"), 16)
            q.wait_ge(cx.sem("ds"), ds_kv[idx] + 48)
            q.dma_start(ar3[64:128, RB:RB + 1024].bitcast(F32),
                        W["rbc"][1:2, :].broadcast_to([64, 512])).then_inc(
                cx.sem("ds"), 16)
            q.wait_ge(cx.sem("ds"), ds_rbc[idx])
        return None
    cx.on("q", qkv)

    def tmain(t):
        for idx in range(NIDX):
            qb, hp = divmod(idx, NH)
            cl = chunks[qb]
            nkc = len(cl)
            ncr = (qb + 1) * 4
            half = idx % 2
            t.wait_ge(cx.sem("gs"), gs_qslab[idx])
            if idx >= 1:
                t.wait_ge(cx.sem("ab"), ab_biascopy[idx - 1])
                pidx = idx - 1
                t.wait_ge(cx.sem("ax"),
                          ax_exp[(pidx, len(chunks[pidx // NH]) - 1)])
            qsl = ar3[:, QS + half * 1024:QS + half * 1024 + 1024]
            last = None
            for k in range(DCH):
                last = t.matmul(p[7][:, 0:512],
                                qsl[:, k * 128:(k + 1) * 128],
                                xnf_of(k)[:, qb * 512:(qb + 1) * 512],
                                start=(k == 0), stop=(k == DCH - 1))
            last.then_inc(cx.sem("pf"), 1)
            t.wait_ge(cx.sem("ph"), ph_sq[idx])
            if idx >= 1:
                t.wait_ge(cx.sem("ay"), ay_recip[idx - 1])
            sq = ar3[:, SQ:SQ + 512]
            t.matmul(p[5][0:1, 0:512], onesb[0:64, 0:1], sq[0:64, 0:512],
                     start=True, stop=True, tile_position=(0, 0))
            t.matmul(p[6][0:1, 0:512], onesb[64:128, 0:1],
                     sq[64:128, 0:512], start=True, stop=True,
                     tile_position=(64, 0)).then_inc(cx.sem("pf"), 1)
            t.wait_ge(cx.sem("sa"), sa_qevac[idx])
            t.wait_ge(cx.sem("ab"), ab_negB[idx])
            t.matmul(p[7][:, 0:1], ones1[0:1, :], tsc[0:1, 56:57],
                     start=True, stop=True)
            t.matmul(p[7][:, 1:2], ones1[0:1, :], tsc[0:1, 57:58],
                     start=True, stop=True).then_inc(cx.sem("pf"), 1)
            t.wait_ge(cx.sem("ds"), ds_kv[idx])
            if idx >= 1:
                t.wait_ge(cx.sem("ay"), ay_yevac[idx - 1])
            if hp == 0 and qb >= 1:
                t.wait_ge(cx.sem("pg"), op_evac[(qb - 1, DCH - 1)])
            qt = ar3[:, QT + half * 512:QT + half * 512 + 512]
            for ck, (r, c, cs, diag) in enumerate(cl):
                koff = half * 8192 + (r * ncr + c) * 128
                bA = p[0] if ck % 2 == 0 else p[4]
                bB = p[1] if ck % 2 == 0 else p[7]
                if ck >= 2:
                    t.wait_ge(cx.sem("ax"), ax_exp[(idx, ck - 2)])
                if ck == 1:
                    t.wait_ge(cx.sem("ab"), ab_biascopy[idx])
                t.matmul(bA[:, cs:512], ar1[0:64, koff:koff + 128],
                         qt[0:64, cs:512], start=True, stop=True,
                         tile_position=(0, 0))
                t.matmul(bB[:, cs:512], ar1[64:128, koff:koff + 128],
                         qt[64:128, cs:512], start=True, stop=True,
                         tile_position=(64, 0)).then_inc(cx.sem("as_"), 1)
                if diag:
                    t.wait_ge(cx.sem("am"), am_mask[(idx, ck)])
                else:
                    t.wait_ge(cx.sem("ax"), ax_exp[(idx, ck)])
                ptA = ar3[:, PT + (ck % 2) * 512:PT + (ck % 2) * 512 + 512]
                ptB = ar3[:, PT + 1024 + (ck % 2) * 512:
                          PT + 1024 + (ck % 2) * 512 + 512]
                first = (ck == 0)
                lastc = (ck == nkc - 1)
                t.matmul(p[2][0:64, cs:512], ar2[:, koff:koff + 64],
                         ptA[:, cs:512], start=first, stop=lastc,
                         tile_position=(0, 0))
                t.matmul(p[3][64:128, cs:512], ar2[:, koff + 64:koff + 128],
                         ptB[:, cs:512], start=first, stop=lastc,
                         tile_position=(0, 64))
                t.matmul(p[5][0:1, cs:512], onesb[:, 0:1], ptA[:, cs:512],
                         start=first, stop=lastc)
                t.matmul(p[6][0:1, cs:512], onesb[:, 1:2], ptB[:, cs:512],
                         start=first, stop=lastc).then_inc(cx.sem("av"), 1)
            if hp == NH - 1:
                t.wait_ge(cx.sem("ay"), ay_yevac[idx])
                for o in range(DCH):
                    si = o // 2
                    t.wait_ge(cx.sem("gs"), gs_wout[(qb, si)])
                    if o >= 2:
                        t.wait_ge(cx.sem("pg"), op_evac[(qb, o - 2)])
                    pp = p[o % 2]
                    lastm = None
                    for k in range(DCH):
                        base = (si % 2) * WSLAB + k * 256 + (o % 2) * 128
                        wt = b["wsl"][:, base:base + 128]
                        lastm = t.matmul(pp[:, 0:512], wt,
                                         ar3[:, YS + k * 512:
                                             YS + (k + 1) * 512],
                                         start=(k == 0), stop=(k == DCH - 1))
                    lastm.then_inc(cx.sem("pb"), 1)
        return None
    cx.on("t", tmain)

    def smain(s):
        for idx in range(NIDX):
            qb, hp = divmod(idx, NH)
            cl = chunks[qb]
            half = idx % 2
            s.wait_ge(cx.sem("pf"), pf_qproj[idx])
            if idx >= 2:
                pidx = idx - 2
                s.wait_ge(cx.sem("as_"),
                          as_scores[(pidx, len(chunks[pidx // NH]) - 1)])
            qt = ar3[:, QT + half * 512:QT + half * 512 + 512]
            s.activation(qt, p[7][:, 0:512], AF.Copy,
                         scale=scale).then_inc(cx.sem("sa"), 1)
            if idx >= 1:
                s.wait_ge(cx.sem("pf"), pf_qnorm[idx - 1])
            s.wait_ge(cx.sem("sa"), sa_qevac[idx])
            s.activation(ar3[:, SQ:SQ + 512], qt,
                         AF.Square).then_inc(cx.sem("ph"), 1)
            s.wait_ge(cx.sem("ab"), ab_biascopy[idx])
            for ck, (r, c, cs, diag) in enumerate(cl):
                s.wait_ge(cx.sem("as_"), as_scores[(idx, ck)])
                if ck >= 2:
                    s.wait_ge(cx.sem("av"), av_done[(idx, ck - 2)])
                ptA = ar3[:, PT + (ck % 2) * 512:PT + (ck % 2) * 512 + 512]
                ptB = ar3[:, PT + 1024 + (ck % 2) * 512:
                          PT + 1024 + (ck % 2) * 512 + 512]
                s.activation(ptA[:, cs:512],
                             (p[0] if ck % 2 == 0 else p[4])[:, cs:512],
                             AF.Exp, bias=tsc[:, 0:1])
                s.activation(ptB[:, cs:512],
                             (p[1] if ck % 2 == 0 else p[7])[:, cs:512],
                             AF.Exp, bias=tsc[:, 1:2]).then_inc(
                    cx.sem("ax"), 1)
        return None
    cx.on("s", smain)

    def vmain(v):
        for idx in range(NIDX):
            qb, hp = divmod(idx, NH)
            cl = chunks[qb]
            nkc = len(cl)
            half = idx % 2
            v.wait_ge(cx.sem("pf"), pf_qnorm[idx])
            v.reduce_max(tsc[0:1, 52:53], p[5][0:1, 0:512], axis=AX.X)
            v.reduce_max(tsc[0:1, 53:54], p[6][0:1, 0:512],
                         axis=AX.X).then_inc(cx.sem("ab"), 1)
            v.wait_ge(cx.sem("ab"), ab_negB[idx] - 2)
            v.tensor_add(tsc[0:1, 54:56], tsc[0:1, 52:54],
                         tsc[0:1, 32 + 2 * hp:34 + 2 * hp]).then_inc(
                cx.sem("ab"), 1)
            v.wait_ge(cx.sem("ab"), ab_negB[idx] - 1)
            v.tensor_scalar_mul(tsc[0:1, 56:58], tsc[0:1, 54:56],
                                -0.5).then_inc(cx.sem("ab"), 1)
            v.wait_ge(cx.sem("pf"), pf_bias[idx])
            v.tensor_copy(tsc[:, 0:2], p[7][:, 0:2]).then_inc(
                cx.sem("ab"), 1)
            for ck, (r, c, cs, diag) in enumerate(cl):
                if not diag:
                    continue
                v.wait_ge(cx.sem("ax"), ax_exp[(idx, ck)])
                ptA = ar3[:, PT + (ck % 2) * 512:PT + (ck % 2) * 512 + 512]
                ptB = ar3[:, PT + 1024 + (ck % 2) * 512:
                          PT + 1024 + (ck % 2) * 512 + 512]
                we = min(cs + 128, 512)
                v.tensor_mul(ptA[:, cs:we], ptA[:, cs:we],
                             msk[:, r * 128:r * 128 + (we - cs)])
                v.tensor_mul(ptB[:, cs:we], ptB[:, cs:we],
                             msk[:, r * 128:r * 128 + (we - cs)]).then_inc(
                    cx.sem("am"), 1)
            v.wait_ge(cx.sem("av"), av_done[(idx, nkc - 1)])
            v.reciprocal(tsc[32:33, 0:512], p[5][0:1, 0:512])
            v.reciprocal(tsc[64:65, 0:512], p[6][0:1, 0:512]).then_inc(
                cx.sem("ay"), 1)
            v.wait_ge(cx.sem("ds"), ds_rbc[idx])
            if hp == 0 and qb >= 1:
                v.wait_ge(cx.sem("pb"), op_mm[(qb - 1, DCH - 1)])
            ys = ar3[:, YS + hp * 512:YS + (hp + 1) * 512]
            v.tensor_tensor(ys[0:64, :], p[2][0:64, 0:512],
                            ar3[0:64, RB:RB + 1024].bitcast(F32), ALU.mult)
            v.tensor_tensor(ys[64:128, :], p[3][64:128, 0:512],
                            ar3[64:128, RB:RB + 1024].bitcast(F32),
                            ALU.mult).then_inc(cx.sem("ay"), 1)
            if hp == NH - 1:
                for o in range(DCH):
                    v.wait_ge(cx.sem("pb"), op_mm[(qb, o)])
                    xso = xs[:, o * TOK + qb * 512:o * TOK + (qb + 1) * 512]
                    v.scalar_tensor_tensor(xso, p[o % 2][:, 0:512],
                                           residual_scale, xso, ALU.mult,
                                           ALU.add).then_inc(cx.sem("pg"), 1)
        return None
    cx.on("v", vmain)
    cx.seal("v", "pg")
    cx.barrier()


def emit_tokproj_rope_fm(cx, W, w_name, col_off, dst_fm_of, ropetab_name,
                         scale=1.0, tab4=False):
    b = cx.buf
    p = b["p"]
    xnf = b["xnf"]
    wsl = b["wsl"]
    ar3 = b["ar3"]
    ident = b["msk"][:, 512:640]
    tch = TOK // 128
    TBW = 2048 if tab4 else 1024
    RT, ST, SC = 0, 2 * TBW, 2 * TBW + 1024

    def xnf_of(ch):
        return xnf[:, ch * TOK:(ch + 1) * TOK]

    wk_slab = b["ar2"]
    gs0 = cx.cnt["gs"]
    wr = W[w_name][0:D, col_off:col_off + D].rearrange(
        "(kc p) o -> p kc o", p=128)
    cx.dma("g", wk_slab[:, 0:8192].rearrange("p (kc o) -> p kc o", kc=DCH),
           wr)

    gs_tab = {}
    pc_proj = {}
    pd_evac = {}
    pa_rope = {}
    ph_tp = {}
    pe_fmev = {}
    gs = cx.cnt["gs"]
    pc = cx.cnt["pc"]
    pd = cx.cnt["pd"]
    pa = cx.cnt["pa"]
    ph = cx.cnt["ph"]
    pe = cx.cnt["pe_"]
    for tc in range(tch):
        gs += 16
        gs_tab[tc] = gs
        for nh in range(2):
            pc += 1
            pc_proj[(tc, nh)] = pc
            pd += 1
            pd_evac[(tc, nh)] = pd
        pa += 7
        pa_rope[tc] = pa
        for tl in range(8):
            ph += 1
            ph_tp[(tc, tl)] = ph
            pe += 1
            pe_fmev[(tc, tl)] = pe
    cx.cnt["gs"] = gs
    cx.cnt["pc"] = pc
    cx.cnt["pd"] = pd
    cx.cnt["pa"] = pa
    cx.cnt["ph"] = ph
    cx.cnt["pe_"] = pe

    def gtab(g):
        prev = gs0 + 16
        for tc in range(tch):
            g.wait_ge(cx.sem("gs"), prev)
            if tc >= 2:
                g.wait_ge(cx.sem("pa"), pa_rope[tc - 2])
            g.dma_start(ar3[:, RT + (tc % 2) * TBW:
                            RT + (tc % 2) * TBW + TBW],
                        W[ropetab_name][tc * 128:(tc + 1) * 128,
                                        :]).then_inc(cx.sem("gs"), 16)
            prev = gs_tab[tc]
        return None
    cx.on("g", gtab)

    def tf(t):
        t.wait_ge(cx.sem("gs"), gs0 + 16)
        for tc in range(tch):
            for nh in range(2):
                if tc >= 1:
                    t.wait_ge(cx.sem("pd"), pd_evac[(tc - 1, nh)])
                pp = p[4 + nh]
                last = None
                for k in range(DCH):
                    last = t.matmul(
                        pp[:, 0:512],
                        xnf_of(k)[:, tc * 128:(tc + 1) * 128],
                        wk_slab[:, k * 1024 + nh * 512:
                                k * 1024 + nh * 512 + 512],
                        start=(k == 0), stop=(k == DCH - 1))
                last.then_inc(cx.sem("pc"), 1)
            t.wait_ge(cx.sem("pa"), pa_rope[tc])
            for tl in range(8):
                i = tc * 8 + tl
                if i >= 2:
                    t.wait_ge(cx.sem("pe_"), pe_fmev[((i - 2) // 8,
                                                      (i - 2) % 8)])
                pt = p[6 + i % 2].bitcast(BF16)[:, 0:128]
                t.transpose(pt, ar3[:, ST + tl * 128:ST + tl * 128 + 128],
                            ident).then_inc(cx.sem("ph"), 1)
        return None
    cx.on("t", tf)

    def sf(s):
        for tc in range(tch):
            for nh in range(2):
                s.wait_ge(cx.sem("pc"), pc_proj[(tc, nh)])
                if tc >= 1:
                    s.wait_ge(cx.sem("ph"), ph_tp[(tc - 1, 7)])
                s.activation(ar3[:, ST + nh * 512:ST + nh * 512 + 512],
                             p[4 + nh][:, 0:512], AF.Copy,
                             scale=scale).then_inc(cx.sem("pd"), 1)
        return None
    cx.on("s", sf)

    def vf(v):
        for tc in range(tch):
            v.wait_ge(cx.sem("pd"), pd_evac[(tc, 1)])
            v.wait_ge(cx.sem("gs"), gs_tab[tc])
            st = ar3[:, ST:ST + 1024]
            e = st.rearrange("p (h i two) -> p h i two", two=2, i=32)
            tb0 = RT + (tc % 2) * TBW
            c3 = ar3[:, tb0:tb0 + 512].rearrange("p (h i) -> p h i", i=32)
            s3 = ar3[:, tb0 + 512:tb0 + 1024].rearrange(
                "p (h i) -> p h i", i=32)
            if tab4:
                s1t = ar3[:, tb0 + 1024:tb0 + 1536].rearrange(
                    "p (h i) -> p h i", i=32)
                c1t = ar3[:, tb0 + 1536:tb0 + 2048].rearrange(
                    "p (h i) -> p h i", i=32)
            else:
                s1t, c1t = s3, c3
            sc1 = ar3[:, SC:SC + 512].rearrange("p (h i) -> p h i", i=32)
            sc2 = ar3[:, SC + 512:SC + 1024].rearrange("p (h i) -> p h i",
                                                       i=32)
            sc3 = ar3[:, SC + 1024:SC + 1536].rearrange("p (h i) -> p h i",
                                                        i=32)
            pa_base = pa_rope[tc] - 7

            def step(ins, k):
                ins.then_inc(cx.sem("pa"), 1)
                v.wait_ge(cx.sem("pa"), pa_base + k)
            step(v.tensor_tensor(sc1, e[..., 1], s3, ALU.mult), 1)
            step(v.tensor_tensor(sc2, e[..., 1], c1t, ALU.mult), 2)
            step(v.tensor_tensor(sc3, e[..., 0], s1t, ALU.mult), 3)
            step(v.tensor_tensor(e[..., 0], e[..., 0], c3, ALU.mult), 4)
            step(v.tensor_tensor(e[..., 0], e[..., 0], sc1, ALU.subtract), 5)
            step(v.tensor_tensor(e[..., 1], sc3, sc2, ALU.add), 6)
            v.engine_nop().then_inc(cx.sem("pa"), 1)
            for tl in range(8):
                v.wait_ge(cx.sem("ph"), ph_tp[(tc, tl)])
                o = tl
                v.tensor_copy(dst_fm_of(o)[:, tc * 128:(tc + 1) * 128],
                              p[6 + (tc * 8 + tl) % 2].bitcast(BF16)[:,
                                                                    0:128]
                              ).then_inc(cx.sem("pe_"), 1)
        return None
    cx.on("v", vf)
    cx.seal("v", "pe_")
    cx.barrier()
def emit_l2_attn(cx, W):
    b = cx.buf
    p = b["p"]
    xs, xnf = b["xs"], b["xnf"]
    ar1, ar2, ar3 = b["ar1"], b["ar2"], b["ar3"]
    tsc, msk = b["tsc"], b["msk"]
    ones1, onesb = b["ones1"], b["onesb"]
    NH = 8
    scale = 0.125

    def xs_of(ch):
        return xs[:, ch * TOK:(ch + 1) * TOK]

    def xnf_of(ch):
        return xnf[:, ch * TOK:(ch + 1) * TOK]

    ident = msk[:, 512:640]

    def gid(g):
        g.memset(msk[:, 640:768], 1.0).then_inc(cx.sem("ga"), 1)
        cx.bump("ga")
        g.wait_ge(cx.sem("ga"), cx.cnt["ga"])
        return g.affine_select(ident, msk[:, 640:768], [[-1, 128]],
                               ALU.is_equal, 0.0, channel_multiplier=1)
    cx.on("g", gid)
    cx.barrier()

    emit_norm(cx, xs_of, xnf_of, kind="ln")

    emit_tokproj_rope_fm(cx, W, "l2a_wk", 0,
                         lambda o: ar1[:, o * TOK:(o + 1) * TOK],
                         "ropek", scale=1.0)

    def ev_cp_ar2(v, o, qc, pp):
        return v.tensor_copy(ar2[:, o * TOK + qc * QCOL:
                                 o * TOK + (qc + 1) * QCOL], pp[:, :])
    emit_proj(cx, W["l2a_wv"], xnf_of, D, D, evac_v=ev_cp_ar2)

    pe0 = cx.cnt["pe_"]
    pc0 = cx.cnt["pc"]
    pd0 = cx.cnt["pd"]

    def sqk(s):
        for hp in range(NH):
            if hp >= 1:
                s.wait_ge(cx.sem("pc"), pc0 + hp * 2 * NQC)
            s.activation(ar3[:, 0:TOK], ar1[:, hp * TOK:(hp + 1) * TOK],
                         AF.Square).then_inc(cx.sem("pe_"), 1)
        cx.bump("pe_", NH)
        return None
    cx.on("s", sqk)

    def tkn(t):
        for hp in range(NH):
            t.wait_ge(cx.sem("pe_"), pe0 + hp + 1)
            for hl in range(2):
                for qc in range(NQC):
                    i = (hp * 2 + hl) * NQC + qc
                    if i >= 2:
                        t.wait_ge(cx.sem("pd"), pd0 + i - 1)
                    t.matmul(p[4 + i % 2][0:1, :],
                             onesb[hl * 64:hl * 64 + 64, 0:1],
                             ar3[hl * 64:hl * 64 + 64,
                                 qc * QCOL:(qc + 1) * QCOL],
                             start=True, stop=True,
                             tile_position=(hl * 64, 0)).then_inc(
                        cx.sem("pc"), 1)
        cx.bump("pc", NH * 2 * NQC)
        return None
    cx.on("t", tkn)

    def vkn(v):
        v.memset(tsc[0:1, 32:48], 0.0).then_inc(cx.sem("pa"), 1)
        cx.bump("pa")
        v.wait_ge(cx.sem("pa"), cx.cnt["pa"])
        for hp in range(NH):
            for hl in range(2):
                for qc in range(NQC):
                    i = (hp * 2 + hl) * NQC + qc
                    v.wait_ge(cx.sem("pc"), pc0 + i + 1)
                    v.reduce_max(tsc[0:1, 48:49], p[4 + i % 2][0:1, :],
                                 axis=AX.X).then_inc(cx.sem("pa"), 1)
                    cx.bump("pa")
                    v.wait_ge(cx.sem("pa"), cx.cnt["pa"])
                    h = hp * 2 + hl
                    v.tensor_tensor(tsc[0:1, 32 + h:33 + h],
                                    tsc[0:1, 32 + h:33 + h],
                                    tsc[0:1, 48:49],
                                    ALU.max).then_inc(cx.sem("pd"), 1)
                    cx.bump("pd")
                    v.wait_ge(cx.sem("pd"), cx.cnt["pd"])
        v.tensor_scalar_mul(tsc[0:1, 32:48], tsc[0:1, 32:48],
                            scale * scale).then_inc(cx.sem("pa"), 1)
        cx.bump("pa")
        return None
    cx.on("v", vkn)
    cx.seal("v", "pa")
    cx.barrier()

    cx.dma("q", W["xgk_in"].rearrange("(c p) t -> p c t", p=128),
           ar1.rearrange("p (c t) -> p c t", c=DCH))
    kmw = cx.dma("q", W["kmx_in"][0:1, 0:16], tsc[0:1, 32:48])
    cx.wait("v", "ds", kmw)
    emit_fm_to_tokmajor(cx, lambda o: ar2[:, o * TOK:(o + 1) * TOK],
                        W["xgv_in"], D, ident)

    def gcoll(g):
        g.wait_ge(cx.sem("ds"), cx.cnt["ds"])
        for kind, op, a, o in (("AllGather", ALU.bypass, "xgk_in",
                                "xgk_out"),
                               ("AllGather", ALU.bypass, "xgv_in",
                                "xgv_out"),
                               ("AllReduce", ALU.max, "kmx_in", "kmx_out")):
            g.collective_compute(kind, op, replica_groups=RG,
                                 ins=[W[a].ap().opt()],
                                 outs=[W[o].ap().opt()]).then_inc(
                cx.sem("cc"), 1)
            cx.bump("cc")
        return None
    cx.on("g", gcoll)
    cx.wait("q", "cc", cx.cnt["cc"])
    kw = cx.dma("q", tsc[0:1, 32:48], W["kmx_out"][0:1, 0:16])
    cx.wait("q", "ds", kw)
    cx.wait("v", "ds", kw)
    cx.barrier()

    GK, GV = W["xgk_out"], W["xgv_out"]
    wout = W["l2a_wo"]
    wqkv = W["l2a_wq"]
    NIDX = NQB * NH
    cl_all = [(r, c, 0, False) for r in range(4) for c in range(16)]
    nkc = len(cl_all)
    ncr = 16

    gq0 = cx.cnt["gs"]
    pf_qproj = {}
    pf_qnorm = {}
    pf_bias = {}
    ph_sq = {}
    sa_qevac = {}
    ab_negB = {}
    ab_biascopy = {}
    as_scores = {}
    ax_exp = {}
    av_done = {}
    ay_recip = {}
    ay_yevac = {}
    ds_kv = {}
    ds_rbc = {}
    gs_qslab = {}
    gs_wout = {}
    gs_qrope = {}
    op_mm = {}
    op_evac = {}
    pa_ropeq = {}
    pe_tpq = {}
    ga_qev = {}

    pf = cx.cnt["pf"]
    ph = cx.cnt["ph"]
    sa = cx.cnt["sa"]
    ab = cx.cnt["ab"]
    asx = cx.cnt["as_"]
    ax = cx.cnt["ax"]
    av = cx.cnt["av"]
    ay = cx.cnt["ay"]
    ds = cx.cnt["ds"]
    gs = gq0
    pb = cx.cnt["pb"]
    pg = cx.cnt["pg"]
    pa = cx.cnt["pa"]
    pe = cx.cnt["pe_"]
    ga = cx.cnt["ga"]
    for qb in range(NQB):
        gs += 16
        gs_qrope[qb] = gs
        for hp in range(NH):
            idx = qb * NH + hp
            gs += 16
            gs_qslab[idx] = gs
            pf += 1
            pf_qproj[idx] = pf
            sa += 1
            sa_qevac[idx] = sa
            pa += 7
            pa_ropeq[idx] = pa
            for tl in range(4):
                pe += 1
                pe_tpq[(idx, tl)] = pe
                ga += 1
                ga_qev[(idx, tl)] = ga
            ph += 1
            ph_sq[idx] = ph
            pf += 1
            pf_qnorm[idx] = pf
            ab += 3
            ab_negB[idx] = ab
            pf += 1
            pf_bias[idx] = pf
            ab += 1
            ab_biascopy[idx] = ab
            ds += 16 * 8
            ds_kv[idx] = ds
            ds += 16 * 4
            ds_rbc[idx] = ds
            for ck in range(nkc):
                asx += 1
                as_scores[(idx, ck)] = asx
                ax += 1
                ax_exp[(idx, ck)] = ax
                av += 1
                av_done[(idx, ck)] = av
            ay += 1
            ay_recip[idx] = ay
            ay += 1
            ay_yevac[idx] = ay
        for si in range(4):
            gs += 16
            gs_wout[(qb, si)] = gs
        for o in range(DCH):
            pb += 1
            op_mm[(qb, o)] = pb
            pg += 1
            op_evac[(qb, o)] = pg
    cx.cnt["pf"] = pf
    cx.cnt["ph"] = ph
    cx.cnt["sa"] = sa
    cx.cnt["ab"] = ab
    cx.cnt["as_"] = asx
    cx.cnt["ax"] = ax
    cx.cnt["av"] = av
    cx.cnt["ay"] = ay
    cx.cnt["ds"] = ds
    cx.cnt["gs"] = gs
    cx.cnt["pb"] = pb
    cx.cnt["pg"] = pg
    cx.cnt["pa"] = pa
    cx.cnt["pe_"] = pe
    cx.cnt["ga"] = ga

    QT = 0
    PT = 1024
    RB = 3072
    YS = 4096
    QS = 8192
    SQ = 10240
    QR = 10752
    QG = 11776
    SCQ = 12288

    def gq(g):
        prev_gs = gq0
        for qb in range(NQB):
            g.wait_ge(cx.sem("gs"), prev_gs)
            if qb >= 1:
                g.wait_ge(cx.sem("pa"), pa_ropeq[(qb - 1) * NH + NH - 1])
            g.dma_start(ar3[:, QR:QR + 512],
                        W["ropeq"][qb * 128:(qb + 1) * 128, :]).then_inc(
                cx.sem("gs"), 16)
            prev_gs = gs_qrope[qb]
            for hp in range(NH):
                idx = qb * NH + hp
                g.wait_ge(cx.sem("gs"), prev_gs)
                if idx >= 2:
                    g.wait_ge(cx.sem("pf"), pf_qproj[idx - 2])
                half = idx % 2
                wr = wqkv[0:D, hp * 128:(hp + 1) * 128].rearrange(
                    "(kc p) o -> p kc o", p=128)
                g.dma_start(ar3[:, QS + half * 1024:QS + half * 1024 +
                                1024].rearrange("p (kc o) -> p kc o", kc=8),
                            wr).then_inc(cx.sem("gs"), 16)
                prev_gs = gs_qslab[idx]
            for si in range(4):
                g.wait_ge(cx.sem("gs"), prev_gs)
                if si >= 2:
                    g.wait_ge(cx.sem("pb"), op_mm[(qb, (si - 2) * 2 + 1)])
                elif qb >= 1:
                    g.wait_ge(cx.sem("pb"),
                              op_mm[(qb - 1, (si + 2) * 2 + 1)])
                wr = wout[0:D, si * 256:(si + 1) * 256].rearrange(
                    "(kc p) o -> p kc o", p=128)
                g.dma_start(b["wsl"][:, (si % 2) * WSLAB:
                                     (si % 2) * WSLAB + 2048].rearrange(
                    "p (kc o) -> p kc o", kc=8), wr).then_inc(
                    cx.sem("gs"), 16)
                prev_gs = gs_wout[(qb, si)]
        return None
    cx.on("g", gq)

    def qkv(q):
        for idx in range(NIDX):
            qb, hp = divmod(idx, NH)
            half = idx % 2
            if idx >= 2:
                q.wait_ge(cx.sem("ay"), ay_yevac[idx - 2])
            for r in range(4):
                q.dma_start(
                    ar1[:, half * 8192 + r * ncr * 128:
                        half * 8192 + (r + 1) * ncr * 128],
                    GK[r * D + hp * 128:r * D + (hp + 1) * 128,
                       0:ncr * 128]).then_inc(cx.sem("ds"), 16)
                src3 = GV[r * TOK:r * TOK + ncr * 128,
                          hp * 128:(hp + 1) * 128].rearrange(
                    "(c p) f -> p c f", p=128)
                dst3 = ar2[:, half * 8192 + r * ncr * 128:
                           half * 8192 + (r + 1) * ncr * 128].rearrange(
                    "p (c f) -> p c f", f=128)
                q.dma_start(dst3, src3).then_inc(cx.sem("ds"), 16)
            q.wait_ge(cx.sem("ay"), ay_recip[idx])
            q.dma_start(W["rbc"][0:1, :], tsc[32:33, 0:512]).then_inc(
                cx.sem("ds"), 16)
            q.wait_ge(cx.sem("ds"), ds_kv[idx] + 16)
            q.dma_start(W["rbc"][1:2, :], tsc[64:65, 0:512]).then_inc(
                cx.sem("ds"), 16)
            q.wait_ge(cx.sem("ds"), ds_kv[idx] + 32)
            q.dma_start(ar3[0:64, RB:RB + 1024].bitcast(F32),
                        W["rbc"][0:1, :].broadcast_to([64, 512])).then_inc(
                cx.sem("ds"), 16)
            q.wait_ge(cx.sem("ds"), ds_kv[idx] + 48)
            q.dma_start(ar3[64:128, RB:RB + 1024].bitcast(F32),
                        W["rbc"][1:2, :].broadcast_to([64, 512])).then_inc(
                cx.sem("ds"), 16)
            q.wait_ge(cx.sem("ds"), ds_rbc[idx])
        return None
    cx.on("q", qkv)

    def tmain(t):
        for idx in range(NIDX):
            qb, hp = divmod(idx, NH)
            half = idx % 2
            t.wait_ge(cx.sem("gs"), gs_qslab[idx])
            if idx >= 1:
                t.wait_ge(cx.sem("ab"), ab_biascopy[idx - 1])
                t.wait_ge(cx.sem("ax"), ax_exp[(idx - 1, nkc - 1)])
            qsl = ar3[:, QS + half * 1024:QS + half * 1024 + 1024]
            last = None
            for tc in range(4):
                for k in range(DCH):
                    last = t.matmul(
                        p[7][:, tc * 128:(tc + 1) * 128],
                        xnf_of(k)[:, qb * 512 + tc * 128:
                                  qb * 512 + (tc + 1) * 128],
                        qsl[:, k * 128:(k + 1) * 128],
                        start=(k == 0), stop=(k == DCH - 1))
            last.then_inc(cx.sem("pf"), 1)
            t.wait_ge(cx.sem("pa"), pa_ropeq[idx])
            if idx >= 1:
                t.wait_ge(cx.sem("ay"), ay_recip[idx - 1])
            qt = ar3[:, QT + half * 512:QT + half * 512 + 512]
            for tl in range(4):
                pt = p[5 + tl % 2].bitcast(BF16)[:, 0:128]
                if tl >= 2:
                    t.wait_ge(cx.sem("ga"), ga_qev[(idx, tl - 2)])
                elif idx >= 1:
                    t.wait_ge(cx.sem("ga"), ga_qev[(idx - 1, tl + 2)])
                t.transpose(pt, ar3[:, QG + tl * 128:QG + tl * 128 + 128],
                            ident).then_inc(cx.sem("pe_"), 1)
            t.wait_ge(cx.sem("ph"), ph_sq[idx])
            sq = ar3[:, SQ:SQ + 512]
            t.matmul(p[5][0:1, 0:512], onesb[0:64, 0:1], sq[0:64, 0:512],
                     start=True, stop=True, tile_position=(0, 0))
            t.matmul(p[6][0:1, 0:512], onesb[64:128, 0:1],
                     sq[64:128, 0:512], start=True, stop=True,
                     tile_position=(64, 0)).then_inc(cx.sem("pf"), 1)
            t.wait_ge(cx.sem("sa"), sa_qevac[idx])
            t.wait_ge(cx.sem("ab"), ab_negB[idx])
            t.matmul(p[7][:, 0:1], ones1[0:1, :], tsc[0:1, 56:57],
                     start=True, stop=True)
            t.matmul(p[7][:, 1:2], ones1[0:1, :], tsc[0:1, 57:58],
                     start=True, stop=True).then_inc(cx.sem("pf"), 1)
            t.wait_ge(cx.sem("ds"), ds_kv[idx])
            if idx >= 1:
                t.wait_ge(cx.sem("ay"), ay_yevac[idx - 1])
            if hp == 0 and qb >= 1:
                t.wait_ge(cx.sem("pg"), op_evac[(qb - 1, DCH - 1)])
            for ck, (r, c, cs, diag) in enumerate(cl_all):
                koff = half * 8192 + (r * ncr + c) * 128
                bA = p[0] if ck % 2 == 0 else p[4]
                bB = p[1] if ck % 2 == 0 else p[7]
                if ck >= 2:
                    t.wait_ge(cx.sem("ax"), ax_exp[(idx, ck - 2)])
                if ck == 1:
                    t.wait_ge(cx.sem("ab"), ab_biascopy[idx])
                t.matmul(bA[:, 0:512], ar1[0:64, koff:koff + 128],
                         qt[0:64, 0:512], start=True, stop=True,
                         tile_position=(0, 0))
                t.matmul(bB[:, 0:512], ar1[64:128, koff:koff + 128],
                         qt[64:128, 0:512], start=True, stop=True,
                         tile_position=(64, 0)).then_inc(cx.sem("as_"), 1)
                t.wait_ge(cx.sem("ax"), ax_exp[(idx, ck)])
                ptA = ar3[:, PT + (ck % 2) * 512:PT + (ck % 2) * 512 + 512]
                ptB = ar3[:, PT + 1024 + (ck % 2) * 512:
                          PT + 1024 + (ck % 2) * 512 + 512]
                first = (ck == 0)
                lastc = (ck == nkc - 1)
                t.matmul(p[2][0:64, 0:512], ar2[:, koff:koff + 64],
                         ptA[:, 0:512], start=first, stop=lastc,
                         tile_position=(0, 0))
                t.matmul(p[3][64:128, 0:512], ar2[:, koff + 64:koff + 128],
                         ptB[:, 0:512], start=first, stop=lastc,
                         tile_position=(0, 64))
                t.matmul(p[5][0:1, 0:512], onesb[:, 0:1], ptA[:, 0:512],
                         start=first, stop=lastc)
                t.matmul(p[6][0:1, 0:512], onesb[:, 1:2], ptB[:, 0:512],
                         start=first, stop=lastc).then_inc(cx.sem("av"), 1)
            if hp == NH - 1:
                t.wait_ge(cx.sem("ay"), ay_yevac[idx])
                for o in range(DCH):
                    si = o // 2
                    t.wait_ge(cx.sem("gs"), gs_wout[(qb, si)])
                    if o >= 2:
                        t.wait_ge(cx.sem("pg"), op_evac[(qb, o - 2)])
                    pp = p[o % 2]
                    lastm = None
                    for k in range(DCH):
                        base = (si % 2) * WSLAB + k * 256 + (o % 2) * 128
                        wt = b["wsl"][:, base:base + 128]
                        lastm = t.matmul(pp[:, 0:512], wt,
                                         ar3[:, YS + k * 512:
                                             YS + (k + 1) * 512],
                                         start=(k == 0),
                                         stop=(k == DCH - 1))
                    lastm.then_inc(cx.sem("pb"), 1)
        return None
    cx.on("t", tmain)

    def smain(s):
        for idx in range(NIDX):
            qb, hp = divmod(idx, NH)
            half = idx % 2
            s.wait_ge(cx.sem("pf"), pf_qproj[idx])
            if idx >= 2:
                s.wait_ge(cx.sem("as_"), as_scores[(idx - 2, nkc - 1)])
            if idx >= 1:
                s.wait_ge(cx.sem("pe_"), pe_tpq[(idx - 1, 3)])
            s.activation(ar3[:, QG:QG + 512], p[7][:, 0:512], AF.Copy,
                         scale=scale).then_inc(cx.sem("sa"), 1)
            qt = ar3[:, QT + half * 512:QT + half * 512 + 512]
            if idx >= 1:
                s.wait_ge(cx.sem("pf"), pf_qnorm[idx - 1])
            s.wait_ge(cx.sem("ga"), ga_qev[(idx, 3)])
            s.activation(ar3[:, SQ:SQ + 512], qt,
                         AF.Square).then_inc(cx.sem("ph"), 1)
            s.wait_ge(cx.sem("ab"), ab_biascopy[idx])
            for ck in range(nkc):
                s.wait_ge(cx.sem("as_"), as_scores[(idx, ck)])
                if ck >= 2:
                    s.wait_ge(cx.sem("av"), av_done[(idx, ck - 2)])
                ptA = ar3[:, PT + (ck % 2) * 512:PT + (ck % 2) * 512 + 512]
                ptB = ar3[:, PT + 1024 + (ck % 2) * 512:
                          PT + 1024 + (ck % 2) * 512 + 512]
                s.activation(ptA[:, 0:512],
                             (p[0] if ck % 2 == 0 else p[4])[:, 0:512],
                             AF.Exp, bias=tsc[:, 0:1])
                s.activation(ptB[:, 0:512],
                             (p[1] if ck % 2 == 0 else p[7])[:, 0:512],
                             AF.Exp, bias=tsc[:, 1:2]).then_inc(
                    cx.sem("ax"), 1)
        return None
    cx.on("s", smain)

    def vmain(v):
        for idx in range(NIDX):
            qb, hp = divmod(idx, NH)
            half = idx % 2
            v.wait_ge(cx.sem("sa"), sa_qevac[idx])
            v.wait_ge(cx.sem("gs"), gs_qrope[qb])
            st = ar3[:, QG:QG + 512]
            e = st.rearrange("p (tc h i two) -> p tc h i two", two=2, i=32,
                             h=2)
            ctab = ar3[:, QR:QR + 256].rearrange(
                "p (tc h i) -> p tc h i", i=32, h=2)
            stab = ar3[:, QR + 256:QR + 512].rearrange(
                "p (tc h i) -> p tc h i", i=32, h=2)
            sc1 = ar3[:, SCQ:SCQ + 256].rearrange(
                "p (tc h i) -> p tc h i", i=32, h=2)
            sc2 = ar3[:, SCQ + 256:SCQ + 512].rearrange(
                "p (tc h i) -> p tc h i", i=32, h=2)
            sc3 = ar3[:, SCQ + 512:SCQ + 768].rearrange(
                "p (tc h i) -> p tc h i", i=32, h=2)
            pa_base = pa_ropeq[idx] - 7

            def step(ins, k):
                ins.then_inc(cx.sem("pa"), 1)
                v.wait_ge(cx.sem("pa"), pa_base + k)
            step(v.tensor_tensor(sc1, e[..., 1], stab, ALU.mult), 1)
            step(v.tensor_tensor(sc2, e[..., 1], ctab, ALU.mult), 2)
            step(v.tensor_tensor(sc3, e[..., 0], stab, ALU.mult), 3)
            step(v.tensor_tensor(e[..., 0], e[..., 0], ctab, ALU.mult), 4)
            step(v.tensor_tensor(e[..., 0], e[..., 0], sc1,
                                 ALU.subtract), 5)
            step(v.tensor_tensor(e[..., 1], sc3, sc2, ALU.add), 6)
            v.engine_nop().then_inc(cx.sem("pa"), 1)
            qt = ar3[:, QT + half * 512:QT + half * 512 + 512]
            for tl in range(4):
                v.wait_ge(cx.sem("pe_"), pe_tpq[(idx, tl)])
                v.tensor_copy(qt[:, tl * 128:(tl + 1) * 128],
                              p[5 + tl % 2].bitcast(BF16)[:, 0:128]
                              ).then_inc(cx.sem("ga"), 1)
            v.wait_ge(cx.sem("pf"), pf_qnorm[idx])
            v.reduce_max(tsc[0:1, 52:53], p[5][0:1, 0:512], axis=AX.X)
            v.reduce_max(tsc[0:1, 53:54], p[6][0:1, 0:512],
                         axis=AX.X).then_inc(cx.sem("ab"), 1)
            v.wait_ge(cx.sem("ab"), ab_negB[idx] - 2)
            v.tensor_add(tsc[0:1, 54:56], tsc[0:1, 52:54],
                         tsc[0:1, 32 + 2 * hp:34 + 2 * hp]).then_inc(
                cx.sem("ab"), 1)
            v.wait_ge(cx.sem("ab"), ab_negB[idx] - 1)
            v.tensor_scalar_mul(tsc[0:1, 56:58], tsc[0:1, 54:56],
                                -0.5).then_inc(cx.sem("ab"), 1)
            v.wait_ge(cx.sem("pf"), pf_bias[idx])
            v.tensor_copy(tsc[:, 0:2], p[7][:, 0:2]).then_inc(
                cx.sem("ab"), 1)
            v.wait_ge(cx.sem("av"), av_done[(idx, nkc - 1)])
            v.reciprocal(tsc[32:33, 0:512], p[5][0:1, 0:512])
            v.reciprocal(tsc[64:65, 0:512], p[6][0:1, 0:512]).then_inc(
                cx.sem("ay"), 1)
            v.wait_ge(cx.sem("ds"), ds_rbc[idx])
            if hp == 0 and qb >= 1:
                v.wait_ge(cx.sem("pb"), op_mm[(qb - 1, DCH - 1)])
            ys = ar3[:, YS + hp * 512:YS + (hp + 1) * 512]
            v.tensor_tensor(ys[0:64, :], p[2][0:64, 0:512],
                            ar3[0:64, RB:RB + 1024].bitcast(F32), ALU.mult)
            v.tensor_tensor(ys[64:128, :], p[3][64:128, 0:512],
                            ar3[64:128, RB:RB + 1024].bitcast(F32),
                            ALU.mult).then_inc(cx.sem("ay"), 1)
            if hp == NH - 1:
                for o in range(DCH):
                    v.wait_ge(cx.sem("pb"), op_mm[(qb, o)])
                    xso = xs[:, o * TOK + qb * 512:o * TOK + (qb + 1) * 512]
                    v.scalar_tensor_tensor(xso, p[o % 2][:, 0:512], 1.0,
                                           xso, ALU.mult,
                                           ALU.add).then_inc(cx.sem("pg"), 1)
        return None
    cx.on("v", vmain)
    cx.seal("v", "pg")
    cx.barrier()


def emit_l1_attn(cx, W):
    b = cx.buf
    p = b["p"]
    xs, xnf = b["xs"], b["xnf"]
    ar1, ar2, ar3 = b["ar1"], b["ar2"], b["ar3"]
    tsc, msk = b["tsc"], b["msk"]
    ones1, onesb = b["ones1"], b["onesb"]
    vrow = b["vrow"]
    NH = 8

    def xs_of(ch):
        return xs[:, ch * TOK:(ch + 1) * TOK]

    def xnf_of(ch):
        return xnf[:, ch * TOK:(ch + 1) * TOK]

    ident = msk[:, 512:640]

    def gid(g):
        g.memset(msk[:, 640:768], 1.0).then_inc(cx.sem("ga"), 1)
        cx.bump("ga")
        g.wait_ge(cx.sem("ga"), cx.cnt["ga"])
        return g.affine_select(ident, msk[:, 640:768], [[-1, 128]],
                               ALU.is_equal, 0.0, channel_multiplier=1)
    cx.on("g", gid)
    cx.dma("g", msk[:, 0:512].rearrange("p (r f) -> p r f", r=4),
           W["trimask"].rearrange("r p f -> p r f"))
    cx.dma("q", vrow[:, 32:40], W["l1a_bg2"][:, :])
    cx.dma("q", vrow[:, 41:49], W["l1a_gamma"][:, :])
    cx.dma("q", vrow[:, 49:57], W["l1a_beta"][:, :])
    cx.dma("q", vrow[0:64, 40:41], W["l1a_bg1"][:, :])
    cx.barrier()

    emit_norm(cx, xs_of, xnf_of, kind="ln")

    def ev_silu_ar1(s, o, qc, pp):
        return s.activation(ar1[:, o * TOK + qc * QCOL:
                                o * TOK + (qc + 1) * QCOL], pp[:, :],
                            AF.Silu)
    emit_proj(cx, W["l1a_wqkv_t"], xnf_of, D, D, evac_s=ev_silu_ar1,
              col_off=D)

    def ev_cp_ar2(v, o, qc, pp):
        return v.tensor_copy(ar2[:, o * TOK + qc * QCOL:
                                 o * TOK + (qc + 1) * QCOL], pp[:, :])
    emit_proj(cx, W["l1a_wqkv_t"], xnf_of, D, D, evac_v=ev_cp_ar2,
              col_off=2 * D)

    cx.dma("q", W["xgk_in"].rearrange("(c p) t -> p c t", p=128),
           ar1.rearrange("p (c t) -> p c t", c=DCH))
    emit_fm_to_tokmajor(cx, lambda o: ar2[:, o * TOK:(o + 1) * TOK],
                        W["xgv_in"], D, ident)

    def gcoll(g):
        g.wait_ge(cx.sem("ds"), cx.cnt["ds"])
        for a, o in (("xgk_in", "xgk_out"), ("xgv_in", "xgv_out")):
            g.collective_compute("AllGather", ALU.bypass, replica_groups=RG,
                                 ins=[W[a].ap().opt()],
                                 outs=[W[o].ap().opt()]).then_inc(
                cx.sem("cc"), 1)
            cx.bump("cc")
        return None
    cx.on("g", gcoll)
    cx.wait("q", "cc", cx.cnt["cc"])
    GW = 13824
    cx.dma("g", ar3[:, GW:GW + 512].rearrange("p (kc o) -> p kc o", kc=8),
           W["l1a_wg1_t"].rearrange("(kc p) o -> p kc o", p=128))
    gw2 = cx.dma("g", ar3[0:64, GW + 512:GW + 1536].rearrange(
        "p (oc o) -> p oc o", oc=8),
        W["l1a_wg2_t"].rearrange("p (oc o) -> p oc o", oc=8))
    cx.wait("t", "gs", gw2)
    cx.barrier()

    GK, GV = W["xgk_out"], W["xgv_out"]
    wqkv = W["l1a_wqkv_t"]
    wout = W["l1a_wout_t"]
    NIDX = NQB * NH
    chunks = {qb: attn_chunks_of(qb) for qb in range(NQB)}

    gq0 = cx.cnt["gs"]
    pf_qproj = {}
    sa_qevac = {}
    as_scores = {}
    ax_ev = {}
    am_mask = {}
    av_done = {}
    ay_yevac = {}
    ds_kv = {}
    gs_qslab = {}
    gs_wout = {}
    op_mm = {}
    op_evac = {}
    pc_g1 = {}
    pd_g1ev = {}
    pc_g2 = {}
    pd_g2 = {}
    pd_sq = {}
    pc_sums = {}
    pe_stats = {}
    sa_rstd = {}
    pf_bc = {}
    ab_apply = {}

    pf = cx.cnt["pf"]
    sa = cx.cnt["sa"]
    ab = cx.cnt["ab"]
    asx = cx.cnt["as_"]
    ax = cx.cnt["ax"]
    am = cx.cnt["am"]
    av = cx.cnt["av"]
    ay = cx.cnt["ay"]
    ds = cx.cnt["ds"]
    gs = gq0
    pb = cx.cnt["pb"]
    pg = cx.cnt["pg"]
    pc = cx.cnt["pc"]
    pd = cx.cnt["pd"]
    pe = cx.cnt["pe_"]
    for qb in range(NQB):
        cl = chunks[qb]
        pc += 1
        pc_g1[qb] = pc
        pd += 1
        pd_g1ev[qb] = pd
        for o in range(DCH):
            pc += 1
            pc_g2[(qb, o)] = pc
            pd += 1
            pd_g2[(qb, o)] = pd
        for hp in range(NH):
            idx = qb * NH + hp
            gs += 16
            gs_qslab[idx] = gs
            pf += 1
            pf_qproj[idx] = pf
            sa += 1
            sa_qevac[idx] = sa
            ds += 16 * 8
            ds_kv[idx] = ds
            for ck in range(len(cl)):
                asx += 1
                as_scores[(idx, ck)] = asx
                ax += 1
                ax_ev[(idx, ck)] = ax
                if cl[ck][3]:
                    am += 1
                    am_mask[(idx, ck)] = am
                av += 1
                av_done[(idx, ck)] = av
            ay += 1
            ay_yevac[idx] = ay
        for ch in range(DCH):
            pd += 1
            pd_sq[(qb, ch)] = pd
            pc += 1
            pc_sums[(qb, ch)] = pc
        pe += 1
        pe_stats[qb] = pe
        sa += 2
        sa_rstd[qb] = sa
        pf += 1
        pf_bc[qb] = pf
        ab += 8
        ab_apply[qb] = ab
        for si in range(4):
            gs += 16
            gs_wout[(qb, si)] = gs
        for o in range(DCH):
            pb += 1
            op_mm[(qb, o)] = pb
            pg += 1
            op_evac[(qb, o)] = pg
    cx.cnt["pf"] = pf
    cx.cnt["sa"] = sa
    cx.cnt["ab"] = ab
    cx.cnt["as_"] = asx
    cx.cnt["ax"] = ax
    cx.cnt["am"] = am
    cx.cnt["av"] = av
    cx.cnt["ay"] = ay
    cx.cnt["ds"] = ds
    cx.cnt["gs"] = gs
    cx.cnt["pb"] = pb
    cx.cnt["pg"] = pg
    cx.cnt["pc"] = pc
    cx.cnt["pd"] = pd
    cx.cnt["pe_"] = pe

    QT = 0
    PT = 1024
    GT = 3072
    YS = 7168
    QS = 11264
    SQ = 13312

    def gq(g):
        prev_gs = gq0
        for qb in range(NQB):
            for hp in range(NH):
                idx = qb * NH + hp
                g.wait_ge(cx.sem("gs"), prev_gs)
                if idx >= 2:
                    g.wait_ge(cx.sem("pf"), pf_qproj[idx - 2])
                half = idx % 2
                wr = wqkv[0:D, hp * 128:(hp + 1) * 128].rearrange(
                    "(kc p) o -> p kc o", p=128)
                g.dma_start(ar3[:, QS + half * 1024:QS + half * 1024 +
                                1024].rearrange("p (kc o) -> p kc o", kc=8),
                            wr).then_inc(cx.sem("gs"), 16)
                prev_gs = gs_qslab[idx]
            for si in range(4):
                g.wait_ge(cx.sem("gs"), prev_gs)
                if si >= 2:
                    g.wait_ge(cx.sem("pb"), op_mm[(qb, (si - 2) * 2 + 1)])
                elif qb >= 1:
                    g.wait_ge(cx.sem("pb"),
                              op_mm[(qb - 1, (si + 2) * 2 + 1)])
                wr = wout[0:D, si * 256:(si + 1) * 256].rearrange(
                    "(kc p) o -> p kc o", p=128)
                g.dma_start(b["wsl"][:, (si % 2) * WSLAB:
                                     (si % 2) * WSLAB + 2048].rearrange(
                    "p (kc o) -> p kc o", kc=8), wr).then_inc(
                    cx.sem("gs"), 16)
                prev_gs = gs_wout[(qb, si)]
        return None
    cx.on("g", gq)

    def qkv(q):
        for idx in range(NIDX):
            qb, hp = divmod(idx, NH)
            ncr = (qb + 1) * 4
            half = idx % 2
            if idx >= 2:
                q.wait_ge(cx.sem("ay"), ay_yevac[idx - 2])
            for r in range(4):
                q.dma_start(
                    ar1[:, half * 8192 + r * ncr * 128:
                        half * 8192 + (r + 1) * ncr * 128],
                    GK[r * D + hp * 128:r * D + (hp + 1) * 128,
                       0:ncr * 128]).then_inc(cx.sem("ds"), 16)
                src3 = GV[r * TOK:r * TOK + ncr * 128,
                          hp * 128:(hp + 1) * 128].rearrange(
                    "(c p) f -> p c f", p=128)
                dst3 = ar2[:, half * 8192 + r * ncr * 128:
                           half * 8192 + (r + 1) * ncr * 128].rearrange(
                    "p (c f) -> p c f", f=128)
                q.dma_start(dst3, src3).then_inc(cx.sem("ds"), 16)
            q.wait_ge(cx.sem("ds"), ds_kv[idx])
        return None
    cx.on("q", qkv)

    def tmain(t):
        for idx in range(NIDX):
            qb, hp = divmod(idx, NH)
            cl = chunks[qb]
            nkc = len(cl)
            ncr = (qb + 1) * 4
            half = idx % 2
            if hp == 0:
                if qb >= 1:
                    t.wait_ge(cx.sem("pg"), op_evac[(qb - 1, DCH - 1)])
                    t.wait_ge(cx.sem("ab"), ab_apply[qb - 1])
                lastm = None
                for k in range(DCH):
                    lastm = t.matmul(p[5][0:64, 0:512],
                                     ar3[:, GW + k * 64:GW + (k + 1) * 64],
                                     xnf_of(k)[:, qb * 512:(qb + 1) * 512],
                                     start=(k == 0), stop=(k == DCH - 1))
                lastm.then_inc(cx.sem("pc"), 1)
                t.wait_ge(cx.sem("pd"), pd_g1ev[qb])
                for o in range(DCH):
                    if o >= 1:
                        t.wait_ge(cx.sem("pd"), pd_g2[(qb, o - 1)])
                    t.matmul(p[6][:, 0:512],
                             ar3[0:64, GW + 512 + o * 128:
                                 GW + 512 + (o + 1) * 128],
                             ar3[0:64, SQ:SQ + 512], start=True,
                             stop=True).then_inc(cx.sem("pc"), 1)
            t.wait_ge(cx.sem("gs"), gs_qslab[idx])
            if idx >= 1:
                t.wait_ge(cx.sem("sa"), sa_qevac[idx - 1])
            qsl = ar3[:, QS + half * 1024:QS + half * 1024 + 1024]
            last = None
            for k in range(DCH):
                last = t.matmul(p[7][:, 0:512],
                                qsl[:, k * 128:(k + 1) * 128],
                                xnf_of(k)[:, qb * 512:(qb + 1) * 512],
                                start=(k == 0), stop=(k == DCH - 1))
            last.then_inc(cx.sem("pf"), 1)
            t.wait_ge(cx.sem("ds"), ds_kv[idx])
            t.wait_ge(cx.sem("sa"), sa_qevac[idx])
            if idx >= 1:
                t.wait_ge(cx.sem("ay"), ay_yevac[idx - 1])
            qt = ar3[:, QT + half * 512:QT + half * 512 + 512]
            for ck, (r, c, cs, diag) in enumerate(cl):
                koff = half * 8192 + (r * ncr + c) * 128
                bA = p[0] if ck % 2 == 0 else p[4]
                bB = p[1] if ck % 2 == 0 else p[6]
                if ck >= 2:
                    t.wait_ge(cx.sem("ax"), ax_ev[(idx, ck - 2)])
                if ck == 1 and hp == 0:
                    t.wait_ge(cx.sem("pd"), pd_g2[(qb, DCH - 1)])
                t.matmul(bA[:, cs:512], ar1[0:64, koff:koff + 128],
                         qt[0:64, cs:512], start=True, stop=True,
                         tile_position=(0, 0))
                t.matmul(bB[:, cs:512], ar1[64:128, koff:koff + 128],
                         qt[64:128, cs:512], start=True, stop=True,
                         tile_position=(64, 0)).then_inc(cx.sem("as_"), 1)
                if diag:
                    t.wait_ge(cx.sem("am"), am_mask[(idx, ck)])
                else:
                    t.wait_ge(cx.sem("ax"), ax_ev[(idx, ck)])
                ptA = ar3[:, PT + (ck % 2) * 512:PT + (ck % 2) * 512 + 512]
                ptB = ar3[:, PT + 1024 + (ck % 2) * 512:
                          PT + 1024 + (ck % 2) * 512 + 512]
                first = (ck == 0)
                lastc = (ck == nkc - 1)
                t.matmul(p[2][0:64, cs:512], ar2[:, koff:koff + 64],
                         ptA[:, cs:512], start=first, stop=lastc,
                         tile_position=(0, 0))
                t.matmul(p[3][64:128, cs:512], ar2[:, koff + 64:koff + 128],
                         ptB[:, cs:512], start=first, stop=lastc,
                         tile_position=(0, 64)).then_inc(cx.sem("av"), 1)
            if hp == NH - 1:
                t.wait_ge(cx.sem("ay"), ay_yevac[idx])
                for ch in range(DCH):
                    t.wait_ge(cx.sem("pd"), pd_sq[(qb, ch)])
                    t.matmul(p[5][0:1, 0:512], onesb[:, 0:1],
                             ar3[:, YS + ch * 512:YS + (ch + 1) * 512],
                             start=(ch == 0), stop=(ch == DCH - 1))
                    t.matmul(p[6][0:1, 0:512], onesb[:, 1:2],
                             ar3[:, SQ:SQ + 512], start=(ch == 0),
                             stop=(ch == DCH - 1)).then_inc(cx.sem("pc"), 1)
                t.wait_ge(cx.sem("sa"), sa_rstd[qb])
                t.matmul(p[5][:, 0:512], ones1[0:1, :], tsc[0:1, 0:512],
                         start=True, stop=True)
                t.matmul(p[6][:, 0:512], ones1[64:65, :],
                         tsc[64:65, 0:512], start=True,
                         stop=True).then_inc(cx.sem("pf"), 1)
                t.wait_ge(cx.sem("ab"), ab_apply[qb])
                for o in range(DCH):
                    si = o // 2
                    t.wait_ge(cx.sem("gs"), gs_wout[(qb, si)])
                    if o >= 2:
                        t.wait_ge(cx.sem("pg"), op_evac[(qb, o - 2)])
                    pp = p[o % 2]
                    lastm = None
                    for k in range(DCH):
                        base = (si % 2) * WSLAB + k * 256 + (o % 2) * 128
                        wt = b["wsl"][:, base:base + 128]
                        lastm = t.matmul(pp[:, 0:512], wt,
                                         ar3[:, YS + k * 512:
                                             YS + (k + 1) * 512],
                                         start=(k == 0),
                                         stop=(k == DCH - 1))
                    lastm.then_inc(cx.sem("pb"), 1)
        return None
    cx.on("t", tmain)

    def smain(s):
        for qb in range(NQB):
            cl = chunks[qb]
            nkc = len(cl)
            for o in range(DCH):
                s.wait_ge(cx.sem("pc"), pc_g2[(qb, o)])
                s.activation(ar3[:, GT + o * 512:GT + (o + 1) * 512],
                             p[6][:, 0:512], AF.Sigmoid,
                             bias=vrow[:, 32 + o:33 + o]).then_inc(
                    cx.sem("pd"), 1)
            for hp in range(NH):
                idx = qb * NH + hp
                half = idx % 2
                s.wait_ge(cx.sem("pf"), pf_qproj[idx])
                if idx >= 2:
                    s.wait_ge(cx.sem("as_"),
                              as_scores[(idx - 2,
                                         len(chunks[(idx - 2) // NH]) - 1)])
                qt = ar3[:, QT + half * 512:QT + half * 512 + 512]
                s.activation(qt, p[7][:, 0:512], AF.Silu).then_inc(
                    cx.sem("sa"), 1)
                for ck, (r, c, cs, diag) in enumerate(cl):
                    s.wait_ge(cx.sem("as_"), as_scores[(idx, ck)])
                    if ck >= 2:
                        s.wait_ge(cx.sem("av"), av_done[(idx, ck - 2)])
                    ptA = ar3[:, PT + (ck % 2) * 512:
                              PT + (ck % 2) * 512 + 512]
                    ptB = ar3[:, PT + 1024 + (ck % 2) * 512:
                              PT + 1024 + (ck % 2) * 512 + 512]
                    s.activation(ptA[:, cs:512],
                                 (p[0] if ck % 2 == 0 else p[4])[:, cs:512],
                                 AF.Copy)
                    s.activation(ptB[:, cs:512],
                                 (p[1] if ck % 2 == 0 else p[6])[:, cs:512],
                                 AF.Copy).then_inc(cx.sem("ax"), 1)
            for ch in range(DCH):
                if ch == 0:
                    s.wait_ge(cx.sem("ay"), ay_yevac[qb * NH + NH - 1])
                else:
                    s.wait_ge(cx.sem("pc"), pc_sums[(qb, ch - 1)])
                s.activation(ar3[:, SQ:SQ + 512],
                             ar3[:, YS + ch * 512:YS + (ch + 1) * 512],
                             AF.Square).then_inc(cx.sem("pd"), 1)
            s.wait_ge(cx.sem("pe_"), pe_stats[qb])
            s.activation(tsc[96:97, 0:512] if False else tsc[0:1, 32:48]
                         if False else tsc[32 + 32:65, 0:512], tsc[32:33,
                                                                   0:512],
                         AF.Ln).then_inc(cx.sem("sa"), 1)
            s.wait_ge(cx.sem("sa"), sa_rstd[qb] - 1)
            s.activation(tsc[64:65, 0:512], tsc[64:65, 0:512], AF.Exp,
                         scale=-0.5).then_inc(cx.sem("sa"), 1)
        return None
    cx.on("s", smain)

    def vmain(v):
        for qb in range(NQB):
            cl = chunks[qb]
            nkc = len(cl)
            v.wait_ge(cx.sem("pc"), pc_g1[qb])
            v.tensor_scalar_add(ar3[0:64, SQ:SQ + 512], p[5][0:64, 0:512],
                                vrow[0:64, 40:41]).then_inc(cx.sem("pd"), 1)
            for hp in range(NH):
                idx = qb * NH + hp
                for ck, (r, c, cs, diag) in enumerate(cl):
                    if not diag:
                        continue
                    v.wait_ge(cx.sem("ax"), ax_ev[(idx, ck)])
                    ptA = ar3[:, PT + (ck % 2) * 512:
                              PT + (ck % 2) * 512 + 512]
                    ptB = ar3[:, PT + 1024 + (ck % 2) * 512:
                              PT + 1024 + (ck % 2) * 512 + 512]
                    we = min(cs + 128, 512)
                    v.tensor_mul(ptA[:, cs:we], ptA[:, cs:we],
                                 msk[:, r * 128:r * 128 + (we - cs)])
                    v.tensor_mul(ptB[:, cs:we], ptB[:, cs:we],
                                 msk[:, r * 128:r * 128 +
                                     (we - cs)]).then_inc(cx.sem("am"), 1)
                v.wait_ge(cx.sem("av"), av_done[(idx, nkc - 1)])
                v.wait_ge(cx.sem("pd"), pd_g2[(qb, hp)])
                ys = ar3[:, YS + hp * 512:YS + (hp + 1) * 512]
                gt = ar3[:, GT + hp * 512:GT + (hp + 1) * 512]
                v.tensor_tensor(ys[0:64, :], p[2][0:64, 0:512],
                                gt[0:64, :], ALU.mult)
                v.tensor_tensor(ys[64:128, :], p[3][64:128, 0:512],
                                gt[64:128, :], ALU.mult).then_inc(
                    cx.sem("ay"), 1)
            v.wait_ge(cx.sem("pc"), pc_sums[(qb, DCH - 1)])
            v.tensor_scalar_mul(tsc[0:1, 0:512], p[5][0:1, 0:512],
                                1.0 / D).then_inc(cx.sem("pa"), 1)
            cx.bump("pa")
            v.wait_ge(cx.sem("pa"), cx.cnt["pa"])
            v.tensor_mul(tsc[96:97, 0:512] if False else
                         tsc[32:33, 0:512], tsc[0:1, 0:512],
                         tsc[0:1, 0:512]).then_inc(cx.sem("pa"), 1)
            cx.bump("pa")
            v.wait_ge(cx.sem("pa"), cx.cnt["pa"])
            v.scalar_tensor_tensor(tsc[32:33, 0:512], p[6][0:1, 0:512],
                                   1.0 / D, tsc[32:33, 0:512], ALU.mult,
                                   ALU.subtract).then_inc(cx.sem("pa"), 1)
            cx.bump("pa")
            v.wait_ge(cx.sem("pa"), cx.cnt["pa"])
            v.tensor_scalar_add(tsc[32:33, 0:512], tsc[32:33, 0:512],
                                EPS_LN).then_inc(cx.sem("pe_"), 1)
            v.wait_ge(cx.sem("pf"), pf_bc[qb])
            for ch in range(DCH):
                if ch >= 1:
                    v.wait_ge(cx.sem("ab"), ab_apply[qb] - 8 + ch)
                ysc = ar3[:, YS + ch * 512:YS + (ch + 1) * 512]
                v.tensor_sub(tsc[:, 0:512],
                             ysc, p[5][:, 0:512]).then_inc(cx.sem("pa"), 1)
                cx.bump("pa")
                v.wait_ge(cx.sem("pa"), cx.cnt["pa"])
                v.scalar_tensor_tensor(tsc[:, 0:512], tsc[:, 0:512],
                                       vrow[:, 41 + ch:42 + ch],
                                       p[6][:, 0:512], ALU.mult,
                                       ALU.mult).then_inc(cx.sem("pa"), 1)
                cx.bump("pa")
                v.wait_ge(cx.sem("pa"), cx.cnt["pa"])
                v.tensor_scalar_add(ysc, tsc[:, 0:512],
                                    vrow[:, 49 + ch:50 + ch]).then_inc(
                    cx.sem("ab"), 1)
            for o in range(DCH):
                v.wait_ge(cx.sem("pb"), op_mm[(qb, o)])
                xso = xs[:, o * TOK + qb * 512:o * TOK + (qb + 1) * 512]
                v.scalar_tensor_tensor(xso, p[o % 2][:, 0:512], 1.0, xso,
                                       ALU.mult, ALU.add).then_inc(
                    cx.sem("pg"), 1)
        return None
    cx.on("v", vmain)
    cx.seal("v", "pg")
    cx.barrier()


def emit_l3_ffn(cx, W):
    b = cx.buf
    p = b["p"]
    xs, xnf = b["xs"], b["xnf"]
    ar1, ar2, ar3 = b["ar1"], b["ar2"], b["ar3"]
    tsc, wsl = b["tsc"], b["wsl"]
    ones1, onesb = b["ones1"], b["onesb"]
    HK = 16

    def xs_of(ch):
        return xs[:, ch * TOK:(ch + 1) * TOK]

    def xnf_of(ch):
        return xnf[:, ch * TOK:(ch + 1) * TOK]

    def h_of(k):
        return (ar1 if k < 8 else ar2)[:, (k % 8) * TOK:(k % 8) * TOK + TOK]

    AST = 0
    SST = 8192
    SCL = 12288
    WB = 13312
    WSE = 13824
    WT = 14336

    def gid(g):
        g.memset(b["msk"][:, 640:768], 1.0).then_inc(cx.sem("ga"), 1)
        cx.bump("ga")
        g.wait_ge(cx.sem("ga"), cx.cnt["ga"])
        return g.affine_select(b["msk"][:, 512:640], b["msk"][:, 640:768],
                               [[-1, 128]], ALU.is_equal, 0.0,
                               channel_multiplier=1)
    cx.on("g", gid)
    cx.barrier()

    emit_norm(cx, xs_of, xnf_of, kind="ln")

    bw = cx.dma("q", tsc[:, 0:16],
                W["l3f_geb"][0:1, :].broadcast_to([128, 16]))
    cx.wait("v", "ds", bw)

    gw2 = None
    cx.dma("g", wsl[:, 0:16].rearrange("p (kc o) -> p kc o", kc=8),
           W["l3f_wg_t"].rearrange("(kc p) o -> p kc o", p=128))
    gw2 = cx.dma("g", wsl[:, 16:80].rearrange("p (kc o) -> p kc o", kc=8),
                 W["l3f_we_t"].rearrange("(kc p) o -> p kc o", p=128))
    cx.wait("t", "gs", gw2)

    NTC = 16
    pc_gl = {}
    ab_rel = {}
    ph_pre = {}
    sa_exp = {}
    am_wse = {}
    pe_tp = {}
    av_wt = {}
    pc = cx.cnt["pc"]
    ab = cx.cnt["ab"]
    ph = cx.cnt["ph"]
    sa = cx.cnt["sa"]
    am = cx.cnt["am"]
    pe = cx.cnt["pe_"]
    av = cx.cnt["av"]
    for tc in range(NTC):
        pc += 1
        pc_gl[tc] = pc
        ab += 1
        ab_rel[tc] = ab
        ph += 1
        ph_pre[tc] = ph
        sa += 1
        sa_exp[tc] = sa
        am += 1
        am_wse[tc] = am
        pe += 1
        pe_tp[tc] = pe
        av += 1
        av_wt[tc] = av
    cx.cnt["pc"] = pc
    cx.cnt["ab"] = ab
    cx.cnt["ph"] = ph
    cx.cnt["sa"] = sa
    cx.cnt["am"] = am
    cx.cnt["pe_"] = pe
    cx.cnt["av"] = av

    def tg(t):
        for tc in range(NTC):
            if tc >= 2:
                t.wait_ge(cx.sem("ab"), ab_rel[tc - 2])
            last = None
            for k in range(DCH):
                last = t.matmul(p[4 + tc % 2][:, 0:2],
                                xnf_of(k)[:, tc * 128:(tc + 1) * 128],
                                wsl[:, k * 2:k * 2 + 2], start=(k == 0),
                                stop=(k == DCH - 1))
            for k in range(DCH):
                last = t.matmul(p[4 + tc % 2][:, 8:16],
                                xnf_of(k)[:, tc * 128:(tc + 1) * 128],
                                wsl[:, 16 + k * 8:16 + k * 8 + 8],
                                start=(k == 0), stop=(k == DCH - 1))
            last.then_inc(cx.sem("pc"), 1)
            t.wait_ge(cx.sem("am"), am_wse[tc])
            if tc >= 2:
                t.wait_ge(cx.sem("av"), av_wt[tc - 2])
            t.transpose(p[6 + tc % 2].bitcast(BF16)[0:8, 0:128],
                        ar3[:, WSE + (tc % 2) * 8:WSE + (tc % 2) * 8 + 8],
                        b["msk"][:, 512:640]).then_inc(cx.sem("pe_"), 1)
        return None
    cx.on("t", tg)

    def sg(s):
        for tc in range(NTC):
            s.wait_ge(cx.sem("ph"), ph_pre[tc])
            s.activation(tsc[:, 62:64], tsc[:, 62:64], AF.Exp)
            s.activation(tsc[:, 52:60], tsc[:, 52:60],
                         AF.Exp).then_inc(cx.sem("sa"), 1)
        return None
    cx.on("s", sg)

    def vg(v):
        pa = cx.cnt["pa"]

        def st(ins):
            nonlocal pa
            ins.then_inc(cx.sem("pa"), 1)
            pa += 1
            v.wait_ge(cx.sem("pa"), pa)
        for tc in range(NTC):
            v.wait_ge(cx.sem("pc"), pc_gl[tc])
            pp = p[4 + tc % 2]
            st(v.tensor_tensor(tsc[:, 32:34], pp[:, 0:2], tsc[:, 0:2],
                               ALU.add))
            v.tensor_tensor(tsc[:, 34:42], pp[:, 8:16], tsc[:, 8:16],
                            ALU.add).then_inc(cx.sem("ab"), 1)
            v.wait_ge(cx.sem("ab"), ab_rel[tc])
            st(v.tensor_tensor(tsc[:, 42:43], tsc[:, 32:33], tsc[:, 33:34],
                               ALU.is_ge))
            st(v.tensor_scalar(tsc[:, 43:44], tsc[:, 42:43], -1.0, 1.0,
                               ALU.mult, ALU.add))
            st(v.tensor_scalar(tsc[:, 44:48], ones1[:, 0:4], tsc[:, 43:44],
                               None, ALU.mult))
            st(v.tensor_scalar(tsc[:, 48:52], ones1[:, 0:4], tsc[:, 42:43],
                               None, ALU.mult))
            st(v.scalar_tensor_tensor(tsc[:, 52:60], tsc[:, 44:52], -1e9,
                                      tsc[:, 34:42], ALU.mult, ALU.add))
            st(v.reduce_max(tsc[:, 60:61], tsc[:, 52:60], axis=AX.X))
            st(v.reduce_max(tsc[:, 61:62], tsc[:, 32:34], axis=AX.X))
            st(v.tensor_scalar(tsc[:, 62:64], tsc[:, 32:34], tsc[:, 61:62],
                               None, ALU.subtract))
            v.tensor_scalar(tsc[:, 52:60], tsc[:, 52:60], tsc[:, 60:61],
                            None, ALU.subtract).then_inc(cx.sem("ph"), 1)
            pa += 0
            v.wait_ge(cx.sem("sa"), sa_exp[tc])
            st(v.reduce_sum(tsc[:, 64:65], tsc[:, 62:64], axis=AX.X))
            st(v.reciprocal(tsc[:, 65:66], tsc[:, 64:65]))
            st(v.tensor_scalar(tsc[:, 62:64], tsc[:, 62:64], tsc[:, 65:66],
                               None, ALU.mult))
            st(v.tensor_tensor(tsc[:, 66:67], tsc[:, 62:63], tsc[:, 63:64],
                               ALU.subtract))
            st(v.scalar_tensor_tensor(tsc[:, 67:68], tsc[:, 66:67],
                                      tsc[:, 42:43], tsc[:, 63:64],
                                      ALU.mult, ALU.add))
            st(v.reduce_sum(tsc[:, 68:69], tsc[:, 52:60], axis=AX.X))
            st(v.reciprocal(tsc[:, 69:70], tsc[:, 68:69]))
            st(v.tensor_scalar(tsc[:, 52:60], tsc[:, 52:60], tsc[:, 69:70],
                               None, ALU.mult))
            st(v.tensor_scalar(tsc[:, 52:60], tsc[:, 52:60], tsc[:, 67:68],
                               None, ALU.mult))
            st(v.reduce_max(tsc[:, 70:71], tsc[:, 52:60], axis=AX.X))
            st(v.tensor_scalar(tsc[:, 71:79], tsc[:, 52:60], tsc[:, 70:71],
                               None, ALU.is_ge))
            st(v.scalar_tensor_tensor(tsc[:, 79:87], tsc[:, 71:79], -1e9,
                                      tsc[:, 52:60], ALU.mult, ALU.add))
            st(v.reduce_max(tsc[:, 87:88], tsc[:, 79:87], axis=AX.X))
            st(v.tensor_scalar(tsc[:, 88:96], tsc[:, 52:60], tsc[:, 87:88],
                               None, ALU.is_ge))
            st(v.tensor_tensor(tsc[:, 96:104], tsc[:, 52:60],
                               tsc[:, 88:96], ALU.mult))
            v.tensor_copy(ar3[:, WSE + (tc % 2) * 8:WSE + (tc % 2) * 8 + 8],
                          tsc[:, 96:104]).then_inc(cx.sem("am"), 1)
            v.wait_ge(cx.sem("pe_"), pe_tp[tc])
            v.tensor_copy(b["ar3"][0:8, WT + tc * 128:WT + (tc + 1) * 128],
                          p[6 + tc % 2].bitcast(BF16)[0:8, 0:128]).then_inc(
                cx.sem("av"), 1)
        return None
    cx.on("v", vg)
    cx.bump("pa", 24 * NTC)
    cx.seal("v", "av")
    cx.barrier()

    pc0 = cx.cnt["pc"]
    pd0 = cx.cnt["pd"]
    pe0 = cx.cnt["pe_"]
    gs0 = cx.cnt["gs"]

    def gh(g):
        prev = gs0
        for o2 in range(HK):
            for half_ab in range(2):
                sl = o2 * 2 + half_ab
                if prev > gs0:
                    g.wait_ge(cx.sem("gs"), prev)
                if sl >= 4:
                    t_sl = sl - 4
                    g.wait_ge(cx.sem("pc"), pc0 + (t_sl // 2) * 2 * NQC +
                              (t_sl % 2) * NQC + NQC)
                col = o2 * 128 + half_ab * 2048
                wr = W["l3f_w1_t"][0:D, col:col + 128].rearrange(
                    "(kc p) o -> p kc o", p=128)
                g.dma_start(wsl[:, (sl % 4) * 1024:(sl % 4) * 1024 +
                                1024].rearrange("p (kc o) -> p kc o", kc=8),
                            wr).then_inc(cx.sem("gs"), 16)
                prev = gs0 + 16 * (sl + 1)
        return None
    cx.on("g", gh)
    cx.bump("gs", 16 * 2 * HK)

    def th(t):
        for o2 in range(HK):
            for half_ab in range(2):
                sl = o2 * 2 + half_ab
                t.wait_ge(cx.sem("gs"), gs0 + 16 * (sl + 1))
                for qc in range(NQC):
                    j = o2 * NQC + qc
                    bank = p[(half_ab * 2) + (qc % 2)]
                    if j >= 2:
                        if half_ab == 0:
                            t.wait_ge(cx.sem("pd"), pd0 + j - 1)
                        else:
                            t.wait_ge(cx.sem("pe_"), pe0 + j - 1)
                    last = None
                    for k in range(DCH):
                        last = t.matmul(
                            bank[:, 0:512],
                            wsl[:, (sl % 4) * 1024 + k * 128:
                                (sl % 4) * 1024 + (k + 1) * 128],
                            xnf_of(k)[:, qc * QCOL:(qc + 1) * QCOL],
                            start=(k == 0), stop=(k == DCH - 1))
                    last.then_inc(cx.sem("pc"), 1)
        return None
    cx.on("t", th)
    cx.bump("pc", 2 * HK * NQC)

    def sh(s):
        for o2 in range(HK):
            for qc in range(NQC):
                i = (o2 * 2 + 0) * NQC + qc
                s.wait_ge(cx.sem("pc"), pc0 + i + 1)
                if o2 >= 1:
                    s.wait_ge(cx.sem("pe_"), pe0 + (o2 - 1) * NQC + qc + 1)
                s.activation(ar3[:, AST + (o2 % 2) * 2048 + qc * 512:
                                 AST + (o2 % 2) * 2048 + (qc + 1) * 512],
                             p[qc % 2][:, 0:512],
                             AF.Silu).then_inc(cx.sem("pd"), 1)
        return None
    cx.on("s", sh)
    cx.bump("pd", HK * NQC)

    def vh(v):
        for o2 in range(HK):
            for qc in range(NQC):
                ia = (o2 * 2) * NQC + qc
                ib = (o2 * 2 + 1) * NQC + qc
                v.wait_ge(cx.sem("pd"), pd0 + o2 * NQC + qc + 1)
                v.wait_ge(cx.sem("pc"), pc0 + ib + 1)
                v.tensor_tensor(
                    h_of(o2)[:, qc * QCOL:(qc + 1) * QCOL],
                    ar3[:, AST + (o2 % 2) * 2048 + qc * 512:
                        AST + (o2 % 2) * 2048 + (qc + 1) * 512],
                    p[2 + qc % 2][:, 0:512], ALU.mult).then_inc(
                    cx.sem("pe_"), 1)
        return None
    cx.on("v", vh)
    cx.bump("pe_", HK * NQC)
    cx.seal("v", "pe_")
    cx.barrier()

    for tb in range(4):
        tb_sl = slice(tb * 512, (tb + 1) * 512)
        pc0 = cx.cnt["pc"]
        pd0 = cx.cnt["pd"]
        gs0 = cx.cnt["gs"]

        def gsh(g, tb=tb, gs0=gs0, pc0=pc0):
            prev = gs0
            for ph_ in range(2):
                for o2 in range(HK):
                    sl = ph_ * HK + o2
                    g.wait_ge(cx.sem("gs"), prev)
                    if sl >= 4:
                        g.wait_ge(cx.sem("pc"), pc0 + sl - 4 + 1)
                    col = o2 * 128 + ph_ * 2048
                    wr = W["l3f_win_t"][0:D, col:col + 128].rearrange(
                        "(kc p) o -> p kc o", p=128)
                    g.dma_start(wsl[:, (sl % 4) * 1024:(sl % 4) * 1024 +
                                    1024].rearrange("p (kc o) -> p kc o",
                                                    kc=8),
                                wr).then_inc(cx.sem("gs"), 16)
                    prev = gs0 + 16 * (sl + 1)
            for o in range(DCH):
                sl = 2 * HK + o
                g.wait_ge(cx.sem("gs"), prev)
                if o == 0:
                    g.wait_ge(cx.sem("pc"), pc0 + 2 * HK - 2)
                elif o == 1:
                    g.wait_ge(cx.sem("pc"), pc0 + 2 * HK)
                else:
                    g.wait_ge(cx.sem("pc"), pc0 + 2 * HK + o - 2 + 1)
                wr = W["l3f_wout_t"][0:2048, o * 128:(o + 1) * 128
                                     ].rearrange("(kc p) o -> p kc o",
                                                 p=128)
                g.dma_start(wsl[:, (o % 2) * WSLAB:(o % 2) * WSLAB +
                                2048].rearrange("p (kc o) -> p kc o",
                                                kc=16),
                            wr).then_inc(cx.sem("gs"), 16)
                prev = gs0 + 16 * (sl + 1)
        cx.on("g", gsh)
        cx.bump("gs", 16 * (2 * HK + DCH))

        pe0 = cx.cnt["pe_"]
        ab0 = cx.cnt["ab"]

        def tsh(t, tb=tb, gs0=gs0, pc0=pc0, pd0=pd0, pe0=pe0, ab0=ab0):
            for ph_ in range(2):
                for o2 in range(HK):
                    sl = ph_ * HK + o2
                    t.wait_ge(cx.sem("gs"), gs0 + 16 * (sl + 1))
                    if ph_ == 0 and o2 >= 2:
                        t.wait_ge(cx.sem("pd"), pd0 + o2 - 2 + 1)
                    if ph_ == 1 and o2 >= 2:
                        t.wait_ge(cx.sem("pe_"), pe0 + o2 - 2 + 1)
                    bank = p[ph_ * 2 + o2 % 2]
                    last = None
                    for k in range(DCH):
                        last = t.matmul(
                            bank[:, 0:512],
                            wsl[:, (sl % 4) * 1024 + k * 128:
                                (sl % 4) * 1024 + (k + 1) * 128],
                            xnf_of(k)[:, tb_sl], start=(k == 0),
                            stop=(k == DCH - 1))
                    last.then_inc(cx.sem("pc"), 1)
            for o in range(DCH):
                sl = 2 * HK + o
                t.wait_ge(cx.sem("gs"), gs0 + 16 * (sl + 1))
                if o >= 2:
                    t.wait_ge(cx.sem("ab"), ab0 + o - 2 + 1)
                bank = p[4 + o % 2]
                last = None
                for k2 in range(HK):
                    t.wait_ge(cx.sem("pe_"), pe0 + k2 + 1)
                    last = t.matmul(bank[:, 0:512],
                                    wsl[:, (o % 2) * WSLAB + k2 * 128:
                                        (o % 2) * WSLAB + (k2 + 1) * 128],
                                    ar3[:, AST + k2 * 512:
                                        AST + (k2 + 1) * 512],
                                    start=(k2 == 0), stop=(k2 == HK - 1))
                last.then_inc(cx.sem("pc"), 1)
        cx.on("t", tsh)

        def ssh(s, tb=tb, pc0=pc0, pd0=pd0):
            for o2 in range(HK):
                s.wait_ge(cx.sem("pc"), pc0 + o2 + 1)
                s.activation(ar3[:, AST + o2 * 512:AST + (o2 + 1) * 512],
                             p[o2 % 2][:, 0:512],
                             AF.Silu).then_inc(cx.sem("pd"), 1)
        cx.on("s", ssh)

        def vsh(v, tb=tb, pc0=pc0, pd0=pd0, pe0=pe0, ab0=ab0):
            for o2 in range(HK):
                v.wait_ge(cx.sem("pd"), pd0 + o2 + 1)
                v.wait_ge(cx.sem("pc"), pc0 + HK + o2 + 1)
                v.tensor_tensor(ar3[:, AST + o2 * 512:AST + (o2 + 1) * 512],
                                ar3[:, AST + o2 * 512:AST + (o2 + 1) * 512],
                                p[2 + o2 % 2][:, 0:512],
                                ALU.mult).then_inc(cx.sem("pe_"), 1)
            for o in range(DCH):
                v.wait_ge(cx.sem("pc"), pc0 + 2 * HK + o + 1)
                v.tensor_copy(ar3[:, SST + o * 512:SST + (o + 1) * 512],
                              p[4 + o % 2][:, 0:512]).then_inc(
                    cx.sem("ab"), 1)
        cx.on("v", vsh)
        cx.bump("pd", HK)
        cx.bump("pc", 2 * HK + DCH)
        cx.bump("pe_", HK)
        cx.bump("ab", DCH)
        cx.seal("v", "ab")
        cx.barrier()

        pc0 = cx.cnt["pc"]
        pd0 = cx.cnt["pd"]
        gs0 = cx.cnt["gs"]
        ds0 = cx.cnt["ds"]
        pa0 = cx.cnt["pa"]

        def gr(g, tb=tb, gs0=gs0, pc0=pc0):
            prev = gs0
            for e in range(8):
                for k2 in range(HK):
                    sl = e * HK + k2
                    g.wait_ge(cx.sem("gs"), prev)
                    if sl >= 4:
                        g.wait_ge(cx.sem("pc"), pc0 + sl - 4 + 1)
                    wr = W["l3f_w2_t"][e, k2 * 128:(k2 + 1) * 128,
                                       0:1024].rearrange("p o -> p o")
                    g.dma_start(wsl[:, (sl % 4) * 1024:(sl % 4) * 1024 +
                                    1024], wr).then_inc(cx.sem("gs"), 16)
                    prev = gs0 + 16 * (sl + 1)
        cx.on("g", gr)

        def qr(q, tb=tb, ds0=ds0, pa0=pa0):
            for e in range(8):
                if e >= 1:
                    q.wait_ge(cx.sem("pa"), pa0 + e * HK)
                q.dma_start(W["rbc2"][0:1, :],
                            ar3[e:e + 1, WT + tb * 512:WT + (tb + 1) * 512]
                            ).then_inc(cx.sem("ds"), 16)
                q.wait_ge(cx.sem("ds"), ds0 + 32 * e + 16)
                q.dma_start(ar3[:, WB + (e % 2) * 256:WB + (e % 2) * 256 +
                                512] if False else
                            ar3[:, WB:WB + 512],
                            W["rbc2"][0:1, :].broadcast_to([128, 512])
                            ).then_inc(cx.sem("ds"), 16)
                q.wait_ge(cx.sem("ds"), ds0 + 32 * e + 32)
        cx.on("q", qr)

        def vr(v, tb=tb, ds0=ds0, pa0=pa0, pc0=pc0):
            pa = pa0
            for e in range(8):
                v.wait_ge(cx.sem("ds"), ds0 + 32 * e + 32)
                for k2 in range(HK):
                    if e * HK + k2 >= 2:
                        v.wait_ge(cx.sem("pc"), pc0 + e * HK + k2 - 2 + 1)
                    v.tensor_tensor(
                        ar3[:, SCL + ((e * HK + k2) % 2) * 512:
                            SCL + ((e * HK + k2) % 2) * 512 + 512],
                        h_of(k2)[:, tb_sl], ar3[:, WB:WB + 512],
                        ALU.mult).then_inc(cx.sem("pa"), 1)
                    pa += 1
        cx.on("v", vr)
        cx.bump("pa", 8 * HK)
        cx.bump("ds", 32 * 8)
        cx.bump("gs", 16 * 8 * HK)
        cx.bump("pc", 8 * HK)

        def tr(t, tb=tb, gs0=gs0, pa0=pa0, pc0=pc0):
            for e in range(8):
                for k2 in range(HK):
                    sl = e * HK + k2
                    t.wait_ge(cx.sem("gs"), gs0 + 16 * (sl + 1))
                    t.wait_ge(cx.sem("pa"), pa0 + sl + 1)
                    scl = ar3[:, SCL + (sl % 2) * 512:
                              SCL + (sl % 2) * 512 + 512]
                    for o in range(DCH):
                        ins = t.matmul(p[o][:, 0:512],
                                       wsl[:, (sl % 4) * 1024 + o * 128:
                                           (sl % 4) * 1024 + (o + 1) * 128],
                                       scl, start=(sl == 0),
                                       stop=(sl == 8 * HK - 1))
                        if o == DCH - 1:
                            ins.then_inc(cx.sem("pc"), 1)
        cx.on("t", tr)

        def vfin(v, tb=tb, pc0=pc0):
            v.wait_ge(cx.sem("pc"), pc0 + 8 * HK)
            pa = cx.cnt["pa"]
            for o in range(DCH):
                v.tensor_tensor(tsc[:, 0:512], p[o][:, 0:512],
                                ar3[:, SST + o * 512:SST + (o + 1) * 512],
                                ALU.add).then_inc(cx.sem("pa"), 1)
                pa += 1
                v.wait_ge(cx.sem("pa"), pa)
                xso = xs[:, o * TOK + tb * 512:o * TOK + (tb + 1) * 512]
                v.scalar_tensor_tensor(xso, tsc[:, 0:512], 0.5, xso,
                                       ALU.mult, ALU.add).then_inc(
                    cx.sem("pa"), 1)
                pa += 1
                v.wait_ge(cx.sem("pa"), pa)
            cx.cnt["pa"] = pa
            return None
        cx.on("v", vfin)
        cx.seal("v", "pa")
        cx.barrier()


def emit_l3_attn(cx, W):
    emit_attn_causal(cx, W, "l3a_wqkv", 0, D, 2 * D, "l3a_wout",
                     0.125, 1.0)


def emit_l1_ffn(cx, W):
    b = cx.buf
    p = b["p"]
    xs, xnf = b["xs"], b["xnf"]
    ar1, ar2, ar3 = b["ar1"], b["ar2"], b["ar3"]
    tsc, wsl, vrow = b["tsc"], b["wsl"], b["vrow"]
    NCHK = 4
    NTT = 16
    ar2f = ar2.bitcast(F32)
    xnff = xnf.bitcast(F32)
    wslf = wsl.bitcast(F32)

    def xs_of(ch):
        return xs[:, ch * TOK:(ch + 1) * TOK]

    def xnf_of(ch):
        return xnf[:, ch * TOK:(ch + 1) * TOK]

    gw = cx.dma("q", vrow[:, 32:40], W["l1f_gx"][:, :])
    cx.wait("v", "ds", gw)
    emit_norm(cx, xs_of, xnf_of, kind="ln",
              w_slice=lambda ch: vrow[:, 32 + ch:33 + ch], eps=0.0)

    def gid(g):
        g.memset(b["msk"][:, 640:768], 1.0).then_inc(cx.sem("ga"), 1)
        cx.bump("ga")
        g.wait_ge(cx.sem("ga"), cx.cnt["ga"])
        g.affine_select(b["msk"][:, 512:640], b["msk"][:, 640:768],
                        [[-1, 128]], ALU.is_equal, 0.0,
                        channel_multiplier=1).then_inc(cx.sem("ga"), 1)
        cx.bump("ga")
        g.wait_ge(cx.sem("ga"), cx.cnt["ga"])
        return g.affine_select(b["idf"][:, :], b["ones1"][:, :],
                               [[-1, 128]], ALU.is_equal, 0.0,
                               channel_multiplier=1)
    cx.on("g", gid)
    cx.barrier()

    if not cx.buf.get("l1_se_done"):
        cx.buf["l1_se_done"] = True
        gb = cx.dma("g", wsl[:, 0:1024],
                    W["l1f_gsrow"][0:1, :].broadcast_to([128, 1024]))
        cx.wait("v", "gs", gb)
        gs0 = cx.cnt["gs"]
        pa0 = cx.cnt["pa"]
        pd0 = cx.cnt["pd"]
        sa0 = cx.cnt["sa"]
        pe0 = cx.cnt["pe_"]
        pf0 = cx.cnt["pf"]
        ab0 = cx.cnt["ab"]
        ds0 = cx.cnt["ds"]

        def seg(g):
            prev = gs0
            for st in range(32):
                g.wait_ge(cx.sem("gs"), prev)
                if st >= 2:
                    g.wait_ge(cx.sem("pa"), pa0 + (st - 2) * 6 + 6)
                g.dma_start(ar1[:, (st % 2) * 1024:(st % 2) * 1024 + 1024],
                            W["l1f_semb"][st * 128:(st + 1) * 128, :]
                            ).then_inc(cx.sem("gs"), 16)
                prev = gs0 + 16 * (st + 1)
        cx.on("g", seg)
        cx.bump("gs", 16 * 32)

        def sev(v):
            pa = pa0
            ab = ab0
            sqs = ar3.bitcast(F32)[:, 0:512]
            for st in range(32):
                tile = ar1[:, (st % 2) * 1024:(st % 2) * 1024 + 1024]
                v.wait_ge(cx.sem("gs"), gs0 + 16 * (st + 1))
                v.tensor_tensor(sqs, tile[:, 0:512],
                                tile[:, 0:512], ALU.mult).then_inc(
                    cx.sem("pa"), 1)
                pa += 1
                v.wait_ge(cx.sem("pa"), pa)
                v.reduce_sum(tsc[:, 496:497], sqs,
                             axis=AX.X).then_inc(cx.sem("pa"), 1)
                pa += 1
                v.wait_ge(cx.sem("pa"), pa)
                v.tensor_tensor(sqs, tile[:, 512:1024],
                                tile[:, 512:1024], ALU.mult).then_inc(
                    cx.sem("pa"), 1)
                pa += 1
                v.wait_ge(cx.sem("pa"), pa)
                v.reduce_sum(tsc[:, 497:498], sqs,
                             axis=AX.X).then_inc(cx.sem("pa"), 1)
                pa += 1
                v.wait_ge(cx.sem("pa"), pa)
                v.tensor_tensor(tsc[:, 498:499], tsc[:, 496:497],
                                tsc[:, 497:498], ALU.add).then_inc(
                    cx.sem("pa"), 1)
                pa += 1
                v.wait_ge(cx.sem("pa"), pa)
                v.tensor_scalar_max(tsc[:, 499:500], tsc[:, 498:499],
                                    1e-24).then_inc(cx.sem("pd"), 1)
                v.wait_ge(cx.sem("sa"), sa0 + st * 3 + 3)
                if st >= 2:
                    v.wait_ge(cx.sem("pf"), pf0 + (st - 2) * 8 + 8)
                v.scalar_tensor_tensor(
                    ar1[:, 2048 + (st % 2) * 1024:
                        2048 + (st % 2) * 1024 + 1024],
                    tile, tsc[:, 502:503], wsl[:, 0:1024],
                    ALU.mult, ALU.mult).then_inc(cx.sem("pa"), 1)
                pa += 1
                v.wait_ge(cx.sem("pa"), pa)
                v.wait_ge(cx.sem("pf"), pf0 + (st + 1) * 8)
                if st >= 16:
                    v.wait_ge(cx.sem("ds"), ds0 + 16 * 8 * (st // 16))
                last = None
                for k in range(DCH):
                    last = v.tensor_copy(
                        ar2[:, k * 2048 + (st % 16) * 128:
                            k * 2048 + (st % 16) * 128 + 128],
                        p[6 + st % 2].bitcast(BF16)[:, k * 128:(k + 1) * 128])
                last.then_inc(cx.sem("ab"), 1)
        cx.on("v", sev)
        cx.bump("pa", 32 * 6)
        cx.bump("pd", 32)
        cx.bump("ab", 32)

        def ses(s):
            for st in range(32):
                s.wait_ge(cx.sem("pd"), pd0 + st + 1)
                s.activation(tsc[:, 500:501], tsc[:, 499:500],
                             AF.Ln).then_inc(cx.sem("sa"), 1)
                s.wait_ge(cx.sem("sa"), sa0 + st * 3 + 1)
                s.activation(tsc[:, 501:502], tsc[:, 500:501], AF.Exp,
                             scale=-0.5).then_inc(cx.sem("sa"), 1)
                s.wait_ge(cx.sem("sa"), sa0 + st * 3 + 2)
                s.mul(tsc[:, 502:503], tsc[:, 501:502],
                      float(np.sqrt(D))).then_inc(cx.sem("sa"), 1)
        cx.on("s", ses)
        cx.bump("sa", 32 * 3)

        def set_(t):
            for st in range(32):
                t.wait_ge(cx.sem("pa"), pa0 + st * 6 + 6)
                if st >= 2:
                    t.wait_ge(cx.sem("ab"), ab0 + (st - 2) + 1)
                for k in range(DCH):
                    t.transpose(
                        p[6 + st % 2].bitcast(BF16)[:, k * 128:(k + 1) * 128],
                        ar1[:, 2048 + (st % 2) * 1024 + k * 128:
                            2048 + (st % 2) * 1024 + (k + 1) * 128],
                        b["msk"][:, 512:640]).then_inc(cx.sem("pf"), 1)
        cx.on("t", set_)
        cx.bump("pf", 32 * 8)

        def seq(q):
            for half in range(2):
                q.wait_ge(cx.sem("ab"), ab0 + (half + 1) * 16)
                for k in range(DCH):
                    if half > 0 or k > 0:
                        q.wait_ge(cx.sem("ds"), cx.cnt["ds"] - ds0 + ds0)
                    q.dma_start(
                        W["l1_se"][:, k * 4096 + half * 2048:
                                   k * 4096 + half * 2048 + 2048],
                        ar2[:, k * 2048:(k + 1) * 2048]).then_inc(
                        cx.sem("ds"), 16)
                    cx.bump("ds", 16)
        cx.on("q", seq)
        cx.wait("q", "ds", cx.cnt["ds"])
        cx.seal("q", "ds")
        cx.seal("v", "ab")
        cx.barrier()
    else:
        cx.barrier()

    pf0 = cx.cnt["pf"]
    pa0 = cx.cnt["pa"]

    def xtt(t):
        for i in range(NTT * DCH):
            tt, k = i // DCH, i % DCH
            if i >= 2:
                t.wait_ge(cx.sem("pa"), pa0 + (i - 2) + 1)
            t.transpose(p[6 + i % 2].bitcast(BF16)[:, 0:128],
                        xnf_of(k)[:, tt * 128:(tt + 1) * 128],
                        b["msk"][:, 512:640]).then_inc(cx.sem("pf"), 1)
    cx.on("t", xtt)
    cx.bump("pf", NTT * DCH)

    def xtv(v):
        for i in range(NTT * DCH):
            tt, k = i // DCH, i % DCH
            v.wait_ge(cx.sem("pf"), pf0 + i + 1)
            v.tensor_copy(ar3[:, tt * 1024 + k * 128:tt * 1024 +
                              (k + 1) * 128],
                          p[6 + i % 2].bitcast(BF16)[:, 0:128]).then_inc(
                cx.sem("pa"), 1)
    cx.on("v", xtv)
    cx.bump("pa", NTT * DCH)
    cx.seal("v", "pa")
    cx.barrier()

    gs0 = cx.cnt["gs"]
    pc0 = cx.cnt["pc"]
    av0 = cx.cnt["av"]
    pa0 = cx.cnt["pa"]
    ds0 = cx.cnt["ds"]

    def pci(c, s, t, half):
        return pc0 + c * 64 + s * 32 + t * 2 + half + 1

    def avi(c, s, t):
        return av0 + c * 32 + s * 16 + t + 1

    def lg1(g):
        prev = gs0
        for c in range(NCHK):
            for hs in range(4):
                s_, half = hs // 2, hs % 2
                gi = c * 4 + hs
                g.wait_ge(cx.sem("gs"), prev)
                if gi >= 2:
                    pc_, ps, ph_ = divmod(gi - 2, 4)[0], ((gi - 2) % 4) // 2, \
                        (gi - 2) % 2
                    g.wait_ge(cx.sem("pc"), pci((gi - 2) // 4, ps, 15, ph_))
                for k4 in range(4):
                    k = half * 4 + k4
                    g.dma_start(
                        wsl[:, (half * 4 + k4 % 4) * 512 * 0 +
                            (half * 2048) + k4 * 512:
                            (half * 2048) + (k4 + 1) * 512],
                        W["l1_se"][:, k * 4096 + c * 1024 + s_ * 512:
                                   k * 4096 + c * 1024 + s_ * 512 + 512]
                    ).then_inc(cx.sem("gs"), 16)
                    prev = gs0 + 16 * (gi * 4 + k4 + 1)
    cx.on("g", lg1)
    cx.bump("gs", 16 * 64)

    def lt1(t):
        for c in range(NCHK):
            for s_ in range(2):
                for half in range(2):
                    t.wait_ge(cx.sem("gs"),
                              gs0 + 16 * ((c * 4 + s_ * 2 + half) * 4 + 4))
                for t_ in range(NTT):
                    seq = c * 32 + s_ * 16 + t_
                    if seq >= 8:
                        pr = seq - 8
                        t.wait_ge(cx.sem("av"),
                                  avi(pr // 32, (pr % 32) // 16, pr % 16))
                    bank = p[t_ % 8]
                    last = None
                    for k in range(DCH):
                        half = k // 4
                        last = t.matmul(
                            bank[:, 0:512],
                            xnf_of(k)[:, t_ * 128:(t_ + 1) * 128],
                            wsl[:, half * 2048 + (k % 4) * 512:
                                half * 2048 + (k % 4) * 512 + 512],
                            start=(k == 0), stop=(k == DCH - 1))
                        if k == 3:
                            last.then_inc(cx.sem("pc"), 1)
                    last.then_inc(cx.sem("pc"), 1)
    cx.on("t", lt1)
    cx.bump("pc", NCHK * 64)

    def lv1(v):
        pa = pa0
        for c in range(NCHK):
            for s_ in range(2):
                for t_ in range(NTT):
                    v.wait_ge(cx.sem("pc"), pci(c, s_, t_, 1))
                    if c >= 1 and s_ == 0 and t_ == 0:
                        v.wait_ge(cx.sem("ds"), ds0 + 16 * c)
                    bank = p[t_ % 8]
                    v.tensor_copy(ar1[:, t_ * 1024 + s_ * 512:
                                      t_ * 1024 + s_ * 512 + 512],
                                  bank[:, 0:512]).then_inc(cx.sem("pa"), 1)
                    pa += 1
                    v.wait_ge(cx.sem("pa"), pa)
                    mreg = ar2f[:, c * 1024 + s_ * 512:
                                c * 1024 + s_ * 512 + 512]
                    if t_ == 0:
                        v.tensor_copy(mreg, bank[:, 0:512]).then_inc(
                            cx.sem("pa"), 1)
                    else:
                        v.tensor_tensor(mreg, mreg, bank[:, 0:512],
                                        ALU.max).then_inc(cx.sem("pa"), 1)
                    pa += 1
                    v.wait_ge(cx.sem("pa"), pa)
                    v.reduce_max(tsc[:, 0:1], bank[:, 0:512],
                                 axis=AX.X).then_inc(cx.sem("pa"), 1)
                    pa += 1
                    v.wait_ge(cx.sem("pa"), pa)
                    if c == 0 and s_ == 0:
                        v.tensor_copy(tsc[:, 16 + t_:17 + t_],
                                      tsc[:, 0:1]).then_inc(cx.sem("av"), 1)
                    else:
                        v.tensor_tensor(tsc[:, 16 + t_:17 + t_],
                                        tsc[:, 16 + t_:17 + t_],
                                        tsc[:, 0:1], ALU.max).then_inc(
                            cx.sem("av"), 1)
    cx.on("v", lv1)
    cx.bump("pa", NCHK * 32 * 3)
    cx.bump("av", NCHK * 32)

    def lq1(q):
        for c in range(NCHK):
            q.wait_ge(cx.sem("av"), avi(c, 1, 15))
            if c >= 1:
                q.wait_ge(cx.sem("ds"), ds0 + 16 * c)
            q.dma_start(W["l1_l"][:, c * 16384:(c + 1) * 16384],
                        ar1[:, :]).then_inc(cx.sem("ds"), 16)
    cx.on("q", lq1)
    cx.bump("ds", 16 * NCHK)
    cx.wait("q", "ds", cx.cnt["ds"])
    cx.seal("q", "ds")
    cx.seal("v", "av")
    cx.barrier()

    pf0 = cx.cnt["pf"]
    pa0 = cx.cnt["pa"]

    def sta_t(t):
        for j in range(32):
            if j >= 2:
                t.wait_ge(cx.sem("pa"), pa0 + (j - 2) + 1)
            t.transpose(p[6 + j % 2][:, 0:128],
                        ar2f[:, j * 128:(j + 1) * 128],
                        b["idf"][:, :]).then_inc(cx.sem("pf"), 1)
        t.wait_ge(cx.sem("pa"), pa0 + 32 + 1)
        t.transpose(p[5][0:32, 0:128], tsc[:, 64:96],
                    b["idf"][:, :]).then_inc(cx.sem("pf"), 1)
    cx.on("t", sta_t)
    cx.bump("pf", 33)

    def sta_v(v):
        pa = pa0
        for j in range(32):
            v.wait_ge(cx.sem("pf"), pf0 + j + 1)
            v.reduce_max(tsc[:, 64 + j:65 + j], p[6 + j % 2][:, 0:128],
                         axis=AX.X).then_inc(cx.sem("pa"), 1)
            pa += 1
            v.wait_ge(cx.sem("pa"), pa)
        v.tensor_scalar_mul(tsc[:, 32:48], tsc[:, 16:32],
                            -1.0).then_inc(cx.sem("pa"), 1)
        pa += 1
        v.wait_ge(cx.sem("pa"), pa)
        v.wait_ge(cx.sem("pf"), pf0 + 33)
        v.tensor_copy(tsc[0:32, 96:224], p[5][0:32, 0:128]).then_inc(
            cx.sem("pa"), 1)
    cx.on("v", sta_v)
    cx.bump("pa", 34)
    cx.wait("q", "pa", cx.cnt["pa"])
    smw = cx.dma("q", W["l1_sm_in"][:, :], tsc[0:32, 96:224])
    cx.wait("g", "ds", smw)

    def sta_g(g):
        g.collective_compute(
            "AllReduce", ALU.max, replica_groups=RG,
            ins=[W["l1_sm_in"].ap().opt()],
            outs=[W["l1_sm_out"].ap().opt()]).then_inc(cx.sem("cc"), 1)
        cx.bump("cc")
    cx.on("g", sta_g)
    cx.seal("v", "pa")
    cx.barrier()

    cc_sm = cx.cnt["cc"]
    ds0 = cx.cnt["ds"]
    gs0 = cx.cnt["gs"]
    pa0 = cx.cnt["pa"]
    pd0 = cx.cnt["pd"]
    sa0 = cx.cnt["sa"]
    pe0 = cx.cnt["pe_"]
    pc0 = cx.cnt["pc"]
    av0 = cx.cnt["av"]

    def q_ld(c):
        return ds0 + c * (32 + 16 * 16) + 16

    def q_bc(c):
        return ds0 + c * (32 + 16 * 16) + 32

    def q_sp(c, idx):
        return ds0 + c * (32 + 16 * 16) + 32 + 16 * (idx + 1)

    def c1_q(q):
        prev = ds0
        for c in range(NCHK):
            q.wait_ge(cx.sem("cc"), cc_sm)
            if prev > ds0:
                q.wait_ge(cx.sem("ds"), prev)
            if c >= 1:
                q.wait_ge(cx.sem("pc"), pc0 + c * 16)
            q.dma_start(ar1[:, :],
                        W["l1_l"][:, c * 16384:(c + 1) * 16384]).then_inc(
                cx.sem("ds"), 16)
            q.wait_ge(cx.sem("ds"), q_ld(c))
            if c >= 1:
                q.wait_ge(cx.sem("pd"), pd0 + c * NTT)
            q.dma_start(ar2f[:, 0:1024],
                        W["l1_sm_out"][c * 8:(c + 1) * 8, :].rearrange(
                            "j q2 -> () (j q2)").broadcast_to([128, 1024])
                        ).then_inc(cx.sem("ds"), 16)
            prev = q_bc(c)
            for idx in range(16):
                q.wait_ge(cx.sem("av"), av0 + c * 16 + idx + 1)
                q.wait_ge(cx.sem("ds"), prev)
                dch, s_ = idx // 2, idx % 2
                q.dma_start(
                    W["l1_sp_in"][:, dch * 4096 + c * 1024 + s_ * 512:
                                  dch * 4096 + c * 1024 + s_ * 512 + 512],
                    xnff[:, idx * 512:(idx + 1) * 512]).then_inc(
                    cx.sem("ds"), 16)
                prev = q_sp(c, idx)
    cx.on("q", c1_q)
    cx.bump("ds", NCHK * (32 + 16 * 16))

    def c1_v(v):
        pa = pa0
        for c in range(NCHK):
            v.wait_ge(cx.sem("ds"), q_bc(c))
            for t_ in range(NTT):
                v.tensor_tensor(ar1[:, t_ * 1024:(t_ + 1) * 1024],
                                ar1[:, t_ * 1024:(t_ + 1) * 1024],
                                ar2f[:, 0:1024], ALU.subtract).then_inc(
                    cx.sem("pd"), 1)
                v.wait_ge(cx.sem("sa"), sa0 + c * NTT + t_ + 1)
                sreg = ar2f[:, 4096 + c * 1024:4096 + (c + 1) * 1024]
                if t_ == 0:
                    v.tensor_copy(sreg,
                                  ar1[:, t_ * 1024:(t_ + 1) * 1024]
                                  ).then_inc(cx.sem("pe_"), 1)
                else:
                    v.tensor_tensor(sreg, sreg,
                                    ar1[:, t_ * 1024:(t_ + 1) * 1024],
                                    ALU.add).then_inc(cx.sem("pe_"), 1)
            for idx in range(16):
                dch, s_ = idx // 2, idx % 2
                v.wait_ge(cx.sem("pc"), pc0 + c * 16 + s_ * 8 + dch + 1)
                if c >= 1:
                    v.wait_ge(cx.sem("ds"), q_sp(c - 1, idx))
                v.tensor_copy(xnff[:, idx * 512:(idx + 1) * 512],
                              p[dch][:, 0:512]).then_inc(cx.sem("av"), 1)
    cx.on("v", c1_v)
    cx.bump("pd", NCHK * NTT)
    cx.bump("pe_", NCHK * NTT)
    cx.bump("av", NCHK * 16)

    def c1_s(s):
        for c in range(NCHK):
            for t_ in range(NTT):
                s.wait_ge(cx.sem("pd"), pd0 + c * NTT + t_ + 1)
                s.activation(ar1[:, t_ * 1024:(t_ + 1) * 1024],
                             ar1[:, t_ * 1024:(t_ + 1) * 1024],
                             AF.Exp).then_inc(cx.sem("sa"), 1)
    cx.on("s", c1_s)
    cx.bump("sa", NCHK * NTT)

    def c1_t(t):
        for c in range(NCHK):
            for s_ in range(2):
                for dch in range(DCH):
                    idx = dch * 2 + s_
                    seq = c * 16 + (s_ * 8 + dch)
                    if seq >= 8:
                        pr = seq - 8
                        prc, pri = pr // 16, pr % 16
                        pidx = (pri % 8) * 2 + pri // 8
                        t.wait_ge(cx.sem("av"), av0 + prc * 16 + pidx + 1)
                    last = None
                    for t_ in range(NTT):
                        t.wait_ge(cx.sem("pe_"), pe0 + c * NTT + t_ + 1)
                        last = t.matmul(
                            p[dch][:, 0:512],
                            ar3[:, t_ * 1024 + dch * 128:
                                t_ * 1024 + (dch + 1) * 128],
                            ar1[:, t_ * 1024 + s_ * 512:
                                t_ * 1024 + s_ * 512 + 512],
                            start=(t_ == 0), stop=(t_ == NTT - 1))
                    last.then_inc(cx.sem("pc"), 1)
    cx.on("t", c1_t)
    cx.bump("pc", NCHK * 16)
    cx.wait("q", "ds", cx.cnt["ds"])
    cx.seal("q", "ds")
    cx.seal("v", "av")
    cx.barrier()

    pf0 = cx.cnt["pf"]
    pa0 = cx.cnt["pa"]
    ds0 = cx.cnt["ds"]

    def stb_g(g):
        g.wait_ge(cx.sem("ds"), ds0)
        g.collective_compute(
            "AllReduce", ALU.add, replica_groups=RG,
            ins=[W["l1_sp_in"].ap().opt()],
            outs=[W["l1_sp_out"].ap().opt()]).then_inc(cx.sem("cc"), 1)
        cx.bump("cc")
        g.wait_ge(cx.sem("ds"), ds0 + 16)
        g.collective_compute(
            "AllReduce", ALU.add, replica_groups=RG,
            ins=[W["l1_sd_in"].ap().opt()],
            outs=[W["l1_sd_out"].ap().opt()]).then_inc(cx.sem("cc"), 1)
        cx.bump("cc")
    cc_sp = cx.cnt["cc"] + 1
    cc_sd = cx.cnt["cc"] + 2

    def stb_t(t):
        for j in range(32):
            if j >= 2:
                t.wait_ge(cx.sem("pa"), pa0 + (j - 2) + 1)
            t.transpose(p[6 + j % 2][:, 0:128],
                        ar2f[:, 4096 + j * 128:4096 + (j + 1) * 128],
                        b["idf"][:, :]).then_inc(cx.sem("pf"), 1)
        t.wait_ge(cx.sem("pa"), pa0 + 33)
        t.transpose(p[5][0:32, 0:128], tsc[:, 64:96],
                    b["idf"][:, :]).then_inc(cx.sem("pf"), 1)
    cx.on("t", stb_t)
    cx.bump("pf", 33)

    def stb_v(v):
        pa = pa0
        for j in range(32):
            v.wait_ge(cx.sem("pf"), pf0 + j + 1)
            v.reduce_sum(tsc[:, 64 + j:65 + j], p[6 + j % 2][:, 0:128],
                         axis=AX.X).then_inc(cx.sem("pa"), 1)
            pa += 1
            v.wait_ge(cx.sem("pa"), pa)
        v.engine_nop().then_inc(cx.sem("pa"), 1)
        pa += 1
        v.wait_ge(cx.sem("pf"), pf0 + 33)
        v.tensor_copy(tsc[0:32, 96:224], p[5][0:32, 0:128]).then_inc(
            cx.sem("pa"), 1)
        pa += 1
        v.wait_ge(cx.sem("pa"), pa)
        v.wait_ge(cx.sem("ds"), ds0 + 32)
        v.reciprocal(tsc[0:32, 352:480],
                     tsc[0:32, 224:352]).then_inc(cx.sem("pa"), 1)
    cx.on("v", stb_v)
    cx.bump("pa", 35)
    cx.on("g", stb_g)
    cx.wait("q", "pa", pa0 + 34)
    sdw = cx.dma("q", W["l1_sd_in"][:, :], tsc[0:32, 96:224])
    cx.wait("q", "ds", sdw)
    cx.wait("q", "cc", cc_sd)
    cx.dma("q", tsc[0:32, 224:352], W["l1_sd_out"][:, :])
    cx.wait("q", "pa", cx.cnt["pa"])
    drw = cx.dma("q", W["l1_dr"][:, :], tsc[0:32, 352:480])
    cx.wait("q", "ds", cx.cnt["ds"])
    cx.seal("q", "ds")
    cx.seal("v", "pa")
    cx.barrier()

    cc_pp0 = cx.cnt["cc"]
    gs0 = cx.cnt["gs"]
    pa0 = cx.cnt["pa"]
    pd0 = cx.cnt["pd"]
    pc0 = cx.cnt["pc"]
    av0 = cx.cnt["av"]

    bw = cx.dma("q", ar2f[:, 0:4096],
                W["l1_dr"].rearrange("j q2 -> () (j q2)").broadcast_to(
                    [128, 4096]))
    cx.wait("v", "ds", bw)
    ds0 = cx.cnt["ds"]

    def h_g(g):
        g.wait_ge(cx.sem("cc"), cc_sp)
        prev = gs0
        for dch in range(DCH):
            g.wait_ge(cx.sem("gs"), prev)
            dst = (xnf if dch < 4 else ar1)[:, (dch % 4) * 4096:
                                            (dch % 4) * 4096 + 4096]
            g.dma_start(dst, W["l1_sp_out"][:, dch * 4096:
                                            (dch + 1) * 4096]).then_inc(
                cx.sem("gs"), 16)
            prev = gs0 + 16 * (dch + 1)
        for hch in range(DCH):
            g.wait_ge(cx.sem("gs"), prev)
            if hch >= 2:
                g.wait_ge(cx.sem("pc"), pc0 + (hch - 1) * 8)
            for k in range(DCH):
                if k > 0:
                    g.wait_ge(cx.sem("gs"), prev)
                g.dma_start(
                    wsl[:, (hch % 2) * 1024 + k * 128:
                        (hch % 2) * 1024 + (k + 1) * 128],
                    W["l1f_w1_loc"][k * 128:(k + 1) * 128,
                                    hch * 128:(hch + 1) * 128]).then_inc(
                    cx.sem("gs"), 16)
                prev = gs0 + 16 * (8 + hch * 8 + k + 1)
        for dh in range(2):
            g.wait_ge(cx.sem("gs"), prev)
            if dh == 0:
                g.wait_ge(cx.sem("pc"), pc0 + 64)
            else:
                g.wait_ge(cx.sem("pc"), pc0 + 64 + 32)
            for k2 in range(DCH):
                if k2 > 0:
                    g.wait_ge(cx.sem("gs"), prev)
                g.dma_start(
                    wsl[:, k2 * 512:(k2 + 1) * 512],
                    W["l1f_w2_loc"][k2 * 128:(k2 + 1) * 128,
                                    dh * 512:(dh + 1) * 512]).then_inc(
                    cx.sem("gs"), 16)
                prev = gs0 + 16 * (8 + 64 + dh * 8 + k2 + 1)
    cx.on("g", h_g)
    cx.bump("gs", 16 * (8 + 64 + 16))

    def h_v(v):
        pa = pa0
        for dch in range(DCH):
            v.wait_ge(cx.sem("gs"), gs0 + 16 * (dch + 1))
            dst = (xnf if dch < 4 else ar1)[:, (dch % 4) * 4096:
                                            (dch % 4) * 4096 + 4096]
            v.tensor_tensor(dst, dst, ar2f[:, 0:4096],
                            ALU.mult).then_inc(cx.sem("pa"), 1)
            pa += 1
            v.wait_ge(cx.sem("pa"), pa)
        v.engine_nop().then_inc(cx.sem("ph"), 1)
        for i in range(64):
            st, dh = i % 32, i // 32
            v.wait_ge(cx.sem("pc"), pc0 + 64 + dh * 32 + st + 1)
            if i >= 16:
                v.wait_ge(cx.sem("ds"), ds0 + 16 * (i - 16 + 1))
            v.tensor_copy(xnff[:, (i % 16) * 512:(i % 16) * 512 + 512],
                          p[(st * 2 + dh) % 8][:, 0:512]).then_inc(
                cx.sem("av"), 1)
    cx.on("v", h_v)
    cx.bump("pa", DCH)
    cx.bump("ph", 1)
    cx.bump("av", 64)
    ph_scale = cx.cnt["ph"]

    def h_t(t):
        for hch in range(DCH):
            t.wait_ge(cx.sem("gs"), gs0 + 16 * (8 + hch * 8 + 8))
            t.wait_ge(cx.sem("pa"), pa0 + DCH)
            for sgrp in range(8):
                if hch >= 1:
                    t.wait_ge(cx.sem("pd"), pd0 + (hch - 1) * 8 + sgrp + 1)
                last = None
                for k in range(DCH):
                    rhs = (xnf if k < 4 else ar1)[
                        :, (k % 4) * 4096 + sgrp * 512:
                        (k % 4) * 4096 + sgrp * 512 + 512]
                    last = t.matmul(
                        p[sgrp][:, 0:512],
                        wsl[:, (hch % 2) * 1024 + k * 128:
                            (hch % 2) * 1024 + (k + 1) * 128],
                        rhs, start=(k == 0), stop=(k == DCH - 1))
                last.then_inc(cx.sem("pc"), 1)
        t.wait_ge(cx.sem("pd"), pd0 + 64)
        for dh in range(2):
            t.wait_ge(cx.sem("gs"), gs0 + 16 * (8 + 64 + dh * 8 + 8))
            for st in range(32):
                bank = (st * 2 + dh) % 8
                if st >= 4:
                    t.wait_ge(cx.sem("av"), av0 + dh * 32 + (st - 4) + 1)
                else:
                    t.wait_ge(cx.sem("pd"), pd0 + 56 + bank + 1)
                last = None
                for k2 in range(DCH):
                    lsrc = (ar3 if k2 < 4 else ar2)[
                        :, (k2 % 4) * 4096 + st * 128:
                        (k2 % 4) * 4096 + st * 128 + 128]
                    last = t.matmul(
                        p[(st * 2 + dh) % 8][:, 0:512],
                        lsrc, wsl[:, k2 * 512:(k2 + 1) * 512],
                        start=(k2 == 0), stop=(k2 == DCH - 1))
                last.then_inc(cx.sem("pc"), 1)
    cx.on("t", h_t)
    cx.bump("pc", 64 + 64)

    def h_s(s):
        for hch in range(DCH):
            for sgrp in range(8):
                s.wait_ge(cx.sem("pc"), pc0 + hch * 8 + sgrp + 1)
                if hch >= 4:
                    s.wait_ge(cx.sem("ph"), ph_scale)
                dst = (ar3 if hch < 4 else ar2)[
                    :, (hch % 4) * 4096 + sgrp * 512:
                    (hch % 4) * 4096 + sgrp * 512 + 512]
                s.activation(dst, p[sgrp][:, 0:512],
                             AF.Gelu).then_inc(cx.sem("pd"), 1)
    cx.on("s", h_s)
    cx.bump("pd", 64)

    def h_q(q):
        prev = ds0
        for i in range(64):
            st, dh = i % 32, i // 32
            q.wait_ge(cx.sem("av"), av0 + i + 1)
            q.wait_ge(cx.sem("ds"), prev)
            q.dma_start(
                W["l1_pp_in"][:, st * 1024 + dh * 512:
                              st * 1024 + dh * 512 + 512],
                xnff[:, (i % 16) * 512:(i % 16) * 512 + 512]).then_inc(
                cx.sem("ds"), 16)
            prev = ds0 + 16 * (i + 1)
    cx.on("q", h_q)
    cx.bump("ds", 16 * 64)

    def pp_g(g):
        g.wait_ge(cx.sem("ds"), ds0 + 16 * 64)
        g.collective_compute(
            "AllReduce", ALU.add, replica_groups=RG,
            ins=[W["l1_pp_in"].ap().opt()],
            outs=[W["l1_pp_out"].ap().opt()]).then_inc(cx.sem("cc"), 1)
        cx.bump("cc")
    cx.on("g", pp_g)
    cc_pp = cx.cnt["cc"]
    cx.seal("v", "av")
    cx.barrier()

    ds0 = cx.cnt["ds"]
    pa0 = cx.cnt["pa"]
    pd0 = cx.cnt["pd"]
    sa0 = cx.cnt["sa"]
    pf0 = cx.cnt["pf"]

    def bi_q(q):
        prev = ds0
        for c in range(NCHK):
            q.wait_ge(cx.sem("ds"), prev)
            if c >= 1:
                q.wait_ge(cx.sem("pa"), pa0 + c * NTT * 2)
            q.dma_start(ar1[:, :],
                        W["l1_l"][:, c * 16384:(c + 1) * 16384]).then_inc(
                cx.sem("ds"), 16)
            prev = ds0 + 16 * (2 * c + 1)
            q.wait_ge(cx.sem("pa"), pa0 + (c + 1) * NTT * 2)
            q.wait_ge(cx.sem("ds"), prev)
            q.dma_start(W["l1_l"][:, c * 16384:(c + 1) * 16384],
                        ar1[:, :]).then_inc(cx.sem("ds"), 16)
            prev = ds0 + 16 * (2 * c + 2)
        q.wait_ge(cx.sem("pa"), pa0 + NCHK * NTT * 2 + 3)
        q.dma_start(W["l1_rr"][:, :], tsc[0:16, 64:192]).then_inc(
            cx.sem("ds"), 16)
        q.wait_ge(cx.sem("ds"), ds0 + 16 * (2 * NCHK + 1))
        q.dma_start(wslf[:, 0:2048],
                    W["l1_rr"].rearrange("t q2 -> () (t q2)").broadcast_to(
                        [128, 2048])).then_inc(cx.sem("ds"), 16)
        q.wait_ge(cx.sem("ds"), ds0 + 16 * (2 * NCHK + 2))
    cx.on("q", bi_q)
    cx.bump("ds", 16 * (2 * NCHK + 2))

    def bi_s(s):
        for c in range(NCHK):
            s.wait_ge(cx.sem("ds"), ds0 + 16 * (2 * c + 1))
            for t_ in range(NTT):
                if t_ >= 1:
                    s.wait_ge(cx.sem("sa"), sa0 + c * NTT + t_)
                s.activation(ar1[:, t_ * 1024:(t_ + 1) * 1024],
                             ar1[:, t_ * 1024:(t_ + 1) * 1024],
                             AF.Exp, bias=tsc[:, 32 + t_:33 + t_]).then_inc(
                    cx.sem("sa"), 1)
    cx.on("s", bi_s)
    cx.bump("sa", NCHK * NTT)

    def bi_v(v):
        pa = pa0
        for c in range(NCHK):
            for t_ in range(NTT):
                v.wait_ge(cx.sem("sa"), sa0 + c * NTT + t_ + 1)
                v.reduce_sum(tsc[:, 0:1], ar1[:, t_ * 1024:(t_ + 1) * 1024],
                             axis=AX.X).then_inc(cx.sem("pa"), 1)
                pa += 1
                v.wait_ge(cx.sem("pa"), pa)
                if c == 0:
                    v.tensor_copy(tsc[:, 48 + t_:49 + t_],
                                  tsc[:, 0:1]).then_inc(cx.sem("pa"), 1)
                else:
                    v.tensor_tensor(tsc[:, 48 + t_:49 + t_],
                                    tsc[:, 48 + t_:49 + t_], tsc[:, 0:1],
                                    ALU.add).then_inc(cx.sem("pa"), 1)
                pa += 1
                v.wait_ge(cx.sem("pa"), pa)
        v.reciprocal(tsc[:, 0:16], tsc[:, 48:64]).then_inc(cx.sem("pa"), 1)
        pa += 1
        v.wait_ge(cx.sem("pa"), pa)
        v.tensor_scalar_mul(tsc[:, 0:16], tsc[:, 0:16], 0.5).then_inc(
            cx.sem("pd"), 1)
        v.wait_ge(cx.sem("pf"), pf0 + 1)
        v.tensor_copy(tsc[0:16, 64:192], p[5][0:16, 0:128]).then_inc(
            cx.sem("pa"), 1)
        pa += 1
        v.wait_ge(cx.sem("pa"), pa)
        v.engine_nop().then_inc(cx.sem("pa"), 1)
    cx.on("v", bi_v)
    cx.bump("pa", NCHK * NTT * 2 + 3)
    cx.bump("pd", 1)

    def bi_t(t):
        t.wait_ge(cx.sem("pd"), pd0 + 1)
        t.transpose(p[5][0:16, 0:128], tsc[:, 0:16],
                    b["idf"][:, :]).then_inc(cx.sem("pf"), 1)
    cx.on("t", bi_t)
    cx.bump("pf", 1)
    cx.wait("q", "ds", cx.cnt["ds"])
    cx.seal("q", "ds")
    cx.seal("v", "pa")
    cx.barrier()

    ds0 = cx.cnt["ds"]
    gs0 = cx.cnt["gs"]
    pa0 = cx.cnt["pa"]
    pf0 = cx.cnt["pf"]
    pc0 = cx.cnt["pc"]
    ab0 = cx.cnt["ab"]

    def bii_q(q):
        prev = ds0
        for c in range(NCHK):
            q.wait_ge(cx.sem("ds"), prev)
            if c >= 1:
                q.wait_ge(cx.sem("pf"), pf0 + c * 128)
            q.dma_start(ar1[:, :],
                        W["l1_l"][:, c * 16384:(c + 1) * 16384]).then_inc(
                cx.sem("ds"), 16)
            prev = ds0 + 16 * (c + 1)
    cx.on("q", bii_q)
    cx.bump("ds", 16 * NCHK)

    def bii_g(g):
        g.wait_ge(cx.sem("cc"), cc_pp)
        prev = gs0
        for c in range(NCHK):
            for st in range(DCH):
                g.wait_ge(cx.sem("gs"), prev)
                if c >= 1 and st == 0:
                    g.wait_ge(cx.sem("pc"), pc0 + c * 32)
                g.dma_start(
                    xnf[:, st * 1024:(st + 1) * 1024],
                    W["l1_pp_out"][:, (c * 8 + st) * 1024:
                                   (c * 8 + st + 1) * 1024]).then_inc(
                    cx.sem("gs"), 16)
                prev = gs0 + 16 * (c * 8 + st + 1)
    cx.on("g", bii_g)
    cx.bump("gs", 16 * NCHK * DCH)

    def bii_t(t):
        for c in range(NCHK):
            t.wait_ge(cx.sem("ds"), ds0 + 16 * (c + 1))
            for st in range(DCH):
                for t_ in range(NTT):
                    i = st * NTT + t_
                    seq = c * 128 + i
                    if seq >= 2:
                        t.wait_ge(cx.sem("pa"), pa0 + seq - 2 + 1)
                    if c >= 1 and i < 2:
                        t.wait_ge(cx.sem("ab"), ab0 + c * 4 * 8)
                    t.transpose(p[6 + i % 2].bitcast(BF16)[:, 0:128],
                                ar1[:, t_ * 1024 + st * 128:
                                    t_ * 1024 + (st + 1) * 128],
                                b["msk"][:, 512:640]).then_inc(
                        cx.sem("pf"), 1)
            for tg in range(4):
                for dch in range(DCH):
                    gi = c * 32 + tg * 8 + dch
                    if gi >= 8:
                        t.wait_ge(cx.sem("ab"), ab0 + gi - 8 + 1)
                    last = None
                    for st in range(DCH):
                        t.wait_ge(cx.sem("pa"),
                                  pa0 + c * 128 + st * NTT + NTT)
                        t.wait_ge(cx.sem("gs"), gs0 + 16 * (c * 8 + st + 1))
                        last = t.matmul(
                            p[dch][:, 0:512],
                            xnf[:, st * 1024 + dch * 128:
                                st * 1024 + (dch + 1) * 128],
                            ar2[:, st * 2048 + tg * 512:
                                st * 2048 + tg * 512 + 512],
                            start=(st == 0), stop=(st == DCH - 1))
                    last.then_inc(cx.sem("pc"), 1)
    cx.on("t", bii_t)
    cx.bump("pf", NCHK * 128)
    cx.bump("pc", NCHK * 32)

    def bii_v(v):
        pa = pa0
        for c in range(NCHK):
            for st in range(DCH):
                for t_ in range(NTT):
                    i = st * NTT + t_
                    v.wait_ge(cx.sem("pf"), pf0 + c * 128 + i + 1)
                    if c >= 1:
                        v.wait_ge(cx.sem("pc"),
                                  pc0 + (c - 1) * 32 + (t_ // 4) * 8 + 8)
                    v.tensor_tensor(
                        ar2[:, st * 2048 + t_ * 128:
                            st * 2048 + (t_ + 1) * 128],
                        p[6 + i % 2].bitcast(BF16)[:, 0:128],
                        wslf[:, t_ * 128:(t_ + 1) * 128],
                        ALU.mult).then_inc(cx.sem("pa"), 1)
                    pa += 1
            for tg in range(4):
                for dch in range(DCH):
                    gi = c * 32 + tg * 8 + dch
                    v.wait_ge(cx.sem("pc"), pc0 + gi + 1)
                    xsl = xs[:, dch * TOK + tg * 512:dch * TOK +
                             (tg + 1) * 512]
                    v.tensor_tensor(xsl, p[dch][:, 0:512], xsl,
                                    ALU.add).then_inc(cx.sem("ab"), 1)
    cx.on("v", bii_v)
    cx.bump("pa", NCHK * 128)
    cx.bump("ab", NCHK * 32)
    cx.seal("v", "ab")
    cx.barrier()


def _ser(cx):
    return cx


def emit_l0_compose_a(cx, W):
    if cx.buf.get("l0a_comp_done"):
        return
    cx.buf["l0a_comp_done"] = True
    b = cx.buf
    p = b["p"]
    tsc = b["tsc"]
    ar3f = b["ar3"].bitcast(F32)
    comp = W["l0a_comp"]

    for c8 in range(DCH):
        lw = cx.dma("q", ar3f[:, 0:1024], W["ident1k"][c8 * 128:
                                                       (c8 + 1) * 128, :])
        cx.wait("q", "ds", lw)
        sw = cx.dma("q", comp[c8 * 128:(c8 + 1) * 128, 0:1024],
                    ar3f[:, 0:1024])
        cx.wait("q", "ds", sw)
    cx.seal("q", "ds")
    cx.barrier()

    specs = []
    for h in range(16):
        specs.append(("l0a_wqd_t", "l0a_upc", 32, h, 3072 + h * 64))
        specs.append(("l0a_wqd_t", "l0a_upe", 32, h, 3072 + h * 64 + 32))
        specs.append(("l0a_wkd_t", "l0a_kupc", 32, h, 1024 + h * 64))
        specs.append(("l0a_wkd_t", "l0a_kupe", 32, h, 1024 + h * 64 + 32))
        specs.append(("l0a_wvd_t", "l0a_vup", 64, h, 2048 + h * 64))
    ds0 = cx.cnt["ds"]
    pc0 = cx.cnt["pc"]
    pa0 = cx.cnt["pa"]
    it = []
    for (wt, up, w, h, dc) in specs:
        for c8 in range(DCH):
            it.append((wt, up, w, h, dc, c8))

    def cq(q):
        for i, (wt, up, w, h, dc, c8) in enumerate(it):
            if i > 0:
                q.wait_ge(cx.sem("pa"), pa0 + i)
                q.wait_ge(cx.sem("ds"), ds0 + 48 * i)
            q.dma_start(tsc[0:64, 0:128],
                        W[wt][h * 64:(h + 1) * 64,
                              c8 * 128:(c8 + 1) * 128]).then_inc(
                cx.sem("ds"), 16)
            q.wait_ge(cx.sem("ds"), ds0 + 48 * i + 16)
            q.dma_start(tsc[0:64, 128:128 + w],
                        W[up][:, :]).then_inc(cx.sem("ds"), 16)
            q.wait_ge(cx.sem("ds"), ds0 + 48 * i + 32)
            q.wait_ge(cx.sem("pa"), pa0 + i + 1)
            q.dma_start(comp[c8 * 128:(c8 + 1) * 128, dc:dc + w],
                        tsc[:, 256:256 + w]).then_inc(cx.sem("ds"), 16)
    cx.on("q", cq)
    cx.bump("ds", 48 * len(it))

    def ct(t):
        for i, (wt, up, w, h, dc, c8) in enumerate(it):
            t.wait_ge(cx.sem("ds"), ds0 + 48 * i + 32)
            t.matmul(p[0][:, 0:w], tsc[0:64, 0:128], tsc[0:64, 128:128 + w],
                     start=True, stop=True).then_inc(cx.sem("pc"), 1)
    cx.on("t", ct)
    cx.bump("pc", len(it))

    def cv(v):
        for i, (wt, up, w, h, dc, c8) in enumerate(it):
            v.wait_ge(cx.sem("pc"), pc0 + i + 1)
            v.tensor_copy(tsc[:, 256:256 + w], p[0][:, 0:w]).then_inc(
                cx.sem("pa"), 1)
    cx.on("v", cv)
    cx.bump("pa", len(it))
    cx.wait("q", "ds", cx.cnt["ds"])
    cx.seal("q", "ds")
    cx.seal("v", "pa")
    cx.barrier()


def emit_l0_compose_f(cx, W):
    if cx.buf.get("l0f_comp_done"):
        return
    cx.buf["l0f_comp_done"] = True
    b = cx.buf
    p = b["p"]
    tsc = b["tsc"]
    ar3f = b["ar3"].bitcast(F32)
    comp = W["l0f_comp"]

    for c8 in range(DCH):
        lw = cx.dma("q", ar3f[:, 0:1024], W["ident1k"][c8 * 128:
                                                       (c8 + 1) * 128, :])
        cx.wait("q", "ds", lw)
        sw = cx.dma("q", comp[c8 * 128:(c8 + 1) * 128, 0:1024],
                    ar3f[:, 0:1024])
        cx.wait("q", "ds", sw)

    def zv(v):
        return v.memset(ar3f[:, 1024:1280], 0.0)
    cx.on("v", zv)
    cx.barrier()
    z3 = ar3f[:, 1024:1280].rearrange("p (c o) -> p c o", o=32)
    for h in range(16):
        sw = cx.dma("q", comp[:, 2048 + h * 64 + 32:2048 + h * 64 +
                              64].rearrange("(c p) o -> p c o", p=128), z3)
        cx.wait("q", "ds", sw)
    lw = cx.dma("q", ar3f[:, 1280:1536].rearrange("p (c o) -> p c o", o=32),
                W["l0f_wkva"][:, 128:160].rearrange("(c p) o -> p c o",
                                                    p=128))
    cx.wait("q", "ds", lw)
    r3 = ar3f[:, 1280:1536].rearrange("p (c o) -> p c o", o=32)
    for h in range(16):
        sw = cx.dma("q", comp[:, 1024 + h * 64 + 32:1024 + h * 64 +
                              64].rearrange("(c p) o -> p c o", p=128), r3)
        cx.wait("q", "ds", sw)
    lw = cx.dma("q", ar3f[:, 2048:2560], W["l0f_wkvb"][:, :])
    cx.wait("q", "ds", lw)
    cx.seal("q", "ds")
    cx.barrier()

    ds0 = cx.cnt["ds"]
    pc0 = cx.cnt["pc"]
    pa0 = cx.cnt["pa"]

    PERC = 16 * (1 + 32)

    def kq(q):
        for c8 in range(DCH):
            if c8 > 0:
                q.wait_ge(cx.sem("ds"), ds0 + PERC * c8)
            q.dma_start(tsc[:, 0:128],
                        W["l0f_wkva_t"][0:128, c8 * 128:
                                        (c8 + 1) * 128]).then_inc(
                cx.sem("ds"), 16)
            q.wait_ge(cx.sem("ds"), ds0 + PERC * c8 + 16)
            q.wait_ge(cx.sem("pa"), pa0 + c8 + 1)
            n = 1
            for h in range(16):
                for off in (1024, 2048):
                    q.dma_start(comp[c8 * 128:(c8 + 1) * 128,
                                     off + h * 64:off + h * 64 + 32],
                                ar3f[:, 2560 + h * 32:
                                     2560 + (h + 1) * 32]).then_inc(
                        cx.sem("ds"), 16)
                    n += 1
                    q.wait_ge(cx.sem("ds"), ds0 + PERC * c8 + 16 * n)
    cx.on("q", kq)
    cx.bump("ds", PERC * DCH)

    def kt(t):
        for c8 in range(DCH):
            t.wait_ge(cx.sem("ds"), ds0 + PERC * c8 + 16)
            if c8 > 0:
                t.wait_ge(cx.sem("pa"), pa0 + c8)
            t.matmul(p[0][:, 0:512], tsc[:, 0:128], ar3f[:, 2048:2560],
                     start=True, stop=True).then_inc(cx.sem("pc"), 1)
    cx.on("t", kt)
    cx.bump("pc", DCH)

    def kv_(v):
        for c8 in range(DCH):
            v.wait_ge(cx.sem("pc"), pc0 + c8 + 1)
            if c8 > 0:
                v.wait_ge(cx.sem("ds"), ds0 + PERC * c8)
            v.tensor_copy(ar3f[:, 2560:3072], p[0][:, 0:512]).then_inc(
                cx.sem("pa"), 1)
    cx.on("v", kv_)
    cx.bump("pa", DCH)
    cx.wait("q", "ds", cx.cnt["ds"])
    cx.seal("q", "ds")
    cx.seal("v", "pa")
    cx.barrier()


def emit_l0_attn(cx, W):
    b = cx.buf
    emit_l0_compose_a(cx, W)

    def kb():
        emit_tokproj_rope_fm(cx, W, "l0a_comp", 1024,
                             lambda o: b["ar1"][:, o * TOK:(o + 1) * TOK],
                             "ropel0a", scale=1.0, tab4=True)

    def qp():
        cx.wait("g", "ds", cx.cnt["ds"])
        emit_tokproj_rope_fm(cx, W, "l0a_comp", 3072,
                             lambda o: b["xnf"][:, o * TOK:(o + 1) * TOK],
                             "ropel0a", scale=1.0, tab4=True)
    emit_attn_causal(cx, W, "l0a_comp", 0, 1024, 2048, "l0a_wc",
                     0.125, 1.0, k_builder=kb, q_pre=qp)


def emit_l0_moe(cx, W):
    b = cx.buf
    p = b["p"]
    xs, xnf = b["xs"], b["xnf"]
    ar3 = b["ar3"]
    tsc, wsl = b["tsc"], b["wsl"]
    ar3f = ar3.bitcast(F32)

    def xnf_of(ch):
        return xnf[:, ch * TOK:(ch + 1) * TOK]

    def gid0(g):
        g.memset(b["msk"][:, 640:768], 1.0).then_inc(cx.sem("ga"), 1)
        cx.bump("ga")
        g.wait_ge(cx.sem("ga"), cx.cnt["ga"])
        return g.affine_select(b["idf"][:, :], b["ones1"][:, :],
                               [[-1, 128]], ALU.is_equal, 0.0,
                               channel_multiplier=1)
    cx.on("g", gid0)
    cx.barrier()

    gw = cx.dma("g", wsl[:, 0:64].rearrange("p (kc o) -> p kc o", kc=8),
                W["l0f_wgate"].rearrange("(kc p) o -> p kc o", p=128))
    bw = cx.dma("q", tsc[:, 8:16],
                W["l0f_ebias"][0:1, :].broadcast_to([128, 8]))
    cx.wait("t", "gs", gw)
    cx.wait("v", "ds", bw)
    NTC = 16
    pc0 = cx.cnt["pc"]
    pa0 = cx.cnt["pa"]
    pf0 = cx.cnt["pf"]
    ds0 = cx.cnt["ds"]

    ab_pre = cx.cnt["ab"]
    sa0 = cx.cnt["sa"]
    ph0 = cx.cnt["ph"]

    def gt(t):
        for tc in range(NTC):
            if tc >= 2:
                t.wait_ge(cx.sem("pa"), pa0 + (tc - 2) * 12 + 1)
            last = None
            for k in range(DCH):
                last = t.matmul(p[4 + tc % 2][:, 0:8],
                                xnf_of(k)[:, tc * 128:(tc + 1) * 128],
                                wsl[:, k * 8:(k + 1) * 8],
                                start=(k == 0), stop=(k == DCH - 1))
            last.then_inc(cx.sem("pc"), 1)
            t.wait_ge(cx.sem("pa"), pa0 + tc * 12 + 12)
            if tc >= 1:
                t.wait_ge(cx.sem("ab"), ab_pre + tc)
            t.transpose(p[6][0:8, 0:128], tsc[:, 96:104],
                        b["idf"][:, :]).then_inc(cx.sem("pf"), 1)
    cx.on("t", gt)
    cx.bump("pc", NTC)
    cx.bump("pf", NTC)

    def gsx(s):
        for tc in range(NTC):
            s.wait_ge(cx.sem("ph"), ph0 + tc + 1)
            s.activation(tsc[:, 48:56], tsc[:, 48:56],
                         AF.Exp).then_inc(cx.sem("sa"), 1)
    cx.on("s", gsx)
    cx.bump("sa", NTC)

    def gv(v):
        pa = pa0
        ab = cx.cnt["ab"]
        for tc in range(NTC):
            v.wait_ge(cx.sem("pc"), pc0 + tc + 1)

            def st(ins):
                nonlocal pa
                ins.then_inc(cx.sem("pa"), 1)
                pa += 1
                v.wait_ge(cx.sem("pa"), pa)
            st(v.tensor_tensor(tsc[:, 32:40], p[4 + tc % 2][:, 0:8],
                               tsc[:, 8:16], ALU.add))
            st(v.reduce_max(tsc[:, 40:41], tsc[:, 32:40], axis=AX.X))
            v.tensor_scalar(tsc[:, 48:56], tsc[:, 32:40], tsc[:, 40:41],
                            None, ALU.subtract).then_inc(cx.sem("ph"), 1)
            v.wait_ge(cx.sem("sa"), sa0 + tc + 1)
            st(v.reduce_max(tsc[:, 56:57], tsc[:, 48:56], axis=AX.X))
            st(v.tensor_scalar(tsc[:, 57:65], tsc[:, 48:56], tsc[:, 56:57],
                               None, ALU.is_ge))
            st(v.scalar_tensor_tensor(tsc[:, 65:73], tsc[:, 57:65], -1e9,
                                      tsc[:, 48:56], ALU.mult, ALU.add))
            st(v.reduce_max(tsc[:, 73:74], tsc[:, 65:73], axis=AX.X))
            st(v.tensor_scalar(tsc[:, 74:82], tsc[:, 48:56], tsc[:, 73:74],
                               None, ALU.is_ge))
            st(v.tensor_tensor(tsc[:, 82:90], tsc[:, 48:56], tsc[:, 74:82],
                               ALU.mult))
            st(v.reduce_sum(tsc[:, 90:91], tsc[:, 82:90], axis=AX.X))
            st(v.reciprocal(tsc[:, 91:92], tsc[:, 90:91]))
            v.tensor_scalar(tsc[:, 96:104], tsc[:, 82:90], tsc[:, 91:92],
                            None, ALU.mult).then_inc(cx.sem("pa"), 1)
            pa += 1
            v.wait_ge(cx.sem("pa"), pa)
            v.engine_nop().then_inc(cx.sem("pa"), 1)
            pa += 1
            v.wait_ge(cx.sem("pa"), pa)
            v.wait_ge(cx.sem("pf"), pf0 + tc + 1)
            if tc >= 1:
                v.wait_ge(cx.sem("ds"), ds0 + 16 * tc)
            v.tensor_copy(tsc[0:8, 104:232],
                          p[6][0:8, 0:128]).then_inc(cx.sem("ab"), 1)
            ab += 1
        return None
    cx.on("v", gv)
    cx.bump("pa", NTC * 12)
    cx.bump("ph", NTC)
    cx.bump("ab", NTC)
    ab0 = cx.cnt["ab"] - NTC

    def gq(q):
        for tc in range(NTC):
            q.wait_ge(cx.sem("ab"), ab0 + tc + 1)
            if tc > 0:
                q.wait_ge(cx.sem("ds"), ds0 + 16 * tc)
            q.dma_start(W["l0_comb"][:, tc * 128:(tc + 1) * 128],
                        tsc[0:8, 104:232]).then_inc(cx.sem("ds"), 16)
    cx.on("q", gq)
    cx.bump("ds", 16 * NTC)
    cx.wait("q", "ds", cx.cnt["ds"])
    cx.seal("q", "ds")
    cx.seal("v", "ab")
    cx.barrier()

    ar1f = b["ar1"].bitcast(F32)
    for ch in range(DCH):
        lw = cx.dma("q", ar1f[:, 0:2048],
                    W["l0_xsave"][ch * 128:(ch + 1) * 128, :])
        cx.wait("v", "ds", lw)

        def rc(v, ch=ch):
            return v.scalar_tensor_tensor(
                xs[:, ch * TOK:(ch + 1) * TOK], xs[:, ch * TOK:(ch + 1) * TOK],
                0.5, ar1f[:, 0:2048], ALU.mult, ALU.add)
        cx.on("v", rc)
        cx.seal("v", "pa")
        cx.barrier()

    SIL = 0
    HST = 8192

    def h_of(ch):
        return ar3[:, HST + ch * TOK:HST + (ch + 1) * TOK]

    for e in range(9):
        shared = (e == 8)
        wg = W["l0f_swg" if shared else f"l0f_wg{e}"]
        wu = W["l0f_swu" if shared else f"l0f_wu{e}"]
        wd = W["l0f_swd" if shared else f"l0f_wd{e}"]

        def ev_silu(s, o, qc, pp):
            return s.activation(ar3[:, SIL + o * TOK + qc * QCOL:
                                    SIL + o * TOK + (qc + 1) * QCOL],
                                pp[:, :], AF.Silu)
        emit_proj(cx, wg, xnf_of, D, 512, evac_s=ev_silu)

        def ev_mul(v, o, qc, pp):
            return v.tensor_tensor(
                h_of(o)[:, qc * QCOL:(qc + 1) * QCOL],
                ar3[:, SIL + o * TOK + qc * QCOL:
                    SIL + o * TOK + (qc + 1) * QCOL], pp[:, :], ALU.mult)
        emit_proj(cx, wu, xnf_of, D, 512, evac_v=ev_mul)

        if not shared:
            for qc in range(NQC):
                bw2 = cx.dma("q", tsc[:, 0:512],
                             W["l0_comb"][e:e + 1, qc * 512:(qc + 1) * 512]
                             .broadcast_to([128, 512]))
                cx.wait("v", "ds", bw2)

                def sc(v, qc=qc):
                    last = None
                    for ch in range(4):
                        last = v.tensor_tensor(
                            h_of(ch)[:, qc * QCOL:(qc + 1) * QCOL],
                            h_of(ch)[:, qc * QCOL:(qc + 1) * QCOL],
                            tsc[:, 0:512], ALU.mult)
                    return last
                cx.on("v", sc)
                cx.seal("v", "pa")
                cx.barrier()

        def ev_acc(v, o, qc, pp):
            return v.scalar_tensor_tensor(
                xs[:, o * TOK + qc * QCOL:o * TOK + (qc + 1) * QCOL],
                pp[:, :], 0.5,
                xs[:, o * TOK + qc * QCOL:o * TOK + (qc + 1) * QCOL],
                ALU.mult, ALU.add)
        emit_proj(cx, wd, h_of, 512, D, evac_v=ev_acc, banks=(2, 3))


def emit_l0_ffn(cx, W):
    b = cx.buf
    xs = b["xs"]

    def xs_of(ch):
        return xs[:, ch * TOK:(ch + 1) * TOK]

    def xnf_of(ch):
        return b["xnf"][:, ch * TOK:(ch + 1) * TOK]

    g1 = cx.dma("q", b["vrow"][:, 48:56], W["l0f_ln1"][:, :])
    g2 = cx.dma("q", b["vrow"][:, 56:64], W["l0f_ln2"][:, :])
    cx.wait("v", "ds", g2)
    xsv = xs.rearrange("p (c t) -> p c t", c=DCH)
    sw = cx.dma("q", W["l0_xsave"].rearrange("(c p) t -> p c t", p=128), xsv)
    cx.wait("q", "ds", sw)
    cx.seal("q", "ds")
    cx.barrier()
    emit_norm(cx, xs_of, xs_of, kind="ln")
    emit_l0_compose_f(cx, W)

    def kb():
        emit_tokproj_rope_fm(cx, W, "l0f_comp", 1024,
                             lambda o: b["ar1"][:, o * TOK:(o + 1) * TOK],
                             "ropel0f", scale=1.0, tab4=True)

    def qp():
        cx.wait("g", "ds", cx.cnt["ds"])
        emit_tokproj_rope_fm(cx, W, "l0f_wq", 0,
                             lambda o: b["xnf"][:, o * TOK:(o + 1) * TOK],
                             "ropel0f", scale=1.0, tab4=True)
    emit_attn_causal(cx, W, "l0f_comp", 0, 1024, 2048, "l0f_wo",
                     0.125, 1.0, norm_kind="rms",
                     norm_w=lambda ch: b["vrow"][:, 48 + ch:49 + ch],
                     k_builder=kb, q_pre=qp)
    emit_norm(cx, xs_of, xnf_of, kind="rms",
              w_slice=lambda ch: b["vrow"][:, 56 + ch:57 + ch])
    emit_l0_moe(cx, W)


SUBLAYER_EMITTERS = {}
SUBLAYER_EMITTERS["l2_ffn"] = emit_l2_ffn
SUBLAYER_EMITTERS["l3_attn"] = emit_l3_attn
SUBLAYER_EMITTERS["l2_attn"] = emit_l2_attn
SUBLAYER_EMITTERS["l1_attn"] = emit_l1_attn
SUBLAYER_EMITTERS["l3_ffn"] = emit_l3_ffn
SUBLAYER_EMITTERS["l1_ffn"] = emit_l1_ffn
SUBLAYER_EMITTERS["l0_attn"] = emit_l0_attn
SUBLAYER_EMITTERS["l0_ffn"] = emit_l0_ffn


def prep_weights(inputs):
    w = {}
    w["l2f_wr_t"] = np.ascontiguousarray(inputs["L2_ffn_W_r"].T)
    w["l2f_wk_t"] = np.ascontiguousarray(inputs["L2_ffn_W_k"].T)
    w["l2f_wv_t"] = np.ascontiguousarray(inputs["L2_ffn_W_v"].T)
    w["l2f_mu_r"] = np.ascontiguousarray(
        np.asarray(inputs["L2_ffn_mu_r"], np.float32).reshape(DCH, 128).T)
    w["l2f_mu_k"] = np.ascontiguousarray(
        np.asarray(inputs["L2_ffn_mu_k"], np.float32).reshape(DCH, 128).T)
    w["l3a_wqkv"] = np.asarray(inputs["L3_attn_w_qkv"], np.float32)
    w["l3a_wout"] = np.asarray(inputs["L3_attn_w_out"], np.float32)
    w["l2a_wq"] = np.asarray(inputs["L2_attn_Wq"], np.float32)
    w["l2a_wk"] = np.asarray(inputs["L2_attn_Wk"], np.float32)
    w["l2a_wv"] = np.asarray(inputs["L2_attn_Wv"], np.float32)
    w["l2a_wo"] = np.asarray(inputs["L2_attn_Wo"], np.float32)
    w["l1a_wqkv_t"] = np.ascontiguousarray(
        np.asarray(inputs["L1_attn_w_qkv"], np.float32).T)
    w["l1a_wout_t"] = np.ascontiguousarray(
        np.asarray(inputs["L1_attn_w_out"], np.float32).T)
    w["l1a_wg1_t"] = np.ascontiguousarray(
        np.asarray(inputs["L1_attn_w_g1"], np.float32).T)
    w["l1a_wg2_t"] = np.ascontiguousarray(
        np.asarray(inputs["L1_attn_w_g2"], np.float32).T)
    w["l1a_bg1"] = np.asarray(inputs["L1_attn_b_g1"],
                              np.float32).reshape(64, 1)
    w["l1a_bg2"] = np.ascontiguousarray(
        np.asarray(inputs["L1_attn_b_g2"], np.float32).reshape(DCH, 128).T)
    w["l1a_gamma"] = np.ascontiguousarray(
        np.asarray(inputs["L1_attn_gamma"], np.float32).reshape(DCH, 128).T)
    w["l1a_beta"] = np.ascontiguousarray(
        np.asarray(inputs["L1_attn_beta"], np.float32).reshape(DCH, 128).T)
    w["l3f_win_t"] = np.ascontiguousarray(
        np.asarray(inputs["L3_ffn_w_in"], np.float32).T)
    w["l3f_wout_t"] = np.ascontiguousarray(
        np.asarray(inputs["L3_ffn_w_out"], np.float32).T)
    w["l3f_w1_t"] = np.ascontiguousarray(
        np.asarray(inputs["L3_ffn_w1"], np.float32).T)
    w["l3f_w2_t"] = np.ascontiguousarray(
        np.asarray(inputs["L3_ffn_w2"], np.float32).transpose(0, 2, 1))
    w["l3f_wg_t"] = np.ascontiguousarray(
        np.asarray(inputs["L3_ffn_wg"], np.float32).T)
    w["l3f_we_t"] = np.ascontiguousarray(
        np.asarray(inputs["L3_ffn_we"], np.float32).T)
    w["ident1k"] = np.eye(1024, dtype=np.float32)
    w["l0a_wqd_t"] = np.ascontiguousarray(
        np.asarray(inputs["L0_attn_Wq_down"], np.float32).T)
    w["l0a_wkd_t"] = np.ascontiguousarray(
        np.asarray(inputs["L0_attn_Wk_down"], np.float32).T)
    w["l0a_wvd_t"] = np.ascontiguousarray(
        np.asarray(inputs["L0_attn_Wv_down"], np.float32).T)
    w["l0a_upc"] = np.asarray(inputs["L0_attn_Wq_up_c"], np.float32)
    w["l0a_upe"] = np.asarray(inputs["L0_attn_Wq_up_e"], np.float32)
    w["l0a_kupc"] = np.asarray(inputs["L0_attn_Wk_up_c"], np.float32)
    w["l0a_kupe"] = np.asarray(inputs["L0_attn_Wk_up_e"], np.float32)
    w["l0a_vup"] = np.asarray(inputs["L0_attn_Wv_up"], np.float32)
    w["l0a_wc"] = np.asarray(inputs["L0_attn_Wc"], np.float32)
    w["l0f_wq"] = np.asarray(inputs["L0_ffn_Wq"], np.float32)
    w["l0f_wkva"] = np.asarray(inputs["L0_ffn_Wkva"], np.float32)
    w["l0f_wkva_t"] = np.ascontiguousarray(
        np.asarray(inputs["L0_ffn_Wkva"], np.float32).T)
    w["l0f_wkvb"] = np.asarray(inputs["L0_ffn_Wkvb"], np.float32)
    w["l0f_wo"] = np.asarray(inputs["L0_ffn_Wo"], np.float32)
    w["l0f_ln1"] = np.ascontiguousarray(
        np.asarray(inputs["L0_ffn_w_ln1"], np.float32).reshape(DCH, 128).T)
    w["l0f_ln2"] = np.ascontiguousarray(
        np.asarray(inputs["L0_ffn_w_ln2"], np.float32).reshape(DCH, 128).T)
    w["l0f_wgate"] = np.asarray(inputs["L0_ffn_Wgate"], np.float32)
    w["l0f_ebias"] = np.asarray(inputs["L0_ffn_expert_bias"],
                                np.float32).reshape(1, 8)
    for e in range(8):
        w[f"l0f_wg{e}"] = np.ascontiguousarray(
            np.asarray(inputs["L0_ffn_Wg"], np.float32)[e])
        w[f"l0f_wu{e}"] = np.ascontiguousarray(
            np.asarray(inputs["L0_ffn_Wu"], np.float32)[e])
        w[f"l0f_wd{e}"] = np.ascontiguousarray(
            np.asarray(inputs["L0_ffn_Wd"], np.float32)[e])
    w["l0f_swg"] = np.ascontiguousarray(
        np.asarray(inputs["L0_ffn_sWg"], np.float32)[0])
    w["l0f_swu"] = np.ascontiguousarray(
        np.asarray(inputs["L0_ffn_sWu"], np.float32)[0])
    w["l0f_swd"] = np.ascontiguousarray(
        np.asarray(inputs["L0_ffn_sWd"], np.float32)[0])
    w["l1f_gx"] = np.ascontiguousarray(
        np.asarray(inputs["L1_ffn_gamma_x"], np.float32).reshape(DCH, 128).T)
    w["l1f_gsrow"] = np.asarray(inputs["L1_ffn_gamma_slot"],
                                np.float32).reshape(1, D)
    w["l1f_semb"] = np.ascontiguousarray(
        np.asarray(inputs["L1_ffn_slot_embeds"], np.float32).reshape(4096, D))
    geb = np.zeros((1, 16), np.float32)
    geb[0, 0:2] = np.asarray(inputs["L3_ffn_gb"], np.float32)
    geb[0, 8:16] = np.asarray(inputs["L3_ffn_eb"], np.float32)
    w["l3f_geb"] = geb
    return w


def prep_percore(c, inputs=None):
    j = c % GRP
    d = {}
    if inputs is not None:
        w1 = np.asarray(inputs["L1_ffn_w1"], np.float32)
        w2 = np.asarray(inputs["L1_ffn_w2"], np.float32)
        d["l1f_w1_loc"] = np.ascontiguousarray(
            w1[:, j * 1024:(j + 1) * 1024])
        d["l1f_w2_loc"] = np.ascontiguousarray(
            w2[j * 1024:(j + 1) * 1024, :])
    pv = ((j - 1) % GRP) * 2 * D + (D if j == 0 else 0)
    pb = np.zeros((1, 16), np.int32)
    pb[0, 0] = pv
    d["prev_base"] = pb
    sel = np.zeros((128, 4), np.float32)
    sel[:, 3 if j == 0 else (j - 1)] = 1.0
    d["pvsel"] = sel
    tm = np.zeros((4, 128, 128), np.float32)
    pp, ff = np.meshgrid(np.arange(128), np.arange(128), indexing="ij")
    for r in range(4):
        delta = 0 if j >= r else -1
        tm[r] = (pp <= ff + delta).astype(np.float32)
    d["trimask"] = tm
    pos = (4 * np.arange(SLOC) + j).astype(np.float64)
    i32 = np.arange(32, dtype=np.float64)
    coss, sins = [], []
    for sc_ in (1.0, 4.0, 16.0):
        freqs = 1.0 / (10000.0 * sc_) ** (2.0 * i32 / 64.0)
        ang = pos[:, None] * freqs[None, :]
        coss.append(np.cos(ang))
        sins.append(np.sin(ang))
    cosv = np.mean(coss, 0).astype(np.float32)
    sinv = np.mean(sins, 0).astype(np.float32)
    ck_ = np.tile(cosv, (1, 16))
    sk_ = np.tile(sinv, (1, 16))
    d["ropek"] = np.concatenate([ck_, sk_], 1).astype(np.float32)
    rq = np.zeros((512, 512), np.float32)
    for qb in range(4):
        for tc in range(4):
            rows = slice(qb * 128, (qb + 1) * 128)
            src_rows = slice(qb * 512 + tc * 128, qb * 512 + (tc + 1) * 128)
            for h in range(2):
                cslc = slice(tc * 64 + h * 32, tc * 64 + (h + 1) * 32)
                rq[rows, cslc.start:cslc.stop] = cosv[src_rows]
                rq[rows, 256 + cslc.start:256 + cslc.stop] = sinv[src_rows]
    d["ropeq"] = rq
    posf = (4 * np.arange(SLOC) + j).astype(np.float64)

    def mk_l0tab(theta, quirk):
        C0 = np.ones((SLOC, 32)); S0 = np.zeros((SLOC, 32))
        S1 = np.zeros((SLOC, 32)); C1 = np.ones((SLOC, 32))
        if quirk:
            invf = 1.0 / theta ** (2.0 * np.arange(16) / 32.0)
            cosv = np.cos(posf[:, None] * invf[None, :])
            sinv = np.sin(posf[:, None] * invf[None, :])
            for m in range(16):
                C0[:, 16 + m] = cosv[:, (2 * m) % 16]
                S0[:, 16 + m] = sinv[:, (2 * m) % 16]
                C1[:, 16 + m] = cosv[:, (2 * m + 1) % 16]
                S1[:, 16 + m] = sinv[:, (2 * m + 1) % 16]
        else:
            invf = 1.0 / theta ** (2.0 * np.arange(16) / 32.0)
            ang = posf[:, None] * invf[None, :]
            for m in range(16):
                C0[:, 16 + m] = np.cos(ang[:, m])
                S0[:, 16 + m] = np.sin(ang[:, m])
                C1[:, 16 + m] = np.cos(ang[:, m])
                S1[:, 16 + m] = np.sin(ang[:, m])
        blocks = [np.tile(B, (1, 16)) for B in (C0, S0, S1, C1)]
        return np.concatenate(blocks, 1).astype(np.float32)
    d["ropel0a"] = mk_l0tab(10000.0, True)
    d["ropel0f"] = mk_l0tab(100000.0, False)
    return d


def _shard_x(x):
    shards = []
    for c in range(NCORE):
        bidx = c // GRP
        j = c % GRP
        tokens = x[bidx, j::GRP, :]
        shards.append(np.ascontiguousarray(tokens.T))
    return shards


def _unshard_x(shards):
    out = np.zeros((B, T, D), np.float32)
    for c in range(NCORE):
        bidx = c // GRP
        j = c % GRP
        out[bidx, j::GRP, :] = np.asarray(shards[c], dtype=np.float32).T
    return out


INTERNAL_DRAM = {
    "xg_in": ([2 * D, TOK], BF16),
    "xg_out": ([GRP * 2 * D, TOK], BF16),
    "xgk_in": ([D, TOK], BF16),
    "xgk_out": ([GRP * D, TOK], BF16),
    "xgv_in": ([TOK, D], BF16),
    "xgv_out": ([GRP * TOK, D], BF16),
    "kmx_in": ([1, 16], F32),
    "kmx_out": ([1, 16], F32),
    "rbc": ([2, 512], F32),
    "rbc2": ([1, 512], BF16),
    "l1_l": ([128, 65536], BF16),
    "l1_se": ([128, 32768], BF16),
    "l1_sp_in": ([128, 32768], F32),
    "l1_sp_out": ([128, 32768], F32),
    "l1_pp_in": ([128, 32768], F32),
    "l1_pp_out": ([128, 32768], F32),
    "l1_sm_in": ([32, 128], F32),
    "l1_sm_out": ([32, 128], F32),
    "l1_sd_in": ([32, 128], F32),
    "l1_sd_out": ([32, 128], F32),
    "l1_dr": ([32, 128], F32),
    "l1_rr": ([16, 128], F32),
    "l0a_comp": ([1024, 4096], F32),
    "l0f_comp": ([1024, 4096], F32),
    "l0_xsave": ([1024, 2048], F32),
    "l0_comb": ([8, 2048], F32),
}

PERCORE_SPECS = {
    "l1f_w1_loc": ([1024, 1024], F32),
    "l1f_w2_loc": ([1024, 1024], F32),
    "prev_base": ([1, 16], mybir.dt.int32),
    "pvsel": ([128, 4], F32),
    "trimask": ([4, 128, 128], F32),
    "ropek": ([SLOC, 1024], F32),
    "ropeq": ([512, 512], F32),
    "ropel0a": ([SLOC, 2048], F32),
    "ropel0f": ([SLOC, 2048], F32),
}


def build_program(sublayers, weights_np):
    nc = bass.Bass("TRN2", target_bir_lowering=False)
    x_in = nc.dram_tensor("x_in", [D, TOK], F32, kind="ExternalInput")
    x_out = nc.dram_tensor("x_out", [D, TOK], F32, kind="ExternalOutput")
    wd = {}
    for name, arr in weights_np.items():
        wd[name] = nc.dram_tensor(name, list(arr.shape), F32,
                                  kind="ExternalInput")
    for name, (shape, dt) in PERCORE_SPECS.items():
        wd[name] = nc.dram_tensor(name, shape, dt, kind="ExternalInput")
    import os as _os
    _ik = "ExternalOutput" if _os.environ.get("DBG_INTERNAL") else "Internal"
    for name, (shape, dt) in INTERNAL_DRAM.items():
        wd[name] = nc.dram_tensor(name, shape, dt, kind=_ik)

    with ExitStack() as stack:
        block = stack.enter_context(nc.Block())
        cx = Cx(nc, block, stack)
        alloc_pool(cx)
        b = cx.buf
        xr = x_in.rearrange("(c p) t -> p c t", p=128)
        xsv = b["xs"].rearrange("p (c t) -> p c t", c=DCH)
        lw = cx.dma("q", xsv, xr)
        cx.wait("v", "ds", lw)
        emit_init_consts(cx)

        for sl in sublayers:
            SUBLAYER_EMITTERS[sl](cx, wd)

        xo = x_out.rearrange("(c p) t -> p c t", p=128)
        sw = cx.dma("q", xo, xsv)
        cx.wait("q", "ds", sw)
    return nc


def run_chain(x_full, inputs, sublayers):
    w = prep_weights(inputs)
    shards = _shard_x(np.asarray(x_full, np.float32))
    nc = build_program(sublayers, w)
    in_maps = []
    for c in range(NCORE):
        m = dict(w)
        m.update(prep_percore(c, inputs))
        m["x_in"] = shards[c]
        in_maps.append(m)
    res = run_bass_kernel_spmd(nc, in_maps, core_ids=list(range(NCORE)))
    return _unshard_x([res.results[c]["x_out"] for c in range(NCORE)])


def run_chain_sim(x_full, inputs, sublayers):
    from concourse.bass_interp import MultiCoreSim
    w = prep_weights(inputs)
    shards = _shard_x(np.asarray(x_full, np.float32))
    nc = build_program(sublayers, w)
    import os
    nw = int(os.environ.get('SIM_WORKERS', '8'))
    sim = MultiCoreSim(nc, NCORE, num_workers=nw)
    for c in range(NCORE):
        cs = sim.cores[c]
        for k, v in w.items():
            cs.tensor(k)[:] = v
        for k, v in prep_percore(c, inputs).items():
            cs.tensor(k)[:] = v
        cs.tensor("x_in")[:] = shards[c]
    sim.simulate(check_with_hw=False)
    return _unshard_x([sim.cores[c].tensor("x_out") for c in range(NCORE)])


FULL_CHAIN = ["l0_ffn", "l0_attn", "l0_ffn",
              "l1_ffn", "l1_attn", "l1_ffn",
              "l2_ffn", "l2_attn", "l2_ffn",
              "l3_ffn", "l3_attn", "l3_ffn"]


def kernel(**inputs):
    x = np.asarray(inputs["L0_attn_x"], np.float32)
    out = run_chain(x, inputs, FULL_CHAIN)
    return np.asarray(out, np.float32)
